# version 05
# baseline (speedup 1.0000x reference)
import numpy as np
import concourse.bass as bass
import concourse.mybir as mybir
from concourse.bass_core_rust import IndirectOffsetOnAxis
from concourse.bass_types import AP as BassAP
from concourse.bass_utils import run_bass_kernel_spmd


def mkap(apref, offset, dims):
    return BassAP(apref.tensor, offset, [list(d) for d in dims])

F32 = mybir.dt.float32
I16 = mybir.dt.int16
I32 = mybir.dt.int32
AF = mybir.ActivationFunctionType
ALU = mybir.AluOpType
AX = mybir.AxisListType.X

LN_EPS = 1e-5
NEG = -1e38


class Geo:
    def __init__(self, T=16384, TC=512):
        self.T = T
        self.NCORES = 8
        self.TL = T // 8
        assert TC >= 128 and TC % 128 == 0 and TC <= 512
        self.TC = TC
        self.NCHK = self.TL // TC
        self.C = 1024
        self.NJ = 8
        self.F = 4096
        self.FS = 512
        self.NFS = self.F // self.FS
        self.I = 2048
        self.NIO = self.I // 128
        self.E = 8
        self.CAP = T // 8
        self.BIG = 20000
        self.STILE = min(512, self.CAP)
        self.NST = self.CAP // self.STILE
        self.NSS = self.STILE // 128


class Sem:
    def __init__(self, nc, name):
        self.h = nc.semaphore(name).__enter__()
        self.n = 0


class Ctx:
    def __init__(self, nc, g):
        self.nc = nc
        self.g = g
        self.LD = Sem(nc, "sLD")
        self.SP = Sem(nc, "sSP")
        self.MM = Sem(nc, "sMM")
        self.VE = Sem(nc, "sVE")
        self.SE = Sem(nc, "sSE")
        self.CC = Sem(nc, "sCC")
        self.LQ = Sem(nc, "sLQ")
        self.SQ = Sem(nc, "sSQ")
        self.sem = {"LD": self.LD, "SP": self.SP, "MM": self.MM,
                    "VE": self.VE, "SE": self.SE, "CC": self.CC,
                    "LQ": self.LQ, "SQ": self.SQ}
        self.last = {}
        self.ACOLS = 43008
        self.arena = nc.alloc_sbuf_tensor("arena", [128, self.ACOLS], F32)
        self.aoff = 0
        self.amax = 0
        self.ps = [nc.alloc_psum_tensor(f"psb{i}", [128, 512], F32) for i in range(8)]
        self.bank_free = [("VE", 0) for _ in range(8)]
        self.dbg_set = set()

    def amark(self):
        return self.aoff

    def areset(self, m):
        self.aoff = m

    def aalloc(self, cols):
        assert self.aoff + cols <= self.ACOLS, f"arena overflow {self.aoff}+{cols}"
        t = self.arena[:, self.aoff:self.aoff + cols]
        self.aoff += cols
        self.amax = max(self.amax, self.aoff)
        return t

    def dram(self, name, shape, dtype=F32):
        kind = "ExternalOutput" if name in self.dbg_set else None
        if kind:
            return self.nc.dram_tensor(name, shape, dtype, kind=kind)
        return self.nc.dram_tensor(name, shape, dtype)

    def _tr(self, eng, inst):
        self.last[eng] = inst
        return inst

    def t(self, op, *a, **k):
        return self._tr("t", getattr(self.nc.tensor, op)(*a, **k))

    DRAINS = True

    def v(self, op, *a, **k):
        inst = getattr(self.nc.vector, op)(*a, **k)
        if self.DRAINS:
            inst = self.nc.vector.drain()
        return self._tr("v", inst)

    def s(self, op, *a, **k):
        inst = getattr(self.nc.scalar, op)(*a, **k)
        if self.DRAINS:
            inst = self.nc.scalar.drain()
        return self._tr("s", inst)

    def mm_inc(self):
        self.last["t"].then_inc(self.MM.h, 1)
        self.MM.n += 1
        return self.MM.n

    def ve_inc(self):
        self.last["v"].then_inc(self.VE.h, 1)
        self.VE.n += 1
        return self.VE.n

    def se_inc(self):
        self.last["s"].then_inc(self.SE.h, 1)
        self.SE.n += 1
        return self.SE.n

    def ld(self, out, in_, transpose=False, slow=False):
        kw = {}
        if transpose:
            kw["transpose"] = True
        if slow:
            kw["allow_slow_non_contiguous"] = True
        self.nc.sync.dma_start(out=out, in_=in_, **kw).then_inc(self.LD.h, 16)
        self.LD.n += 16
        return self.LD.n

    def sp(self, out, in_):
        self.nc.gpsimd.dma_start(out=out, in_=in_).then_inc(self.SP.h, 16)
        self.SP.n += 16
        return self.SP.n

    def lq(self):
        self.nc.sync.wait_ge(self.LD.h, self.LD.n)
        self.nc.sync.sem_inc(self.LQ.h, 1)
        self.LQ.n += 1
        return self.LQ.n

    def sq(self):
        self.nc.gpsimd.wait_ge(self.SP.h, self.SP.n)
        self.nc.gpsimd.sem_inc(self.SQ.h, 1)
        self.SQ.n += 1
        return self.SQ.n

    def wait(self, eng, semname, val):
        if val <= 0:
            return
        e = {"t": self.nc.tensor, "v": self.nc.vector, "s": self.nc.scalar,
             "g": self.nc.gpsimd, "y": self.nc.sync}[eng]
        e.wait_ge(self.sem[semname].h, val)

    def bank_wait(self, i):
        s, v = self.bank_free[i]
        self.wait("t", s, v)

    def bank_set(self, i, semname, val):
        self.bank_free[i] = (semname, val)

    def phase_barrier(self):
        self.lq()
        self.sq()
        cur = {"LQ": self.LQ.n, "SQ": self.SQ.n, "MM": self.MM.n,
               "VE": self.VE.n, "SE": self.SE.n}
        for eng in ("y", "g", "v", "s", "t"):
            for k, v in cur.items():
                self.wait(eng, k, v)

    def dbg_dump(self, name, ap_, shape, dtype=F32):
        if name not in self.dbg_set:
            return
        t = self.nc.dram_tensor(name, shape, dtype, kind="ExternalOutput")
        self.wait("g", "VE", self.VE.n)
        self.wait("g", "SE", self.SE.n)
        self.sp(t.ap(), ap_)

    def allgather(self, bin_ap, bout_ap):
        self.nc.gpsimd.collective_compute(
            "AllGather", ALU.bypass,
            replica_groups=[list(range(self.g.NCORES))],
            ins=[bin_ap.opt()], outs=[bout_ap.opt()],
        ).then_inc(self.CC.h, 1)
        self.CC.n += 1
        return self.CC.n


def build_program(g: Geo, dbg=()):
    nc = bass.Bass("TRN2", target_bir_lowering=False, debug=False,
                   detect_race_conditions=False)
    ctx = Ctx(nc, g)
    ctx.dbg_set = set(dbg)
    C, NJ, TC, NCHK, TL, T = g.C, g.NJ, g.TC, g.NCHK, g.TL, g.T
    V = ctx.v
    S = ctx.s
    T_ = ctx.t

    ap = {}
    def inp(name, shape, dtype=F32):
        ap[name] = nc.dram_tensor(name, shape, dtype, kind="ExternalInput").ap()

    inp("xT", [C, TL + 1])
    for nm in ("kwT", "vwT", "rwT", "owT", "frwT"):
        inp(nm, [C, C])
    inp("fkwT", [C, g.F]); inp("fvwT", [g.F, C])
    inp("w1T", [C, g.I]); inp("w2T", [g.I, C])
    inp("embT", [C, g.E]); inp("wmod", [C, 6 * C])
    inp("cmod", [128, NJ]); inp("vecs", [11, 128, NJ])
    inp("ident", [128, 128]); inp("iotarev", [128, TC])
    inp("oh8id", [8, 8]); inp("ohmy", [8, 1]); inp("ohs", [8, 1]); inp("ecap", [8, 1])
    inp("masklt8", [8, 8]); inp("mneg8", [8, 8]); inp("tlm8", [8, 8])
    inp("mprev8", [8, 8]); inp("ohown", [8, 8])
    inp("maskfirst", [8, 1]); inp("maskfirstP", [128, 1])
    inp("xbnd_in", [128, NJ])

    y_out = nc.dram_tensor("y", [TL, C], F32, kind="ExternalOutput").ap()

    kvs_sp = ctx.dram("kvs_sp", [3 * C, TL]).ap()
    h0_sp = ctx.dram("h0_sp", [C, TL]).ap()
    a_sp = ctx.dram("a_sp", [C, TL]).ap()
    xk2_sp = ctx.dram("xk2_sp", [C, TL]).ap()
    sig_sp = ctx.dram("sig_sp", [C, TL]).ap()
    xmid_sp = ctx.dram("xmid_sp", [TL, C]).ap()
    svrow_d = ctx.dram("svrow_d", [1, T], I16).ap()
    fld_d = ctx.dram("fld_d", [4, 16 * 128], I16).ap()
    m12_d = ctx.dram("m12_d", [16, TL]).ap()
    gbrow_d = ctx.dram("gbrow_d", [1, C]).ap()
    xbuf = ctx.dram("xbuf", [g.CAP, C]).ap()
    if "h1_dbg" in ctx.dbg_set:
        h1_dbg = nc.dram_tensor("h1_dbg", [TL, C], F32, kind="ExternalOutput").ap()
    if "eout_dbg" in ctx.dbg_set:
        eout_dbg = nc.dram_tensor("eout_dbg", [g.CAP, C], F32, kind="ExternalOutput").ap()
    ag1_in = ctx.dram("ag1_in", [24, 128]).ap()
    ag1_out = ctx.dram("ag1_out", [192, 128]).ap()
    ag1b_in = ctx.dram("ag1b_in", [8, 128]).ap()
    ag1b_out = ctx.dram("ag1b_out", [64, 128]).ap()
    ag2_in = ctx.dram("ag2_in", [TL, C]).ap()
    ag2_out = ctx.dram("ag2_out", [T, C]).ap()
    ag2b_in = ctx.dram("ag2b_in", [8, TL]).ap()
    ag2b_out = ctx.dram("ag2b_out", [64, TL]).ap()
    ag3_in = ctx.dram("ag3_in", [g.CAP, C]).ap()
    ag3_out = ctx.dram("ag3_out", [8 * g.CAP, C]).ap()

    A = nc.alloc_sbuf_tensor
    vconst = A("vconst", [128, 11 * NJ], F32)
    dconst = A("dconst", [128, 8 * NJ], F32)
    ident = A("identt", [128, 128], F32)
    embTs = A("embTs", [128, NJ * g.E], F32)
    oh8id = A("oh8idt", [8, 8], F32)
    ohmy = A("ohmyt", [8, 1], F32)
    ohs = A("ohst", [8, 1], F32)
    ecap = A("ecapt", [8, 1], F32)
    masklt8 = A("masklt8t", [8, 8], F32)
    mneg8 = A("mneg8t", [8, 8], F32)
    tlm8 = A("tlm8t", [8, 8], F32)
    mprev8 = A("mprev8t", [8, 8], F32)
    ohown = A("ohownt", [8, 8], F32)
    maskfirst = A("maskfirstt", [8, 1], F32)
    maskfirstP = A("maskfirstPt", [128, 1], F32)
    onescol = A("onescol", [128, 1], F32)
    onesrow = A("onesrow", [1, 128], F32)
    ones8 = A("ones8", [8, 1], F32)
    state0 = A("state0", [128, 3 * NJ], F32)
    runsum = A("runsum", [128, 3 * NJ], F32)
    csum = A("csum", [128, 3 * NJ], F32)
    u2bnd = A("u2bnd", [128, NJ], F32)
    ubnd = A("ubnd", [128, NJ], F32)
    usave = A("usave", [128, NJ], F32)
    u2save = A("u2save", [128, NJ], F32)
    trow = A("trow", [8, 3 * 128], F32)
    routebuf = A("routebuf", [128, (TL // 128) * 18], F32)
    svtok = A("svtok", [128, T // 128], I16)
    fldtok = A("fldtok", [128, 64], I16)
    fld32 = A("fld32", [128, 64], F32)
    idxf1 = A("idxf1", [128, TL // 128], I32)
    idxf2 = A("idxf2", [128, TL // 128], I32)
    idx32 = A("idx32", [128, T // 128], I32)
    stats = A("stats", [1, 2608], F32)
    R16 = A("R16", [1, 1024], I16)
    SW = max(TC, 512)
    S1 = A("S1", [128, SW], F32)
    S2 = A("S2", [128, SW], F32)
    S3 = A("S3", [128, SW], F32)
    S4 = A("S4", [128, SW], F32)
    SQ2 = A("SQ2", [128, SW], F32)
    SB = A("SBs", [128, 32], F32)
    zz = A("zzt", [128, 1024], F32)
    lnscr = A("lnscr", [128, 2 * TC], F32)
    xbnd = A("xbndt", [128, NJ], F32)

    co = {"w": 0, "sa": 1, "ca1": 2, "ga": 3, "sb": 4, "cb1": 5, "gb": 6}
    vi = {"ln1w": 0, "ln1b": 1, "ln2w": 2, "ln2b": 3, "tdec": 4, "tfir": 5,
          "tmk": 6, "tmv": 7, "tmr": 8, "ftmk": 9, "ftmr": 10}

    def vc(name, j):
        return vconst[:, vi[name] * NJ + j: vi[name] * NJ + j + 1]

    def vct(name):
        return vconst[:, vi[name] * NJ:(vi[name] + 1) * NJ]

    def dc(name, j):
        return dconst[:, co[name] * NJ + j: co[name] * NJ + j + 1]

    def dct(name):
        return dconst[:, co[name] * NJ:(co[name] + 1) * NJ]

    def a_(buf, j, w=TC):
        return buf[:, j * w:(j + 1) * w]

    def a1_(buf, j):
        return buf[:, j * (TC + 1):(j + 1) * (TC + 1)]

    ctx.ld(vconst[:], mkap(ap["vecs"], 0, [[8, 128], [1024, 11], [1, NJ]]))
    ctx.ld(ident[:], ap["ident"])
    ctx.ld(embTs[:], mkap(ap["embT"], 0, [[g.E, 128], [128 * g.E, NJ], [1, g.E]]))
    for nm, tt in (("oh8id", oh8id), ("ohmy", ohmy), ("ohs", ohs), ("ecap", ecap),
                   ("masklt8", masklt8), ("mneg8", mneg8), ("tlm8", tlm8),
                   ("mprev8", mprev8), ("ohown", ohown), ("maskfirst", maskfirst),
                   ("maskfirstP", maskfirstP)):
        ctx.ld(tt[:], ap[nm])
    ctx.ld(SB[:, 0:NJ], ap["cmod"])
    ld_xb = ctx.ld(xbnd[:], ap["xbnd_in"])
    ld_const = ctx.lq()

    ctx.wait("v", "LQ", ld_const)
    V("memset", onescol[:], 1.0)
    V("memset", onesrow[:], 1.0)
    V("memset", ones8[:], 1.0)
    V("memset", zz[:], 0.0)
    V("memset", dconst[:], 0.0)
    V("memset", runsum[:, 0:NJ], NEG)
    V("memset", runsum[:, NJ:3 * NJ], 0.0)
    ve0 = ctx.ve_inc()
    ctx.wait("s", "LQ", ld_const)
    ctx.wait("s", "VE", ve0)
    S("activation", dct("w"), vct("tdec"), AF.Exp)
    S("activation", SB[:, NJ:2 * NJ], SB[:, 0:NJ], AF.Silu)
    se_silu = ctx.se_inc()
    ctx.wait("v", "SE", se_silu)
    V("tensor_scalar_mul", dct("w"), dct("w"), -1.0)
    ve_w = ctx.ve_inc()

    am0 = ctx.amark()
    modsb = ctx.aalloc(6 * C)
    wmbuf = ctx.aalloc(2 * NJ * 512)
    mm_prev = {}
    for o in range(12):
        ld_wm = 0
        for j in range(NJ):
            if o >= 2:
                ctx.wait("y", "MM", mm_prev[o - 2])
            ld_wm = ctx.ld(wmbuf[:, (o % 2) * NJ * 512 + j * 512:(o % 2) * NJ * 512 + (j + 1) * 512],
                           ap["wmod"][j * 128:(j + 1) * 128, o * 512:(o + 1) * 512])
        ld_wm = ctx.lq()
        bank = o % 2
        ctx.bank_wait(bank)
        ctx.wait("t", "LQ", ld_wm)
        ctx.wait("t", "SE", se_silu)
        for j in range(NJ):
            T_("matmul", ctx.ps[bank][0:1, 0:512], SB[:, NJ + j:NJ + j + 1],
               wmbuf[:, (o % 2) * NJ * 512 + j * 512:(o % 2) * NJ * 512 + (j + 1) * 512],
               start=(j == 0), stop=(j == NJ - 1))
        mm_prev[o] = ctx.mm_inc()
        ctx.wait("v", "MM", mm_prev[o])
        V("tensor_copy", modsb[0:1, o * 512:(o + 1) * 512], ctx.ps[bank][0:1, 0:512])
        ctx.bank_set(bank, "VE", ctx.ve_inc())

    bank = 2
    ctx.bank_wait(bank)
    ctx.wait("t", "VE", ctx.VE.n)
    for v6 in range(6):
        for j in range(NJ):
            T_("matmul", ctx.ps[bank][:, v6 * NJ + j:v6 * NJ + j + 1],
               modsb[0:1, v6 * C + j * 128:v6 * C + (j + 1) * 128],
               onesrow[0:1, 0:1], start=True, stop=True)
    mm_modT = ctx.mm_inc()
    ctx.wait("v", "MM", mm_modT)
    V("tensor_copy", dconst[:, NJ:7 * NJ], ctx.ps[bank][:, 0:48])
    V("tensor_scalar_add", dct("ca1"), dct("ca1"), 1.0)
    V("tensor_scalar_add", dct("cb1"), dct("cb1"), 1.0)
    for nm in ("ga", "gb"):
        V("tensor_single_scalar", SB[:, 16:16 + NJ], dct(nm), 0.0, ALU.is_ge)
        V("tensor_scalar", SB[:, 16:16 + NJ], SB[:, 16:16 + NJ], 2.0, 1.0, ALU.mult, ALU.subtract)
        V("tensor_mul", SB[:, 24:24 + NJ], dct(nm), SB[:, 16:16 + NJ])
        V("tensor_scalar_max", SB[:, 24:24 + NJ], SB[:, 24:24 + NJ], 0.25)
        V("tensor_mul", dct(nm), SB[:, 16:16 + NJ], SB[:, 24:24 + NJ])
    ve_dconst = ctx.ve_inc()
    ctx.bank_set(bank, "VE", ve_dconst)

    ctx.dbg_dump("dconst_d", dconst[:], [128, 8 * NJ])
    ctx.dbg_dump("vconst_d", vconst[:], [128, 11 * NJ])
    bank = 3
    ctx.bank_wait(bank)
    ctx.wait("t", "VE", ve_dconst)
    T_("transpose", ctx.ps[bank][0:8, 0:128], dct("w"), ident[:])
    T_("transpose", ctx.ps[bank][0:8, 128:256], vct("ln2w"), ident[:])
    T_("transpose", ctx.ps[bank][0:8, 256:384], vct("ln2b"), ident[:])
    mm_trow = ctx.mm_inc()
    ctx.wait("v", "MM", mm_trow)
    V("tensor_copy", trow[:], ctx.ps[bank][0:8, 0:384])
    ctx.bank_set(bank, "VE", ctx.ve_inc())

    def mini_ln(src, dst):
        bank = 4
        ctx.bank_wait(bank)
        ctx.wait("t", "VE", ctx.VE.n)
        ctx.wait("t", "SE", ctx.SE.n)
        ctx.wait("t", "LQ", ctx.LQ.n)
        T_("matmul", ctx.ps[bank][0:1, 0:NJ], onescol[:, 0:1], src, start=True, stop=True)
        ctx.mm_inc()
        ctx.wait("s", "VE", ctx.VE.n)
        ctx.wait("s", "LQ", ctx.LQ.n)
        ctx.wait("s", "MM", ctx.MM.n)
        S("activation", SB[:, 24:24 + NJ], src, AF.Square)
        se1 = ctx.se_inc()
        ctx.wait("t", "SE", se1)
        T_("matmul", ctx.ps[bank][0:1, NJ:2 * NJ], onescol[:, 0:1], SB[:, 24:24 + NJ],
           start=True, stop=True)
        mm2 = ctx.mm_inc()
        ctx.wait("v", "MM", mm2)
        st = stats
        V("reduce_sum", st[0:1, 0:1], ctx.ps[bank][0:1, 0:NJ], axis=AX)
        V("reduce_sum", st[0:1, 1:2], ctx.ps[bank][0:1, NJ:2 * NJ], axis=AX)
        V("tensor_scalar_mul", st[0:1, 0:2], st[0:1, 0:2], 1.0 / C)
        V("tensor_mul", st[0:1, 2:3], st[0:1, 0:1], st[0:1, 0:1])
        V("tensor_sub", st[0:1, 1:2], st[0:1, 1:2], st[0:1, 2:3])
        V("tensor_scalar_add", st[0:1, 1:2], st[0:1, 1:2], LN_EPS)
        ve1 = ctx.ve_inc()
        ctx.wait("s", "VE", ve1)
        S("activation", st[0:1, 3:4], st[0:1, 1:2], AF.Ln)
        se2 = ctx.se_inc()
        ctx.wait("v", "SE", se2)
        V("tensor_scalar_mul", st[0:1, 3:4], st[0:1, 3:4], -0.5)
        ve1b = ctx.ve_inc()
        ctx.wait("s", "VE", ve1b)
        S("activation", st[0:1, 4:5], st[0:1, 3:4], AF.Exp)
        se2b = ctx.se_inc()
        ctx.wait("v", "SE", se2b)
        V("tensor_copy", st[0:1, 4:5], st[0:1, 4:5])
        ve2 = ctx.ve_inc()
        ctx.wait("t", "VE", ve2)
        T_("matmul", ctx.ps[bank][:, 16:17], onesrow[0:1, :], st[0:1, 4:5], start=True, stop=True)
        T_("matmul", ctx.ps[bank][:, 17:18], onesrow[0:1, :], st[0:1, 0:1], start=True, stop=True)
        mm3 = ctx.mm_inc()
        ctx.wait("v", "MM", mm3)
        V("tensor_single_scalar", dst, src, ctx.ps[bank][:, 17:18], ALU.subtract)
        V("tensor_single_scalar", dst, dst, ctx.ps[bank][:, 16:17], ALU.mult)
        ctx.bank_set(bank, "VE", ctx.ve_inc())

    ctx.dbg_dump("xbnd_d", xbnd[:], [128, NJ])
    mini_ln(xbnd[:], SB[:, 0:NJ])
    ctx.dbg_dump("lnb_d", SB[:, 0:NJ], [128, NJ])
    ctx.dbg_dump("stats1_d", stats[0:1, 0:8], [1, 8])
    V("tensor_mul", SB[:, 0:NJ], SB[:, 0:NJ], dct("ca1"))
    V("tensor_add", SB[:, 0:NJ], SB[:, 0:NJ], dct("sa"))
    ctx.ve_inc()
    ctx.dbg_dump("h0b_d", SB[:, 0:NJ], [128, NJ])
    mini_ln(SB[:, 0:NJ], ubnd[:])
    V("tensor_mul", ubnd[:], ubnd[:], vct("ln1w"))
    V("tensor_add", ubnd[:], ubnd[:], vct("ln1b"))
    V("tensor_single_scalar", ubnd[:], ubnd[:], maskfirstP[:, 0:1], ALU.mult)
    ve_ubnd = ctx.ve_inc()
    ctx.dbg_dump("ubnd_d", ubnd[:], [128, NJ])
    ctx.phase_barrier()
    ctx.areset(am0)

    def ln_stats(tile_of, width, waits=()):
        bs, bq = 4, 5
        ctx.bank_wait(bs)
        ctx.bank_wait(bq)
        vew = ctx.VE.n
        ctx.wait("t", "VE", vew)
        for eng_sem_val in waits:
            ctx.wait("t", eng_sem_val[0], eng_sem_val[1])
        mmq = {}
        for j in range(NJ):
            T_("matmul", ctx.ps[bs][0:1, 0:width], onescol[:, 0:1], tile_of(j),
               start=(j == 0), stop=(j == NJ - 1))
            ctx.mm_inc()
        ctx.wait("s", "VE", vew)
        for eng_sem_val in waits:
            ctx.wait("s", eng_sem_val[0], eng_sem_val[1])
        for j in range(NJ):
            buf = S4 if (j % 2 == 0) else SQ2
            if j >= 2:
                ctx.wait("s", "MM", mmq[j - 2])
            S("activation", buf[:, 0:width], tile_of(j), AF.Square)
            seq = ctx.se_inc()
            ctx.wait("t", "SE", seq)
            T_("matmul", ctx.ps[bq][0:1, 0:width], onescol[:, 0:1], buf[:, 0:width],
               start=(j == 0), stop=(j == NJ - 1))
            mmq[j] = ctx.mm_inc()
        ctx.wait("v", "MM", mmq[NJ - 1])
        st = stats
        V("tensor_scalar_mul", st[0:1, 0:width], ctx.ps[bs][0:1, 0:width], 1.0 / C)
        V("tensor_scalar_mul", st[0:1, 512:512 + width], ctx.ps[bq][0:1, 0:width], 1.0 / C)
        V("tensor_mul", st[0:1, 1024:1024 + width], st[0:1, 0:width], st[0:1, 0:width])
        V("tensor_sub", st[0:1, 512:512 + width], st[0:1, 512:512 + width],
          st[0:1, 1024:1024 + width])
        V("tensor_scalar_add", st[0:1, 512:512 + width], st[0:1, 512:512 + width], LN_EPS)
        ve1 = ctx.ve_inc()
        ctx.wait("s", "VE", ve1)
        S("activation", st[0:1, 1024:1024 + width], st[0:1, 512:512 + width], AF.Ln)
        se1 = ctx.se_inc()
        ctx.wait("v", "SE", se1)
        V("tensor_scalar_mul", st[0:1, 1024:1024 + width], st[0:1, 1024:1024 + width], -0.5)
        ve1b = ctx.ve_inc()
        ctx.wait("s", "VE", ve1b)
        S("activation", st[0:1, 1536:1536 + width], st[0:1, 1024:1024 + width], AF.Exp)
        se1b = ctx.se_inc()
        ctx.wait("v", "SE", se1b)
        V("tensor_mul", st[0:1, 0:width], st[0:1, 0:width], st[0:1, 1536:1536 + width])
        V("tensor_scalar_mul", st[0:1, 0:width], st[0:1, 0:width], -1.0)
        ve2 = ctx.ve_inc()
        ctx.wait("t", "VE", ve2)
        T_("matmul", ctx.ps[bs][:, 0:width], onesrow[0:1, :], st[0:1, 1536:1536 + width],
           start=True, stop=True)
        T_("matmul", ctx.ps[bq][:, 0:width], onesrow[0:1, :], st[0:1, 0:width],
           start=True, stop=True)
        mmb = ctx.mm_inc()
        ctx.wait("v", "MM", mmb)
        V("tensor_copy", lnscr[:, 0:width], ctx.ps[bs][:, 0:width])
        V("tensor_copy", lnscr[:, TC:TC + width], ctx.ps[bq][:, 0:width])
        vee = ctx.ve_inc()
        ctx.bank_set(bs, "VE", vee)
        ctx.bank_set(bq, "VE", vee)

    am1 = ctx.amark()
    iotarev = ctx.aalloc(TC)
    ctx.ld(iotarev, ap["iotarev"])
    A0 = ctx.aalloc(NJ * (TC + 1))
    A1 = ctx.aalloc(NJ * TC)
    A2 = ctx.aalloc(NJ * (TC + 1))
    A3 = ctx.aalloc(NJ * TC)
    KB = ctx.aalloc(NJ * TC)
    VB = ctx.aalloc(NJ * TC)
    RB = ctx.aalloc(NJ * TC)
    WBUF = ctx.aalloc(NJ * 1024)

    wname = {"k": "kwT", "v": "vwT", "r": "rwT"}
    spill_done = 0
    sq_chunk = 0
    mm_w_last = 0
    ve_mix_done = 0
    for c in range(NCHK):
        base = c * TC
        ld_x = 0
        for j in range(NJ):
            if c > 0:
                ctx.wait("y", "VE", ve_h0_prev)
            ld_x = ctx.ld(a1_(A0, j), ap["xT"][j * 128:(j + 1) * 128, base:base + TC + 1])
        ld_x = ctx.lq()
        ln_stats(lambda j: a1_(A0, j)[:, 1:TC + 1], TC, waits=(("LQ", ld_x),))
        ctx.wait("v", "LQ", ld_x)
        ctx.wait("v", "SQ", sq_chunk)
        for j in range(NJ):
            V("tensor_mul", a_(A1, j), a1_(A0, j)[:, 1:TC + 1], lnscr[:, 0:TC])
            V("tensor_add", a_(A1, j), a_(A1, j), lnscr[:, TC:2 * TC])
            V("tensor_scalar", a_(A1, j), a_(A1, j), dc("ca1", j), dc("sa", j),
              ALU.mult, ALU.add)
        ve_h0 = ctx.ve_inc()
        ve_h0_prev = ve_h0
        for j in range(NJ):
            ctx.wait("g", "VE", ve_h0)
            spill_done = ctx.sp(h0_sp[j * 128:(j + 1) * 128, base:base + TC], a_(A1, j))
        ln_stats(lambda j: a_(A1, j), TC)
        for j in range(NJ):
            V("tensor_mul", a1_(A2, j)[:, 1:TC + 1], a_(A1, j), lnscr[:, 0:TC])
            V("tensor_add", a1_(A2, j)[:, 1:TC + 1], a1_(A2, j)[:, 1:TC + 1],
              lnscr[:, TC:2 * TC])
            V("tensor_scalar", a1_(A2, j)[:, 1:TC + 1], a1_(A2, j)[:, 1:TC + 1],
              vc("ln1w", j), vc("ln1b", j), ALU.mult, ALU.add)
            V("tensor_copy", a1_(A2, j)[:, 0:1],
              ubnd[:, j:j + 1] if c == 0 else usave[:, j:j + 1])
        ctx.ve_inc()
        for j in range(NJ):
            V("tensor_copy", usave[:, j:j + 1], a1_(A2, j)[:, TC:TC + 1])
        ctx.ve_inc()

        for nm in ("k", "v", "r"):
            tm = {"k": "tmk", "v": "tmv", "r": "tmr"}[nm]
            ctx.wait("v", "MM", mm_w_last)
            for j in range(NJ):
                V("tensor_sub", a_(A3, j), a1_(A2, j)[:, 1:TC + 1], a1_(A2, j)[:, 0:TC])
                V("scalar_tensor_tensor", a_(A3, j), a_(A3, j), vc(tm, j),
                  a1_(A2, j)[:, 0:TC], ALU.mult, ALU.add)
            ve_mix = ctx.ve_inc()
            ld_w = 0
            for j in range(NJ):
                ctx.wait("y", "MM", mm_w_last)
                ld_w = ctx.ld(WBUF[:, j * 1024:(j + 1) * 1024],
                              ap[wname[nm]][j * 128:(j + 1) * 128, :])
            ld_w = ctx.lq()
            dstbuf = {"k": KB, "v": VB, "r": RB}[nm]
            for cout in range(NJ):
                bank = cout % 2
                ctx.bank_wait(bank)
                ctx.wait("t", "LQ", ld_w)
                ctx.wait("t", "VE", ve_mix)
                for cin in range(NJ):
                    T_("matmul", ctx.ps[bank][:, 0:TC],
                       WBUF[:, cin * 1024 + cout * 128:cin * 1024 + (cout + 1) * 128],
                       a_(A3, cin), start=(cin == 0), stop=(cin == NJ - 1))
                mmv = ctx.mm_inc()
                if nm == "r":
                    ctx.wait("s", "MM", mmv)
                    S("activation", a_(dstbuf, cout), ctx.ps[bank][:, 0:TC], AF.Sigmoid)
                    ctx.bank_set(bank, "SE", ctx.se_inc())
                else:
                    ctx.wait("v", "MM", mmv)
                    V("tensor_copy", a_(dstbuf, cout), ctx.ps[bank][:, 0:TC])
                    ctx.bank_set(bank, "VE", ctx.ve_inc())
            mm_w_last = ctx.MM.n

        ve_k = ctx.VE.n
        se_r = ctx.SE.n
        for vi2, buf in ((0, KB), (1, VB), (2, RB)):
            for j in range(NJ):
                ctx.wait("g", "VE", ve_k)
                ctx.wait("g", "SE", se_r)
                spill_done = ctx.sp(
                    kvs_sp[vi2 * C + j * 128:vi2 * C + (j + 1) * 128, base:base + TC],
                    a_(buf, j))
        sq_chunk = ctx.sq()

        for j in range(NJ):
            V("tensor_scalar_mul", S1[:, 0:TC], iotarev, dc("w", j))
            V("tensor_add", S1[:, 0:TC], S1[:, 0:TC], a_(KB, j))
            V("reduce_max", csum[:, j:j + 1], S1[:, 0:TC], axis=AX)
            V("tensor_single_scalar", S1[:, 0:TC], S1[:, 0:TC], csum[:, j:j + 1], ALU.subtract)
            ve_e = ctx.ve_inc()
            ctx.wait("s", "VE", ve_e)
            S("activation", S2[:, 0:TC], S1[:, 0:TC], AF.Exp)
            se_e = ctx.se_inc()
            ctx.wait("v", "SE", se_e)
            V("reduce_sum", csum[:, 2 * NJ + j:2 * NJ + j + 1], S2[:, 0:TC], axis=AX)
            V("tensor_mul", S2[:, 0:TC], S2[:, 0:TC], a_(VB, j))
            V("reduce_sum", csum[:, NJ + j:NJ + j + 1], S2[:, 0:TC], axis=AX)
            ctx.ve_inc()
        Mr, Ar, Br = runsum[:, 0:NJ], runsum[:, NJ:2 * NJ], runsum[:, 2 * NJ:3 * NJ]
        Mc_, Ac_, Bc_ = csum[:, 0:NJ], csum[:, NJ:2 * NJ], csum[:, 2 * NJ:3 * NJ]
        V("scalar_tensor_tensor", SB[:, 0:NJ], dct("w"), float(TC), Mr, ALU.mult, ALU.add)
        V("tensor_max", SB[:, 8:8 + NJ], SB[:, 0:NJ], Mc_)
        V("tensor_sub", SB[:, 0:NJ], SB[:, 0:NJ], SB[:, 8:8 + NJ])
        V("tensor_sub", SB[:, 16:16 + NJ], Mc_, SB[:, 8:8 + NJ])
        ve_c1 = ctx.ve_inc()
        ctx.wait("s", "VE", ve_c1)
        S("activation", SB[:, 0:NJ], SB[:, 0:NJ], AF.Exp)
        S("activation", SB[:, 16:16 + NJ], SB[:, 16:16 + NJ], AF.Exp)
        se_c = ctx.se_inc()
        ctx.wait("v", "SE", se_c)
        V("tensor_mul", Ar, Ar, SB[:, 0:NJ])
        V("tensor_mul", SB[:, 24:24 + NJ], Ac_, SB[:, 16:16 + NJ])
        V("tensor_add", Ar, Ar, SB[:, 24:24 + NJ])
        V("tensor_mul", Br, Br, SB[:, 0:NJ])
        V("tensor_mul", SB[:, 24:24 + NJ], Bc_, SB[:, 16:16 + NJ])
        V("tensor_add", Br, Br, SB[:, 24:24 + NJ])
        V("tensor_copy", Mr, SB[:, 8:8 + NJ])
        ve_runsum = ctx.ve_inc()

    bank = 6
    ctx.bank_wait(bank)
    ctx.wait("t", "VE", ve_runsum)
    T_("transpose", ctx.ps[bank][0:24, 0:128], runsum[:], ident[:])
    mm_ag1 = ctx.mm_inc()
    ctx.wait("v", "MM", mm_ag1)
    V("tensor_copy", S1[0:24, 0:128], ctx.ps[bank][0:24, 0:128])
    ve_ag1 = ctx.ve_inc()
    ctx.bank_set(bank, "VE", ve_ag1)
    ctx.wait("g", "VE", ve_ag1)
    sp_ag1 = ctx.sp(ag1_in, S1[0:24, 0:128])
    ctx.wait("g", "SP", sp_ag1)
    cc_ag1 = ctx.allgather(ag1_in, ag1_out)

    allrM = ctx.aalloc(8 * 128)
    allrA = ctx.aalloc(8 * 128)
    allrB = ctx.aalloc(8 * 128)
    stP = ctx.aalloc(128)
    stA = ctx.aalloc(128)
    stB = ctx.aalloc(128)
    q1 = ctx.aalloc(128)
    q2 = ctx.aalloc(128)
    q3 = ctx.aalloc(128)
    ctx.wait("y", "CC", cc_ag1)
    for vv, dst in ((0, allrM), (1, allrA), (2, allrB)):
        ld_allr = ctx.ld(dst[0:8, :],
                         mkap(ag1_out, vv * 8 * 128, [[128, 8], [24 * 128, 8], [1, 128]]))
    ld_allr = ctx.lq()
    ctx.wait("v", "LQ", ld_allr)
    V("memset", stP[0:8, :], NEG)
    V("memset", stA[0:8, :], 0.0)
    V("memset", stB[0:8, :], 0.0)
    ctx.ve_inc()
    wrow = trow[0:8, 0:128]
    for k in range(8):
        Msl = allrM[0:8, k * 128:(k + 1) * 128]
        Asl = allrA[0:8, k * 128:(k + 1) * 128]
        Bsl = allrB[0:8, k * 128:(k + 1) * 128]
        V("tensor_scalar", q1[0:8, :], Msl, masklt8[:, k:k + 1], mneg8[:, k:k + 1],
          ALU.mult, ALU.add)
        V("scalar_tensor_tensor", q2[0:8, :], wrow, tlm8[:, k:k + 1], stP[0:8, :],
          ALU.mult, ALU.add)
        V("tensor_max", q3[0:8, :], q2[0:8, :], q1[0:8, :])
        V("tensor_sub", q2[0:8, :], q2[0:8, :], q3[0:8, :])
        V("tensor_sub", q1[0:8, :], q1[0:8, :], q3[0:8, :])
        ve_cb = ctx.ve_inc()
        ctx.wait("s", "VE", ve_cb)
        S("activation", q2[0:8, :], q2[0:8, :], AF.Exp)
        S("activation", q1[0:8, :], q1[0:8, :], AF.Exp)
        se_cb = ctx.se_inc()
        ctx.wait("v", "SE", se_cb)
        V("tensor_copy", stP[0:8, :], q3[0:8, :])
        V("tensor_mul", stA[0:8, :], stA[0:8, :], q2[0:8, :])
        V("tensor_single_scalar", q3[0:8, :], Asl, masklt8[:, k:k + 1], ALU.mult)
        V("tensor_mul", q3[0:8, :], q3[0:8, :], q1[0:8, :])
        V("tensor_add", stA[0:8, :], stA[0:8, :], q3[0:8, :])
        V("tensor_mul", stB[0:8, :], stB[0:8, :], q2[0:8, :])
        V("tensor_single_scalar", q3[0:8, :], Bsl, masklt8[:, k:k + 1], ALU.mult)
        V("tensor_mul", q3[0:8, :], q3[0:8, :], q1[0:8, :])
        V("tensor_add", stB[0:8, :], stB[0:8, :], q3[0:8, :])
        ctx.ve_inc()
    bank = 7
    ctx.bank_wait(bank)
    ctx.wait("t", "VE", ctx.VE.n)
    for sidx, st_t in enumerate((stP, stA, stB)):
        for j in range(NJ):
            T_("matmul", ctx.ps[bank][:, sidx * NJ + j:sidx * NJ + j + 1], st_t[0:8, :],
               oh8id[:, j:j + 1], start=True, stop=True)
    mm_st0 = ctx.mm_inc()
    ctx.wait("v", "MM", mm_st0)
    V("tensor_copy", state0[:], ctx.ps[bank][:, 0:3 * NJ])
    ve_state0 = ctx.ve_inc()
    ctx.dbg_dump("state0_d", state0[:], [128, 3 * NJ])
    ctx.dbg_dump("runsum_d", runsum[:], [128, 3 * NJ])
    ctx.bank_set(bank, "VE", ve_state0)
    ctx.phase_barrier()
    ctx.areset(am1)

    am4 = ctx.amark()
    KB = ctx.aalloc(NJ * TC)
    VB = ctx.aalloc(NJ * TC)
    RB = ctx.aalloc(NJ * TC)
    BPP = ctx.aalloc(NJ * (TC + 1))
    BAA = ctx.aalloc(NJ * (TC + 1))
    BBB = ctx.aalloc(NJ * (TC + 1))
    RW = ctx.aalloc(NJ * TC)
    AB = ctx.aalloc(NJ * TC)
    H0R = ctx.aalloc(2 * TC)
    D4 = ctx.aalloc(4 * TC)
    U4 = ctx.aalloc(4 * TC)
    UV4 = ctx.aalloc(4 * TC)
    WOB = ctx.aalloc(2 * NJ * 128)

    def ld_w_cout(wap, cout, slot):
        t = 0
        for cin in range(NJ):
            t = ctx.ld(WOB[:, slot * 1024 + cin * 128:slot * 1024 + (cin + 1) * 128],
                       wap[cin * 128:(cin + 1) * 128, cout * 128:(cout + 1) * 128])
        return ctx.lq()

    sp_a = 0
    sq_a = 0
    ve_kdone = 0
    mm_op_last = 0
    ve_adone_prev = 0
    cc_ag1b = 0
    for c in range(NCHK):
        base = c * TC
        ld_kvs = 0
        ctx.wait("y", "SQ", sq_chunk)
        ctx.wait("y", "VE", ve_kdone)
        for vi2, buf in ((0, KB), (1, VB), (2, RB)):
            for j in range(NJ):
                ld_kvs = ctx.ld(a_(buf, j),
                                kvs_sp[vi2 * C + j * 128:vi2 * C + (j + 1) * 128,
                                       base:base + TC])
        ld_kvs = ctx.lq()
        ctx.wait("v", "LQ", ld_kvs)
        ctx.wait("v", "VE", ve_state0)
        ctx.wait("v", "MM", mm_op_last)
        ctx.wait("v", "SQ", sq_a)
        for half in range(2):
            js = list(range(half * 4, half * 4 + 4))
            for jj, j in enumerate(js):
                bpp, baa, bbb = a1_(BPP, j), a1_(BAA, j), a1_(BBB, j)
                if c == 0:
                    V("tensor_copy", bpp[:, 0:1], state0[:, j:j + 1])
                    V("tensor_copy", baa[:, 0:1], state0[:, NJ + j:NJ + j + 1])
                    V("tensor_copy", bbb[:, 0:1], state0[:, 2 * NJ + j:2 * NJ + j + 1])
                else:
                    V("tensor_copy", bpp[:, 0:1], bpp[:, TC:TC + 1])
                    V("tensor_copy", baa[:, 0:1], baa[:, TC:TC + 1])
                    V("tensor_copy", bbb[:, 0:1], bbb[:, TC:TC + 1])
                V("tensor_tensor_scan", bpp[:, 1:TC + 1],
                  dc("w", j).broadcast_to([128, TC]), a_(KB, j), bpp[:, 0:1],
                  ALU.add, ALU.max)
            for jj, j in enumerate(js):
                bpp = a1_(BPP, j)
                V("scalar_tensor_tensor", a_(D4, jj), bpp[:, 0:TC], dc("w", j),
                  bpp[:, 1:TC + 1], ALU.add, ALU.subtract)
                V("tensor_sub", a_(U4, jj), a_(KB, j), bpp[:, 1:TC + 1])
            ve_b1 = ctx.ve_inc()
            ctx.wait("s", "VE", ve_b1)
            for jj in range(4):
                S("activation", a_(D4, jj), a_(D4, jj), AF.Exp)
                S("activation", a_(U4, jj), a_(U4, jj), AF.Exp)
            se_b1 = ctx.se_inc()
            ctx.wait("v", "SE", se_b1)
            for jj, j in enumerate(js):
                bpp, baa, bbb = a1_(BPP, j), a1_(BAA, j), a1_(BBB, j)
                V("tensor_mul", a_(UV4, jj), a_(U4, jj), a_(VB, j))
                V("tensor_tensor_scan", bbb[:, 1:TC + 1], a_(D4, jj), a_(U4, jj),
                  bbb[:, 0:1], ALU.mult, ALU.add)
                V("tensor_tensor_scan", baa[:, 1:TC + 1], a_(D4, jj), a_(UV4, jj),
                  baa[:, 0:1], ALU.mult, ALU.add)
            for jj, j in enumerate(js):
                bpp = a1_(BPP, j)
                V("tensor_single_scalar", a_(KB, j), a_(KB, j), vc("tfir", j), ALU.add)
                V("tensor_max", a_(UV4, jj), bpp[:, 0:TC], a_(KB, j))
                V("tensor_sub", a_(D4, jj), bpp[:, 0:TC], a_(UV4, jj))
                V("tensor_sub", a_(KB, j), a_(KB, j), a_(UV4, jj))
            ve_b2 = ctx.ve_inc()
            ctx.wait("s", "VE", ve_b2)
            for jj, j in enumerate(js):
                S("activation", a_(D4, jj), a_(D4, jj), AF.Exp)
                S("activation", a_(KB, j), a_(KB, j), AF.Exp)
            se_b2 = ctx.se_inc()
            ctx.wait("v", "SE", se_b2)
            for jj, j in enumerate(js):
                bbb = a1_(BBB, j)
                V("tensor_mul", a_(UV4, jj), a_(D4, jj), bbb[:, 0:TC])
                V("tensor_add", a_(UV4, jj), a_(UV4, jj), a_(KB, j))
            ve_b3 = ctx.ve_inc()
            ctx.wait("s", "VE", ve_b3)
            for jj in range(4):
                S("activation", a_(UV4, jj), a_(UV4, jj), AF.Ln)
            se_b3 = ctx.se_inc()
            ctx.wait("v", "SE", se_b3)
            for jj in range(4):
                V("tensor_scalar_mul", a_(UV4, jj), a_(UV4, jj), -1.0)
            ve_b4 = ctx.ve_inc()
            ctx.wait("s", "VE", ve_b4)
            for jj in range(4):
                S("activation", a_(UV4, jj), a_(UV4, jj), AF.Exp)
            se_b4 = ctx.se_inc()
            ctx.wait("v", "SE", se_b4)
            for jj, j in enumerate(js):
                baa = a1_(BAA, j)
                V("tensor_mul", a_(D4, jj), a_(D4, jj), baa[:, 0:TC])
                V("tensor_mul", a_(KB, j), a_(KB, j), a_(VB, j))
                V("tensor_add", a_(D4, jj), a_(D4, jj), a_(KB, j))
                V("tensor_mul", a_(D4, jj), a_(D4, jj), a_(UV4, jj))
                V("tensor_mul", a_(RW, j), a_(D4, jj), a_(RB, j))
            ctx.ve_inc()
        ve_kdone = ctx.VE.n

        evac_a = {}
        mmo_hist = {}
        lqw = {}

        def issue_op_load(cout):
            if cout >= 2:
                ctx.wait("y", "MM", mmo_hist[cout - 2])
                ctx.wait("y", "VE", evac_a[cout - 2])
            else:
                ctx.wait("y", "MM", mm_op_last)
                ctx.wait("y", "VE", ve_adone_prev)
            ld_w_cout(ap["owT"], cout, cout % 2)
            ctx.ld(H0R[:, (cout % 2) * TC:(cout % 2) * TC + TC],
                   h0_sp[cout * 128:(cout + 1) * 128, base:base + TC])
            lqw[cout] = ctx.lq()

        issue_op_load(0)
        issue_op_load(1)
        for cout in range(NJ):
            bank = cout % 2
            ctx.bank_wait(bank)
            ctx.wait("t", "LQ", lqw[cout])
            ctx.wait("t", "VE", ve_kdone)
            for cin in range(NJ):
                T_("matmul", ctx.ps[bank][:, 0:TC],
                   WOB[:, (cout % 2) * 1024 + cin * 128:(cout % 2) * 1024 + (cin + 1) * 128],
                   a_(RW, cin), start=(cin == 0), stop=(cin == NJ - 1))
            mmo = ctx.mm_inc()
            mmo_hist[cout] = mmo
            ctx.wait("v", "MM", mmo)
            V("tensor_add", a_(AB, cout), ctx.ps[bank][:, 0:TC],
              H0R[:, (cout % 2) * TC:(cout % 2) * TC + TC])
            evac_a[cout] = ctx.ve_inc()
            ctx.bank_set(bank, "VE", evac_a[cout])
            if cout + 2 < NJ:
                issue_op_load(cout + 2)
        mm_op_last = ctx.MM.n
        ve_adone = ctx.VE.n
        ve_adone_prev = ve_adone
        for j in range(NJ):
            ctx.wait("g", "VE", ve_adone)
            sp_a = ctx.sp(a_sp[j * 128:(j + 1) * 128, base:base + TC], a_(AB, j))
        sq_a = ctx.sq()
        if c == NCHK - 1:
            for j in range(NJ):
                V("tensor_copy", SB[:, j:j + 1], a_(AB, j)[:, TC - 1:TC])
            ve_al = ctx.ve_inc()
            bank = 6
            ctx.bank_wait(bank)
            ctx.wait("t", "VE", ve_al)
            T_("transpose", ctx.ps[bank][0:8, 0:128], SB[:, 0:NJ], ident[:])
            mm_al = ctx.mm_inc()
            ctx.wait("v", "MM", mm_al)
            V("tensor_copy", S1[0:8, 0:128], ctx.ps[bank][0:8, 0:128])
            ve_al2 = ctx.ve_inc()
            ctx.bank_set(bank, "VE", ve_al2)
            ctx.wait("g", "VE", ve_al2)
            sp_ag1b = ctx.sp(ag1b_in, S1[0:8, 0:128])
            ctx.wait("g", "SP", sp_ag1b)
            cc_ag1b = ctx.allgather(ag1b_in, ag1b_out)

    ctx.phase_barrier()
    ctx.areset(am4)
    ab8 = ctx.aalloc(8 * 128)
    ctx.wait("y", "CC", cc_ag1b)
    ld_ab8 = ctx.ld(ab8[0:8, :],
                    mkap(ag1b_out, 0, [[128, 8], [8 * 128, 8], [1, 128]]))
    ld_ab8 = ctx.lq()
    ctx.wait("v", "LQ", ld_ab8)
    V("memset", S1[0:8, 0:128], 0.0)
    for k in range(8):
        V("scalar_tensor_tensor", S1[0:8, 0:128], ab8[0:8, k * 128:(k + 1) * 128],
          mprev8[:, k:k + 1], S1[0:8, 0:128], ALU.mult, ALU.add)
    ve_am = ctx.ve_inc()
    bank = 4
    ctx.bank_wait(bank)
    V("reduce_sum", S2[0:8, 0:1], S1[0:8, 0:128], axis=AX)
    ve_rs = ctx.ve_inc()
    ctx.wait("s", "VE", ve_rs)
    S("activation", S3[0:8, 0:128], S1[0:8, 0:128], AF.Square)
    se_sq = ctx.se_inc()
    ctx.wait("v", "SE", se_sq)
    V("reduce_sum", S2[0:8, 1:2], S3[0:8, 0:128], axis=AX)
    ve_rs2 = ctx.ve_inc()
    ctx.wait("t", "VE", ve_rs2)
    T_("matmul", ctx.ps[bank][0:1, 0:2], ones8[:, 0:1], S2[0:8, 0:2], start=True, stop=True)
    mm_bn = ctx.mm_inc()
    ctx.wait("v", "MM", mm_bn)
    st = stats
    V("tensor_scalar_mul", st[0:1, 0:2], ctx.ps[bank][0:1, 0:2], 1.0 / C)
    V("tensor_mul", st[0:1, 2:3], st[0:1, 0:1], st[0:1, 0:1])
    V("tensor_sub", st[0:1, 1:2], st[0:1, 1:2], st[0:1, 2:3])
    V("tensor_scalar_add", st[0:1, 1:2], st[0:1, 1:2], LN_EPS)
    ve_bv = ctx.ve_inc()
    ctx.wait("s", "VE", ve_bv)
    S("activation", st[0:1, 3:4], st[0:1, 1:2], AF.Ln)
    se_bs = ctx.se_inc()
    ctx.wait("v", "SE", se_bs)
    V("tensor_scalar_mul", st[0:1, 3:4], st[0:1, 3:4], -0.5)
    ve_bv2 = ctx.ve_inc()
    ctx.wait("s", "VE", ve_bv2)
    S("activation", st[0:1, 4:5], st[0:1, 3:4], AF.Exp)
    se_bs2 = ctx.se_inc()
    ctx.wait("v", "SE", se_bs2)
    V("tensor_copy", st[0:1, 4:5], st[0:1, 4:5])
    ve_bA = ctx.ve_inc()
    ctx.wait("t", "VE", ve_bA)
    T_("matmul", ctx.ps[bank][0:8, 4:5], onesrow[0:1, 0:8], st[0:1, 4:5], start=True, stop=True)
    T_("matmul", ctx.ps[bank][0:8, 5:6], onesrow[0:1, 0:8], st[0:1, 0:1], start=True, stop=True)
    mm_bb = ctx.mm_inc()
    ctx.wait("v", "MM", mm_bb)
    V("tensor_single_scalar", S1[0:8, 0:128], S1[0:8, 0:128], ctx.ps[bank][0:8, 5:6],
      ALU.subtract)
    V("tensor_single_scalar", S1[0:8, 0:128], S1[0:8, 0:128], ctx.ps[bank][0:8, 4:5],
      ALU.mult)
    V("tensor_mul", S1[0:8, 0:128], S1[0:8, 0:128], trow[0:8, 128:256])
    V("tensor_add", S1[0:8, 0:128], S1[0:8, 0:128], trow[0:8, 256:384])
    V("tensor_single_scalar", S1[0:8, 0:128], S1[0:8, 0:128], maskfirst[:, 0:1], ALU.mult)
    ve_u2b = ctx.ve_inc()
    ctx.bank_set(bank, "VE", ve_u2b)
    bank = 7
    ctx.bank_wait(bank)
    ctx.wait("t", "VE", ve_u2b)
    for j in range(NJ):
        T_("matmul", ctx.ps[bank][:, j:j + 1], S1[0:8, 0:128], oh8id[:, j:j + 1],
           start=True, stop=True)
    mm_u2b = ctx.mm_inc()
    ctx.wait("v", "MM", mm_u2b)
    V("tensor_copy", u2bnd[:], ctx.ps[bank][:, 0:NJ])
    ve_u2bnd = ctx.ve_inc()
    ctx.bank_set(bank, "VE", ve_u2bnd)
    ctx.phase_barrier()
    ctx.areset(am4)

    am4b = ctx.amark()
    AB = ctx.aalloc(NJ * TC)
    U2 = ctx.aalloc(NJ * (TC + 1))
    A3 = ctx.aalloc(NJ * TC)
    FRB = ctx.aalloc(NJ * TC)
    WFR = ctx.aalloc(NJ * 1024)
    ld_wfr = 0
    for j in range(NJ):
        ld_wfr = ctx.ld(WFR[:, j * 1024:(j + 1) * 1024],
                        ap["frwT"][j * 128:(j + 1) * 128, :])
    ld_wfr = ctx.lq()

    sp_4b = 0
    sq_4b = 0
    sq_xk2 = 0
    ve_4b = 0
    mm_fr_last = 0
    for c in range(NCHK):
        base = c * TC
        ld_a = 0
        ctx.wait("y", "SQ", sq_a)
        ctx.wait("y", "VE", ve_4b)
        for j in range(NJ):
            ld_a = ctx.ld(a_(AB, j), a_sp[j * 128:(j + 1) * 128, base:base + TC])
        ld_a = ctx.lq()
        ln_stats(lambda j: a_(AB, j), TC, waits=(("LQ", ld_a),))
        ctx.wait("v", "LQ", ld_a)
        for j in range(NJ):
            V("tensor_mul", a1_(U2, j)[:, 1:TC + 1], a_(AB, j), lnscr[:, 0:TC])
            V("tensor_add", a1_(U2, j)[:, 1:TC + 1], a1_(U2, j)[:, 1:TC + 1],
              lnscr[:, TC:2 * TC])
            V("tensor_scalar", a1_(U2, j)[:, 1:TC + 1], a1_(U2, j)[:, 1:TC + 1],
              vc("ln2w", j), vc("ln2b", j), ALU.mult, ALU.add)
            V("tensor_copy", a1_(U2, j)[:, 0:1],
              u2bnd[:, j:j + 1] if c == 0 else u2save[:, j:j + 1])
        ctx.ve_inc()
        for j in range(NJ):
            V("tensor_copy", u2save[:, j:j + 1], a1_(U2, j)[:, TC:TC + 1])
        ctx.ve_inc()
        ctx.wait("v", "SQ", sq_4b)
        for j in range(NJ):
            V("tensor_sub", a_(A3, j), a1_(U2, j)[:, 1:TC + 1], a1_(U2, j)[:, 0:TC])
            V("scalar_tensor_tensor", a_(A3, j), a_(A3, j), vc("ftmk", j),
              a1_(U2, j)[:, 0:TC], ALU.mult, ALU.add)
        ve_xk2 = ctx.ve_inc()
        sp_xk2 = 0
        for j in range(NJ):
            ctx.wait("g", "VE", ve_xk2)
            sp_xk2 = ctx.sp(xk2_sp[j * 128:(j + 1) * 128, base:base + TC], a_(A3, j))
        sq_xk2 = ctx.sq()
        ctx.wait("v", "MM", mm_fr_last)
        for j in range(NJ):
            V("tensor_sub", a_(FRB, j), a1_(U2, j)[:, 1:TC + 1], a1_(U2, j)[:, 0:TC])
            V("scalar_tensor_tensor", a_(FRB, j), a_(FRB, j), vc("ftmr", j),
              a1_(U2, j)[:, 0:TC], ALU.mult, ALU.add)
        ve_xr2 = ctx.ve_inc()
        for cout in range(NJ):
            bank = cout % 2
            ctx.bank_wait(bank)
            ctx.wait("t", "LQ", ld_wfr)
            ctx.wait("t", "VE", ve_xr2)
            for cin in range(NJ):
                T_("matmul", ctx.ps[bank][:, 0:TC],
                   WFR[:, cin * 1024 + cout * 128:cin * 1024 + (cout + 1) * 128],
                   a_(FRB, cin), start=(cin == 0), stop=(cin == NJ - 1))
            mmf = ctx.mm_inc()
            ctx.wait("s", "MM", mmf)
            ctx.wait("s", "SQ", sq_xk2)
            S("activation", a_(A3, cout), ctx.ps[bank][:, 0:TC], AF.Sigmoid)
            ctx.bank_set(bank, "SE", ctx.se_inc())
        mm_fr_last = ctx.MM.n
        se_fr = ctx.SE.n
        for j in range(NJ):
            ctx.wait("g", "SE", se_fr)
            sp_4b = ctx.sp(sig_sp[j * 128:(j + 1) * 128, base:base + TC], a_(A3, j))
        sq_4b = ctx.sq()
        ve_4b = ctx.VE.n
    ctx.phase_barrier()
    ctx.areset(am4b)

    am5 = ctx.amark()
    KVSB = ctx.aalloc(NJ * TL)
    am5b = ctx.amark()
    WFK = ctx.aalloc(NJ * g.FS)
    WFV = ctx.aalloc((g.FS // 128) * 1024)
    XK2T = ctx.aalloc(NJ * TC)
    FK2 = ctx.aalloc((g.FS // 128) * TC)
    NFO = g.FS // 128

    ctx.wait("v", "SQ", sq_4b)
    V("memset", KVSB[:, :], 0.0)
    ve_kv0 = ctx.ve_inc()
    mm_cm_last = 0
    for fs in range(g.NFS):
        ld_wf = 0
        for j in range(NJ):
            ctx.wait("y", "MM", mm_cm_last)
            ld_wf = ctx.ld(WFK[:, j * g.FS:(j + 1) * g.FS],
                           ap["fkwT"][j * 128:(j + 1) * 128, fs * g.FS:(fs + 1) * g.FS])
        for fo in range(NFO):
            ld_wf = ctx.ld(WFV[:, fo * 1024:(fo + 1) * 1024],
                           ap["fvwT"][fs * g.FS + fo * 128:fs * g.FS + (fo + 1) * 128, :])
        for tc in range(NCHK):
            base = tc * TC
            ld_x2 = 0
            for j in range(NJ):
                ctx.wait("y", "MM", mm_cm_last)
                ld_x2 = ctx.ld(a_(XK2T, j), xk2_sp[j * 128:(j + 1) * 128, base:base + TC])
            ld_wx = ctx.lq()
            se_fk = {}
            ctx.wait("s", "MM", mm_cm_last)
            for fo in range(NFO):
                bank = fo % 2
                ctx.bank_wait(bank)
                ctx.wait("t", "LQ", ld_wx)
                for cin in range(NJ):
                    T_("matmul", ctx.ps[bank][:, 0:TC],
                       WFK[:, cin * g.FS + fo * 128:cin * g.FS + (fo + 1) * 128],
                       a_(XK2T, cin), start=(cin == 0), stop=(cin == NJ - 1))
                mmk = ctx.mm_inc()
                ctx.wait("s", "MM", mmk)
                S("activation", S4[:, 0:TC], ctx.ps[bank][:, 0:TC], AF.Relu)
                S("activation", FK2[:, fo * TC:(fo + 1) * TC], S4[:, 0:TC], AF.Square)
                se_fk[fo] = ctx.se_inc()
                ctx.bank_set(bank, "SE", se_fk[fo])
            for cout in range(NJ):
                bank = 2 + (cout % 2)
                ctx.bank_wait(bank)
                ctx.wait("t", "SE", se_fk[NFO - 1])
                for fo in range(NFO):
                    T_("matmul", ctx.ps[bank][:, 0:TC],
                       WFV[:, fo * 1024 + cout * 128:fo * 1024 + (cout + 1) * 128],
                       FK2[:, fo * TC:(fo + 1) * TC], start=(fo == 0), stop=(fo == NFO - 1))
                mmkv = ctx.mm_inc()
                ctx.wait("v", "MM", mmkv)
                ctx.wait("v", "VE", ve_kv0)
                V("tensor_add", KVSB[:, cout * TL + base:cout * TL + base + TC],
                  KVSB[:, cout * TL + base:cout * TL + base + TC], ctx.ps[bank][:, 0:TC])
                ctx.bank_set(bank, "VE", ctx.ve_inc())
            mm_cm_last = ctx.MM.n
    ctx.phase_barrier()
    ctx.areset(am5b)

    AB5 = ctx.aalloc(NJ * TC)
    SGB = ctx.aalloc(NJ * TC)
    XB5 = ctx.aalloc(NJ * TC)
    H1B = ctx.aalloc(NJ * TC)
    TOK = ctx.aalloc((TC // 128) * 1024)
    sp_ag2 = 0
    sq_ag2 = 0
    ve_rt = 0
    for tc in range(NCHK):
        base = tc * TC
        ld_f = 0
        ctx.wait("y", "VE", ve_rt)
        ctx.wait("y", "SQ", sq_ag2)
        for j in range(NJ):
            ld_f = ctx.ld(a_(AB5, j), a_sp[j * 128:(j + 1) * 128, base:base + TC])
            ld_f = ctx.ld(a_(SGB, j), sig_sp[j * 128:(j + 1) * 128, base:base + TC])
            ld_f = ctx.ld(a_(XB5, j), ap["xT"][j * 128:(j + 1) * 128, 1 + base:1 + base + TC])
        ld_f = ctx.lq()
        ctx.wait("v", "LQ", ld_f)
        for j in range(NJ):
            V("tensor_mul", S1[:, 0:TC], a_(SGB, j), KVSB[:, j * TL + base:j * TL + base + TC])
            V("tensor_add", S1[:, 0:TC], S1[:, 0:TC], a_(AB5, j))
            V("scalar_tensor_tensor", a_(AB5, j), S1[:, 0:TC], dc("ga", j), a_(XB5, j),
              ALU.mult, ALU.add)
        ve_xm = ctx.ve_inc()
        ln_stats(lambda j: a_(AB5, j), TC)
        for j in range(NJ):
            V("tensor_mul", a_(H1B, j), a_(AB5, j), lnscr[:, 0:TC])
            V("tensor_add", a_(H1B, j), a_(H1B, j), lnscr[:, TC:2 * TC])
            V("tensor_scalar", a_(H1B, j), a_(H1B, j), dc("cb1", j), dc("sb", j),
              ALU.mult, ALU.add)
        ve_h1 = ctx.ve_inc()
        for buf, dst in ((AB5, xmid_sp), (H1B, ag2_in)):
            tok_free = sq_ag2
            for tsub in range(TC // 128):
                for half in range(2):
                    bank = 4 + half
                    ctx.bank_wait(bank)
                    ctx.wait("t", "VE", ve_h1)
                    for jj in range(4):
                        j = half * 4 + jj
                        T_("transpose", ctx.ps[bank][:, jj * 128:(jj + 1) * 128],
                           a_(buf, j)[:, tsub * 128:(tsub + 1) * 128], ident[:])
                    mm_tp = ctx.mm_inc()
                    ctx.wait("v", "MM", mm_tp)
                    ctx.wait("v", "SQ", tok_free)
                    V("tensor_copy",
                      TOK[:, tsub * 1024 + half * 512:tsub * 1024 + (half + 1) * 512],
                      ctx.ps[bank][:, 0:512])
                    ctx.bank_set(bank, "VE", ctx.ve_inc())
            ve_tok = ctx.VE.n
            for tsub in range(TC // 128):
                ctx.wait("g", "VE", ve_tok)
                sp_ag2 = ctx.sp(dst[base + tsub * 128:base + (tsub + 1) * 128, :],
                                TOK[:, tsub * 1024:(tsub + 1) * 1024])
                if buf is H1B and "h1_dbg" in ctx.dbg_set:
                    sp_ag2 = ctx.sp(h1_dbg[base + tsub * 128:base + (tsub + 1) * 128, :],
                                    TOK[:, tsub * 1024:(tsub + 1) * 1024])
            sq_ag2 = ctx.sq()
        for tsub in range(TC // 128):
            bank = 6
            ctx.bank_wait(bank)
            ctx.wait("t", "VE", ve_h1)
            for j in range(NJ):
                T_("matmul", ctx.ps[bank][:, 0:g.E],
                   a_(H1B, j)[:, tsub * 128:(tsub + 1) * 128],
                   embTs[:, j * g.E:(j + 1) * g.E], start=(j == 0), stop=(j == NJ - 1))
            mm_lg = ctx.mm_inc()
            ti = (base // 128) + tsub
            rb = routebuf[:, ti * 18:(ti + 1) * 18]
            ctx.wait("v", "MM", mm_lg)
            V("tensor_copy", SB[:, 0:8], ctx.ps[bank][:, 0:g.E])
            ve_l = ctx.ve_inc()
            ctx.bank_set(bank, "VE", ve_l)
            V("max", out=SB[:, 8:16], in_=SB[:, 0:8])
            V("tensor_single_scalar", rb[:, 0:8], SB[:, 0:8], SB[:, 8:9], ALU.is_ge)
            V("tensor_single_scalar", SB[:, 16:24], SB[:, 0:8], SB[:, 9:10], ALU.is_ge)
            V("tensor_sub", rb[:, 8:16], SB[:, 16:24], rb[:, 0:8])
            V("tensor_sub", SB[:, 24:25], SB[:, 9:10], SB[:, 8:9])
            ve_r1 = ctx.ve_inc()
            ctx.wait("s", "VE", ve_r1)
            S("activation", SB[:, 24:25], SB[:, 24:25], AF.Exp)
            se_r1 = ctx.se_inc()
            ctx.wait("v", "SE", se_r1)
            V("tensor_scalar_add", SB[:, 25:26], SB[:, 24:25], 1.0)
            ve_r1b = ctx.ve_inc()
            ctx.wait("s", "VE", ve_r1b)
            S("activation", SB[:, 26:27], SB[:, 25:26], AF.Ln)
            se_r2 = ctx.se_inc()
            ctx.wait("v", "SE", se_r2)
            V("tensor_scalar_mul", SB[:, 26:27], SB[:, 26:27], -1.0)
            ve_r1c = ctx.ve_inc()
            ctx.wait("s", "VE", ve_r1c)
            S("activation", SB[:, 27:28], SB[:, 26:27], AF.Exp)
            se_r3 = ctx.se_inc()
            ctx.wait("v", "SE", se_r3)
            V("tensor_copy", rb[:, 16:17], SB[:, 27:28])
            V("tensor_mul", rb[:, 17:18], SB[:, 24:25], rb[:, 16:17])
            ve_r2 = ctx.ve_inc()
            bank = 7
            ctx.bank_wait(bank)
            ctx.wait("t", "VE", ve_r2)
            T_("transpose", ctx.ps[bank][0:8, 0:128], SB[:, 16:24], ident[:])
            T_("transpose", ctx.ps[bank][0:8, 128:256], rb[:, 0:8], ident[:])
            T_("transpose", ctx.ps[bank][0:8, 256:384], rb[:, 8:16], ident[:])
            mm_tr = ctx.mm_inc()
            ctx.wait("v", "MM", mm_tr)
            ctx.wait("v", "SQ", sq_ag2)
            V("tensor_copy", S1[0:8, 0:384], ctx.ps[bank][0:8, 0:384])
            ve_tr = ctx.ve_inc()
            ctx.bank_set(bank, "VE", ve_tr)
            ctx.wait("g", "VE", ve_tr)
            ctx.sp(ag2b_in[0:8, ti * 128:(ti + 1) * 128], S1[0:8, 0:128])
            ctx.sp(m12_d[0:8, ti * 128:(ti + 1) * 128], S1[0:8, 128:256])
            sp_ag2 = ctx.sp(m12_d[8:16, ti * 128:(ti + 1) * 128], S1[0:8, 256:384])
            sq_ag2 = ctx.sq()
        ve_rt = ctx.VE.n
    ctx.wait("g", "SP", sp_ag2)
    cc_ag2 = ctx.allgather(ag2_in, ag2_out)
    cc_ag2b = ctx.allgather(ag2b_in, ag2b_out)
    ctx.phase_barrier()
    ctx.areset(am5)

    am6 = ctx.amark()
    GA = ctx.aalloc(T)
    GB = ctx.aalloc(T)
    QT = ctx.aalloc(max(TL, 512))
    M1E = ctx.aalloc(TL)
    M2E = ctx.aalloc(TL)
    ctx.wait("y", "CC", cc_ag2b)
    ld_og = ctx.ld(GA[0:8, :], mkap(ag2b_out, 0, [[TL, 8], [8 * TL, 8], [1, TL]]))
    ld_m1 = ctx.ld(M1E[0:8, 0:TL], m12_d[0:8, :])
    ld_m2 = ctx.ld(M2E[0:8, 0:TL], m12_d[8:16, :])
    ld_og = ctx.lq()
    ctx.wait("v", "LQ", ld_og)
    V("tensor_tensor_scan", GB[0:8, :], GA[0:8, :],
      zz[0:8, 0:1].broadcast_to([8, T]), 0.0, ALU.add, ALU.add)
    V("scalar_tensor_tensor", GA[0:8, :], GB[0:8, :], float(g.CAP), GA[0:8, :],
      ALU.is_le, ALU.mult)
    V("scalar_tensor_tensor", GB[0:8, :], GB[0:8, :], ecap[:, 0:1], GA[0:8, :],
      ALU.add, ALU.mult)
    ve_g = ctx.ve_inc()
    sp_sv = 0
    sp_sv_hist = {}
    for o in range(T // 512):
        bank = o % 2
        ctx.bank_wait(bank)
        ctx.wait("t", "VE", ve_g)
        T_("matmul", ctx.ps[bank][0:1, 0:512], ohmy[:, 0:1], GB[0:8, o * 512:(o + 1) * 512],
           start=True, stop=False)
        T_("matmul", ctx.ps[bank][0:1, 0:512], ohs[:, 0:1], GA[0:8, o * 512:(o + 1) * 512],
           start=False, stop=True)
        mm_sv = ctx.mm_inc()
        ctx.wait("v", "MM", mm_sv)
        if o >= 2:
            ctx.wait("v", "SQ", sp_sv_hist[o - 2])
        V("tensor_scalar_add", R16[0:1, (o % 2) * 512:(o % 2) * 512 + 512],
          ctx.ps[bank][0:1, 0:512], float(g.BIG))
        ve_sv = ctx.ve_inc()
        ctx.bank_set(bank, "VE", ve_sv)
        ctx.wait("g", "VE", ve_sv)
        sp_sv = ctx.sp(svrow_d[0:1, o * 512:(o + 1) * 512],
                       R16[0:1, (o % 2) * 512:(o % 2) * 512 + 512])
        sp_sv_hist[o] = ctx.sq()
    ctx.wait("y", "SQ", ctx.SQ.n)
    ld_svt = ctx.ld(svtok[:], svrow_d.rearrange("o (a b) -> (o a) b", b=128),
                    transpose=True)
    ld_svt = ctx.lq()
    ctx.wait("v", "LQ", ld_svt)
    V("tensor_copy", idx32[:], svtok[:])
    ve_idx = ctx.ve_inc()
    sp_fld = 0
    sq_fld = 0
    for fi in range(4):
        src_em = M1E if fi < 2 else M2E
        G_t = GA if (fi % 2 == 0) else GB
        V("memset", QT[0:8, 0:TL], 0.0)
        ve_z = ctx.ve_inc()
        for k in range(8):
            V("scalar_tensor_tensor", QT[0:8, 0:TL], G_t[0:8, k * TL:(k + 1) * TL],
              ohown[:, k:k + 1], QT[0:8, 0:TL], ALU.mult, ALU.add)
        V("tensor_mul", QT[0:8, 0:TL], QT[0:8, 0:TL], src_em[0:8, 0:TL])
        ve_q = ctx.ve_inc()
        for o in range(max(1, TL // 512)):
            w_ = min(512, TL)
            bank = 2
            ctx.bank_wait(bank)
            ctx.wait("t", "VE", ve_q)
            T_("matmul", ctx.ps[bank][0:1, 0:w_], ones8[:, 0:1],
               QT[0:8, o * 512:o * 512 + w_], start=True, stop=True)
            mm_q = ctx.mm_inc()
            ctx.wait("v", "MM", mm_q)
            ctx.wait("v", "SQ", sq_fld)
            V("tensor_copy", R16[0:1, 0:w_], ctx.ps[bank][0:1, 0:w_])
            ve_fq = ctx.ve_inc()
            ctx.bank_set(bank, "VE", ve_fq)
            ctx.wait("g", "VE", ve_fq)
            sp_fld = ctx.sp(fld_d[fi:fi + 1, o * 512:o * 512 + w_], R16[0:1, 0:w_])
            sq_fld = ctx.sq()
    ctx.wait("y", "SQ", sq_fld)
    ld_fld = 0
    for fi in range(4):
        ld_fld = ctx.ld(fldtok[:, fi * 16:(fi + 1) * 16],
                        fld_d[fi:fi + 1, :].rearrange("o (a b) -> (o a) b", b=128),
                        transpose=True)
    ld_fld = ctx.lq()
    ctx.wait("v", "LQ", ld_fld)
    V("tensor_copy", fld32[:], fldtok[:])
    V("tensor_copy", idxf1[:], fldtok[:, 16:16 + TL // 128])
    V("tensor_copy", idxf2[:], fldtok[:, 48:48 + TL // 128])
    ve_fld = ctx.ve_inc()
    ctx.phase_barrier()
    ctx.areset(am6)

    am7 = ctx.amark()
    NSLOT = 4
    HROT = ctx.aalloc(NSLOT * 1024)
    sp_z = 0
    for r in range(0, g.CAP, 128):
        sp_z = ctx.sp(xbuf[r:r + 128, :], zz[:])
    bc = nc.gpsimd.alloc_register("bcreg")
    nc.gpsimd.reg_mov(bc, g.CAP - 1)
    ctx.wait("g", "SP", sp_z)
    ctx.wait("g", "VE", ve_idx)
    sq_scat = {}
    ld_hrow = {}
    for i in range(T // 128):
        ctx.wait("y", "CC", ctx.CC.n)
        if i >= NSLOT:
            ctx.wait("y", "SQ", sq_scat[i - NSLOT])
        ld_hrow[i] = ctx.ld(HROT[:, (i % NSLOT) * 1024:(i % NSLOT) * 1024 + 1024],
                            ag2_out[i * 128:(i + 1) * 128, :])
        ld_hrow[i] = ctx.lq()
        ctx.wait("g", "LQ", ld_hrow[i])
        nc.gpsimd.indirect_dma_start(
            xbuf, IndirectOffsetOnAxis(idx32[:, i:i + 1], 0),
            HROT[:, (i % NSLOT) * 1024:(i % NSLOT) * 1024 + 1024], None,
            bounds_check=bc, oob_is_err=False,
        ).then_inc(ctx.SP.h, 16)
        ctx.SP.n += 16
        if i % NSLOT == NSLOT - 1 or i == T // 128 - 1:
            s = ctx.sq()
            for k2 in range(max(0, i - NSLOT + 1), i + 1):
                sq_scat[k2] = s
    sp_scat_all = ctx.SQ.n

    W1B = ctx.aalloc(NJ * g.I)
    W2B = ctx.aalloc(2 * 1024)
    XTOK = ctx.aalloc(g.NSS * 1024)
    XFM = ctx.aalloc(NJ * g.STILE)
    INTB = ctx.aalloc(g.NIO * g.STILE)
    EOUT = ctx.aalloc(g.NSS * 1024)
    ld_w1 = 0
    for j in range(NJ):
        ld_w1 = ctx.ld(W1B[:, j * g.I:(j + 1) * g.I], ap["w1T"][j * 128:(j + 1) * 128, :])
    ld_w1 = ctx.lq()
    sp_e = 0
    sq_e = 0
    ve_eout_prev = 0
    for stile in range(g.NST):
        sbase = stile * g.STILE
        ld_xt = 0
        ctx.wait("y", "SQ", sp_scat_all)
        if stile > 0:
            ctx.wait("y", "VE", ve_eout_prev)
        for ss in range(g.NSS):
            ld_xt = ctx.ld(XTOK[:, ss * 1024:(ss + 1) * 1024],
                           xbuf[sbase + ss * 128:sbase + (ss + 1) * 128, :])
        ld_xt = ctx.lq()
        for j in range(NJ):
            bank = 4 + (j % 2)
            ctx.bank_wait(bank)
            ctx.wait("t", "LQ", ld_xt)
            if stile > 0 and j == 0:
                ctx.wait("v", "MM", mm_w1_prev)
            for ss in range(g.NSS):
                T_("transpose", ctx.ps[bank][:, ss * 128:(ss + 1) * 128],
                   XTOK[:, ss * 1024 + j * 128:ss * 1024 + (j + 1) * 128], ident[:])
            mm_tx = ctx.mm_inc()
            ctx.wait("v", "MM", mm_tx)
            V("tensor_copy", XFM[:, j * g.STILE:(j + 1) * g.STILE],
              ctx.ps[bank][:, 0:g.STILE])
            ctx.bank_set(bank, "VE", ctx.ve_inc())
        ve_xfm = ctx.VE.n
        if stile > 0:
            ctx.wait("s", "MM", mm_e_prev)
        for io in range(g.NIO):
            bank = io % 2
            ctx.bank_wait(bank)
            ctx.wait("t", "LQ", ld_w1)
            ctx.wait("t", "VE", ve_xfm)
            for cin in range(NJ):
                T_("matmul", ctx.ps[bank][:, 0:g.STILE],
                   W1B[:, cin * g.I + io * 128:cin * g.I + (io + 1) * 128],
                   XFM[:, cin * g.STILE:(cin + 1) * g.STILE],
                   start=(cin == 0), stop=(cin == NJ - 1))
            mm1 = ctx.mm_inc()
            ctx.wait("s", "MM", mm1)
            S("activation", INTB[:, io * g.STILE:(io + 1) * g.STILE],
              ctx.ps[bank][:, 0:g.STILE], AF.Relu)
            ctx.bank_set(bank, "SE", ctx.se_inc())
        se_int = ctx.SE.n
        mm_w1_prev = ctx.MM.n
        nh = 2 if C > 512 else 1
        ld_w2 = {}
        mm_e = {}
        for io in range(min(2, g.NIO)):
            ld_w2[io] = ctx.ld(W2B[:, (io % 2) * 1024:(io % 2) * 1024 + 1024],
                               ap["w2T"][io * 128:(io + 1) * 128, :])
            ld_w2[io] = ctx.lq()
        for io in range(g.NIO):
            ctx.wait("t", "LQ", ld_w2[io])
            ctx.wait("t", "SE", se_int)
            if io == 0:
                for b in range(g.NSS * nh):
                    ctx.bank_wait(b)
            for ss in range(g.NSS):
                for h in range(nh):
                    bank = ss * nh + h
                    T_("matmul", ctx.ps[bank][:, 0:512],
                       INTB[:, io * g.STILE + ss * 128:io * g.STILE + (ss + 1) * 128],
                       W2B[:, (io % 2) * 1024 + h * 512:(io % 2) * 1024 + (h + 1) * 512],
                       start=(io == 0), stop=(io == g.NIO - 1))
            mm_e[io] = ctx.mm_inc()
            if io + 2 < g.NIO:
                ctx.wait("y", "MM", mm_e[io])
                ld_w2[io + 2] = ctx.ld(W2B[:, (io % 2) * 1024:(io % 2) * 1024 + 1024],
                                       ap["w2T"][(io + 2) * 128:(io + 3) * 128, :])
                ld_w2[io + 2] = ctx.lq()
        ctx.wait("v", "MM", mm_e[g.NIO - 1])
        ctx.wait("v", "SQ", sq_e)
        for ss in range(g.NSS):
            for h in range(nh):
                bank = ss * nh + h
                V("tensor_copy", EOUT[:, ss * 1024 + h * 512:ss * 1024 + (h + 1) * 512],
                  ctx.ps[bank][:, 0:512])
                ctx.bank_set(bank, "VE", ctx.ve_inc())
        ve_eout_prev = ctx.VE.n
        mm_e_prev = mm_e[g.NIO - 1]
        for ss in range(g.NSS):
            ctx.wait("g", "VE", ve_eout_prev)
            sp_e = ctx.sp(ag3_in[sbase + ss * 128:sbase + (ss + 1) * 128, :],
                          EOUT[:, ss * 1024:(ss + 1) * 1024])
            if "eout_dbg" in ctx.dbg_set:
                sp_e = ctx.sp(eout_dbg[sbase + ss * 128:sbase + (ss + 1) * 128, :],
                              EOUT[:, ss * 1024:(ss + 1) * 1024])
        sq_e = ctx.sq()
    ctx.wait("g", "SP", sp_e)
    cc_ag3 = ctx.allgather(ag3_in, ag3_out)
    ctx.phase_barrier()
    ctx.areset(am7)

    am8 = ctx.amark()
    gbbc = ctx.aalloc(C)
    bank = 6
    ctx.bank_wait(bank)
    T_("transpose", ctx.ps[bank][0:8, 0:128], dct("gb"), ident[:])
    mm_gb = ctx.mm_inc()
    ctx.wait("v", "MM", mm_gb)
    V("tensor_copy", S1[0:8, 0:128], ctx.ps[bank][0:8, 0:128])
    ve_gb1 = ctx.ve_inc()
    ctx.bank_set(bank, "VE", ve_gb1)
    ctx.wait("g", "VE", ve_gb1)
    sp_gb = ctx.sp(gbrow_d.rearrange("o (j p) -> (o j) p", p=128), S1[0:8, 0:128])
    sq_gb = ctx.sq()
    ctx.wait("y", "SQ", sq_gb)
    gbrow = ctx.aalloc(C)
    ld_gb = ctx.ld(gbrow[0:1, 0:C], gbrow_d)
    ld_gb = ctx.lq()
    for h in range(2 if C > 512 else 1):
        w_ = min(512, C)
        bank = 6 + h
        ctx.bank_wait(bank)
        ctx.wait("t", "LQ", ld_gb)
        T_("matmul", ctx.ps[bank][:, 0:w_], onesrow[0:1, :], gbrow[0:1, h * 512:h * 512 + w_],
           start=True, stop=True)
        mm_gb2 = ctx.mm_inc()
        ctx.wait("v", "MM", mm_gb2)
        V("tensor_copy", gbbc[:, h * 512:h * 512 + w_], ctx.ps[bank][:, 0:w_])
        ctx.bank_set(bank, "VE", ctx.ve_inc())
    ve_gbbc = ctx.VE.n

    R1 = ctx.aalloc(2 * 1024)
    R2 = ctx.aalloc(2 * 1024)
    XMB = ctx.aalloc(2 * 1024)
    YB = ctx.aalloc(2 * 1024)
    sp_y = 0
    ve_y = {}
    sp_y_hist = {}
    for i in range(TL // 128):
        par = i % 2
        ctx.wait("g", "CC", cc_ag3)
        ctx.wait("g", "VE", ve_fld)
        if i >= 2:
            ctx.wait("g", "VE", ve_y[i - 2])
        nc.gpsimd.indirect_dma_start(
            R1[:, par * 1024:(par + 1) * 1024], None,
            ag3_out, IndirectOffsetOnAxis(idxf1[:, i:i + 1], 0),
        ).then_inc(ctx.SP.h, 16)
        ctx.SP.n += 16
        nc.gpsimd.indirect_dma_start(
            R2[:, par * 1024:(par + 1) * 1024], None,
            ag3_out, IndirectOffsetOnAxis(idxf2[:, i:i + 1], 0),
        ).then_inc(ctx.SP.h, 16)
        ctx.SP.n += 16
        sq_g = ctx.sq()
        if i >= 2:
            ctx.wait("y", "VE", ve_y[i - 2])
        ld_xm = ctx.ld(XMB[:, par * 1024:(par + 1) * 1024],
                       xmid_sp[i * 128:(i + 1) * 128, :])
        ld_xm = ctx.lq()
        ctx.wait("v", "SQ", sq_g)
        ctx.wait("v", "LQ", ld_xm)
        if i >= 2:
            ctx.wait("v", "SQ", sp_y_hist[i - 2])
        rb = routebuf[:, i * 18:(i + 1) * 18]
        V("tensor_mul", SB[:, 0:1], rb[:, 16:17], fld32[:, i:i + 1])
        V("tensor_mul", SB[:, 1:2], rb[:, 17:18], fld32[:, 32 + i:33 + i])
        V("tensor_single_scalar", YB[:, par * 1024:(par + 1) * 1024],
          R1[:, par * 1024:(par + 1) * 1024], SB[:, 0:1], ALU.mult)
        V("scalar_tensor_tensor", YB[:, par * 1024:(par + 1) * 1024],
          R2[:, par * 1024:(par + 1) * 1024], SB[:, 1:2],
          YB[:, par * 1024:(par + 1) * 1024], ALU.mult, ALU.add)
        ctx.wait("v", "VE", ve_gbbc)
        V("tensor_mul", YB[:, par * 1024:(par + 1) * 1024],
          YB[:, par * 1024:(par + 1) * 1024], gbbc[:])
        V("tensor_add", YB[:, par * 1024:(par + 1) * 1024],
          YB[:, par * 1024:(par + 1) * 1024], XMB[:, par * 1024:(par + 1) * 1024])
        ve_y[i] = ctx.ve_inc()
        ctx.wait("g", "VE", ve_y[i])
        sp_y = ctx.sp(y_out[i * 128:(i + 1) * 128, :], YB[:, par * 1024:(par + 1) * 1024])
        sp_y_hist[i] = ctx.sq()
    ctx.wait("g", "SP", sp_y)
    ctx.phase_barrier()
    ctx.areset(am8)
    return nc, ctx



def _vec_tile(v):
    return np.ascontiguousarray(np.asarray(v, np.float32).reshape(8, 128).T)


def make_inmaps(g: Geo, inputs):
    x = np.asarray(inputs["L0_attn_x"], np.float32).reshape(g.T, g.C)
    vecs = np.stack([_vec_tile(inputs[k]) for k in (
        "L0_attn_ln1_w", "L0_attn_ln1_b", "L0_attn_ln2_w", "L0_attn_ln2_b",
        "L0_attn_time_decay", "L0_attn_time_first", "L0_attn_tmk", "L0_attn_tmv",
        "L0_attn_tmr", "L0_attn_f_tmk", "L0_attn_f_tmr")], 0)
    shared = {
        "kwT": np.ascontiguousarray(np.asarray(inputs["L0_attn_att_kw"], np.float32).T),
        "vwT": np.ascontiguousarray(np.asarray(inputs["L0_attn_att_vw"], np.float32).T),
        "rwT": np.ascontiguousarray(np.asarray(inputs["L0_attn_att_rw"], np.float32).T),
        "owT": np.ascontiguousarray(np.asarray(inputs["L0_attn_att_ow"], np.float32).T),
        "frwT": np.ascontiguousarray(np.asarray(inputs["L0_attn_f_rw"], np.float32).T),
        "fkwT": np.ascontiguousarray(np.asarray(inputs["L0_attn_f_kw"], np.float32).T),
        "fvwT": np.ascontiguousarray(np.asarray(inputs["L0_attn_f_vw"], np.float32).T),
        "embT": np.ascontiguousarray(np.asarray(inputs["L0_ffn_expert_embeddings"], np.float32).T),
        "wmod": np.ascontiguousarray(np.asarray(inputs["L0_wmod"], np.float32)),
        "cmod": _vec_tile(np.asarray(inputs["L0_c"], np.float32).reshape(g.C)),
        "vecs": vecs,
        "ident": np.eye(128, dtype=np.float32),
        "iotarev": np.ascontiguousarray(np.broadcast_to(
            np.arange(g.TC - 1, -1, -1, dtype=np.float32), (128, g.TC))),
        "oh8id": np.eye(8, dtype=np.float32),
        "ecap": (np.arange(8, dtype=np.float32) * g.CAP - 1.0).reshape(8, 1),
    }
    w1 = np.asarray(inputs["L0_ffn_first_linear"], np.float32)
    w2 = np.asarray(inputs["L0_ffn_second_linear"], np.float32)
    maps = []
    for i in range(g.NCORES):
        seg = x[i * g.TL:(i + 1) * g.TL]
        bnd = np.zeros((1, g.C), np.float32) if i == 0 else x[i * g.TL - 1:i * g.TL]
        xT = np.ascontiguousarray(np.concatenate([bnd, seg], 0).T)
        masklt = (np.arange(8) < i).astype(np.float32)
        m = dict(shared)
        m.update({
            "xT": xT,
            "w1T": np.ascontiguousarray(w1[i].T),
            "w2T": np.ascontiguousarray(w2[i].T),
            "ohmy": np.eye(8, dtype=np.float32)[:, i:i + 1].copy(),
            "ohs": (np.eye(8, dtype=np.float32)[:, i:i + 1] *
                    (-(i * g.CAP + g.BIG))).astype(np.float32),
            "masklt8": np.tile(masklt[None, :], (8, 1)).astype(np.float32),
            "mneg8": ((1.0 - np.tile(masklt[None, :], (8, 1))) * NEG).astype(np.float32),
            "tlm8": (np.tile(masklt[None, :], (8, 1)) * g.TL).astype(np.float32),
            "mprev8": np.tile((np.arange(8) == i - 1).astype(np.float32)[None, :], (8, 1)).copy(),
            "ohown": np.tile((np.arange(8) == i).astype(np.float32)[None, :], (8, 1)).copy(),
            "maskfirst": np.full((8, 1), 0.0 if i == 0 else 1.0, np.float32),
            "maskfirstP": np.full((128, 1), 0.0 if i == 0 else 1.0, np.float32),
            "xbnd_in": _vec_tile(bnd[0]),
        })
        maps.append(m)
    return maps


def kernel(**inputs):
    g = Geo(16384, 512)
    nc, ctx = build_program(g)
    maps = make_inmaps(g, inputs)
    res = run_bass_kernel_spmd(nc, maps, core_ids=list(range(g.NCORES)))
    y = np.concatenate([res.results[i]["y"] for i in range(g.NCORES)], 0)
    return y.reshape(1, g.T, g.C).astype(np.float32)


# version 06
# speedup vs baseline: 1.0445x; 1.0445x over previous
import numpy as np
import concourse.bass as bass
import concourse.mybir as mybir
from concourse.bass_core_rust import IndirectOffsetOnAxis
from concourse.bass_types import AP as BassAP
from concourse.bass_utils import run_bass_kernel_spmd


def mkap(apref, offset, dims):
    return BassAP(apref.tensor, offset, [list(d) for d in dims])

F32 = mybir.dt.float32
I16 = mybir.dt.int16
I32 = mybir.dt.int32
AF = mybir.ActivationFunctionType
ALU = mybir.AluOpType
AX = mybir.AxisListType.X

LN_EPS = 1e-5
NEG = -1e38


class Geo:
    def __init__(self, T=16384, TC=512):
        self.T = T
        self.NCORES = 8
        self.TL = T // 8
        assert TC >= 128 and TC % 128 == 0 and TC <= 512
        self.TC = TC
        self.NCHK = self.TL // TC
        self.C = 1024
        self.NJ = 8
        self.F = 4096
        self.FS = 512
        self.NFS = self.F // self.FS
        self.I = 2048
        self.NIO = self.I // 128
        self.E = 8
        self.CAP = T // 8
        self.BIG = 20000
        self.STILE = min(512, self.CAP)
        self.NST = self.CAP // self.STILE
        self.NSS = self.STILE // 128


class Sem:
    def __init__(self, nc, name):
        self.h = nc.semaphore(name).__enter__()
        self.n = 0


class Ctx:
    def __init__(self, nc, g):
        self.nc = nc
        self.g = g
        self.LD = Sem(nc, "sLD")
        self.SP = Sem(nc, "sSP")
        self.MM = Sem(nc, "sMM")
        self.VE = Sem(nc, "sVE")
        self.SE = Sem(nc, "sSE")
        self.CC = Sem(nc, "sCC")
        self.LQ = Sem(nc, "sLQ")
        self.SQ = Sem(nc, "sSQ")
        self.sem = {"LD": self.LD, "SP": self.SP, "MM": self.MM,
                    "VE": self.VE, "SE": self.SE, "CC": self.CC,
                    "LQ": self.LQ, "SQ": self.SQ}
        self.last = {}
        self.ACOLS = 43008
        self.arena = nc.alloc_sbuf_tensor("arena", [128, self.ACOLS], F32)
        self.aoff = 0
        self.amax = 0
        self.ps = [nc.alloc_psum_tensor(f"psb{i}", [128, 512], F32) for i in range(8)]
        self.bank_free = [("VE", 0) for _ in range(8)]
        self.dbg_set = set()

    def amark(self):
        return self.aoff

    def areset(self, m):
        self.aoff = m

    def aalloc(self, cols):
        assert self.aoff + cols <= self.ACOLS, f"arena overflow {self.aoff}+{cols}"
        t = self.arena[:, self.aoff:self.aoff + cols]
        self.aoff += cols
        self.amax = max(self.amax, self.aoff)
        return t

    def dram(self, name, shape, dtype=F32):
        kind = "ExternalOutput" if name in self.dbg_set else None
        if kind:
            return self.nc.dram_tensor(name, shape, dtype, kind=kind)
        return self.nc.dram_tensor(name, shape, dtype)

    def _tr(self, eng, inst):
        self.last[eng] = inst
        return inst

    def t(self, op, *a, **k):
        return self._tr("t", getattr(self.nc.tensor, op)(*a, **k))

    DRAINS = True

    def v(self, op, *a, **k):
        inst = getattr(self.nc.vector, op)(*a, **k)
        if self.DRAINS:
            inst = self.nc.vector.drain()
        return self._tr("v", inst)

    def s(self, op, *a, **k):
        inst = getattr(self.nc.scalar, op)(*a, **k)
        if self.DRAINS:
            inst = self.nc.scalar.drain()
        return self._tr("s", inst)

    def mm_inc(self):
        self.last["t"].then_inc(self.MM.h, 1)
        self.MM.n += 1
        return self.MM.n

    def ve_inc(self):
        self.last["v"].then_inc(self.VE.h, 1)
        self.VE.n += 1
        return self.VE.n

    def se_inc(self):
        self.last["s"].then_inc(self.SE.h, 1)
        self.SE.n += 1
        return self.SE.n

    def ld(self, out, in_, transpose=False, slow=False):
        kw = {}
        if transpose:
            kw["transpose"] = True
        if slow:
            kw["allow_slow_non_contiguous"] = True
        self.nc.sync.dma_start(out=out, in_=in_, **kw).then_inc(self.LD.h, 16)
        self.LD.n += 16
        return self.LD.n

    def sp(self, out, in_):
        self.nc.gpsimd.dma_start(out=out, in_=in_).then_inc(self.SP.h, 16)
        self.SP.n += 16
        return self.SP.n

    def lq(self):
        self.nc.sync.wait_ge(self.LD.h, self.LD.n)
        self.nc.sync.sem_inc(self.LQ.h, 1)
        self.LQ.n += 1
        return self.LQ.n

    def sq(self):
        self.nc.gpsimd.wait_ge(self.SP.h, self.SP.n)
        self.nc.gpsimd.sem_inc(self.SQ.h, 1)
        self.SQ.n += 1
        return self.SQ.n

    def wait(self, eng, semname, val):
        if val <= 0:
            return
        e = {"t": self.nc.tensor, "v": self.nc.vector, "s": self.nc.scalar,
             "g": self.nc.gpsimd, "y": self.nc.sync}[eng]
        e.wait_ge(self.sem[semname].h, val)

    def bank_wait(self, i):
        s, v = self.bank_free[i]
        self.wait("t", s, v)

    def bank_set(self, i, semname, val):
        self.bank_free[i] = (semname, val)

    def phase_barrier(self):
        self.lq()
        self.sq()
        cur = {"LQ": self.LQ.n, "SQ": self.SQ.n, "MM": self.MM.n,
               "VE": self.VE.n, "SE": self.SE.n}
        for eng in ("y", "g", "v", "s", "t"):
            for k, v in cur.items():
                self.wait(eng, k, v)

    def dbg_dump(self, name, ap_, shape, dtype=F32):
        if name not in self.dbg_set:
            return
        t = self.nc.dram_tensor(name, shape, dtype, kind="ExternalOutput")
        self.wait("g", "VE", self.VE.n)
        self.wait("g", "SE", self.SE.n)
        self.sp(t.ap(), ap_)

    def allgather(self, bin_ap, bout_ap):
        self.nc.gpsimd.collective_compute(
            "AllGather", ALU.bypass,
            replica_groups=[list(range(self.g.NCORES))],
            ins=[bin_ap.opt()], outs=[bout_ap.opt()],
        ).then_inc(self.CC.h, 1)
        self.CC.n += 1
        return self.CC.n


def build_program(g: Geo, dbg=()):
    nc = bass.Bass("TRN2", target_bir_lowering=False, debug=False,
                   detect_race_conditions=False)
    ctx = Ctx(nc, g)
    ctx.dbg_set = set(dbg)
    C, NJ, TC, NCHK, TL, T = g.C, g.NJ, g.TC, g.NCHK, g.TL, g.T
    V = ctx.v
    S = ctx.s
    T_ = ctx.t

    ap = {}
    def inp(name, shape, dtype=F32):
        ap[name] = nc.dram_tensor(name, shape, dtype, kind="ExternalInput").ap()

    inp("xT", [C, TL + 1])
    for nm in ("kwT", "vwT", "rwT", "owT", "frwT"):
        inp(nm, [C, C])
    inp("fkwT", [C, g.F]); inp("fvwT", [g.F, C])
    inp("w1T", [C, g.I]); inp("w2T", [g.I, C])
    inp("embT", [C, g.E]); inp("wmod", [C, 6 * C])
    inp("cmod", [128, NJ]); inp("vecs", [11, 128, NJ])
    inp("ident", [128, 128]); inp("iotarev", [128, TC])
    inp("oh8id", [8, 8]); inp("ohmy", [8, 1]); inp("ohs", [8, 1]); inp("ecap", [8, 1])
    inp("masklt8", [8, 8]); inp("mneg8", [8, 8]); inp("tlm8", [8, 8])
    inp("mprev8", [8, 8]); inp("ohown", [8, 8])
    inp("maskfirst", [8, 1]); inp("maskfirstP", [128, 1])
    inp("xbnd_in", [128, NJ])

    y_out = nc.dram_tensor("y", [TL, C], F32, kind="ExternalOutput").ap()

    kvs_sp = ctx.dram("kvs_sp", [3 * C, TL]).ap()
    h0_sp = ctx.dram("h0_sp", [C, TL]).ap()
    a_sp = ctx.dram("a_sp", [C, TL]).ap()
    xk2_sp = ctx.dram("xk2_sp", [C, TL]).ap()
    sig_sp = ctx.dram("sig_sp", [C, TL]).ap()
    xmid_sp = ctx.dram("xmid_sp", [TL, C]).ap()
    svrow_d = ctx.dram("svrow_d", [1, T], I16).ap()
    fld_d = ctx.dram("fld_d", [4, 16 * 128], I16).ap()
    m12_d = ctx.dram("m12_d", [16, TL]).ap()
    gbrow_d = ctx.dram("gbrow_d", [1, C]).ap()
    xbuf = ctx.dram("xbuf", [g.CAP, C]).ap()
    if "h1_dbg" in ctx.dbg_set:
        h1_dbg = nc.dram_tensor("h1_dbg", [TL, C], F32, kind="ExternalOutput").ap()
    if "eout_dbg" in ctx.dbg_set:
        eout_dbg = nc.dram_tensor("eout_dbg", [g.CAP, C], F32, kind="ExternalOutput").ap()
    ag1_in = ctx.dram("ag1_in", [24, 128]).ap()
    ag1_out = ctx.dram("ag1_out", [192, 128]).ap()
    ag1b_in = ctx.dram("ag1b_in", [8, 128]).ap()
    ag1b_out = ctx.dram("ag1b_out", [64, 128]).ap()
    ag2_in = ctx.dram("ag2_in", [TL, C]).ap()
    ag2_out = ctx.dram("ag2_out", [T, C]).ap()
    ag2b_in = ctx.dram("ag2b_in", [8, TL]).ap()
    ag2b_out = ctx.dram("ag2b_out", [64, TL]).ap()
    ag3_in = ctx.dram("ag3_in", [g.CAP, C]).ap()
    ag3_out = ctx.dram("ag3_out", [8 * g.CAP, C]).ap()

    A = nc.alloc_sbuf_tensor
    vconst = A("vconst", [128, 11 * NJ], F32)
    dconst = A("dconst", [128, 8 * NJ], F32)
    ident = A("identt", [128, 128], F32)
    embTs = A("embTs", [128, NJ * g.E], F32)
    oh8id = A("oh8idt", [8, 8], F32)
    ohmy = A("ohmyt", [8, 1], F32)
    ohs = A("ohst", [8, 1], F32)
    ecap = A("ecapt", [8, 1], F32)
    masklt8 = A("masklt8t", [8, 8], F32)
    mneg8 = A("mneg8t", [8, 8], F32)
    tlm8 = A("tlm8t", [8, 8], F32)
    mprev8 = A("mprev8t", [8, 8], F32)
    ohown = A("ohownt", [8, 8], F32)
    maskfirst = A("maskfirstt", [8, 1], F32)
    maskfirstP = A("maskfirstPt", [128, 1], F32)
    onescol = A("onescol", [128, 1], F32)
    onesrow = A("onesrow", [1, 128], F32)
    ones8 = A("ones8", [8, 1], F32)
    state0 = A("state0", [128, 3 * NJ], F32)
    runsum = A("runsum", [128, 3 * NJ], F32)
    csum = A("csum", [128, 3 * NJ], F32)
    u2bnd = A("u2bnd", [128, NJ], F32)
    ubnd = A("ubnd", [128, NJ], F32)
    usave = A("usave", [128, NJ], F32)
    u2save = A("u2save", [128, NJ], F32)
    trow = A("trow", [8, 3 * 128], F32)
    routebuf = A("routebuf", [128, (TL // 128) * 18], F32)
    svtok = A("svtok", [128, T // 128], I16)
    fldtok = A("fldtok", [128, 64], I16)
    fld32 = A("fld32", [128, 64], F32)
    idxf1 = A("idxf1", [128, TL // 128], I32)
    idxf2 = A("idxf2", [128, TL // 128], I32)
    idx32 = A("idx32", [128, T // 128], I32)
    stats = A("stats", [1, 2608], F32)
    R16 = A("R16", [1, 1024], I16)
    SW = max(TC, 512)
    S1 = A("S1", [128, SW], F32)
    S2 = A("S2", [128, SW], F32)
    S3 = A("S3", [128, SW], F32)
    S4 = A("S4", [128, SW], F32)
    SQ2 = A("SQ2", [128, SW], F32)
    SB = A("SBs", [128, 32], F32)
    zz = A("zzt", [128, 1024], F32)
    lnscr = A("lnscr", [128, 2 * TC], F32)
    xbnd = A("xbndt", [128, NJ], F32)

    co = {"w": 0, "sa": 1, "ca1": 2, "ga": 3, "sb": 4, "cb1": 5, "gb": 6}
    vi = {"ln1w": 0, "ln1b": 1, "ln2w": 2, "ln2b": 3, "tdec": 4, "tfir": 5,
          "tmk": 6, "tmv": 7, "tmr": 8, "ftmk": 9, "ftmr": 10}

    def vc(name, j):
        return vconst[:, vi[name] * NJ + j: vi[name] * NJ + j + 1]

    def vct(name):
        return vconst[:, vi[name] * NJ:(vi[name] + 1) * NJ]

    def dc(name, j):
        return dconst[:, co[name] * NJ + j: co[name] * NJ + j + 1]

    def dct(name):
        return dconst[:, co[name] * NJ:(co[name] + 1) * NJ]

    def a_(buf, j, w=TC):
        return buf[:, j * w:(j + 1) * w]

    def a1_(buf, j):
        return buf[:, j * (TC + 1):(j + 1) * (TC + 1)]

    ctx.ld(vconst[:], mkap(ap["vecs"], 0, [[8, 128], [1024, 11], [1, NJ]]))
    ctx.ld(ident[:], ap["ident"])
    ctx.ld(embTs[:], mkap(ap["embT"], 0, [[g.E, 128], [128 * g.E, NJ], [1, g.E]]))
    for nm, tt in (("oh8id", oh8id), ("ohmy", ohmy), ("ohs", ohs), ("ecap", ecap),
                   ("masklt8", masklt8), ("mneg8", mneg8), ("tlm8", tlm8),
                   ("mprev8", mprev8), ("ohown", ohown), ("maskfirst", maskfirst),
                   ("maskfirstP", maskfirstP)):
        ctx.ld(tt[:], ap[nm])
    ctx.ld(SB[:, 0:NJ], ap["cmod"])
    ld_xb = ctx.ld(xbnd[:], ap["xbnd_in"])
    ld_const = ctx.lq()

    ctx.wait("v", "LQ", ld_const)
    V("memset", onescol[:], 1.0)
    V("memset", onesrow[:], 1.0)
    V("memset", ones8[:], 1.0)
    V("memset", zz[:], 0.0)
    V("memset", dconst[:], 0.0)
    V("memset", runsum[:, 0:NJ], NEG)
    V("memset", runsum[:, NJ:3 * NJ], 0.0)
    ve0 = ctx.ve_inc()
    ctx.wait("s", "LQ", ld_const)
    ctx.wait("s", "VE", ve0)
    S("activation", dct("w"), vct("tdec"), AF.Exp)
    S("activation", SB[:, NJ:2 * NJ], SB[:, 0:NJ], AF.Silu)
    se_silu = ctx.se_inc()
    ctx.wait("v", "SE", se_silu)
    V("tensor_scalar_mul", dct("w"), dct("w"), -1.0)
    ve_w = ctx.ve_inc()

    am0 = ctx.amark()
    modsb = ctx.aalloc(6 * C)
    wmbuf = ctx.aalloc(2 * NJ * 512)
    mm_prev = {}
    for o in range(12):
        ld_wm = 0
        for j in range(NJ):
            if o >= 2:
                ctx.wait("y", "MM", mm_prev[o - 2])
            ld_wm = ctx.ld(wmbuf[:, (o % 2) * NJ * 512 + j * 512:(o % 2) * NJ * 512 + (j + 1) * 512],
                           ap["wmod"][j * 128:(j + 1) * 128, o * 512:(o + 1) * 512])
        ld_wm = ctx.lq()
        bank = o % 2
        ctx.bank_wait(bank)
        ctx.wait("t", "LQ", ld_wm)
        ctx.wait("t", "SE", se_silu)
        for j in range(NJ):
            T_("matmul", ctx.ps[bank][0:1, 0:512], SB[:, NJ + j:NJ + j + 1],
               wmbuf[:, (o % 2) * NJ * 512 + j * 512:(o % 2) * NJ * 512 + (j + 1) * 512],
               start=(j == 0), stop=(j == NJ - 1))
        mm_prev[o] = ctx.mm_inc()
        ctx.wait("v", "MM", mm_prev[o])
        V("tensor_copy", modsb[0:1, o * 512:(o + 1) * 512], ctx.ps[bank][0:1, 0:512])
        ctx.bank_set(bank, "VE", ctx.ve_inc())

    bank = 2
    ctx.bank_wait(bank)
    ctx.wait("t", "VE", ctx.VE.n)
    for v6 in range(6):
        for j in range(NJ):
            T_("matmul", ctx.ps[bank][:, v6 * NJ + j:v6 * NJ + j + 1],
               modsb[0:1, v6 * C + j * 128:v6 * C + (j + 1) * 128],
               onesrow[0:1, 0:1], start=True, stop=True)
    mm_modT = ctx.mm_inc()
    ctx.wait("v", "MM", mm_modT)
    V("tensor_copy", dconst[:, NJ:7 * NJ], ctx.ps[bank][:, 0:48])
    V("tensor_scalar_add", dct("ca1"), dct("ca1"), 1.0)
    V("tensor_scalar_add", dct("cb1"), dct("cb1"), 1.0)
    for nm in ("ga", "gb"):
        V("tensor_single_scalar", SB[:, 16:16 + NJ], dct(nm), 0.0, ALU.is_ge)
        V("tensor_scalar", SB[:, 16:16 + NJ], SB[:, 16:16 + NJ], 2.0, 1.0, ALU.mult, ALU.subtract)
        V("tensor_mul", SB[:, 24:24 + NJ], dct(nm), SB[:, 16:16 + NJ])
        V("tensor_scalar_max", SB[:, 24:24 + NJ], SB[:, 24:24 + NJ], 0.25)
        V("tensor_mul", dct(nm), SB[:, 16:16 + NJ], SB[:, 24:24 + NJ])
    ve_dconst = ctx.ve_inc()
    ctx.bank_set(bank, "VE", ve_dconst)

    ctx.dbg_dump("dconst_d", dconst[:], [128, 8 * NJ])
    ctx.dbg_dump("vconst_d", vconst[:], [128, 11 * NJ])
    bank = 3
    ctx.bank_wait(bank)
    ctx.wait("t", "VE", ve_dconst)
    T_("transpose", ctx.ps[bank][0:8, 0:128], dct("w"), ident[:])
    T_("transpose", ctx.ps[bank][0:8, 128:256], vct("ln2w"), ident[:])
    T_("transpose", ctx.ps[bank][0:8, 256:384], vct("ln2b"), ident[:])
    mm_trow = ctx.mm_inc()
    ctx.wait("v", "MM", mm_trow)
    V("tensor_copy", trow[:], ctx.ps[bank][0:8, 0:384])
    ctx.bank_set(bank, "VE", ctx.ve_inc())

    def mini_ln(src, dst):
        bank = 4
        ctx.bank_wait(bank)
        ctx.wait("t", "VE", ctx.VE.n)
        ctx.wait("t", "SE", ctx.SE.n)
        ctx.wait("t", "LQ", ctx.LQ.n)
        T_("matmul", ctx.ps[bank][0:1, 0:NJ], onescol[:, 0:1], src, start=True, stop=True)
        ctx.mm_inc()
        ctx.wait("s", "VE", ctx.VE.n)
        ctx.wait("s", "LQ", ctx.LQ.n)
        ctx.wait("s", "MM", ctx.MM.n)
        S("activation", SB[:, 24:24 + NJ], src, AF.Square)
        se1 = ctx.se_inc()
        ctx.wait("t", "SE", se1)
        T_("matmul", ctx.ps[bank][0:1, NJ:2 * NJ], onescol[:, 0:1], SB[:, 24:24 + NJ],
           start=True, stop=True)
        mm2 = ctx.mm_inc()
        ctx.wait("v", "MM", mm2)
        st = stats
        V("reduce_sum", st[0:1, 0:1], ctx.ps[bank][0:1, 0:NJ], axis=AX)
        V("reduce_sum", st[0:1, 1:2], ctx.ps[bank][0:1, NJ:2 * NJ], axis=AX)
        V("tensor_scalar_mul", st[0:1, 0:2], st[0:1, 0:2], 1.0 / C)
        V("tensor_mul", st[0:1, 2:3], st[0:1, 0:1], st[0:1, 0:1])
        V("tensor_sub", st[0:1, 1:2], st[0:1, 1:2], st[0:1, 2:3])
        V("tensor_scalar_add", st[0:1, 1:2], st[0:1, 1:2], LN_EPS)
        ve1 = ctx.ve_inc()
        ctx.wait("s", "VE", ve1)
        S("activation", st[0:1, 3:4], st[0:1, 1:2], AF.Ln)
        se2 = ctx.se_inc()
        ctx.wait("v", "SE", se2)
        V("tensor_scalar_mul", st[0:1, 3:4], st[0:1, 3:4], -0.5)
        ve1b = ctx.ve_inc()
        ctx.wait("s", "VE", ve1b)
        S("activation", st[0:1, 4:5], st[0:1, 3:4], AF.Exp)
        se2b = ctx.se_inc()
        ctx.wait("v", "SE", se2b)
        V("tensor_copy", st[0:1, 4:5], st[0:1, 4:5])
        ve2 = ctx.ve_inc()
        ctx.wait("t", "VE", ve2)
        T_("matmul", ctx.ps[bank][:, 16:17], onesrow[0:1, :], st[0:1, 4:5], start=True, stop=True)
        T_("matmul", ctx.ps[bank][:, 17:18], onesrow[0:1, :], st[0:1, 0:1], start=True, stop=True)
        mm3 = ctx.mm_inc()
        ctx.wait("v", "MM", mm3)
        V("tensor_single_scalar", dst, src, ctx.ps[bank][:, 17:18], ALU.subtract)
        V("tensor_single_scalar", dst, dst, ctx.ps[bank][:, 16:17], ALU.mult)
        ctx.bank_set(bank, "VE", ctx.ve_inc())

    ctx.dbg_dump("xbnd_d", xbnd[:], [128, NJ])
    mini_ln(xbnd[:], SB[:, 0:NJ])
    ctx.dbg_dump("lnb_d", SB[:, 0:NJ], [128, NJ])
    ctx.dbg_dump("stats1_d", stats[0:1, 0:8], [1, 8])
    V("tensor_mul", SB[:, 0:NJ], SB[:, 0:NJ], dct("ca1"))
    V("tensor_add", SB[:, 0:NJ], SB[:, 0:NJ], dct("sa"))
    ctx.ve_inc()
    ctx.dbg_dump("h0b_d", SB[:, 0:NJ], [128, NJ])
    mini_ln(SB[:, 0:NJ], ubnd[:])
    V("tensor_mul", ubnd[:], ubnd[:], vct("ln1w"))
    V("tensor_add", ubnd[:], ubnd[:], vct("ln1b"))
    V("tensor_single_scalar", ubnd[:], ubnd[:], maskfirstP[:, 0:1], ALU.mult)
    ve_ubnd = ctx.ve_inc()
    ctx.dbg_dump("ubnd_d", ubnd[:], [128, NJ])
    ctx.phase_barrier()
    ctx.areset(am0)

    def ln_stats(tile_of, width, waits=()):
        bs, bq = 4, 5
        ctx.bank_wait(bs)
        ctx.bank_wait(bq)
        vew = ctx.VE.n
        ctx.wait("t", "VE", vew)
        for eng_sem_val in waits:
            ctx.wait("t", eng_sem_val[0], eng_sem_val[1])
        mmq = {}
        for j in range(NJ):
            T_("matmul", ctx.ps[bs][0:1, 0:width], onescol[:, 0:1], tile_of(j),
               start=(j == 0), stop=(j == NJ - 1))
            ctx.mm_inc()
        ctx.wait("s", "VE", vew)
        for eng_sem_val in waits:
            ctx.wait("s", eng_sem_val[0], eng_sem_val[1])
        for j in range(NJ):
            buf = S4 if (j % 2 == 0) else SQ2
            if j >= 2:
                ctx.wait("s", "MM", mmq[j - 2])
            S("activation", buf[:, 0:width], tile_of(j), AF.Square)
            seq = ctx.se_inc()
            ctx.wait("t", "SE", seq)
            T_("matmul", ctx.ps[bq][0:1, 0:width], onescol[:, 0:1], buf[:, 0:width],
               start=(j == 0), stop=(j == NJ - 1))
            mmq[j] = ctx.mm_inc()
        ctx.wait("v", "MM", mmq[NJ - 1])
        st = stats
        V("tensor_scalar_mul", st[0:1, 0:width], ctx.ps[bs][0:1, 0:width], 1.0 / C)
        V("tensor_scalar_mul", st[0:1, 512:512 + width], ctx.ps[bq][0:1, 0:width], 1.0 / C)
        V("tensor_mul", st[0:1, 1024:1024 + width], st[0:1, 0:width], st[0:1, 0:width])
        V("tensor_sub", st[0:1, 512:512 + width], st[0:1, 512:512 + width],
          st[0:1, 1024:1024 + width])
        V("tensor_scalar_add", st[0:1, 512:512 + width], st[0:1, 512:512 + width], LN_EPS)
        ve1 = ctx.ve_inc()
        ctx.wait("s", "VE", ve1)
        S("activation", st[0:1, 1024:1024 + width], st[0:1, 512:512 + width], AF.Ln)
        se1 = ctx.se_inc()
        ctx.wait("v", "SE", se1)
        V("tensor_scalar_mul", st[0:1, 1024:1024 + width], st[0:1, 1024:1024 + width], -0.5)
        ve1b = ctx.ve_inc()
        ctx.wait("s", "VE", ve1b)
        S("activation", st[0:1, 1536:1536 + width], st[0:1, 1024:1024 + width], AF.Exp)
        se1b = ctx.se_inc()
        ctx.wait("v", "SE", se1b)
        V("tensor_mul", st[0:1, 0:width], st[0:1, 0:width], st[0:1, 1536:1536 + width])
        V("tensor_scalar_mul", st[0:1, 0:width], st[0:1, 0:width], -1.0)
        ve2 = ctx.ve_inc()
        ctx.wait("t", "VE", ve2)
        T_("matmul", ctx.ps[bs][:, 0:width], onesrow[0:1, :], st[0:1, 1536:1536 + width],
           start=True, stop=True)
        T_("matmul", ctx.ps[bq][:, 0:width], onesrow[0:1, :], st[0:1, 0:width],
           start=True, stop=True)
        mmb = ctx.mm_inc()
        ctx.wait("v", "MM", mmb)
        V("tensor_copy", lnscr[:, 0:width], ctx.ps[bs][:, 0:width])
        V("tensor_copy", lnscr[:, TC:TC + width], ctx.ps[bq][:, 0:width])
        vee = ctx.ve_inc()
        ctx.bank_set(bs, "VE", vee)
        ctx.bank_set(bq, "VE", vee)

    am1 = ctx.amark()
    iotarev = ctx.aalloc(TC)
    ctx.ld(iotarev, ap["iotarev"])
    A0 = ctx.aalloc(NJ * (TC + 1))
    A1 = ctx.aalloc(NJ * TC)
    A2 = ctx.aalloc(NJ * (TC + 1))
    A3 = ctx.aalloc(NJ * TC)
    KB = ctx.aalloc(NJ * TC)
    VB = ctx.aalloc(NJ * TC)
    RB = ctx.aalloc(NJ * TC)
    WBUF = ctx.aalloc(NJ * 1024)

    wname = {"k": "kwT", "v": "vwT", "r": "rwT"}
    spill_done = 0
    sq_chunk = 0
    mm_w_last = 0
    ve_mix_done = 0
    for c in range(NCHK):
        base = c * TC
        ld_x = 0
        for j in range(NJ):
            if c > 0:
                ctx.wait("y", "VE", ve_h0_prev)
            ld_x = ctx.ld(a1_(A0, j), ap["xT"][j * 128:(j + 1) * 128, base:base + TC + 1])
        ld_x = ctx.lq()
        ln_stats(lambda j: a1_(A0, j)[:, 1:TC + 1], TC, waits=(("LQ", ld_x),))
        ctx.wait("v", "LQ", ld_x)
        ctx.wait("v", "SQ", sq_chunk)
        for j in range(NJ):
            V("tensor_mul", a_(A1, j), a1_(A0, j)[:, 1:TC + 1], lnscr[:, 0:TC])
            V("tensor_add", a_(A1, j), a_(A1, j), lnscr[:, TC:2 * TC])
            V("tensor_scalar", a_(A1, j), a_(A1, j), dc("ca1", j), dc("sa", j),
              ALU.mult, ALU.add)
        ve_h0 = ctx.ve_inc()
        ve_h0_prev = ve_h0
        for j in range(NJ):
            ctx.wait("g", "VE", ve_h0)
            spill_done = ctx.sp(h0_sp[j * 128:(j + 1) * 128, base:base + TC], a_(A1, j))
        ln_stats(lambda j: a_(A1, j), TC)
        for j in range(NJ):
            V("tensor_mul", a1_(A2, j)[:, 1:TC + 1], a_(A1, j), lnscr[:, 0:TC])
            V("tensor_add", a1_(A2, j)[:, 1:TC + 1], a1_(A2, j)[:, 1:TC + 1],
              lnscr[:, TC:2 * TC])
            V("tensor_scalar", a1_(A2, j)[:, 1:TC + 1], a1_(A2, j)[:, 1:TC + 1],
              vc("ln1w", j), vc("ln1b", j), ALU.mult, ALU.add)
            V("tensor_copy", a1_(A2, j)[:, 0:1],
              ubnd[:, j:j + 1] if c == 0 else usave[:, j:j + 1])
        ctx.ve_inc()
        for j in range(NJ):
            V("tensor_copy", usave[:, j:j + 1], a1_(A2, j)[:, TC:TC + 1])
        ctx.ve_inc()

        for nm in ("k", "v", "r"):
            tm = {"k": "tmk", "v": "tmv", "r": "tmr"}[nm]
            ctx.wait("v", "MM", mm_w_last)
            for j in range(NJ):
                V("tensor_sub", a_(A3, j), a1_(A2, j)[:, 1:TC + 1], a1_(A2, j)[:, 0:TC])
                V("scalar_tensor_tensor", a_(A3, j), a_(A3, j), vc(tm, j),
                  a1_(A2, j)[:, 0:TC], ALU.mult, ALU.add)
            ve_mix = ctx.ve_inc()
            ld_w = 0
            for j in range(NJ):
                ctx.wait("y", "MM", mm_w_last)
                ld_w = ctx.ld(WBUF[:, j * 1024:(j + 1) * 1024],
                              ap[wname[nm]][j * 128:(j + 1) * 128, :])
            ld_w = ctx.lq()
            dstbuf = {"k": KB, "v": VB, "r": RB}[nm]
            for cout in range(NJ):
                bank = cout % 2
                ctx.bank_wait(bank)
                ctx.wait("t", "LQ", ld_w)
                ctx.wait("t", "VE", ve_mix)
                for cin in range(NJ):
                    T_("matmul", ctx.ps[bank][:, 0:TC],
                       WBUF[:, cin * 1024 + cout * 128:cin * 1024 + (cout + 1) * 128],
                       a_(A3, cin), start=(cin == 0), stop=(cin == NJ - 1))
                mmv = ctx.mm_inc()
                if nm == "r":
                    ctx.wait("s", "MM", mmv)
                    S("activation", a_(dstbuf, cout), ctx.ps[bank][:, 0:TC], AF.Sigmoid)
                    ctx.bank_set(bank, "SE", ctx.se_inc())
                else:
                    ctx.wait("v", "MM", mmv)
                    V("tensor_copy", a_(dstbuf, cout), ctx.ps[bank][:, 0:TC])
                    ctx.bank_set(bank, "VE", ctx.ve_inc())
            mm_w_last = ctx.MM.n

        ve_k = ctx.VE.n
        se_r = ctx.SE.n
        for vi2, buf in ((0, KB), (1, VB), (2, RB)):
            for j in range(NJ):
                ctx.wait("g", "VE", ve_k)
                ctx.wait("g", "SE", se_r)
                spill_done = ctx.sp(
                    kvs_sp[vi2 * C + j * 128:vi2 * C + (j + 1) * 128, base:base + TC],
                    a_(buf, j))
        sq_chunk = ctx.sq()

        for j in range(NJ):
            V("tensor_scalar_mul", S1[:, 0:TC], iotarev, dc("w", j))
            V("tensor_add", S1[:, 0:TC], S1[:, 0:TC], a_(KB, j))
            V("reduce_max", csum[:, j:j + 1], S1[:, 0:TC], axis=AX)
            V("tensor_single_scalar", S1[:, 0:TC], S1[:, 0:TC], csum[:, j:j + 1], ALU.subtract)
            ve_e = ctx.ve_inc()
            ctx.wait("s", "VE", ve_e)
            S("activation", S2[:, 0:TC], S1[:, 0:TC], AF.Exp)
            se_e = ctx.se_inc()
            ctx.wait("v", "SE", se_e)
            V("reduce_sum", csum[:, 2 * NJ + j:2 * NJ + j + 1], S2[:, 0:TC], axis=AX)
            V("tensor_mul", S2[:, 0:TC], S2[:, 0:TC], a_(VB, j))
            V("reduce_sum", csum[:, NJ + j:NJ + j + 1], S2[:, 0:TC], axis=AX)
            ctx.ve_inc()
        Mr, Ar, Br = runsum[:, 0:NJ], runsum[:, NJ:2 * NJ], runsum[:, 2 * NJ:3 * NJ]
        Mc_, Ac_, Bc_ = csum[:, 0:NJ], csum[:, NJ:2 * NJ], csum[:, 2 * NJ:3 * NJ]
        V("scalar_tensor_tensor", SB[:, 0:NJ], dct("w"), float(TC), Mr, ALU.mult, ALU.add)
        V("tensor_max", SB[:, 8:8 + NJ], SB[:, 0:NJ], Mc_)
        V("tensor_sub", SB[:, 0:NJ], SB[:, 0:NJ], SB[:, 8:8 + NJ])
        V("tensor_sub", SB[:, 16:16 + NJ], Mc_, SB[:, 8:8 + NJ])
        ve_c1 = ctx.ve_inc()
        ctx.wait("s", "VE", ve_c1)
        S("activation", SB[:, 0:NJ], SB[:, 0:NJ], AF.Exp)
        S("activation", SB[:, 16:16 + NJ], SB[:, 16:16 + NJ], AF.Exp)
        se_c = ctx.se_inc()
        ctx.wait("v", "SE", se_c)
        V("tensor_mul", Ar, Ar, SB[:, 0:NJ])
        V("tensor_mul", SB[:, 24:24 + NJ], Ac_, SB[:, 16:16 + NJ])
        V("tensor_add", Ar, Ar, SB[:, 24:24 + NJ])
        V("tensor_mul", Br, Br, SB[:, 0:NJ])
        V("tensor_mul", SB[:, 24:24 + NJ], Bc_, SB[:, 16:16 + NJ])
        V("tensor_add", Br, Br, SB[:, 24:24 + NJ])
        V("tensor_copy", Mr, SB[:, 8:8 + NJ])
        ve_runsum = ctx.ve_inc()

    bank = 6
    ctx.bank_wait(bank)
    ctx.wait("t", "VE", ve_runsum)
    T_("transpose", ctx.ps[bank][0:24, 0:128], runsum[:], ident[:])
    mm_ag1 = ctx.mm_inc()
    ctx.wait("v", "MM", mm_ag1)
    V("tensor_copy", S1[0:24, 0:128], ctx.ps[bank][0:24, 0:128])
    ve_ag1 = ctx.ve_inc()
    ctx.bank_set(bank, "VE", ve_ag1)
    ctx.wait("g", "VE", ve_ag1)
    sp_ag1 = ctx.sp(ag1_in, S1[0:24, 0:128])
    ctx.wait("g", "SP", sp_ag1)
    cc_ag1 = ctx.allgather(ag1_in, ag1_out)

    allrM = ctx.aalloc(8 * 128)
    allrA = ctx.aalloc(8 * 128)
    allrB = ctx.aalloc(8 * 128)
    stP = ctx.aalloc(128)
    stA = ctx.aalloc(128)
    stB = ctx.aalloc(128)
    q1 = ctx.aalloc(128)
    q2 = ctx.aalloc(128)
    q3 = ctx.aalloc(128)
    ctx.wait("y", "CC", cc_ag1)
    for vv, dst in ((0, allrM), (1, allrA), (2, allrB)):
        ld_allr = ctx.ld(dst[0:8, :],
                         mkap(ag1_out, vv * 8 * 128, [[128, 8], [24 * 128, 8], [1, 128]]))
    ld_allr = ctx.lq()
    ctx.wait("v", "LQ", ld_allr)
    V("memset", stP[0:8, :], NEG)
    V("memset", stA[0:8, :], 0.0)
    V("memset", stB[0:8, :], 0.0)
    ctx.ve_inc()
    wrow = trow[0:8, 0:128]
    for k in range(8):
        Msl = allrM[0:8, k * 128:(k + 1) * 128]
        Asl = allrA[0:8, k * 128:(k + 1) * 128]
        Bsl = allrB[0:8, k * 128:(k + 1) * 128]
        V("tensor_scalar", q1[0:8, :], Msl, masklt8[:, k:k + 1], mneg8[:, k:k + 1],
          ALU.mult, ALU.add)
        V("scalar_tensor_tensor", q2[0:8, :], wrow, tlm8[:, k:k + 1], stP[0:8, :],
          ALU.mult, ALU.add)
        V("tensor_max", q3[0:8, :], q2[0:8, :], q1[0:8, :])
        V("tensor_sub", q2[0:8, :], q2[0:8, :], q3[0:8, :])
        V("tensor_sub", q1[0:8, :], q1[0:8, :], q3[0:8, :])
        ve_cb = ctx.ve_inc()
        ctx.wait("s", "VE", ve_cb)
        S("activation", q2[0:8, :], q2[0:8, :], AF.Exp)
        S("activation", q1[0:8, :], q1[0:8, :], AF.Exp)
        se_cb = ctx.se_inc()
        ctx.wait("v", "SE", se_cb)
        V("tensor_copy", stP[0:8, :], q3[0:8, :])
        V("tensor_mul", stA[0:8, :], stA[0:8, :], q2[0:8, :])
        V("tensor_single_scalar", q3[0:8, :], Asl, masklt8[:, k:k + 1], ALU.mult)
        V("tensor_mul", q3[0:8, :], q3[0:8, :], q1[0:8, :])
        V("tensor_add", stA[0:8, :], stA[0:8, :], q3[0:8, :])
        V("tensor_mul", stB[0:8, :], stB[0:8, :], q2[0:8, :])
        V("tensor_single_scalar", q3[0:8, :], Bsl, masklt8[:, k:k + 1], ALU.mult)
        V("tensor_mul", q3[0:8, :], q3[0:8, :], q1[0:8, :])
        V("tensor_add", stB[0:8, :], stB[0:8, :], q3[0:8, :])
        ctx.ve_inc()
    bank = 7
    ctx.bank_wait(bank)
    ctx.wait("t", "VE", ctx.VE.n)
    for sidx, st_t in enumerate((stP, stA, stB)):
        for j in range(NJ):
            T_("matmul", ctx.ps[bank][:, sidx * NJ + j:sidx * NJ + j + 1], st_t[0:8, :],
               oh8id[:, j:j + 1], start=True, stop=True)
    mm_st0 = ctx.mm_inc()
    ctx.wait("v", "MM", mm_st0)
    V("tensor_copy", state0[:], ctx.ps[bank][:, 0:3 * NJ])
    ve_state0 = ctx.ve_inc()
    ctx.dbg_dump("state0_d", state0[:], [128, 3 * NJ])
    ctx.dbg_dump("runsum_d", runsum[:], [128, 3 * NJ])
    ctx.bank_set(bank, "VE", ve_state0)
    ctx.phase_barrier()
    ctx.areset(am1)

    am4 = ctx.amark()
    KB = ctx.aalloc(NJ * TC)
    VB = ctx.aalloc(NJ * TC)
    RB = ctx.aalloc(NJ * TC)
    BPP = ctx.aalloc(NJ * (TC + 1))
    BAA = ctx.aalloc(NJ * (TC + 1))
    BBB = ctx.aalloc(NJ * (TC + 1))
    RW = ctx.aalloc(NJ * TC)
    AB = ctx.aalloc(NJ * TC)
    H0R = ctx.aalloc(2 * TC)
    D4 = ctx.aalloc(4 * TC)
    U4 = ctx.aalloc(4 * TC)
    UV4 = ctx.aalloc(4 * TC)
    WOB = ctx.aalloc(2 * NJ * 128)

    def ld_w_cout(wap, cout, slot):
        t = 0
        for cin in range(NJ):
            t = ctx.ld(WOB[:, slot * 1024 + cin * 128:slot * 1024 + (cin + 1) * 128],
                       wap[cin * 128:(cin + 1) * 128, cout * 128:(cout + 1) * 128])
        return ctx.lq()

    sp_a = 0
    sq_a = 0
    ve_kdone = 0
    mm_op_last = 0
    ve_adone_prev = 0
    cc_ag1b = 0
    for c in range(NCHK):
        base = c * TC
        ld_kvs = 0
        ctx.wait("y", "SQ", sq_chunk)
        ctx.wait("y", "VE", ve_kdone)
        for vi2, buf in ((0, KB), (1, VB), (2, RB)):
            for j in range(NJ):
                ld_kvs = ctx.ld(a_(buf, j),
                                kvs_sp[vi2 * C + j * 128:vi2 * C + (j + 1) * 128,
                                       base:base + TC])
        ld_kvs = ctx.lq()
        ctx.wait("v", "LQ", ld_kvs)
        ctx.wait("v", "VE", ve_state0)
        ctx.wait("v", "MM", mm_op_last)
        ctx.wait("v", "SQ", sq_a)
        for half in range(2):
            js = list(range(half * 4, half * 4 + 4))
            for jj, j in enumerate(js):
                bpp, baa, bbb = a1_(BPP, j), a1_(BAA, j), a1_(BBB, j)
                if c == 0:
                    V("tensor_copy", bpp[:, 0:1], state0[:, j:j + 1])
                    V("tensor_copy", baa[:, 0:1], state0[:, NJ + j:NJ + j + 1])
                    V("tensor_copy", bbb[:, 0:1], state0[:, 2 * NJ + j:2 * NJ + j + 1])
                else:
                    V("tensor_copy", bpp[:, 0:1], bpp[:, TC:TC + 1])
                    V("tensor_copy", baa[:, 0:1], baa[:, TC:TC + 1])
                    V("tensor_copy", bbb[:, 0:1], bbb[:, TC:TC + 1])
                V("tensor_tensor_scan", bpp[:, 1:TC + 1],
                  dc("w", j).broadcast_to([128, TC]), a_(KB, j), bpp[:, 0:1],
                  ALU.add, ALU.max)
            for jj, j in enumerate(js):
                bpp = a1_(BPP, j)
                V("scalar_tensor_tensor", a_(D4, jj), bpp[:, 0:TC], dc("w", j),
                  bpp[:, 1:TC + 1], ALU.add, ALU.subtract)
                V("tensor_sub", a_(U4, jj), a_(KB, j), bpp[:, 1:TC + 1])
            ve_b1 = ctx.ve_inc()
            ctx.wait("s", "VE", ve_b1)
            for jj in range(4):
                S("activation", a_(D4, jj), a_(D4, jj), AF.Exp)
                S("activation", a_(U4, jj), a_(U4, jj), AF.Exp)
            se_b1 = ctx.se_inc()
            ctx.wait("v", "SE", se_b1)
            for jj, j in enumerate(js):
                bpp, baa, bbb = a1_(BPP, j), a1_(BAA, j), a1_(BBB, j)
                V("tensor_mul", a_(UV4, jj), a_(U4, jj), a_(VB, j))
                V("tensor_tensor_scan", bbb[:, 1:TC + 1], a_(D4, jj), a_(U4, jj),
                  bbb[:, 0:1], ALU.mult, ALU.add)
                V("tensor_tensor_scan", baa[:, 1:TC + 1], a_(D4, jj), a_(UV4, jj),
                  baa[:, 0:1], ALU.mult, ALU.add)
            for jj, j in enumerate(js):
                bpp = a1_(BPP, j)
                V("tensor_single_scalar", a_(KB, j), a_(KB, j), vc("tfir", j), ALU.add)
                V("tensor_max", a_(UV4, jj), bpp[:, 0:TC], a_(KB, j))
                V("tensor_sub", a_(D4, jj), bpp[:, 0:TC], a_(UV4, jj))
                V("tensor_sub", a_(KB, j), a_(KB, j), a_(UV4, jj))
            ve_b2 = ctx.ve_inc()
            ctx.wait("s", "VE", ve_b2)
            for jj, j in enumerate(js):
                S("activation", a_(D4, jj), a_(D4, jj), AF.Exp)
                S("activation", a_(KB, j), a_(KB, j), AF.Exp)
            se_b2 = ctx.se_inc()
            ctx.wait("v", "SE", se_b2)
            for jj, j in enumerate(js):
                bbb = a1_(BBB, j)
                V("tensor_mul", a_(UV4, jj), a_(D4, jj), bbb[:, 0:TC])
                V("tensor_add", a_(UV4, jj), a_(UV4, jj), a_(KB, j))
            ve_b3 = ctx.ve_inc()
            ctx.wait("s", "VE", ve_b3)
            for jj in range(4):
                S("activation", a_(UV4, jj), a_(UV4, jj), AF.Ln)
            se_b3 = ctx.se_inc()
            ctx.wait("v", "SE", se_b3)
            for jj in range(4):
                V("tensor_scalar_mul", a_(UV4, jj), a_(UV4, jj), -1.0)
            ve_b4 = ctx.ve_inc()
            ctx.wait("s", "VE", ve_b4)
            for jj in range(4):
                S("activation", a_(UV4, jj), a_(UV4, jj), AF.Exp)
            se_b4 = ctx.se_inc()
            ctx.wait("v", "SE", se_b4)
            for jj, j in enumerate(js):
                baa = a1_(BAA, j)
                V("tensor_mul", a_(D4, jj), a_(D4, jj), baa[:, 0:TC])
                V("tensor_mul", a_(KB, j), a_(KB, j), a_(VB, j))
                V("tensor_add", a_(D4, jj), a_(D4, jj), a_(KB, j))
                V("tensor_mul", a_(D4, jj), a_(D4, jj), a_(UV4, jj))
                V("tensor_mul", a_(RW, j), a_(D4, jj), a_(RB, j))
            ctx.ve_inc()
        ve_kdone = ctx.VE.n

        evac_a = {}
        mmo_hist = {}
        lqw = {}

        def issue_op_load(cout):
            if cout >= 2:
                ctx.wait("y", "MM", mmo_hist[cout - 2])
                ctx.wait("y", "VE", evac_a[cout - 2])
            else:
                ctx.wait("y", "MM", mm_op_last)
                ctx.wait("y", "VE", ve_adone_prev)
            ld_w_cout(ap["owT"], cout, cout % 2)
            ctx.ld(H0R[:, (cout % 2) * TC:(cout % 2) * TC + TC],
                   h0_sp[cout * 128:(cout + 1) * 128, base:base + TC])
            lqw[cout] = ctx.lq()

        issue_op_load(0)
        issue_op_load(1)
        for cout in range(NJ):
            bank = cout % 2
            ctx.bank_wait(bank)
            ctx.wait("t", "LQ", lqw[cout])
            ctx.wait("t", "VE", ve_kdone)
            for cin in range(NJ):
                T_("matmul", ctx.ps[bank][:, 0:TC],
                   WOB[:, (cout % 2) * 1024 + cin * 128:(cout % 2) * 1024 + (cin + 1) * 128],
                   a_(RW, cin), start=(cin == 0), stop=(cin == NJ - 1))
            mmo = ctx.mm_inc()
            mmo_hist[cout] = mmo
            ctx.wait("v", "MM", mmo)
            V("tensor_add", a_(AB, cout), ctx.ps[bank][:, 0:TC],
              H0R[:, (cout % 2) * TC:(cout % 2) * TC + TC])
            evac_a[cout] = ctx.ve_inc()
            ctx.bank_set(bank, "VE", evac_a[cout])
            if cout + 2 < NJ:
                issue_op_load(cout + 2)
        mm_op_last = ctx.MM.n
        ve_adone = ctx.VE.n
        ve_adone_prev = ve_adone
        for j in range(NJ):
            ctx.wait("g", "VE", ve_adone)
            sp_a = ctx.sp(a_sp[j * 128:(j + 1) * 128, base:base + TC], a_(AB, j))
        sq_a = ctx.sq()
        if c == NCHK - 1:
            for j in range(NJ):
                V("tensor_copy", SB[:, j:j + 1], a_(AB, j)[:, TC - 1:TC])
            ve_al = ctx.ve_inc()
            bank = 6
            ctx.bank_wait(bank)
            ctx.wait("t", "VE", ve_al)
            T_("transpose", ctx.ps[bank][0:8, 0:128], SB[:, 0:NJ], ident[:])
            mm_al = ctx.mm_inc()
            ctx.wait("v", "MM", mm_al)
            V("tensor_copy", S1[0:8, 0:128], ctx.ps[bank][0:8, 0:128])
            ve_al2 = ctx.ve_inc()
            ctx.bank_set(bank, "VE", ve_al2)
            ctx.wait("g", "VE", ve_al2)
            sp_ag1b = ctx.sp(ag1b_in, S1[0:8, 0:128])
            ctx.wait("g", "SP", sp_ag1b)
            cc_ag1b = ctx.allgather(ag1b_in, ag1b_out)

    ctx.phase_barrier()
    ctx.areset(am4)
    ab8 = ctx.aalloc(8 * 128)
    ctx.wait("y", "CC", cc_ag1b)
    ld_ab8 = ctx.ld(ab8[0:8, :],
                    mkap(ag1b_out, 0, [[128, 8], [8 * 128, 8], [1, 128]]))
    ld_ab8 = ctx.lq()
    ctx.wait("v", "LQ", ld_ab8)
    V("memset", S1[0:8, 0:128], 0.0)
    for k in range(8):
        V("scalar_tensor_tensor", S1[0:8, 0:128], ab8[0:8, k * 128:(k + 1) * 128],
          mprev8[:, k:k + 1], S1[0:8, 0:128], ALU.mult, ALU.add)
    ve_am = ctx.ve_inc()
    bank = 4
    ctx.bank_wait(bank)
    V("reduce_sum", S2[0:8, 0:1], S1[0:8, 0:128], axis=AX)
    ve_rs = ctx.ve_inc()
    ctx.wait("s", "VE", ve_rs)
    S("activation", S3[0:8, 0:128], S1[0:8, 0:128], AF.Square)
    se_sq = ctx.se_inc()
    ctx.wait("v", "SE", se_sq)
    V("reduce_sum", S2[0:8, 1:2], S3[0:8, 0:128], axis=AX)
    ve_rs2 = ctx.ve_inc()
    ctx.wait("t", "VE", ve_rs2)
    T_("matmul", ctx.ps[bank][0:1, 0:2], ones8[:, 0:1], S2[0:8, 0:2], start=True, stop=True)
    mm_bn = ctx.mm_inc()
    ctx.wait("v", "MM", mm_bn)
    st = stats
    V("tensor_scalar_mul", st[0:1, 0:2], ctx.ps[bank][0:1, 0:2], 1.0 / C)
    V("tensor_mul", st[0:1, 2:3], st[0:1, 0:1], st[0:1, 0:1])
    V("tensor_sub", st[0:1, 1:2], st[0:1, 1:2], st[0:1, 2:3])
    V("tensor_scalar_add", st[0:1, 1:2], st[0:1, 1:2], LN_EPS)
    ve_bv = ctx.ve_inc()
    ctx.wait("s", "VE", ve_bv)
    S("activation", st[0:1, 3:4], st[0:1, 1:2], AF.Ln)
    se_bs = ctx.se_inc()
    ctx.wait("v", "SE", se_bs)
    V("tensor_scalar_mul", st[0:1, 3:4], st[0:1, 3:4], -0.5)
    ve_bv2 = ctx.ve_inc()
    ctx.wait("s", "VE", ve_bv2)
    S("activation", st[0:1, 4:5], st[0:1, 3:4], AF.Exp)
    se_bs2 = ctx.se_inc()
    ctx.wait("v", "SE", se_bs2)
    V("tensor_copy", st[0:1, 4:5], st[0:1, 4:5])
    ve_bA = ctx.ve_inc()
    ctx.wait("t", "VE", ve_bA)
    T_("matmul", ctx.ps[bank][0:8, 4:5], onesrow[0:1, 0:8], st[0:1, 4:5], start=True, stop=True)
    T_("matmul", ctx.ps[bank][0:8, 5:6], onesrow[0:1, 0:8], st[0:1, 0:1], start=True, stop=True)
    mm_bb = ctx.mm_inc()
    ctx.wait("v", "MM", mm_bb)
    V("tensor_single_scalar", S1[0:8, 0:128], S1[0:8, 0:128], ctx.ps[bank][0:8, 5:6],
      ALU.subtract)
    V("tensor_single_scalar", S1[0:8, 0:128], S1[0:8, 0:128], ctx.ps[bank][0:8, 4:5],
      ALU.mult)
    V("tensor_mul", S1[0:8, 0:128], S1[0:8, 0:128], trow[0:8, 128:256])
    V("tensor_add", S1[0:8, 0:128], S1[0:8, 0:128], trow[0:8, 256:384])
    V("tensor_single_scalar", S1[0:8, 0:128], S1[0:8, 0:128], maskfirst[:, 0:1], ALU.mult)
    ve_u2b = ctx.ve_inc()
    ctx.bank_set(bank, "VE", ve_u2b)
    bank = 7
    ctx.bank_wait(bank)
    ctx.wait("t", "VE", ve_u2b)
    for j in range(NJ):
        T_("matmul", ctx.ps[bank][:, j:j + 1], S1[0:8, 0:128], oh8id[:, j:j + 1],
           start=True, stop=True)
    mm_u2b = ctx.mm_inc()
    ctx.wait("v", "MM", mm_u2b)
    V("tensor_copy", u2bnd[:], ctx.ps[bank][:, 0:NJ])
    ve_u2bnd = ctx.ve_inc()
    ctx.bank_set(bank, "VE", ve_u2bnd)
    ctx.phase_barrier()
    ctx.areset(am4)

    am4b = ctx.amark()
    AB = ctx.aalloc(NJ * TC)
    U2 = ctx.aalloc(NJ * (TC + 1))
    A3 = ctx.aalloc(NJ * TC)
    FRB = ctx.aalloc(NJ * TC)
    WFR = ctx.aalloc(NJ * 1024)
    ld_wfr = 0
    for j in range(NJ):
        ld_wfr = ctx.ld(WFR[:, j * 1024:(j + 1) * 1024],
                        ap["frwT"][j * 128:(j + 1) * 128, :])
    ld_wfr = ctx.lq()

    sp_4b = 0
    sq_4b = 0
    sq_xk2 = 0
    ve_4b = 0
    mm_fr_last = 0
    for c in range(NCHK):
        base = c * TC
        ld_a = 0
        ctx.wait("y", "SQ", sq_a)
        ctx.wait("y", "VE", ve_4b)
        for j in range(NJ):
            ld_a = ctx.ld(a_(AB, j), a_sp[j * 128:(j + 1) * 128, base:base + TC])
        ld_a = ctx.lq()
        ln_stats(lambda j: a_(AB, j), TC, waits=(("LQ", ld_a),))
        ctx.wait("v", "LQ", ld_a)
        for j in range(NJ):
            V("tensor_mul", a1_(U2, j)[:, 1:TC + 1], a_(AB, j), lnscr[:, 0:TC])
            V("tensor_add", a1_(U2, j)[:, 1:TC + 1], a1_(U2, j)[:, 1:TC + 1],
              lnscr[:, TC:2 * TC])
            V("tensor_scalar", a1_(U2, j)[:, 1:TC + 1], a1_(U2, j)[:, 1:TC + 1],
              vc("ln2w", j), vc("ln2b", j), ALU.mult, ALU.add)
            V("tensor_copy", a1_(U2, j)[:, 0:1],
              u2bnd[:, j:j + 1] if c == 0 else u2save[:, j:j + 1])
        ctx.ve_inc()
        for j in range(NJ):
            V("tensor_copy", u2save[:, j:j + 1], a1_(U2, j)[:, TC:TC + 1])
        ctx.ve_inc()
        ctx.wait("v", "SQ", sq_4b)
        for j in range(NJ):
            V("tensor_sub", a_(A3, j), a1_(U2, j)[:, 1:TC + 1], a1_(U2, j)[:, 0:TC])
            V("scalar_tensor_tensor", a_(A3, j), a_(A3, j), vc("ftmk", j),
              a1_(U2, j)[:, 0:TC], ALU.mult, ALU.add)
        ve_xk2 = ctx.ve_inc()
        sp_xk2 = 0
        for j in range(NJ):
            ctx.wait("g", "VE", ve_xk2)
            sp_xk2 = ctx.sp(xk2_sp[j * 128:(j + 1) * 128, base:base + TC], a_(A3, j))
        sq_xk2 = ctx.sq()
        ctx.wait("v", "MM", mm_fr_last)
        for j in range(NJ):
            V("tensor_sub", a_(FRB, j), a1_(U2, j)[:, 1:TC + 1], a1_(U2, j)[:, 0:TC])
            V("scalar_tensor_tensor", a_(FRB, j), a_(FRB, j), vc("ftmr", j),
              a1_(U2, j)[:, 0:TC], ALU.mult, ALU.add)
        ve_xr2 = ctx.ve_inc()
        for cout in range(NJ):
            bank = cout % 2
            ctx.bank_wait(bank)
            ctx.wait("t", "LQ", ld_wfr)
            ctx.wait("t", "VE", ve_xr2)
            for cin in range(NJ):
                T_("matmul", ctx.ps[bank][:, 0:TC],
                   WFR[:, cin * 1024 + cout * 128:cin * 1024 + (cout + 1) * 128],
                   a_(FRB, cin), start=(cin == 0), stop=(cin == NJ - 1))
            mmf = ctx.mm_inc()
            ctx.wait("s", "MM", mmf)
            ctx.wait("s", "SQ", sq_xk2)
            S("activation", a_(A3, cout), ctx.ps[bank][:, 0:TC], AF.Sigmoid)
            ctx.bank_set(bank, "SE", ctx.se_inc())
        mm_fr_last = ctx.MM.n
        se_fr = ctx.SE.n
        for j in range(NJ):
            ctx.wait("g", "SE", se_fr)
            sp_4b = ctx.sp(sig_sp[j * 128:(j + 1) * 128, base:base + TC], a_(A3, j))
        sq_4b = ctx.sq()
        ve_4b = ctx.VE.n
    ctx.phase_barrier()
    ctx.areset(am4b)

    am5 = ctx.amark()
    KVSB = ctx.aalloc(NJ * TL)
    am5b = ctx.amark()
    WFK = ctx.aalloc(NJ * g.FS)
    WFV = ctx.aalloc((g.FS // 128) * 1024)
    XK2T = ctx.aalloc(NJ * TC)
    FK2 = ctx.aalloc((g.FS // 128) * TC)
    NFO = g.FS // 128

    ctx.wait("v", "SQ", sq_4b)
    V("memset", KVSB[:, :], 0.0)
    ve_kv0 = ctx.ve_inc()
    mm_cm_last = 0
    for fs in range(g.NFS):
        ld_wf = 0
        for j in range(NJ):
            ctx.wait("y", "MM", mm_cm_last)
            ld_wf = ctx.ld(WFK[:, j * g.FS:(j + 1) * g.FS],
                           ap["fkwT"][j * 128:(j + 1) * 128, fs * g.FS:(fs + 1) * g.FS])
        for fo in range(NFO):
            ld_wf = ctx.ld(WFV[:, fo * 1024:(fo + 1) * 1024],
                           ap["fvwT"][fs * g.FS + fo * 128:fs * g.FS + (fo + 1) * 128, :])
        for tc in range(NCHK):
            base = tc * TC
            ld_x2 = 0
            for j in range(NJ):
                ctx.wait("y", "MM", mm_cm_last)
                ld_x2 = ctx.ld(a_(XK2T, j), xk2_sp[j * 128:(j + 1) * 128, base:base + TC])
            ld_wx = ctx.lq()
            se_fk = {}
            ctx.wait("s", "MM", mm_cm_last)
            for fo in range(NFO):
                bank = fo % 2
                ctx.bank_wait(bank)
                ctx.wait("t", "LQ", ld_wx)
                for cin in range(NJ):
                    T_("matmul", ctx.ps[bank][:, 0:TC],
                       WFK[:, cin * g.FS + fo * 128:cin * g.FS + (fo + 1) * 128],
                       a_(XK2T, cin), start=(cin == 0), stop=(cin == NJ - 1))
                mmk = ctx.mm_inc()
                ctx.wait("s", "MM", mmk)
                S("activation", S4[:, 0:TC], ctx.ps[bank][:, 0:TC], AF.Relu)
                S("activation", FK2[:, fo * TC:(fo + 1) * TC], S4[:, 0:TC], AF.Square)
                se_fk[fo] = ctx.se_inc()
                ctx.bank_set(bank, "SE", se_fk[fo])
            for cout in range(NJ):
                bank = 2 + (cout % 2)
                ctx.bank_wait(bank)
                ctx.wait("t", "SE", se_fk[NFO - 1])
                for fo in range(NFO):
                    T_("matmul", ctx.ps[bank][:, 0:TC],
                       WFV[:, fo * 1024 + cout * 128:fo * 1024 + (cout + 1) * 128],
                       FK2[:, fo * TC:(fo + 1) * TC], start=(fo == 0), stop=(fo == NFO - 1))
                mmkv = ctx.mm_inc()
                ctx.wait("v", "MM", mmkv)
                ctx.wait("v", "VE", ve_kv0)
                V("tensor_add", KVSB[:, cout * TL + base:cout * TL + base + TC],
                  KVSB[:, cout * TL + base:cout * TL + base + TC], ctx.ps[bank][:, 0:TC])
                ctx.bank_set(bank, "VE", ctx.ve_inc())
            mm_cm_last = ctx.MM.n
    ctx.phase_barrier()
    ctx.areset(am5b)

    AB5 = ctx.aalloc(NJ * TC)
    SGB = ctx.aalloc(NJ * TC)
    XB5 = ctx.aalloc(NJ * TC)
    H1B = ctx.aalloc(NJ * TC)
    TOK = ctx.aalloc((TC // 128) * 1024)
    sp_ag2 = 0
    sq_ag2 = 0
    ve_rt = 0
    for tc in range(NCHK):
        base = tc * TC
        ld_f = 0
        ctx.wait("y", "VE", ve_rt)
        ctx.wait("y", "SQ", sq_ag2)
        for j in range(NJ):
            ld_f = ctx.ld(a_(AB5, j), a_sp[j * 128:(j + 1) * 128, base:base + TC])
            ld_f = ctx.ld(a_(SGB, j), sig_sp[j * 128:(j + 1) * 128, base:base + TC])
            ld_f = ctx.ld(a_(XB5, j), ap["xT"][j * 128:(j + 1) * 128, 1 + base:1 + base + TC])
        ld_f = ctx.lq()
        ctx.wait("v", "LQ", ld_f)
        for j in range(NJ):
            V("tensor_mul", S1[:, 0:TC], a_(SGB, j), KVSB[:, j * TL + base:j * TL + base + TC])
            V("tensor_add", S1[:, 0:TC], S1[:, 0:TC], a_(AB5, j))
            V("scalar_tensor_tensor", a_(AB5, j), S1[:, 0:TC], dc("ga", j), a_(XB5, j),
              ALU.mult, ALU.add)
        ve_xm = ctx.ve_inc()
        ln_stats(lambda j: a_(AB5, j), TC)
        for j in range(NJ):
            V("tensor_mul", a_(H1B, j), a_(AB5, j), lnscr[:, 0:TC])
            V("tensor_add", a_(H1B, j), a_(H1B, j), lnscr[:, TC:2 * TC])
            V("tensor_scalar", a_(H1B, j), a_(H1B, j), dc("cb1", j), dc("sb", j),
              ALU.mult, ALU.add)
        ve_h1 = ctx.ve_inc()
        for buf, dst in ((AB5, xmid_sp), (H1B, ag2_in)):
            tok_free = sq_ag2
            for tsub in range(TC // 128):
                for half in range(2):
                    bank = 4 + half
                    ctx.bank_wait(bank)
                    ctx.wait("t", "VE", ve_h1)
                    for jj in range(4):
                        j = half * 4 + jj
                        T_("transpose", ctx.ps[bank][:, jj * 128:(jj + 1) * 128],
                           a_(buf, j)[:, tsub * 128:(tsub + 1) * 128], ident[:])
                    mm_tp = ctx.mm_inc()
                    ctx.wait("v", "MM", mm_tp)
                    ctx.wait("v", "SQ", tok_free)
                    V("tensor_copy",
                      TOK[:, tsub * 1024 + half * 512:tsub * 1024 + (half + 1) * 512],
                      ctx.ps[bank][:, 0:512])
                    ctx.bank_set(bank, "VE", ctx.ve_inc())
            ve_tok = ctx.VE.n
            for tsub in range(TC // 128):
                ctx.wait("g", "VE", ve_tok)
                sp_ag2 = ctx.sp(dst[base + tsub * 128:base + (tsub + 1) * 128, :],
                                TOK[:, tsub * 1024:(tsub + 1) * 1024])
                if buf is H1B and "h1_dbg" in ctx.dbg_set:
                    sp_ag2 = ctx.sp(h1_dbg[base + tsub * 128:base + (tsub + 1) * 128, :],
                                    TOK[:, tsub * 1024:(tsub + 1) * 1024])
            sq_ag2 = ctx.sq()
        for tsub in range(TC // 128):
            bank = 6
            ctx.bank_wait(bank)
            ctx.wait("t", "VE", ve_h1)
            for j in range(NJ):
                T_("matmul", ctx.ps[bank][:, 0:g.E],
                   a_(H1B, j)[:, tsub * 128:(tsub + 1) * 128],
                   embTs[:, j * g.E:(j + 1) * g.E], start=(j == 0), stop=(j == NJ - 1))
            mm_lg = ctx.mm_inc()
            ti = (base // 128) + tsub
            rb = routebuf[:, ti * 18:(ti + 1) * 18]
            ctx.wait("v", "MM", mm_lg)
            V("tensor_copy", SB[:, 0:8], ctx.ps[bank][:, 0:g.E])
            ve_l = ctx.ve_inc()
            ctx.bank_set(bank, "VE", ve_l)
            V("max", out=SB[:, 8:16], in_=SB[:, 0:8])
            V("tensor_single_scalar", rb[:, 0:8], SB[:, 0:8], SB[:, 8:9], ALU.is_ge)
            V("tensor_single_scalar", SB[:, 16:24], SB[:, 0:8], SB[:, 9:10], ALU.is_ge)
            V("tensor_sub", rb[:, 8:16], SB[:, 16:24], rb[:, 0:8])
            V("tensor_sub", SB[:, 24:25], SB[:, 9:10], SB[:, 8:9])
            ve_r1 = ctx.ve_inc()
            ctx.wait("s", "VE", ve_r1)
            S("activation", SB[:, 24:25], SB[:, 24:25], AF.Exp)
            se_r1 = ctx.se_inc()
            ctx.wait("v", "SE", se_r1)
            V("tensor_scalar_add", SB[:, 25:26], SB[:, 24:25], 1.0)
            ve_r1b = ctx.ve_inc()
            ctx.wait("s", "VE", ve_r1b)
            S("activation", SB[:, 26:27], SB[:, 25:26], AF.Ln)
            se_r2 = ctx.se_inc()
            ctx.wait("v", "SE", se_r2)
            V("tensor_scalar_mul", SB[:, 26:27], SB[:, 26:27], -1.0)
            ve_r1c = ctx.ve_inc()
            ctx.wait("s", "VE", ve_r1c)
            S("activation", SB[:, 27:28], SB[:, 26:27], AF.Exp)
            se_r3 = ctx.se_inc()
            ctx.wait("v", "SE", se_r3)
            V("tensor_copy", rb[:, 16:17], SB[:, 27:28])
            V("tensor_mul", rb[:, 17:18], SB[:, 24:25], rb[:, 16:17])
            ve_r2 = ctx.ve_inc()
            bank = 7
            ctx.bank_wait(bank)
            ctx.wait("t", "VE", ve_r2)
            T_("transpose", ctx.ps[bank][0:8, 0:128], SB[:, 16:24], ident[:])
            T_("transpose", ctx.ps[bank][0:8, 128:256], rb[:, 0:8], ident[:])
            T_("transpose", ctx.ps[bank][0:8, 256:384], rb[:, 8:16], ident[:])
            mm_tr = ctx.mm_inc()
            ctx.wait("v", "MM", mm_tr)
            ctx.wait("v", "SQ", sq_ag2)
            V("tensor_copy", S1[0:8, 0:384], ctx.ps[bank][0:8, 0:384])
            ve_tr = ctx.ve_inc()
            ctx.bank_set(bank, "VE", ve_tr)
            ctx.wait("g", "VE", ve_tr)
            ctx.sp(ag2b_in[0:8, ti * 128:(ti + 1) * 128], S1[0:8, 0:128])
            ctx.sp(m12_d[0:8, ti * 128:(ti + 1) * 128], S1[0:8, 128:256])
            sp_ag2 = ctx.sp(m12_d[8:16, ti * 128:(ti + 1) * 128], S1[0:8, 256:384])
            sq_ag2 = ctx.sq()
        ve_rt = ctx.VE.n
    ctx.wait("g", "SP", sp_ag2)
    cc_ag2b = ctx.allgather(ag2b_in, ag2b_out)
    cc_ag2 = ctx.allgather(ag2_in, ag2_out)
    ctx.phase_barrier()
    ctx.areset(am5)

    am6 = ctx.amark()
    GA = ctx.aalloc(T)
    GB = ctx.aalloc(T)
    QT = ctx.aalloc(max(TL, 512))
    M1E = ctx.aalloc(TL)
    M2E = ctx.aalloc(TL)
    ctx.wait("y", "CC", cc_ag2b)
    ld_og = ctx.ld(GA[0:8, :], mkap(ag2b_out, 0, [[TL, 8], [8 * TL, 8], [1, TL]]))
    ld_m1 = ctx.ld(M1E[0:8, 0:TL], m12_d[0:8, :])
    ld_m2 = ctx.ld(M2E[0:8, 0:TL], m12_d[8:16, :])
    ld_og = ctx.lq()
    ctx.wait("v", "LQ", ld_og)
    V("tensor_tensor_scan", GB[0:8, :], GA[0:8, :],
      zz[0:8, 0:1].broadcast_to([8, T]), 0.0, ALU.add, ALU.add)
    V("scalar_tensor_tensor", GA[0:8, :], GB[0:8, :], float(g.CAP), GA[0:8, :],
      ALU.is_le, ALU.mult)
    V("scalar_tensor_tensor", GB[0:8, :], GB[0:8, :], ecap[:, 0:1], GA[0:8, :],
      ALU.add, ALU.mult)
    ve_g = ctx.ve_inc()
    sp_sv = 0
    sp_sv_hist = {}
    for o in range(T // 512):
        bank = o % 2
        ctx.bank_wait(bank)
        ctx.wait("t", "VE", ve_g)
        T_("matmul", ctx.ps[bank][0:1, 0:512], ohmy[:, 0:1], GB[0:8, o * 512:(o + 1) * 512],
           start=True, stop=False)
        T_("matmul", ctx.ps[bank][0:1, 0:512], ohs[:, 0:1], GA[0:8, o * 512:(o + 1) * 512],
           start=False, stop=True)
        mm_sv = ctx.mm_inc()
        ctx.wait("v", "MM", mm_sv)
        if o >= 2:
            ctx.wait("v", "SQ", sp_sv_hist[o - 2])
        V("tensor_scalar_add", R16[0:1, (o % 2) * 512:(o % 2) * 512 + 512],
          ctx.ps[bank][0:1, 0:512], float(g.BIG))
        ve_sv = ctx.ve_inc()
        ctx.bank_set(bank, "VE", ve_sv)
        ctx.wait("g", "VE", ve_sv)
        sp_sv = ctx.sp(svrow_d[0:1, o * 512:(o + 1) * 512],
                       R16[0:1, (o % 2) * 512:(o % 2) * 512 + 512])
        sp_sv_hist[o] = ctx.sq()
    ctx.wait("y", "SQ", ctx.SQ.n)
    ld_svt = ctx.ld(svtok[:], svrow_d.rearrange("o (a b) -> (o a) b", b=128),
                    transpose=True)
    ld_svt = ctx.lq()
    ctx.wait("v", "LQ", ld_svt)
    V("tensor_copy", idx32[:], svtok[:])
    ve_idx = ctx.ve_inc()
    sp_fld = 0
    sq_fld = 0
    for fi in range(4):
        src_em = M1E if fi < 2 else M2E
        G_t = GA if (fi % 2 == 0) else GB
        V("memset", QT[0:8, 0:TL], 0.0)
        ve_z = ctx.ve_inc()
        for k in range(8):
            V("scalar_tensor_tensor", QT[0:8, 0:TL], G_t[0:8, k * TL:(k + 1) * TL],
              ohown[:, k:k + 1], QT[0:8, 0:TL], ALU.mult, ALU.add)
        V("tensor_mul", QT[0:8, 0:TL], QT[0:8, 0:TL], src_em[0:8, 0:TL])
        ve_q = ctx.ve_inc()
        for o in range(max(1, TL // 512)):
            w_ = min(512, TL)
            bank = 2
            ctx.bank_wait(bank)
            ctx.wait("t", "VE", ve_q)
            T_("matmul", ctx.ps[bank][0:1, 0:w_], ones8[:, 0:1],
               QT[0:8, o * 512:o * 512 + w_], start=True, stop=True)
            mm_q = ctx.mm_inc()
            ctx.wait("v", "MM", mm_q)
            ctx.wait("v", "SQ", sq_fld)
            V("tensor_copy", R16[0:1, 0:w_], ctx.ps[bank][0:1, 0:w_])
            ve_fq = ctx.ve_inc()
            ctx.bank_set(bank, "VE", ve_fq)
            ctx.wait("g", "VE", ve_fq)
            sp_fld = ctx.sp(fld_d[fi:fi + 1, o * 512:o * 512 + w_], R16[0:1, 0:w_])
            sq_fld = ctx.sq()
    ctx.wait("y", "SQ", sq_fld)
    ld_fld = 0
    for fi in range(4):
        ld_fld = ctx.ld(fldtok[:, fi * 16:(fi + 1) * 16],
                        fld_d[fi:fi + 1, :].rearrange("o (a b) -> (o a) b", b=128),
                        transpose=True)
    ld_fld = ctx.lq()
    ctx.wait("v", "LQ", ld_fld)
    V("tensor_copy", fld32[:], fldtok[:])
    V("tensor_copy", idxf1[:], fldtok[:, 16:16 + TL // 128])
    V("tensor_copy", idxf2[:], fldtok[:, 48:48 + TL // 128])
    ve_fld = ctx.ve_inc()
    ctx.phase_barrier()
    ctx.areset(am6)

    gbbc = ctx.aalloc(C)
    bank = 6
    ctx.bank_wait(bank)
    T_("transpose", ctx.ps[bank][0:8, 0:128], dct("gb"), ident[:])
    mm_gb = ctx.mm_inc()
    ctx.wait("v", "MM", mm_gb)
    V("tensor_copy", S1[0:8, 0:128], ctx.ps[bank][0:8, 0:128])
    ve_gb1 = ctx.ve_inc()
    ctx.bank_set(bank, "VE", ve_gb1)
    ctx.wait("g", "VE", ve_gb1)
    sp_gb = ctx.sp(gbrow_d.rearrange("o (j p) -> (o j) p", p=128), S1[0:8, 0:128])
    sq_gb = ctx.sq()
    ctx.wait("y", "SQ", sq_gb)
    gbrow = ctx.aalloc(C)
    ld_gb = ctx.ld(gbrow[0:1, 0:C], gbrow_d)
    ld_gb = ctx.lq()
    for h in range(2 if C > 512 else 1):
        w_ = min(512, C)
        bank = 6 + h
        ctx.bank_wait(bank)
        ctx.wait("t", "LQ", ld_gb)
        T_("matmul", ctx.ps[bank][:, 0:w_], onesrow[0:1, :], gbrow[0:1, h * 512:h * 512 + w_],
           start=True, stop=True)
        mm_gb2 = ctx.mm_inc()
        ctx.wait("v", "MM", mm_gb2)
        V("tensor_copy", gbbc[:, h * 512:h * 512 + w_], ctx.ps[bank][:, 0:w_])
        ctx.bank_set(bank, "VE", ctx.ve_inc())
    ve_gbbc = ctx.VE.n


    am7 = ctx.amark()
    GRP = 4
    NGRPF = 2
    HROT = ctx.aalloc(GRP * NGRPF * 1024)
    W1B = ctx.aalloc(NJ * g.I)
    ld_w1 = 0
    for j in range(NJ):
        ld_w1 = ctx.ld(W1B[:, j * g.I:(j + 1) * g.I], ap["w1T"][j * 128:(j + 1) * 128, :])
    ld_w1 = ctx.lq()
    sp_z = 0
    for r in range(0, g.CAP, 128):
        sp_z = ctx.sp(xbuf[r:r + 128, :], zz[:])
    bc = nc.gpsimd.alloc_register("bcreg")
    nc.gpsimd.reg_mov(bc, g.CAP - 1)
    ctx.wait("g", "SP", sp_z)
    ctx.wait("g", "VE", ve_idx)
    ctx.wait("y", "CC", ctx.CC.n)
    sqg = {}
    lqg = {}
    NG = T // 128 // GRP
    for grp in range(NG):
        if grp >= NGRPF:
            ctx.wait("y", "SQ", sqg[grp - NGRPF])
        for t in range(GRP):
            i = grp * GRP + t
            slot = i % (GRP * NGRPF)
            ctx.ld(HROT[:, slot * 1024:(slot + 1) * 1024],
                   ag2_out[i * 128:(i + 1) * 128, :])
        lqg[grp] = ctx.lq()
        ctx.wait("g", "LQ", lqg[grp])
        for t in range(GRP):
            i = grp * GRP + t
            slot = i % (GRP * NGRPF)
            nc.gpsimd.indirect_dma_start(
                xbuf, IndirectOffsetOnAxis(idx32[:, i:i + 1], 0),
                HROT[:, slot * 1024:(slot + 1) * 1024], None,
                bounds_check=bc, oob_is_err=False,
            ).then_inc(ctx.SP.h, 16)
            ctx.SP.n += 16
        sqg[grp] = ctx.sq()
    sp_scat_all = ctx.SQ.n

    W2B = ctx.aalloc(2 * 1024)
    XTOK = ctx.aalloc(2 * 1024)
    XFM = ctx.aalloc(NJ * g.STILE)
    INTB = ctx.aalloc(g.NIO * g.STILE)
    sp_e = 0
    sq_e = 0
    for stile in range(g.NST):
        sbase = stile * g.STILE
        ctx.wait("y", "SQ", sp_scat_all)
        lq_xt = {}
        mm_xt = {}
        for ss in range(min(2, g.NSS)):
            ctx.ld(XTOK[:, ss * 1024:(ss + 1) * 1024],
                   xbuf[sbase + ss * 128:sbase + (ss + 1) * 128, :])
            lq_xt[ss] = ctx.lq()
        for ss in range(g.NSS):
            b0 = 4 + (ss % 2) * 2
            ctx.bank_wait(b0)
            ctx.bank_wait(b0 + 1)
            ctx.wait("t", "LQ", lq_xt[ss])
            if stile > 0 and ss == 0:
                ctx.wait("t", "MM", mm_w1_prev)
            for j in range(NJ):
                bank = b0 + j // 4
                T_("transpose", ctx.ps[bank][:, (j % 4) * 128:(j % 4 + 1) * 128],
                   XTOK[:, (ss % 2) * 1024 + j * 128:(ss % 2) * 1024 + (j + 1) * 128],
                   ident[:])
            mm_xt[ss] = ctx.mm_inc()
            if ss + 2 < g.NSS:
                ctx.wait("y", "MM", mm_xt[ss])
                ctx.ld(XTOK[:, (ss % 2) * 1024:(ss % 2) * 1024 + 1024],
                       xbuf[sbase + (ss + 2) * 128:sbase + (ss + 3) * 128, :])
                lq_xt[ss + 2] = ctx.lq()
            ctx.wait("v", "MM", mm_xt[ss])
            for j in range(NJ):
                bank = b0 + j // 4
                V("tensor_copy", XFM[:, j * g.STILE + ss * 128:j * g.STILE + (ss + 1) * 128],
                  ctx.ps[bank][:, (j % 4) * 128:(j % 4 + 1) * 128])
            ve_x = ctx.ve_inc()
            ctx.bank_set(b0, "VE", ve_x)
            ctx.bank_set(b0 + 1, "VE", ve_x)
        ve_xfm = ctx.VE.n
        if stile > 0:
            ctx.wait("s", "MM", mm_e_prev)
        for io in range(g.NIO):
            bank = io % 2
            ctx.bank_wait(bank)
            ctx.wait("t", "LQ", ld_w1)
            ctx.wait("t", "VE", ve_xfm)
            for cin in range(NJ):
                T_("matmul", ctx.ps[bank][:, 0:g.STILE],
                   W1B[:, cin * g.I + io * 128:cin * g.I + (io + 1) * 128],
                   XFM[:, cin * g.STILE:(cin + 1) * g.STILE],
                   start=(cin == 0), stop=(cin == NJ - 1))
            mm1 = ctx.mm_inc()
            ctx.wait("s", "MM", mm1)
            S("activation", INTB[:, io * g.STILE:(io + 1) * g.STILE],
              ctx.ps[bank][:, 0:g.STILE], AF.Relu)
            ctx.bank_set(bank, "SE", ctx.se_inc())
        se_int = ctx.SE.n
        mm_w1_prev = ctx.MM.n
        nh = 2 if C > 512 else 1
        ld_w2 = {}
        mm_e = {}
        for io in range(min(2, g.NIO)):
            ld_w2[io] = ctx.ld(W2B[:, (io % 2) * 1024:(io % 2) * 1024 + 1024],
                               ap["w2T"][io * 128:(io + 1) * 128, :])
            ld_w2[io] = ctx.lq()
        for io in range(g.NIO):
            ctx.wait("t", "LQ", ld_w2[io])
            ctx.wait("t", "SE", se_int)
            if io == 0:
                for b in range(g.NSS * nh):
                    ctx.bank_wait(b)
            for ss in range(g.NSS):
                for h in range(nh):
                    bank = ss * nh + h
                    T_("matmul", ctx.ps[bank][:, 0:512],
                       INTB[:, io * g.STILE + ss * 128:io * g.STILE + (ss + 1) * 128],
                       W2B[:, (io % 2) * 1024 + h * 512:(io % 2) * 1024 + (h + 1) * 512],
                       start=(io == 0), stop=(io == g.NIO - 1))
            mm_e[io] = ctx.mm_inc()
            if io + 2 < g.NIO:
                ctx.wait("y", "MM", mm_e[io])
                ld_w2[io + 2] = ctx.ld(W2B[:, (io % 2) * 1024:(io % 2) * 1024 + 1024],
                                       ap["w2T"][(io + 2) * 128:(io + 3) * 128, :])
                ld_w2[io + 2] = ctx.lq()
        mm_e_prev = mm_e[g.NIO - 1]
        ctx.wait("v", "MM", mm_e_prev)
        sq_ss = {}
        for ss in range(g.NSS):
            pair = (S1, S2) if ss % 2 == 0 else (S3, S4)
            if ss >= 2:
                ctx.wait("v", "SQ", sq_ss[ss - 2])
            elif stile > 0:
                ctx.wait("v", "SQ", sq_e)
            for h in range(nh):
                bank = ss * nh + h
                V("tensor_copy", pair[h][:, 0:512], ctx.ps[bank][:, 0:512])
                ctx.bank_set(bank, "VE", ctx.VE.n + 1)
            ve_ss = ctx.ve_inc()
            ctx.wait("g", "VE", ve_ss)
            for h in range(nh):
                sp_e = ctx.sp(ag3_in[sbase + ss * 128:sbase + (ss + 1) * 128,
                                     h * 512:h * 512 + 512], pair[h][:, 0:512])
                if "eout_dbg" in ctx.dbg_set:
                    sp_e = ctx.sp(eout_dbg[sbase + ss * 128:sbase + (ss + 1) * 128,
                                           h * 512:h * 512 + 512], pair[h][:, 0:512])
            sq_ss[ss] = ctx.sq()
        sq_e = ctx.SQ.n
    ctx.wait("g", "SP", sp_e)
    cc_ag3 = ctx.allgather(ag3_in, ag3_out)
    ctx.phase_barrier()
    ctx.areset(am7)

    am8 = ctx.amark()
    R1 = ctx.aalloc(2 * 1024)
    R2 = ctx.aalloc(2 * 1024)
    XMB = ctx.aalloc(2 * 1024)
    YB = ctx.aalloc(2 * 1024)
    sp_y = 0
    ve_y = {}
    sp_y_hist = {}
    for i in range(TL // 128):
        par = i % 2
        ctx.wait("g", "CC", cc_ag3)
        ctx.wait("g", "VE", ve_fld)
        if i >= 2:
            ctx.wait("g", "VE", ve_y[i - 2])
        nc.gpsimd.indirect_dma_start(
            R1[:, par * 1024:(par + 1) * 1024], None,
            ag3_out, IndirectOffsetOnAxis(idxf1[:, i:i + 1], 0),
        ).then_inc(ctx.SP.h, 16)
        ctx.SP.n += 16
        nc.gpsimd.indirect_dma_start(
            R2[:, par * 1024:(par + 1) * 1024], None,
            ag3_out, IndirectOffsetOnAxis(idxf2[:, i:i + 1], 0),
        ).then_inc(ctx.SP.h, 16)
        ctx.SP.n += 16
        sq_g = ctx.sq()
        if i >= 2:
            ctx.wait("y", "VE", ve_y[i - 2])
        ld_xm = ctx.ld(XMB[:, par * 1024:(par + 1) * 1024],
                       xmid_sp[i * 128:(i + 1) * 128, :])
        ld_xm = ctx.lq()
        ctx.wait("v", "SQ", sq_g)
        ctx.wait("v", "LQ", ld_xm)
        if i >= 2:
            ctx.wait("v", "SQ", sp_y_hist[i - 2])
        rb = routebuf[:, i * 18:(i + 1) * 18]
        V("tensor_mul", SB[:, 0:1], rb[:, 16:17], fld32[:, i:i + 1])
        V("tensor_mul", SB[:, 1:2], rb[:, 17:18], fld32[:, 32 + i:33 + i])
        V("tensor_single_scalar", YB[:, par * 1024:(par + 1) * 1024],
          R1[:, par * 1024:(par + 1) * 1024], SB[:, 0:1], ALU.mult)
        V("scalar_tensor_tensor", YB[:, par * 1024:(par + 1) * 1024],
          R2[:, par * 1024:(par + 1) * 1024], SB[:, 1:2],
          YB[:, par * 1024:(par + 1) * 1024], ALU.mult, ALU.add)
        ctx.wait("v", "VE", ve_gbbc)
        V("tensor_mul", YB[:, par * 1024:(par + 1) * 1024],
          YB[:, par * 1024:(par + 1) * 1024], gbbc[:])
        V("tensor_add", YB[:, par * 1024:(par + 1) * 1024],
          YB[:, par * 1024:(par + 1) * 1024], XMB[:, par * 1024:(par + 1) * 1024])
        ve_y[i] = ctx.ve_inc()
        ctx.wait("g", "VE", ve_y[i])
        sp_y = ctx.sp(y_out[i * 128:(i + 1) * 128, :], YB[:, par * 1024:(par + 1) * 1024])
        sp_y_hist[i] = ctx.sq()
    ctx.wait("g", "SP", sp_y)
    ctx.phase_barrier()
    ctx.areset(am8)
    return nc, ctx



def _vec_tile(v):
    return np.ascontiguousarray(np.asarray(v, np.float32).reshape(8, 128).T)


def make_inmaps(g: Geo, inputs):
    x = np.asarray(inputs["L0_attn_x"], np.float32).reshape(g.T, g.C)
    vecs = np.stack([_vec_tile(inputs[k]) for k in (
        "L0_attn_ln1_w", "L0_attn_ln1_b", "L0_attn_ln2_w", "L0_attn_ln2_b",
        "L0_attn_time_decay", "L0_attn_time_first", "L0_attn_tmk", "L0_attn_tmv",
        "L0_attn_tmr", "L0_attn_f_tmk", "L0_attn_f_tmr")], 0)
    shared = {
        "kwT": np.ascontiguousarray(np.asarray(inputs["L0_attn_att_kw"], np.float32).T),
        "vwT": np.ascontiguousarray(np.asarray(inputs["L0_attn_att_vw"], np.float32).T),
        "rwT": np.ascontiguousarray(np.asarray(inputs["L0_attn_att_rw"], np.float32).T),
        "owT": np.ascontiguousarray(np.asarray(inputs["L0_attn_att_ow"], np.float32).T),
        "frwT": np.ascontiguousarray(np.asarray(inputs["L0_attn_f_rw"], np.float32).T),
        "fkwT": np.ascontiguousarray(np.asarray(inputs["L0_attn_f_kw"], np.float32).T),
        "fvwT": np.ascontiguousarray(np.asarray(inputs["L0_attn_f_vw"], np.float32).T),
        "embT": np.ascontiguousarray(np.asarray(inputs["L0_ffn_expert_embeddings"], np.float32).T),
        "wmod": np.ascontiguousarray(np.asarray(inputs["L0_wmod"], np.float32)),
        "cmod": _vec_tile(np.asarray(inputs["L0_c"], np.float32).reshape(g.C)),
        "vecs": vecs,
        "ident": np.eye(128, dtype=np.float32),
        "iotarev": np.ascontiguousarray(np.broadcast_to(
            np.arange(g.TC - 1, -1, -1, dtype=np.float32), (128, g.TC))),
        "oh8id": np.eye(8, dtype=np.float32),
        "ecap": (np.arange(8, dtype=np.float32) * g.CAP - 1.0).reshape(8, 1),
    }
    w1 = np.asarray(inputs["L0_ffn_first_linear"], np.float32)
    w2 = np.asarray(inputs["L0_ffn_second_linear"], np.float32)
    maps = []
    for i in range(g.NCORES):
        seg = x[i * g.TL:(i + 1) * g.TL]
        bnd = np.zeros((1, g.C), np.float32) if i == 0 else x[i * g.TL - 1:i * g.TL]
        xT = np.ascontiguousarray(np.concatenate([bnd, seg], 0).T)
        masklt = (np.arange(8) < i).astype(np.float32)
        m = dict(shared)
        m.update({
            "xT": xT,
            "w1T": np.ascontiguousarray(w1[i].T),
            "w2T": np.ascontiguousarray(w2[i].T),
            "ohmy": np.eye(8, dtype=np.float32)[:, i:i + 1].copy(),
            "ohs": (np.eye(8, dtype=np.float32)[:, i:i + 1] *
                    (-(i * g.CAP + g.BIG))).astype(np.float32),
            "masklt8": np.tile(masklt[None, :], (8, 1)).astype(np.float32),
            "mneg8": ((1.0 - np.tile(masklt[None, :], (8, 1))) * NEG).astype(np.float32),
            "tlm8": (np.tile(masklt[None, :], (8, 1)) * g.TL).astype(np.float32),
            "mprev8": np.tile((np.arange(8) == i - 1).astype(np.float32)[None, :], (8, 1)).copy(),
            "ohown": np.tile((np.arange(8) == i).astype(np.float32)[None, :], (8, 1)).copy(),
            "maskfirst": np.full((8, 1), 0.0 if i == 0 else 1.0, np.float32),
            "maskfirstP": np.full((128, 1), 0.0 if i == 0 else 1.0, np.float32),
            "xbnd_in": _vec_tile(bnd[0]),
        })
        maps.append(m)
    return maps


def kernel(**inputs):
    g = Geo(16384, 512)
    nc, ctx = build_program(g)
    maps = make_inmaps(g, inputs)
    res = run_bass_kernel_spmd(nc, maps, core_ids=list(range(g.NCORES)))
    y = np.concatenate([res.results[i]["y"] for i in range(g.NCORES)], 0)
    return y.reshape(1, g.T, g.C).astype(np.float32)


# version 08
# speedup vs baseline: 1.0467x; 1.0021x over previous
import numpy as np
import concourse.bass as bass
import concourse.mybir as mybir
from concourse.bass_core_rust import IndirectOffsetOnAxis
from concourse.bass_types import AP as BassAP
from concourse.bass_utils import run_bass_kernel_spmd


def mkap(apref, offset, dims):
    return BassAP(apref.tensor, offset, [list(d) for d in dims])

F32 = mybir.dt.float32
I16 = mybir.dt.int16
I32 = mybir.dt.int32
AF = mybir.ActivationFunctionType
ALU = mybir.AluOpType
AX = mybir.AxisListType.X

LN_EPS = 1e-5
NEG = -1e38


class Geo:
    def __init__(self, T=16384, TC=512):
        self.T = T
        self.NCORES = 8
        self.TL = T // 8
        assert TC >= 128 and TC % 128 == 0 and TC <= 512
        self.TC = TC
        self.NCHK = self.TL // TC
        self.C = 1024
        self.NJ = 8
        self.F = 4096
        self.FS = 512
        self.NFS = self.F // self.FS
        self.I = 2048
        self.NIO = self.I // 128
        self.E = 8
        self.CAP = T // 8
        self.BIG = 20000
        self.STILE = min(512, self.CAP)
        self.NST = self.CAP // self.STILE
        self.NSS = self.STILE // 128


class Sem:
    def __init__(self, nc, name):
        self.h = nc.semaphore(name).__enter__()
        self.n = 0


class Ctx:
    def __init__(self, nc, g):
        self.nc = nc
        self.g = g
        self.LD = Sem(nc, "sLD")
        self.SP = Sem(nc, "sSP")
        self.MM = Sem(nc, "sMM")
        self.VE = Sem(nc, "sVE")
        self.SE = Sem(nc, "sSE")
        self.CC = Sem(nc, "sCC")
        self.LQ = Sem(nc, "sLQ")
        self.SQ = Sem(nc, "sSQ")
        self.sem = {"LD": self.LD, "SP": self.SP, "MM": self.MM,
                    "VE": self.VE, "SE": self.SE, "CC": self.CC,
                    "LQ": self.LQ, "SQ": self.SQ}
        self.last = {}
        self.ACOLS = 43008
        self.arena = nc.alloc_sbuf_tensor("arena", [128, self.ACOLS], F32)
        self.aoff = 0
        self.amax = 0
        self.ps = [nc.alloc_psum_tensor(f"psb{i}", [128, 512], F32) for i in range(8)]
        self.bank_free = [("VE", 0) for _ in range(8)]
        self.dbg_set = set()

    def amark(self):
        return self.aoff

    def areset(self, m):
        self.aoff = m

    def aalloc(self, cols):
        assert self.aoff + cols <= self.ACOLS, f"arena overflow {self.aoff}+{cols}"
        t = self.arena[:, self.aoff:self.aoff + cols]
        self.aoff += cols
        self.amax = max(self.amax, self.aoff)
        return t

    def dram(self, name, shape, dtype=F32):
        kind = "ExternalOutput" if name in self.dbg_set else None
        if kind:
            return self.nc.dram_tensor(name, shape, dtype, kind=kind)
        return self.nc.dram_tensor(name, shape, dtype)

    def _tr(self, eng, inst):
        self.last[eng] = inst
        return inst

    def t(self, op, *a, **k):
        return self._tr("t", getattr(self.nc.tensor, op)(*a, **k))

    DRAINS = True

    def v(self, op, *a, **k):
        inst = getattr(self.nc.vector, op)(*a, **k)
        if self.DRAINS:
            inst = self.nc.vector.drain()
        return self._tr("v", inst)

    def s(self, op, *a, **k):
        inst = getattr(self.nc.scalar, op)(*a, **k)
        if self.DRAINS:
            inst = self.nc.scalar.drain()
        return self._tr("s", inst)

    def mm_inc(self):
        self.last["t"].then_inc(self.MM.h, 1)
        self.MM.n += 1
        return self.MM.n

    def ve_inc(self):
        self.last["v"].then_inc(self.VE.h, 1)
        self.VE.n += 1
        return self.VE.n

    def se_inc(self):
        self.last["s"].then_inc(self.SE.h, 1)
        self.SE.n += 1
        return self.SE.n

    def ld(self, out, in_, transpose=False, slow=False):
        kw = {}
        if transpose:
            kw["transpose"] = True
        if slow:
            kw["allow_slow_non_contiguous"] = True
        self.nc.sync.dma_start(out=out, in_=in_, **kw).then_inc(self.LD.h, 16)
        self.LD.n += 16
        return self.LD.n

    def sp(self, out, in_):
        self.nc.gpsimd.dma_start(out=out, in_=in_).then_inc(self.SP.h, 16)
        self.SP.n += 16
        return self.SP.n

    def lq(self):
        self.nc.sync.wait_ge(self.LD.h, self.LD.n)
        self.nc.sync.sem_inc(self.LQ.h, 1)
        self.LQ.n += 1
        return self.LQ.n

    def sq(self):
        self.nc.gpsimd.wait_ge(self.SP.h, self.SP.n)
        self.nc.gpsimd.sem_inc(self.SQ.h, 1)
        self.SQ.n += 1
        return self.SQ.n

    def wait(self, eng, semname, val):
        if val <= 0:
            return
        e = {"t": self.nc.tensor, "v": self.nc.vector, "s": self.nc.scalar,
             "g": self.nc.gpsimd, "y": self.nc.sync}[eng]
        e.wait_ge(self.sem[semname].h, val)

    def bank_wait(self, i):
        s, v = self.bank_free[i]
        self.wait("t", s, v)

    def bank_set(self, i, semname, val):
        self.bank_free[i] = (semname, val)

    def phase_barrier(self):
        self.lq()
        self.sq()
        cur = {"LQ": self.LQ.n, "SQ": self.SQ.n, "MM": self.MM.n,
               "VE": self.VE.n, "SE": self.SE.n}
        for eng in ("y", "g", "v", "s", "t"):
            for k, v in cur.items():
                self.wait(eng, k, v)

    def dbg_dump(self, name, ap_, shape, dtype=F32):
        if name not in self.dbg_set:
            return
        t = self.nc.dram_tensor(name, shape, dtype, kind="ExternalOutput")
        self.wait("g", "VE", self.VE.n)
        self.wait("g", "SE", self.SE.n)
        self.sp(t.ap(), ap_)

    def allgather(self, bin_ap, bout_ap):
        self.nc.gpsimd.collective_compute(
            "AllGather", ALU.bypass,
            replica_groups=[list(range(self.g.NCORES))],
            ins=[bin_ap.opt()], outs=[bout_ap.opt()],
        ).then_inc(self.CC.h, 1)
        self.CC.n += 1
        return self.CC.n


def build_program(g: Geo, dbg=()):
    nc = bass.Bass("TRN2", target_bir_lowering=False, debug=False,
                   detect_race_conditions=False)
    ctx = Ctx(nc, g)
    ctx.dbg_set = set(dbg)
    C, NJ, TC, NCHK, TL, T = g.C, g.NJ, g.TC, g.NCHK, g.TL, g.T
    V = ctx.v
    S = ctx.s
    T_ = ctx.t

    ap = {}
    def inp(name, shape, dtype=F32):
        ap[name] = nc.dram_tensor(name, shape, dtype, kind="ExternalInput").ap()

    inp("xT", [C, TL + 1])
    for nm in ("kwT", "vwT", "rwT", "owT", "frwT"):
        inp(nm, [C, C])
    inp("fkwT", [C, g.F]); inp("fvwT", [g.F, C])
    inp("w1T", [C, g.I]); inp("w2T", [g.I, C])
    inp("embT", [C, g.E]); inp("wmod", [C, 6 * C])
    inp("cmod", [128, NJ]); inp("vecs", [11, 128, NJ])
    inp("ident", [128, 128]); inp("iotarev", [128, TC])
    inp("oh8id", [8, 8]); inp("ohmy", [8, 1]); inp("ohs", [8, 1]); inp("ecap", [8, 1])
    inp("masklt8", [8, 8]); inp("mneg8", [8, 8]); inp("tlm8", [8, 8])
    inp("mprev8", [8, 8]); inp("ohown", [8, 8])
    inp("maskfirst", [8, 1]); inp("maskfirstP", [128, 1])
    inp("xbnd_in", [128, NJ])

    y_out = nc.dram_tensor("y", [TL, C], F32, kind="ExternalOutput").ap()

    kvs_sp = ctx.dram("kvs_sp", [3 * C, TL]).ap()
    h0_sp = ctx.dram("h0_sp", [C, TL]).ap()
    a_sp = ctx.dram("a_sp", [C, TL]).ap()
    xk2_sp = ctx.dram("xk2_sp", [C, TL]).ap()
    sig_sp = ctx.dram("sig_sp", [C, TL]).ap()
    xmid_sp = ctx.dram("xmid_sp", [TL, C]).ap()
    svrow_d = ctx.dram("svrow_d", [1, T], I16).ap()
    fld_d = ctx.dram("fld_d", [4, 16 * 128], I16).ap()
    m12_d = ctx.dram("m12_d", [16, TL]).ap()
    gbrow_d = ctx.dram("gbrow_d", [1, C]).ap()
    xbuf = ctx.dram("xbuf", [g.CAP, C]).ap()
    if "h1_dbg" in ctx.dbg_set:
        h1_dbg = nc.dram_tensor("h1_dbg", [TL, C], F32, kind="ExternalOutput").ap()
    if "eout_dbg" in ctx.dbg_set:
        eout_dbg = nc.dram_tensor("eout_dbg", [g.CAP, C], F32, kind="ExternalOutput").ap()
    ag1_in = ctx.dram("ag1_in", [24, 128]).ap()
    ag1_out = ctx.dram("ag1_out", [192, 128]).ap()
    ag1b_in = ctx.dram("ag1b_in", [8, 128]).ap()
    ag1b_out = ctx.dram("ag1b_out", [64, 128]).ap()
    ag2_in = ctx.dram("ag2_in", [TL, C]).ap()
    ag2_out = ctx.dram("ag2_out", [T, C]).ap()
    ag2b_in = ctx.dram("ag2b_in", [8, TL]).ap()
    ag2b_out = ctx.dram("ag2b_out", [64, TL]).ap()
    ag3_in = ctx.dram("ag3_in", [g.CAP, C]).ap()
    ag3c_out = [ctx.dram(f"ag3c_out{s}", [8 * g.STILE, C]).ap() for s in range(g.NST)]

    A = nc.alloc_sbuf_tensor
    vconst = A("vconst", [128, 11 * NJ], F32)
    dconst = A("dconst", [128, 8 * NJ], F32)
    ident = A("identt", [128, 128], F32)
    embTs = A("embTs", [128, NJ * g.E], F32)
    oh8id = A("oh8idt", [8, 8], F32)
    ohmy = A("ohmyt", [8, 1], F32)
    ohs = A("ohst", [8, 1], F32)
    ecap = A("ecapt", [8, 1], F32)
    masklt8 = A("masklt8t", [8, 8], F32)
    mneg8 = A("mneg8t", [8, 8], F32)
    tlm8 = A("tlm8t", [8, 8], F32)
    mprev8 = A("mprev8t", [8, 8], F32)
    ohown = A("ohownt", [8, 8], F32)
    maskfirst = A("maskfirstt", [8, 1], F32)
    maskfirstP = A("maskfirstPt", [128, 1], F32)
    onescol = A("onescol", [128, 1], F32)
    onesrow = A("onesrow", [1, 128], F32)
    ones8 = A("ones8", [8, 1], F32)
    state0 = A("state0", [128, 3 * NJ], F32)
    runsum = A("runsum", [128, 3 * NJ], F32)
    csum = A("csum", [128, 3 * NJ], F32)
    u2bnd = A("u2bnd", [128, NJ], F32)
    ubnd = A("ubnd", [128, NJ], F32)
    usave = A("usave", [128, NJ], F32)
    u2save = A("u2save", [128, NJ], F32)
    trow = A("trow", [8, 3 * 128], F32)
    routebuf = A("routebuf", [128, (TL // 128) * 18], F32)
    svtok = A("svtok", [128, T // 128], I16)
    fldtok = A("fldtok", [128, 64], I16)
    fld32 = A("fld32", [128, 64], F32)
    idxf1 = A("idxf1", [128, TL // 128], I32)
    idxf2 = A("idxf2", [128, TL // 128], I32)
    idxf1s = A("idxf1s", [128, 4 * (TL // 128)], I32)
    idxf2s = A("idxf2s", [128, 4 * (TL // 128)], I32)
    idxt = A("idxt", [128, 6 * (TL // 128)], I32)
    idx32 = A("idx32", [128, T // 128], I32)
    stats = A("stats", [1, 2608], F32)
    R16 = A("R16", [1, 1024], I16)
    SW = max(TC, 512)
    S1 = A("S1", [128, SW], F32)
    S2 = A("S2", [128, SW], F32)
    S3 = A("S3", [128, SW], F32)
    S4 = A("S4", [128, SW], F32)
    SQ2 = A("SQ2", [128, SW], F32)
    SB = A("SBs", [128, 32], F32)
    zz = A("zzt", [128, 1024], F32)
    lnscr = A("lnscr", [128, 2 * TC], F32)
    xbnd = A("xbndt", [128, NJ], F32)

    co = {"w": 0, "sa": 1, "ca1": 2, "ga": 3, "sb": 4, "cb1": 5, "gb": 6}
    vi = {"ln1w": 0, "ln1b": 1, "ln2w": 2, "ln2b": 3, "tdec": 4, "tfir": 5,
          "tmk": 6, "tmv": 7, "tmr": 8, "ftmk": 9, "ftmr": 10}

    def vc(name, j):
        return vconst[:, vi[name] * NJ + j: vi[name] * NJ + j + 1]

    def vct(name):
        return vconst[:, vi[name] * NJ:(vi[name] + 1) * NJ]

    def dc(name, j):
        return dconst[:, co[name] * NJ + j: co[name] * NJ + j + 1]

    def dct(name):
        return dconst[:, co[name] * NJ:(co[name] + 1) * NJ]

    def a_(buf, j, w=TC):
        return buf[:, j * w:(j + 1) * w]

    def a1_(buf, j):
        return buf[:, j * (TC + 1):(j + 1) * (TC + 1)]

    ctx.ld(vconst[:], mkap(ap["vecs"], 0, [[8, 128], [1024, 11], [1, NJ]]))
    ctx.ld(ident[:], ap["ident"])
    ctx.ld(embTs[:], mkap(ap["embT"], 0, [[g.E, 128], [128 * g.E, NJ], [1, g.E]]))
    for nm, tt in (("oh8id", oh8id), ("ohmy", ohmy), ("ohs", ohs), ("ecap", ecap),
                   ("masklt8", masklt8), ("mneg8", mneg8), ("tlm8", tlm8),
                   ("mprev8", mprev8), ("ohown", ohown), ("maskfirst", maskfirst),
                   ("maskfirstP", maskfirstP)):
        ctx.ld(tt[:], ap[nm])
    ctx.ld(SB[:, 0:NJ], ap["cmod"])
    ld_xb = ctx.ld(xbnd[:], ap["xbnd_in"])
    ld_const = ctx.lq()

    ctx.wait("v", "LQ", ld_const)
    V("memset", onescol[:], 1.0)
    V("memset", onesrow[:], 1.0)
    V("memset", ones8[:], 1.0)
    V("memset", zz[:], 0.0)
    V("memset", dconst[:], 0.0)
    V("memset", runsum[:, 0:NJ], NEG)
    V("memset", runsum[:, NJ:3 * NJ], 0.0)
    ve0 = ctx.ve_inc()
    ctx.wait("s", "LQ", ld_const)
    ctx.wait("s", "VE", ve0)
    S("activation", dct("w"), vct("tdec"), AF.Exp)
    S("activation", SB[:, NJ:2 * NJ], SB[:, 0:NJ], AF.Silu)
    se_silu = ctx.se_inc()
    ctx.wait("v", "SE", se_silu)
    V("tensor_scalar_mul", dct("w"), dct("w"), -1.0)
    ve_w = ctx.ve_inc()

    am0 = ctx.amark()
    modsb = ctx.aalloc(6 * C)
    wmbuf = ctx.aalloc(2 * NJ * 512)
    mm_prev = {}
    for o in range(12):
        ld_wm = 0
        for j in range(NJ):
            if o >= 2:
                ctx.wait("y", "MM", mm_prev[o - 2])
            ld_wm = ctx.ld(wmbuf[:, (o % 2) * NJ * 512 + j * 512:(o % 2) * NJ * 512 + (j + 1) * 512],
                           ap["wmod"][j * 128:(j + 1) * 128, o * 512:(o + 1) * 512])
        ld_wm = ctx.lq()
        bank = o % 2
        ctx.bank_wait(bank)
        ctx.wait("t", "LQ", ld_wm)
        ctx.wait("t", "SE", se_silu)
        for j in range(NJ):
            T_("matmul", ctx.ps[bank][0:1, 0:512], SB[:, NJ + j:NJ + j + 1],
               wmbuf[:, (o % 2) * NJ * 512 + j * 512:(o % 2) * NJ * 512 + (j + 1) * 512],
               start=(j == 0), stop=(j == NJ - 1))
        mm_prev[o] = ctx.mm_inc()
        ctx.wait("v", "MM", mm_prev[o])
        V("tensor_copy", modsb[0:1, o * 512:(o + 1) * 512], ctx.ps[bank][0:1, 0:512])
        ctx.bank_set(bank, "VE", ctx.ve_inc())

    bank = 2
    ctx.bank_wait(bank)
    ctx.wait("t", "VE", ctx.VE.n)
    for v6 in range(6):
        for j in range(NJ):
            T_("matmul", ctx.ps[bank][:, v6 * NJ + j:v6 * NJ + j + 1],
               modsb[0:1, v6 * C + j * 128:v6 * C + (j + 1) * 128],
               onesrow[0:1, 0:1], start=True, stop=True)
    mm_modT = ctx.mm_inc()
    ctx.wait("v", "MM", mm_modT)
    V("tensor_copy", dconst[:, NJ:7 * NJ], ctx.ps[bank][:, 0:48])
    V("tensor_scalar_add", dct("ca1"), dct("ca1"), 1.0)
    V("tensor_scalar_add", dct("cb1"), dct("cb1"), 1.0)
    for nm in ("ga", "gb"):
        V("tensor_single_scalar", SB[:, 16:16 + NJ], dct(nm), 0.0, ALU.is_ge)
        V("tensor_scalar", SB[:, 16:16 + NJ], SB[:, 16:16 + NJ], 2.0, 1.0, ALU.mult, ALU.subtract)
        V("tensor_mul", SB[:, 24:24 + NJ], dct(nm), SB[:, 16:16 + NJ])
        V("tensor_scalar_max", SB[:, 24:24 + NJ], SB[:, 24:24 + NJ], 0.25)
        V("tensor_mul", dct(nm), SB[:, 16:16 + NJ], SB[:, 24:24 + NJ])
    ve_dconst = ctx.ve_inc()
    ctx.bank_set(bank, "VE", ve_dconst)

    ctx.dbg_dump("dconst_d", dconst[:], [128, 8 * NJ])
    ctx.dbg_dump("vconst_d", vconst[:], [128, 11 * NJ])
    bank = 3
    ctx.bank_wait(bank)
    ctx.wait("t", "VE", ve_dconst)
    T_("transpose", ctx.ps[bank][0:8, 0:128], dct("w"), ident[:])
    T_("transpose", ctx.ps[bank][0:8, 128:256], vct("ln2w"), ident[:])
    T_("transpose", ctx.ps[bank][0:8, 256:384], vct("ln2b"), ident[:])
    mm_trow = ctx.mm_inc()
    ctx.wait("v", "MM", mm_trow)
    V("tensor_copy", trow[:], ctx.ps[bank][0:8, 0:384])
    ctx.bank_set(bank, "VE", ctx.ve_inc())

    def mini_ln(src, dst):
        bank = 4
        ctx.bank_wait(bank)
        ctx.wait("t", "VE", ctx.VE.n)
        ctx.wait("t", "SE", ctx.SE.n)
        ctx.wait("t", "LQ", ctx.LQ.n)
        T_("matmul", ctx.ps[bank][0:1, 0:NJ], onescol[:, 0:1], src, start=True, stop=True)
        ctx.mm_inc()
        ctx.wait("s", "VE", ctx.VE.n)
        ctx.wait("s", "LQ", ctx.LQ.n)
        ctx.wait("s", "MM", ctx.MM.n)
        S("activation", SB[:, 24:24 + NJ], src, AF.Square)
        se1 = ctx.se_inc()
        ctx.wait("t", "SE", se1)
        T_("matmul", ctx.ps[bank][0:1, NJ:2 * NJ], onescol[:, 0:1], SB[:, 24:24 + NJ],
           start=True, stop=True)
        mm2 = ctx.mm_inc()
        ctx.wait("v", "MM", mm2)
        st = stats
        V("reduce_sum", st[0:1, 0:1], ctx.ps[bank][0:1, 0:NJ], axis=AX)
        V("reduce_sum", st[0:1, 1:2], ctx.ps[bank][0:1, NJ:2 * NJ], axis=AX)
        V("tensor_scalar_mul", st[0:1, 0:2], st[0:1, 0:2], 1.0 / C)
        V("tensor_mul", st[0:1, 2:3], st[0:1, 0:1], st[0:1, 0:1])
        V("tensor_sub", st[0:1, 1:2], st[0:1, 1:2], st[0:1, 2:3])
        V("tensor_scalar_add", st[0:1, 1:2], st[0:1, 1:2], LN_EPS)
        ve1 = ctx.ve_inc()
        ctx.wait("s", "VE", ve1)
        S("activation", st[0:1, 3:4], st[0:1, 1:2], AF.Ln)
        se2 = ctx.se_inc()
        ctx.wait("v", "SE", se2)
        V("tensor_scalar_mul", st[0:1, 3:4], st[0:1, 3:4], -0.5)
        ve1b = ctx.ve_inc()
        ctx.wait("s", "VE", ve1b)
        S("activation", st[0:1, 4:5], st[0:1, 3:4], AF.Exp)
        se2b = ctx.se_inc()
        ctx.wait("v", "SE", se2b)
        V("tensor_copy", st[0:1, 4:5], st[0:1, 4:5])
        ve2 = ctx.ve_inc()
        ctx.wait("t", "VE", ve2)
        T_("matmul", ctx.ps[bank][:, 16:17], onesrow[0:1, :], st[0:1, 4:5], start=True, stop=True)
        T_("matmul", ctx.ps[bank][:, 17:18], onesrow[0:1, :], st[0:1, 0:1], start=True, stop=True)
        mm3 = ctx.mm_inc()
        ctx.wait("v", "MM", mm3)
        V("tensor_single_scalar", dst, src, ctx.ps[bank][:, 17:18], ALU.subtract)
        V("tensor_single_scalar", dst, dst, ctx.ps[bank][:, 16:17], ALU.mult)
        ctx.bank_set(bank, "VE", ctx.ve_inc())

    ctx.dbg_dump("xbnd_d", xbnd[:], [128, NJ])
    mini_ln(xbnd[:], SB[:, 0:NJ])
    ctx.dbg_dump("lnb_d", SB[:, 0:NJ], [128, NJ])
    ctx.dbg_dump("stats1_d", stats[0:1, 0:8], [1, 8])
    V("tensor_mul", SB[:, 0:NJ], SB[:, 0:NJ], dct("ca1"))
    V("tensor_add", SB[:, 0:NJ], SB[:, 0:NJ], dct("sa"))
    ctx.ve_inc()
    ctx.dbg_dump("h0b_d", SB[:, 0:NJ], [128, NJ])
    mini_ln(SB[:, 0:NJ], ubnd[:])
    V("tensor_mul", ubnd[:], ubnd[:], vct("ln1w"))
    V("tensor_add", ubnd[:], ubnd[:], vct("ln1b"))
    V("tensor_single_scalar", ubnd[:], ubnd[:], maskfirstP[:, 0:1], ALU.mult)
    ve_ubnd = ctx.ve_inc()
    ctx.dbg_dump("ubnd_d", ubnd[:], [128, NJ])
    ctx.phase_barrier()
    ctx.areset(am0)

    def ln_stats(tile_of, width, waits=()):
        bs, bq = 4, 5
        ctx.bank_wait(bs)
        ctx.bank_wait(bq)
        vew = ctx.VE.n
        ctx.wait("t", "VE", vew)
        for eng_sem_val in waits:
            ctx.wait("t", eng_sem_val[0], eng_sem_val[1])
        mmq = {}
        for j in range(NJ):
            T_("matmul", ctx.ps[bs][0:1, 0:width], onescol[:, 0:1], tile_of(j),
               start=(j == 0), stop=(j == NJ - 1))
            ctx.mm_inc()
        ctx.wait("s", "VE", vew)
        for eng_sem_val in waits:
            ctx.wait("s", eng_sem_val[0], eng_sem_val[1])
        for j in range(NJ):
            buf = S4 if (j % 2 == 0) else SQ2
            if j >= 2:
                ctx.wait("s", "MM", mmq[j - 2])
            S("activation", buf[:, 0:width], tile_of(j), AF.Square)
            seq = ctx.se_inc()
            ctx.wait("t", "SE", seq)
            T_("matmul", ctx.ps[bq][0:1, 0:width], onescol[:, 0:1], buf[:, 0:width],
               start=(j == 0), stop=(j == NJ - 1))
            mmq[j] = ctx.mm_inc()
        ctx.wait("v", "MM", mmq[NJ - 1])
        st = stats
        V("tensor_scalar_mul", st[0:1, 0:width], ctx.ps[bs][0:1, 0:width], 1.0 / C)
        V("tensor_scalar_mul", st[0:1, 512:512 + width], ctx.ps[bq][0:1, 0:width], 1.0 / C)
        V("tensor_mul", st[0:1, 1024:1024 + width], st[0:1, 0:width], st[0:1, 0:width])
        V("tensor_sub", st[0:1, 512:512 + width], st[0:1, 512:512 + width],
          st[0:1, 1024:1024 + width])
        V("tensor_scalar_add", st[0:1, 512:512 + width], st[0:1, 512:512 + width], LN_EPS)
        ve1 = ctx.ve_inc()
        ctx.wait("s", "VE", ve1)
        S("activation", st[0:1, 1024:1024 + width], st[0:1, 512:512 + width], AF.Ln)
        se1 = ctx.se_inc()
        ctx.wait("v", "SE", se1)
        V("tensor_scalar_mul", st[0:1, 1024:1024 + width], st[0:1, 1024:1024 + width], -0.5)
        ve1b = ctx.ve_inc()
        ctx.wait("s", "VE", ve1b)
        S("activation", st[0:1, 1536:1536 + width], st[0:1, 1024:1024 + width], AF.Exp)
        se1b = ctx.se_inc()
        ctx.wait("v", "SE", se1b)
        V("tensor_mul", st[0:1, 0:width], st[0:1, 0:width], st[0:1, 1536:1536 + width])
        V("tensor_scalar_mul", st[0:1, 0:width], st[0:1, 0:width], -1.0)
        ve2 = ctx.ve_inc()
        ctx.wait("t", "VE", ve2)
        T_("matmul", ctx.ps[bs][:, 0:width], onesrow[0:1, :], st[0:1, 1536:1536 + width],
           start=True, stop=True)
        T_("matmul", ctx.ps[bq][:, 0:width], onesrow[0:1, :], st[0:1, 0:width],
           start=True, stop=True)
        mmb = ctx.mm_inc()
        ctx.wait("v", "MM", mmb)
        V("tensor_copy", lnscr[:, 0:width], ctx.ps[bs][:, 0:width])
        V("tensor_copy", lnscr[:, TC:TC + width], ctx.ps[bq][:, 0:width])
        vee = ctx.ve_inc()
        ctx.bank_set(bs, "VE", vee)
        ctx.bank_set(bq, "VE", vee)

    am1 = ctx.amark()
    iotarev = ctx.aalloc(TC)
    ctx.ld(iotarev, ap["iotarev"])
    A0 = ctx.aalloc(NJ * (TC + 1))
    A1 = ctx.aalloc(NJ * TC)
    A2 = ctx.aalloc(NJ * (TC + 1))
    A3 = ctx.aalloc(NJ * TC)
    KB = ctx.aalloc(NJ * TC)
    VB = ctx.aalloc(NJ * TC)
    RB = ctx.aalloc(NJ * TC)
    WBUF = ctx.aalloc(NJ * 1024)

    wname = {"k": "kwT", "v": "vwT", "r": "rwT"}
    spill_done = 0
    sq_chunk = 0
    mm_w_last = 0
    ve_mix_done = 0
    for c in range(NCHK):
        base = c * TC
        ld_x = 0
        for j in range(NJ):
            if c > 0:
                ctx.wait("y", "VE", ve_h0_prev)
            ld_x = ctx.ld(a1_(A0, j), ap["xT"][j * 128:(j + 1) * 128, base:base + TC + 1])
        ld_x = ctx.lq()
        ln_stats(lambda j: a1_(A0, j)[:, 1:TC + 1], TC, waits=(("LQ", ld_x),))
        ctx.wait("v", "LQ", ld_x)
        ctx.wait("v", "SQ", sq_chunk)
        for j in range(NJ):
            V("tensor_mul", a_(A1, j), a1_(A0, j)[:, 1:TC + 1], lnscr[:, 0:TC])
            V("tensor_add", a_(A1, j), a_(A1, j), lnscr[:, TC:2 * TC])
            V("tensor_scalar", a_(A1, j), a_(A1, j), dc("ca1", j), dc("sa", j),
              ALU.mult, ALU.add)
        ve_h0 = ctx.ve_inc()
        ve_h0_prev = ve_h0
        for j in range(NJ):
            ctx.wait("g", "VE", ve_h0)
            spill_done = ctx.sp(h0_sp[j * 128:(j + 1) * 128, base:base + TC], a_(A1, j))
        ln_stats(lambda j: a_(A1, j), TC)
        for j in range(NJ):
            V("tensor_mul", a1_(A2, j)[:, 1:TC + 1], a_(A1, j), lnscr[:, 0:TC])
            V("tensor_add", a1_(A2, j)[:, 1:TC + 1], a1_(A2, j)[:, 1:TC + 1],
              lnscr[:, TC:2 * TC])
            V("tensor_scalar", a1_(A2, j)[:, 1:TC + 1], a1_(A2, j)[:, 1:TC + 1],
              vc("ln1w", j), vc("ln1b", j), ALU.mult, ALU.add)
            V("tensor_copy", a1_(A2, j)[:, 0:1],
              ubnd[:, j:j + 1] if c == 0 else usave[:, j:j + 1])
        ctx.ve_inc()
        for j in range(NJ):
            V("tensor_copy", usave[:, j:j + 1], a1_(A2, j)[:, TC:TC + 1])
        ctx.ve_inc()

        for nm in ("k", "v", "r"):
            tm = {"k": "tmk", "v": "tmv", "r": "tmr"}[nm]
            ctx.wait("v", "MM", mm_w_last)
            for j in range(NJ):
                V("tensor_sub", a_(A3, j), a1_(A2, j)[:, 1:TC + 1], a1_(A2, j)[:, 0:TC])
                V("scalar_tensor_tensor", a_(A3, j), a_(A3, j), vc(tm, j),
                  a1_(A2, j)[:, 0:TC], ALU.mult, ALU.add)
            ve_mix = ctx.ve_inc()
            ld_w = 0
            for j in range(NJ):
                ctx.wait("y", "MM", mm_w_last)
                ld_w = ctx.ld(WBUF[:, j * 1024:(j + 1) * 1024],
                              ap[wname[nm]][j * 128:(j + 1) * 128, :])
            ld_w = ctx.lq()
            dstbuf = {"k": KB, "v": VB, "r": RB}[nm]
            for cout in range(NJ):
                bank = cout % 2
                ctx.bank_wait(bank)
                ctx.wait("t", "LQ", ld_w)
                ctx.wait("t", "VE", ve_mix)
                for cin in range(NJ):
                    T_("matmul", ctx.ps[bank][:, 0:TC],
                       WBUF[:, cin * 1024 + cout * 128:cin * 1024 + (cout + 1) * 128],
                       a_(A3, cin), start=(cin == 0), stop=(cin == NJ - 1))
                mmv = ctx.mm_inc()
                if nm == "r":
                    ctx.wait("s", "MM", mmv)
                    S("activation", a_(dstbuf, cout), ctx.ps[bank][:, 0:TC], AF.Sigmoid)
                    ctx.bank_set(bank, "SE", ctx.se_inc())
                else:
                    ctx.wait("v", "MM", mmv)
                    V("tensor_copy", a_(dstbuf, cout), ctx.ps[bank][:, 0:TC])
                    ctx.bank_set(bank, "VE", ctx.ve_inc())
            mm_w_last = ctx.MM.n

        ve_k = ctx.VE.n
        se_r = ctx.SE.n
        for vi2, buf in ((0, KB), (1, VB), (2, RB)):
            for j in range(NJ):
                ctx.wait("g", "VE", ve_k)
                ctx.wait("g", "SE", se_r)
                spill_done = ctx.sp(
                    kvs_sp[vi2 * C + j * 128:vi2 * C + (j + 1) * 128, base:base + TC],
                    a_(buf, j))
        sq_chunk = ctx.sq()

        for j in range(NJ):
            V("tensor_scalar_mul", S1[:, 0:TC], iotarev, dc("w", j))
            V("tensor_add", S1[:, 0:TC], S1[:, 0:TC], a_(KB, j))
            V("reduce_max", csum[:, j:j + 1], S1[:, 0:TC], axis=AX)
            V("tensor_single_scalar", S1[:, 0:TC], S1[:, 0:TC], csum[:, j:j + 1], ALU.subtract)
            ve_e = ctx.ve_inc()
            ctx.wait("s", "VE", ve_e)
            S("activation", S2[:, 0:TC], S1[:, 0:TC], AF.Exp)
            se_e = ctx.se_inc()
            ctx.wait("v", "SE", se_e)
            V("reduce_sum", csum[:, 2 * NJ + j:2 * NJ + j + 1], S2[:, 0:TC], axis=AX)
            V("tensor_mul", S2[:, 0:TC], S2[:, 0:TC], a_(VB, j))
            V("reduce_sum", csum[:, NJ + j:NJ + j + 1], S2[:, 0:TC], axis=AX)
            ctx.ve_inc()
        Mr, Ar, Br = runsum[:, 0:NJ], runsum[:, NJ:2 * NJ], runsum[:, 2 * NJ:3 * NJ]
        Mc_, Ac_, Bc_ = csum[:, 0:NJ], csum[:, NJ:2 * NJ], csum[:, 2 * NJ:3 * NJ]
        V("scalar_tensor_tensor", SB[:, 0:NJ], dct("w"), float(TC), Mr, ALU.mult, ALU.add)
        V("tensor_max", SB[:, 8:8 + NJ], SB[:, 0:NJ], Mc_)
        V("tensor_sub", SB[:, 0:NJ], SB[:, 0:NJ], SB[:, 8:8 + NJ])
        V("tensor_sub", SB[:, 16:16 + NJ], Mc_, SB[:, 8:8 + NJ])
        ve_c1 = ctx.ve_inc()
        ctx.wait("s", "VE", ve_c1)
        S("activation", SB[:, 0:NJ], SB[:, 0:NJ], AF.Exp)
        S("activation", SB[:, 16:16 + NJ], SB[:, 16:16 + NJ], AF.Exp)
        se_c = ctx.se_inc()
        ctx.wait("v", "SE", se_c)
        V("tensor_mul", Ar, Ar, SB[:, 0:NJ])
        V("tensor_mul", SB[:, 24:24 + NJ], Ac_, SB[:, 16:16 + NJ])
        V("tensor_add", Ar, Ar, SB[:, 24:24 + NJ])
        V("tensor_mul", Br, Br, SB[:, 0:NJ])
        V("tensor_mul", SB[:, 24:24 + NJ], Bc_, SB[:, 16:16 + NJ])
        V("tensor_add", Br, Br, SB[:, 24:24 + NJ])
        V("tensor_copy", Mr, SB[:, 8:8 + NJ])
        ve_runsum = ctx.ve_inc()

    bank = 6
    ctx.bank_wait(bank)
    ctx.wait("t", "VE", ve_runsum)
    T_("transpose", ctx.ps[bank][0:24, 0:128], runsum[:], ident[:])
    mm_ag1 = ctx.mm_inc()
    ctx.wait("v", "MM", mm_ag1)
    V("tensor_copy", S1[0:24, 0:128], ctx.ps[bank][0:24, 0:128])
    ve_ag1 = ctx.ve_inc()
    ctx.bank_set(bank, "VE", ve_ag1)
    ctx.wait("g", "VE", ve_ag1)
    sp_ag1 = ctx.sp(ag1_in, S1[0:24, 0:128])
    ctx.wait("g", "SP", sp_ag1)
    cc_ag1 = ctx.allgather(ag1_in, ag1_out)

    allrM = ctx.aalloc(8 * 128)
    allrA = ctx.aalloc(8 * 128)
    allrB = ctx.aalloc(8 * 128)
    stP = ctx.aalloc(128)
    stA = ctx.aalloc(128)
    stB = ctx.aalloc(128)
    q1 = ctx.aalloc(128)
    q2 = ctx.aalloc(128)
    q3 = ctx.aalloc(128)
    ctx.wait("y", "CC", cc_ag1)
    for vv, dst in ((0, allrM), (1, allrA), (2, allrB)):
        ld_allr = ctx.ld(dst[0:8, :],
                         mkap(ag1_out, vv * 8 * 128, [[128, 8], [24 * 128, 8], [1, 128]]))
    ld_allr = ctx.lq()
    ctx.wait("v", "LQ", ld_allr)
    V("memset", stP[0:8, :], NEG)
    V("memset", stA[0:8, :], 0.0)
    V("memset", stB[0:8, :], 0.0)
    ctx.ve_inc()
    wrow = trow[0:8, 0:128]
    for k in range(8):
        Msl = allrM[0:8, k * 128:(k + 1) * 128]
        Asl = allrA[0:8, k * 128:(k + 1) * 128]
        Bsl = allrB[0:8, k * 128:(k + 1) * 128]
        V("tensor_scalar", q1[0:8, :], Msl, masklt8[:, k:k + 1], mneg8[:, k:k + 1],
          ALU.mult, ALU.add)
        V("scalar_tensor_tensor", q2[0:8, :], wrow, tlm8[:, k:k + 1], stP[0:8, :],
          ALU.mult, ALU.add)
        V("tensor_max", q3[0:8, :], q2[0:8, :], q1[0:8, :])
        V("tensor_sub", q2[0:8, :], q2[0:8, :], q3[0:8, :])
        V("tensor_sub", q1[0:8, :], q1[0:8, :], q3[0:8, :])
        ve_cb = ctx.ve_inc()
        ctx.wait("s", "VE", ve_cb)
        S("activation", q2[0:8, :], q2[0:8, :], AF.Exp)
        S("activation", q1[0:8, :], q1[0:8, :], AF.Exp)
        se_cb = ctx.se_inc()
        ctx.wait("v", "SE", se_cb)
        V("tensor_copy", stP[0:8, :], q3[0:8, :])
        V("tensor_mul", stA[0:8, :], stA[0:8, :], q2[0:8, :])
        V("tensor_single_scalar", q3[0:8, :], Asl, masklt8[:, k:k + 1], ALU.mult)
        V("tensor_mul", q3[0:8, :], q3[0:8, :], q1[0:8, :])
        V("tensor_add", stA[0:8, :], stA[0:8, :], q3[0:8, :])
        V("tensor_mul", stB[0:8, :], stB[0:8, :], q2[0:8, :])
        V("tensor_single_scalar", q3[0:8, :], Bsl, masklt8[:, k:k + 1], ALU.mult)
        V("tensor_mul", q3[0:8, :], q3[0:8, :], q1[0:8, :])
        V("tensor_add", stB[0:8, :], stB[0:8, :], q3[0:8, :])
        ctx.ve_inc()
    bank = 7
    ctx.bank_wait(bank)
    ctx.wait("t", "VE", ctx.VE.n)
    for sidx, st_t in enumerate((stP, stA, stB)):
        for j in range(NJ):
            T_("matmul", ctx.ps[bank][:, sidx * NJ + j:sidx * NJ + j + 1], st_t[0:8, :],
               oh8id[:, j:j + 1], start=True, stop=True)
    mm_st0 = ctx.mm_inc()
    ctx.wait("v", "MM", mm_st0)
    V("tensor_copy", state0[:], ctx.ps[bank][:, 0:3 * NJ])
    ve_state0 = ctx.ve_inc()
    ctx.dbg_dump("state0_d", state0[:], [128, 3 * NJ])
    ctx.dbg_dump("runsum_d", runsum[:], [128, 3 * NJ])
    ctx.bank_set(bank, "VE", ve_state0)
    ctx.phase_barrier()
    ctx.areset(am1)

    am4 = ctx.amark()
    KB = ctx.aalloc(NJ * TC)
    VB = ctx.aalloc(NJ * TC)
    RB = ctx.aalloc(NJ * TC)
    BPP = ctx.aalloc(NJ * (TC + 1))
    BAA = ctx.aalloc(NJ * (TC + 1))
    BBB = ctx.aalloc(NJ * (TC + 1))
    RW = ctx.aalloc(NJ * TC)
    AB = ctx.aalloc(NJ * TC)
    H0R = ctx.aalloc(2 * TC)
    D4 = ctx.aalloc(4 * TC)
    U4 = ctx.aalloc(4 * TC)
    UV4 = ctx.aalloc(4 * TC)
    WOB = ctx.aalloc(2 * NJ * 128)

    def ld_w_cout(wap, cout, slot):
        t = 0
        for cin in range(NJ):
            t = ctx.ld(WOB[:, slot * 1024 + cin * 128:slot * 1024 + (cin + 1) * 128],
                       wap[cin * 128:(cin + 1) * 128, cout * 128:(cout + 1) * 128])
        return ctx.lq()

    sp_a = 0
    sq_a = 0
    ve_kdone = 0
    mm_op_last = 0
    ve_adone_prev = 0
    cc_ag1b = 0
    for c in range(NCHK):
        base = c * TC
        ld_kvs = 0
        ctx.wait("y", "SQ", sq_chunk)
        ctx.wait("y", "VE", ve_kdone)
        for vi2, buf in ((0, KB), (1, VB), (2, RB)):
            for j in range(NJ):
                ld_kvs = ctx.ld(a_(buf, j),
                                kvs_sp[vi2 * C + j * 128:vi2 * C + (j + 1) * 128,
                                       base:base + TC])
        ld_kvs = ctx.lq()
        ctx.wait("v", "LQ", ld_kvs)
        ctx.wait("v", "VE", ve_state0)
        ctx.wait("v", "MM", mm_op_last)
        ctx.wait("v", "SQ", sq_a)
        for half in range(2):
            js = list(range(half * 4, half * 4 + 4))
            for jj, j in enumerate(js):
                bpp, baa, bbb = a1_(BPP, j), a1_(BAA, j), a1_(BBB, j)
                if c == 0:
                    V("tensor_copy", bpp[:, 0:1], state0[:, j:j + 1])
                    V("tensor_copy", baa[:, 0:1], state0[:, NJ + j:NJ + j + 1])
                    V("tensor_copy", bbb[:, 0:1], state0[:, 2 * NJ + j:2 * NJ + j + 1])
                else:
                    V("tensor_copy", bpp[:, 0:1], bpp[:, TC:TC + 1])
                    V("tensor_copy", baa[:, 0:1], baa[:, TC:TC + 1])
                    V("tensor_copy", bbb[:, 0:1], bbb[:, TC:TC + 1])
                V("tensor_tensor_scan", bpp[:, 1:TC + 1],
                  dc("w", j).broadcast_to([128, TC]), a_(KB, j), bpp[:, 0:1],
                  ALU.add, ALU.max)
            for jj, j in enumerate(js):
                bpp = a1_(BPP, j)
                V("scalar_tensor_tensor", a_(D4, jj), bpp[:, 0:TC], dc("w", j),
                  bpp[:, 1:TC + 1], ALU.add, ALU.subtract)
                V("tensor_sub", a_(U4, jj), a_(KB, j), bpp[:, 1:TC + 1])
            ve_b1 = ctx.ve_inc()
            ctx.wait("s", "VE", ve_b1)
            for jj in range(4):
                S("activation", a_(D4, jj), a_(D4, jj), AF.Exp)
                S("activation", a_(U4, jj), a_(U4, jj), AF.Exp)
            se_b1 = ctx.se_inc()
            ctx.wait("v", "SE", se_b1)
            for jj, j in enumerate(js):
                bpp, baa, bbb = a1_(BPP, j), a1_(BAA, j), a1_(BBB, j)
                V("tensor_mul", a_(UV4, jj), a_(U4, jj), a_(VB, j))
                V("tensor_tensor_scan", bbb[:, 1:TC + 1], a_(D4, jj), a_(U4, jj),
                  bbb[:, 0:1], ALU.mult, ALU.add)
                V("tensor_tensor_scan", baa[:, 1:TC + 1], a_(D4, jj), a_(UV4, jj),
                  baa[:, 0:1], ALU.mult, ALU.add)
            for jj, j in enumerate(js):
                bpp = a1_(BPP, j)
                V("tensor_single_scalar", a_(KB, j), a_(KB, j), vc("tfir", j), ALU.add)
                V("tensor_max", a_(UV4, jj), bpp[:, 0:TC], a_(KB, j))
                V("tensor_sub", a_(D4, jj), bpp[:, 0:TC], a_(UV4, jj))
                V("tensor_sub", a_(KB, j), a_(KB, j), a_(UV4, jj))
            ve_b2 = ctx.ve_inc()
            ctx.wait("s", "VE", ve_b2)
            for jj, j in enumerate(js):
                S("activation", a_(D4, jj), a_(D4, jj), AF.Exp)
                S("activation", a_(KB, j), a_(KB, j), AF.Exp)
            se_b2 = ctx.se_inc()
            ctx.wait("v", "SE", se_b2)
            for jj, j in enumerate(js):
                bbb = a1_(BBB, j)
                V("tensor_mul", a_(UV4, jj), a_(D4, jj), bbb[:, 0:TC])
                V("tensor_add", a_(UV4, jj), a_(UV4, jj), a_(KB, j))
            ve_b3 = ctx.ve_inc()
            ctx.wait("s", "VE", ve_b3)
            for jj in range(4):
                S("activation", a_(UV4, jj), a_(UV4, jj), AF.Ln)
            se_b3 = ctx.se_inc()
            ctx.wait("v", "SE", se_b3)
            for jj in range(4):
                V("tensor_scalar_mul", a_(UV4, jj), a_(UV4, jj), -1.0)
            ve_b4 = ctx.ve_inc()
            ctx.wait("s", "VE", ve_b4)
            for jj in range(4):
                S("activation", a_(UV4, jj), a_(UV4, jj), AF.Exp)
            se_b4 = ctx.se_inc()
            ctx.wait("v", "SE", se_b4)
            for jj, j in enumerate(js):
                baa = a1_(BAA, j)
                V("tensor_mul", a_(D4, jj), a_(D4, jj), baa[:, 0:TC])
                V("tensor_mul", a_(KB, j), a_(KB, j), a_(VB, j))
                V("tensor_add", a_(D4, jj), a_(D4, jj), a_(KB, j))
                V("tensor_mul", a_(D4, jj), a_(D4, jj), a_(UV4, jj))
                V("tensor_mul", a_(RW, j), a_(D4, jj), a_(RB, j))
            ctx.ve_inc()
        ve_kdone = ctx.VE.n

        evac_a = {}
        mmo_hist = {}
        lqw = {}

        def issue_op_load(cout):
            if cout >= 2:
                ctx.wait("y", "MM", mmo_hist[cout - 2])
                ctx.wait("y", "VE", evac_a[cout - 2])
            else:
                ctx.wait("y", "MM", mm_op_last)
                ctx.wait("y", "VE", ve_adone_prev)
            ld_w_cout(ap["owT"], cout, cout % 2)
            ctx.ld(H0R[:, (cout % 2) * TC:(cout % 2) * TC + TC],
                   h0_sp[cout * 128:(cout + 1) * 128, base:base + TC])
            lqw[cout] = ctx.lq()

        issue_op_load(0)
        issue_op_load(1)
        for cout in range(NJ):
            bank = cout % 2
            ctx.bank_wait(bank)
            ctx.wait("t", "LQ", lqw[cout])
            ctx.wait("t", "VE", ve_kdone)
            for cin in range(NJ):
                T_("matmul", ctx.ps[bank][:, 0:TC],
                   WOB[:, (cout % 2) * 1024 + cin * 128:(cout % 2) * 1024 + (cin + 1) * 128],
                   a_(RW, cin), start=(cin == 0), stop=(cin == NJ - 1))
            mmo = ctx.mm_inc()
            mmo_hist[cout] = mmo
            ctx.wait("v", "MM", mmo)
            V("tensor_add", a_(AB, cout), ctx.ps[bank][:, 0:TC],
              H0R[:, (cout % 2) * TC:(cout % 2) * TC + TC])
            evac_a[cout] = ctx.ve_inc()
            ctx.bank_set(bank, "VE", evac_a[cout])
            if cout + 2 < NJ:
                issue_op_load(cout + 2)
        mm_op_last = ctx.MM.n
        ve_adone = ctx.VE.n
        ve_adone_prev = ve_adone
        for j in range(NJ):
            ctx.wait("g", "VE", ve_adone)
            sp_a = ctx.sp(a_sp[j * 128:(j + 1) * 128, base:base + TC], a_(AB, j))
        sq_a = ctx.sq()
        if c == NCHK - 1:
            for j in range(NJ):
                V("tensor_copy", SB[:, j:j + 1], a_(AB, j)[:, TC - 1:TC])
            ve_al = ctx.ve_inc()
            bank = 6
            ctx.bank_wait(bank)
            ctx.wait("t", "VE", ve_al)
            T_("transpose", ctx.ps[bank][0:8, 0:128], SB[:, 0:NJ], ident[:])
            mm_al = ctx.mm_inc()
            ctx.wait("v", "MM", mm_al)
            V("tensor_copy", S1[0:8, 0:128], ctx.ps[bank][0:8, 0:128])
            ve_al2 = ctx.ve_inc()
            ctx.bank_set(bank, "VE", ve_al2)
            ctx.wait("g", "VE", ve_al2)
            sp_ag1b = ctx.sp(ag1b_in, S1[0:8, 0:128])
            ctx.wait("g", "SP", sp_ag1b)
            cc_ag1b = ctx.allgather(ag1b_in, ag1b_out)

    ctx.phase_barrier()
    ctx.areset(am4)
    ab8 = ctx.aalloc(8 * 128)
    ctx.wait("y", "CC", cc_ag1b)
    ld_ab8 = ctx.ld(ab8[0:8, :],
                    mkap(ag1b_out, 0, [[128, 8], [8 * 128, 8], [1, 128]]))
    ld_ab8 = ctx.lq()
    ctx.wait("v", "LQ", ld_ab8)
    V("memset", S1[0:8, 0:128], 0.0)
    for k in range(8):
        V("scalar_tensor_tensor", S1[0:8, 0:128], ab8[0:8, k * 128:(k + 1) * 128],
          mprev8[:, k:k + 1], S1[0:8, 0:128], ALU.mult, ALU.add)
    ve_am = ctx.ve_inc()
    bank = 4
    ctx.bank_wait(bank)
    V("reduce_sum", S2[0:8, 0:1], S1[0:8, 0:128], axis=AX)
    ve_rs = ctx.ve_inc()
    ctx.wait("s", "VE", ve_rs)
    S("activation", S3[0:8, 0:128], S1[0:8, 0:128], AF.Square)
    se_sq = ctx.se_inc()
    ctx.wait("v", "SE", se_sq)
    V("reduce_sum", S2[0:8, 1:2], S3[0:8, 0:128], axis=AX)
    ve_rs2 = ctx.ve_inc()
    ctx.wait("t", "VE", ve_rs2)
    T_("matmul", ctx.ps[bank][0:1, 0:2], ones8[:, 0:1], S2[0:8, 0:2], start=True, stop=True)
    mm_bn = ctx.mm_inc()
    ctx.wait("v", "MM", mm_bn)
    st = stats
    V("tensor_scalar_mul", st[0:1, 0:2], ctx.ps[bank][0:1, 0:2], 1.0 / C)
    V("tensor_mul", st[0:1, 2:3], st[0:1, 0:1], st[0:1, 0:1])
    V("tensor_sub", st[0:1, 1:2], st[0:1, 1:2], st[0:1, 2:3])
    V("tensor_scalar_add", st[0:1, 1:2], st[0:1, 1:2], LN_EPS)
    ve_bv = ctx.ve_inc()
    ctx.wait("s", "VE", ve_bv)
    S("activation", st[0:1, 3:4], st[0:1, 1:2], AF.Ln)
    se_bs = ctx.se_inc()
    ctx.wait("v", "SE", se_bs)
    V("tensor_scalar_mul", st[0:1, 3:4], st[0:1, 3:4], -0.5)
    ve_bv2 = ctx.ve_inc()
    ctx.wait("s", "VE", ve_bv2)
    S("activation", st[0:1, 4:5], st[0:1, 3:4], AF.Exp)
    se_bs2 = ctx.se_inc()
    ctx.wait("v", "SE", se_bs2)
    V("tensor_copy", st[0:1, 4:5], st[0:1, 4:5])
    ve_bA = ctx.ve_inc()
    ctx.wait("t", "VE", ve_bA)
    T_("matmul", ctx.ps[bank][0:8, 4:5], onesrow[0:1, 0:8], st[0:1, 4:5], start=True, stop=True)
    T_("matmul", ctx.ps[bank][0:8, 5:6], onesrow[0:1, 0:8], st[0:1, 0:1], start=True, stop=True)
    mm_bb = ctx.mm_inc()
    ctx.wait("v", "MM", mm_bb)
    V("tensor_single_scalar", S1[0:8, 0:128], S1[0:8, 0:128], ctx.ps[bank][0:8, 5:6],
      ALU.subtract)
    V("tensor_single_scalar", S1[0:8, 0:128], S1[0:8, 0:128], ctx.ps[bank][0:8, 4:5],
      ALU.mult)
    V("tensor_mul", S1[0:8, 0:128], S1[0:8, 0:128], trow[0:8, 128:256])
    V("tensor_add", S1[0:8, 0:128], S1[0:8, 0:128], trow[0:8, 256:384])
    V("tensor_single_scalar", S1[0:8, 0:128], S1[0:8, 0:128], maskfirst[:, 0:1], ALU.mult)
    ve_u2b = ctx.ve_inc()
    ctx.bank_set(bank, "VE", ve_u2b)
    bank = 7
    ctx.bank_wait(bank)
    ctx.wait("t", "VE", ve_u2b)
    for j in range(NJ):
        T_("matmul", ctx.ps[bank][:, j:j + 1], S1[0:8, 0:128], oh8id[:, j:j + 1],
           start=True, stop=True)
    mm_u2b = ctx.mm_inc()
    ctx.wait("v", "MM", mm_u2b)
    V("tensor_copy", u2bnd[:], ctx.ps[bank][:, 0:NJ])
    ve_u2bnd = ctx.ve_inc()
    ctx.bank_set(bank, "VE", ve_u2bnd)
    ctx.phase_barrier()
    ctx.areset(am4)

    am4b = ctx.amark()
    AB = ctx.aalloc(NJ * TC)
    U2 = ctx.aalloc(NJ * (TC + 1))
    A3 = ctx.aalloc(NJ * TC)
    FRB = ctx.aalloc(NJ * TC)
    WFR = ctx.aalloc(NJ * 1024)
    ld_wfr = 0
    for j in range(NJ):
        ld_wfr = ctx.ld(WFR[:, j * 1024:(j + 1) * 1024],
                        ap["frwT"][j * 128:(j + 1) * 128, :])
    ld_wfr = ctx.lq()

    sp_4b = 0
    sq_4b = 0
    sq_xk2 = 0
    ve_4b = 0
    mm_fr_last = 0
    for c in range(NCHK):
        base = c * TC
        ld_a = 0
        ctx.wait("y", "SQ", sq_a)
        ctx.wait("y", "VE", ve_4b)
        for j in range(NJ):
            ld_a = ctx.ld(a_(AB, j), a_sp[j * 128:(j + 1) * 128, base:base + TC])
        ld_a = ctx.lq()
        ln_stats(lambda j: a_(AB, j), TC, waits=(("LQ", ld_a),))
        ctx.wait("v", "LQ", ld_a)
        for j in range(NJ):
            V("tensor_mul", a1_(U2, j)[:, 1:TC + 1], a_(AB, j), lnscr[:, 0:TC])
            V("tensor_add", a1_(U2, j)[:, 1:TC + 1], a1_(U2, j)[:, 1:TC + 1],
              lnscr[:, TC:2 * TC])
            V("tensor_scalar", a1_(U2, j)[:, 1:TC + 1], a1_(U2, j)[:, 1:TC + 1],
              vc("ln2w", j), vc("ln2b", j), ALU.mult, ALU.add)
            V("tensor_copy", a1_(U2, j)[:, 0:1],
              u2bnd[:, j:j + 1] if c == 0 else u2save[:, j:j + 1])
        ctx.ve_inc()
        for j in range(NJ):
            V("tensor_copy", u2save[:, j:j + 1], a1_(U2, j)[:, TC:TC + 1])
        ctx.ve_inc()
        ctx.wait("v", "SQ", sq_4b)
        for j in range(NJ):
            V("tensor_sub", a_(A3, j), a1_(U2, j)[:, 1:TC + 1], a1_(U2, j)[:, 0:TC])
            V("scalar_tensor_tensor", a_(A3, j), a_(A3, j), vc("ftmk", j),
              a1_(U2, j)[:, 0:TC], ALU.mult, ALU.add)
        ve_xk2 = ctx.ve_inc()
        sp_xk2 = 0
        for j in range(NJ):
            ctx.wait("g", "VE", ve_xk2)
            sp_xk2 = ctx.sp(xk2_sp[j * 128:(j + 1) * 128, base:base + TC], a_(A3, j))
        sq_xk2 = ctx.sq()
        ctx.wait("v", "MM", mm_fr_last)
        for j in range(NJ):
            V("tensor_sub", a_(FRB, j), a1_(U2, j)[:, 1:TC + 1], a1_(U2, j)[:, 0:TC])
            V("scalar_tensor_tensor", a_(FRB, j), a_(FRB, j), vc("ftmr", j),
              a1_(U2, j)[:, 0:TC], ALU.mult, ALU.add)
        ve_xr2 = ctx.ve_inc()
        for cout in range(NJ):
            bank = cout % 2
            ctx.bank_wait(bank)
            ctx.wait("t", "LQ", ld_wfr)
            ctx.wait("t", "VE", ve_xr2)
            for cin in range(NJ):
                T_("matmul", ctx.ps[bank][:, 0:TC],
                   WFR[:, cin * 1024 + cout * 128:cin * 1024 + (cout + 1) * 128],
                   a_(FRB, cin), start=(cin == 0), stop=(cin == NJ - 1))
            mmf = ctx.mm_inc()
            ctx.wait("s", "MM", mmf)
            ctx.wait("s", "SQ", sq_xk2)
            S("activation", a_(A3, cout), ctx.ps[bank][:, 0:TC], AF.Sigmoid)
            ctx.bank_set(bank, "SE", ctx.se_inc())
        mm_fr_last = ctx.MM.n
        se_fr = ctx.SE.n
        for j in range(NJ):
            ctx.wait("g", "SE", se_fr)
            sp_4b = ctx.sp(sig_sp[j * 128:(j + 1) * 128, base:base + TC], a_(A3, j))
        sq_4b = ctx.sq()
        ve_4b = ctx.VE.n
    ctx.phase_barrier()
    ctx.areset(am4b)

    am5 = ctx.amark()
    KVSB = ctx.aalloc(NJ * TL)
    am5b = ctx.amark()
    WFK = ctx.aalloc(NJ * g.FS)
    WFV = ctx.aalloc((g.FS // 128) * 1024)
    XK2T = ctx.aalloc(NJ * TC)
    FK2 = ctx.aalloc((g.FS // 128) * TC)
    NFO = g.FS // 128

    ctx.wait("v", "SQ", sq_4b)
    V("memset", KVSB[:, :], 0.0)
    ve_kv0 = ctx.ve_inc()
    mm_cm_last = 0
    for fs in range(g.NFS):
        ld_wf = 0
        for j in range(NJ):
            ctx.wait("y", "MM", mm_cm_last)
            ld_wf = ctx.ld(WFK[:, j * g.FS:(j + 1) * g.FS],
                           ap["fkwT"][j * 128:(j + 1) * 128, fs * g.FS:(fs + 1) * g.FS])
        for fo in range(NFO):
            ld_wf = ctx.ld(WFV[:, fo * 1024:(fo + 1) * 1024],
                           ap["fvwT"][fs * g.FS + fo * 128:fs * g.FS + (fo + 1) * 128, :])
        for tc in range(NCHK):
            base = tc * TC
            ld_x2 = 0
            for j in range(NJ):
                ctx.wait("y", "MM", mm_cm_last)
                ld_x2 = ctx.ld(a_(XK2T, j), xk2_sp[j * 128:(j + 1) * 128, base:base + TC])
            ld_wx = ctx.lq()
            se_fk = {}
            ctx.wait("s", "MM", mm_cm_last)
            for fo in range(NFO):
                bank = fo % 2
                ctx.bank_wait(bank)
                ctx.wait("t", "LQ", ld_wx)
                for cin in range(NJ):
                    T_("matmul", ctx.ps[bank][:, 0:TC],
                       WFK[:, cin * g.FS + fo * 128:cin * g.FS + (fo + 1) * 128],
                       a_(XK2T, cin), start=(cin == 0), stop=(cin == NJ - 1))
                mmk = ctx.mm_inc()
                ctx.wait("s", "MM", mmk)
                S("activation", S4[:, 0:TC], ctx.ps[bank][:, 0:TC], AF.Relu)
                S("activation", FK2[:, fo * TC:(fo + 1) * TC], S4[:, 0:TC], AF.Square)
                se_fk[fo] = ctx.se_inc()
                ctx.bank_set(bank, "SE", se_fk[fo])
            for cout in range(NJ):
                bank = 2 + (cout % 2)
                ctx.bank_wait(bank)
                ctx.wait("t", "SE", se_fk[NFO - 1])
                for fo in range(NFO):
                    T_("matmul", ctx.ps[bank][:, 0:TC],
                       WFV[:, fo * 1024 + cout * 128:fo * 1024 + (cout + 1) * 128],
                       FK2[:, fo * TC:(fo + 1) * TC], start=(fo == 0), stop=(fo == NFO - 1))
                mmkv = ctx.mm_inc()
                ctx.wait("v", "MM", mmkv)
                ctx.wait("v", "VE", ve_kv0)
                V("tensor_add", KVSB[:, cout * TL + base:cout * TL + base + TC],
                  KVSB[:, cout * TL + base:cout * TL + base + TC], ctx.ps[bank][:, 0:TC])
                ctx.bank_set(bank, "VE", ctx.ve_inc())
            mm_cm_last = ctx.MM.n
    ctx.phase_barrier()
    ctx.areset(am5b)

    AB5 = ctx.aalloc(NJ * TC)
    SGB = ctx.aalloc(NJ * TC)
    XB5 = ctx.aalloc(NJ * TC)
    H1B = ctx.aalloc(NJ * TC)
    TOK = ctx.aalloc((TC // 128) * 1024)
    sp_ag2 = 0
    sq_ag2 = 0
    ve_rt = 0
    for tc in range(NCHK):
        base = tc * TC
        ld_f = 0
        ctx.wait("y", "VE", ve_rt)
        ctx.wait("y", "SQ", sq_ag2)
        for j in range(NJ):
            ld_f = ctx.ld(a_(AB5, j), a_sp[j * 128:(j + 1) * 128, base:base + TC])
            ld_f = ctx.ld(a_(SGB, j), sig_sp[j * 128:(j + 1) * 128, base:base + TC])
            ld_f = ctx.ld(a_(XB5, j), ap["xT"][j * 128:(j + 1) * 128, 1 + base:1 + base + TC])
        ld_f = ctx.lq()
        ctx.wait("v", "LQ", ld_f)
        for j in range(NJ):
            V("tensor_mul", S1[:, 0:TC], a_(SGB, j), KVSB[:, j * TL + base:j * TL + base + TC])
            V("tensor_add", S1[:, 0:TC], S1[:, 0:TC], a_(AB5, j))
            V("scalar_tensor_tensor", a_(AB5, j), S1[:, 0:TC], dc("ga", j), a_(XB5, j),
              ALU.mult, ALU.add)
        ve_xm = ctx.ve_inc()
        ln_stats(lambda j: a_(AB5, j), TC)
        for j in range(NJ):
            V("tensor_mul", a_(H1B, j), a_(AB5, j), lnscr[:, 0:TC])
            V("tensor_add", a_(H1B, j), a_(H1B, j), lnscr[:, TC:2 * TC])
            V("tensor_scalar", a_(H1B, j), a_(H1B, j), dc("cb1", j), dc("sb", j),
              ALU.mult, ALU.add)
        ve_h1 = ctx.ve_inc()
        for buf, dst in ((AB5, xmid_sp), (H1B, ag2_in)):
            tok_free = sq_ag2
            for tsub in range(TC // 128):
                for half in range(2):
                    bank = 4 + half
                    ctx.bank_wait(bank)
                    ctx.wait("t", "VE", ve_h1)
                    for jj in range(4):
                        j = half * 4 + jj
                        T_("transpose", ctx.ps[bank][:, jj * 128:(jj + 1) * 128],
                           a_(buf, j)[:, tsub * 128:(tsub + 1) * 128], ident[:])
                    mm_tp = ctx.mm_inc()
                    ctx.wait("v", "MM", mm_tp)
                    ctx.wait("v", "SQ", tok_free)
                    V("tensor_copy",
                      TOK[:, tsub * 1024 + half * 512:tsub * 1024 + (half + 1) * 512],
                      ctx.ps[bank][:, 0:512])
                    ctx.bank_set(bank, "VE", ctx.ve_inc())
            ve_tok = ctx.VE.n
            for tsub in range(TC // 128):
                ctx.wait("g", "VE", ve_tok)
                sp_ag2 = ctx.sp(dst[base + tsub * 128:base + (tsub + 1) * 128, :],
                                TOK[:, tsub * 1024:(tsub + 1) * 1024])
                if buf is H1B and "h1_dbg" in ctx.dbg_set:
                    sp_ag2 = ctx.sp(h1_dbg[base + tsub * 128:base + (tsub + 1) * 128, :],
                                    TOK[:, tsub * 1024:(tsub + 1) * 1024])
            sq_ag2 = ctx.sq()
        for tsub in range(TC // 128):
            bank = 6
            ctx.bank_wait(bank)
            ctx.wait("t", "VE", ve_h1)
            for j in range(NJ):
                T_("matmul", ctx.ps[bank][:, 0:g.E],
                   a_(H1B, j)[:, tsub * 128:(tsub + 1) * 128],
                   embTs[:, j * g.E:(j + 1) * g.E], start=(j == 0), stop=(j == NJ - 1))
            mm_lg = ctx.mm_inc()
            ti = (base // 128) + tsub
            rb = routebuf[:, ti * 18:(ti + 1) * 18]
            ctx.wait("v", "MM", mm_lg)
            V("tensor_copy", SB[:, 0:8], ctx.ps[bank][:, 0:g.E])
            ve_l = ctx.ve_inc()
            ctx.bank_set(bank, "VE", ve_l)
            V("max", out=SB[:, 8:16], in_=SB[:, 0:8])
            V("tensor_single_scalar", rb[:, 0:8], SB[:, 0:8], SB[:, 8:9], ALU.is_ge)
            V("tensor_single_scalar", SB[:, 16:24], SB[:, 0:8], SB[:, 9:10], ALU.is_ge)
            V("tensor_sub", rb[:, 8:16], SB[:, 16:24], rb[:, 0:8])
            V("tensor_sub", SB[:, 24:25], SB[:, 9:10], SB[:, 8:9])
            ve_r1 = ctx.ve_inc()
            ctx.wait("s", "VE", ve_r1)
            S("activation", SB[:, 24:25], SB[:, 24:25], AF.Exp)
            se_r1 = ctx.se_inc()
            ctx.wait("v", "SE", se_r1)
            V("tensor_scalar_add", SB[:, 25:26], SB[:, 24:25], 1.0)
            ve_r1b = ctx.ve_inc()
            ctx.wait("s", "VE", ve_r1b)
            S("activation", SB[:, 26:27], SB[:, 25:26], AF.Ln)
            se_r2 = ctx.se_inc()
            ctx.wait("v", "SE", se_r2)
            V("tensor_scalar_mul", SB[:, 26:27], SB[:, 26:27], -1.0)
            ve_r1c = ctx.ve_inc()
            ctx.wait("s", "VE", ve_r1c)
            S("activation", SB[:, 27:28], SB[:, 26:27], AF.Exp)
            se_r3 = ctx.se_inc()
            ctx.wait("v", "SE", se_r3)
            V("tensor_copy", rb[:, 16:17], SB[:, 27:28])
            V("tensor_mul", rb[:, 17:18], SB[:, 24:25], rb[:, 16:17])
            ve_r2 = ctx.ve_inc()
            bank = 7
            ctx.bank_wait(bank)
            ctx.wait("t", "VE", ve_r2)
            T_("transpose", ctx.ps[bank][0:8, 0:128], SB[:, 16:24], ident[:])
            T_("transpose", ctx.ps[bank][0:8, 128:256], rb[:, 0:8], ident[:])
            T_("transpose", ctx.ps[bank][0:8, 256:384], rb[:, 8:16], ident[:])
            mm_tr = ctx.mm_inc()
            ctx.wait("v", "MM", mm_tr)
            ctx.wait("v", "SQ", sq_ag2)
            V("tensor_copy", S1[0:8, 0:384], ctx.ps[bank][0:8, 0:384])
            ve_tr = ctx.ve_inc()
            ctx.bank_set(bank, "VE", ve_tr)
            ctx.wait("g", "VE", ve_tr)
            ctx.sp(ag2b_in[0:8, ti * 128:(ti + 1) * 128], S1[0:8, 0:128])
            ctx.sp(m12_d[0:8, ti * 128:(ti + 1) * 128], S1[0:8, 128:256])
            sp_ag2 = ctx.sp(m12_d[8:16, ti * 128:(ti + 1) * 128], S1[0:8, 256:384])
            sq_ag2 = ctx.sq()
        ve_rt = ctx.VE.n
    ctx.wait("g", "SP", sp_ag2)
    cc_ag2b = ctx.allgather(ag2b_in, ag2b_out)
    cc_ag2 = ctx.allgather(ag2_in, ag2_out)
    ctx.phase_barrier()
    ctx.areset(am5)

    am6 = ctx.amark()
    GA = ctx.aalloc(T)
    GB = ctx.aalloc(T)
    QT = ctx.aalloc(max(TL, 512))
    M1E = ctx.aalloc(TL)
    M2E = ctx.aalloc(TL)
    ctx.wait("y", "CC", cc_ag2b)
    ld_og = ctx.ld(GA[0:8, :], mkap(ag2b_out, 0, [[TL, 8], [8 * TL, 8], [1, TL]]))
    ld_m1 = ctx.ld(M1E[0:8, 0:TL], m12_d[0:8, :])
    ld_m2 = ctx.ld(M2E[0:8, 0:TL], m12_d[8:16, :])
    ld_og = ctx.lq()
    ctx.wait("v", "LQ", ld_og)
    V("tensor_tensor_scan", GB[0:8, :], GA[0:8, :],
      zz[0:8, 0:1].broadcast_to([8, T]), 0.0, ALU.add, ALU.add)
    V("scalar_tensor_tensor", GA[0:8, :], GB[0:8, :], float(g.CAP), GA[0:8, :],
      ALU.is_le, ALU.mult)
    V("scalar_tensor_tensor", GB[0:8, :], GB[0:8, :], ecap[:, 0:1], GA[0:8, :],
      ALU.add, ALU.mult)
    ve_g = ctx.ve_inc()
    sp_sv = 0
    sp_sv_hist = {}
    for o in range(T // 512):
        bank = o % 2
        ctx.bank_wait(bank)
        ctx.wait("t", "VE", ve_g)
        T_("matmul", ctx.ps[bank][0:1, 0:512], ohmy[:, 0:1], GB[0:8, o * 512:(o + 1) * 512],
           start=True, stop=False)
        T_("matmul", ctx.ps[bank][0:1, 0:512], ohs[:, 0:1], GA[0:8, o * 512:(o + 1) * 512],
           start=False, stop=True)
        mm_sv = ctx.mm_inc()
        ctx.wait("v", "MM", mm_sv)
        if o >= 2:
            ctx.wait("v", "SQ", sp_sv_hist[o - 2])
        V("tensor_scalar_add", R16[0:1, (o % 2) * 512:(o % 2) * 512 + 512],
          ctx.ps[bank][0:1, 0:512], float(g.BIG))
        ve_sv = ctx.ve_inc()
        ctx.bank_set(bank, "VE", ve_sv)
        ctx.wait("g", "VE", ve_sv)
        sp_sv = ctx.sp(svrow_d[0:1, o * 512:(o + 1) * 512],
                       R16[0:1, (o % 2) * 512:(o % 2) * 512 + 512])
        sp_sv_hist[o] = ctx.sq()
    ctx.wait("y", "SQ", ctx.SQ.n)
    ld_svt = ctx.ld(svtok[:], svrow_d.rearrange("o (a b) -> (o a) b", b=128),
                    transpose=True)
    ld_svt = ctx.lq()
    ctx.wait("v", "LQ", ld_svt)
    V("tensor_copy", idx32[:], svtok[:])
    ve_idx = ctx.ve_inc()
    sp_fld = 0
    sq_fld = 0
    for fi in range(4):
        src_em = M1E if fi < 2 else M2E
        G_t = GA if (fi % 2 == 0) else GB
        V("memset", QT[0:8, 0:TL], 0.0)
        ve_z = ctx.ve_inc()
        for k in range(8):
            V("scalar_tensor_tensor", QT[0:8, 0:TL], G_t[0:8, k * TL:(k + 1) * TL],
              ohown[:, k:k + 1], QT[0:8, 0:TL], ALU.mult, ALU.add)
        V("tensor_mul", QT[0:8, 0:TL], QT[0:8, 0:TL], src_em[0:8, 0:TL])
        ve_q = ctx.ve_inc()
        for o in range(max(1, TL // 512)):
            w_ = min(512, TL)
            bank = 2
            ctx.bank_wait(bank)
            ctx.wait("t", "VE", ve_q)
            T_("matmul", ctx.ps[bank][0:1, 0:w_], ones8[:, 0:1],
               QT[0:8, o * 512:o * 512 + w_], start=True, stop=True)
            mm_q = ctx.mm_inc()
            ctx.wait("v", "MM", mm_q)
            ctx.wait("v", "SQ", sq_fld)
            V("tensor_copy", R16[0:1, 0:w_], ctx.ps[bank][0:1, 0:w_])
            ve_fq = ctx.ve_inc()
            ctx.bank_set(bank, "VE", ve_fq)
            ctx.wait("g", "VE", ve_fq)
            sp_fld = ctx.sp(fld_d[fi:fi + 1, o * 512:o * 512 + w_], R16[0:1, 0:w_])
            sq_fld = ctx.sq()
    ctx.wait("y", "SQ", sq_fld)
    ld_fld = 0
    for fi in range(4):
        ld_fld = ctx.ld(fldtok[:, fi * 16:(fi + 1) * 16],
                        fld_d[fi:fi + 1, :].rearrange("o (a b) -> (o a) b", b=128),
                        transpose=True)
    ld_fld = ctx.lq()
    ctx.wait("v", "LQ", ld_fld)
    V("tensor_copy", fld32[:], fldtok[:])
    V("tensor_copy", idxf1[:], fldtok[:, 16:16 + TL // 128])
    V("tensor_copy", idxf2[:], fldtok[:, 48:48 + TL // 128])
    NTI = TL // 128
    capbits = g.CAP.bit_length() - 1
    stbits = g.STILE.bit_length() - 1
    for f, (idxf, idxfs) in enumerate(((idxf1, idxf1s), (idxf2, idxf2s))):
        E_, SL, BA = idxt[:, 0:NTI], idxt[:, NTI:2 * NTI], idxt[:, 2 * NTI:3 * NTI]
        M0, M1, TQ = (idxt[:, 3 * NTI:4 * NTI], idxt[:, 4 * NTI:5 * NTI],
                      idxt[:, 5 * NTI:6 * NTI])
        V("tensor_scalar", E_, idxf[:], capbits, None, ALU.logical_shift_right)
        V("tensor_scalar", SL, idxf[:], g.CAP - 1, None, ALU.bitwise_and)
        V("tensor_scalar", BA, E_, stbits, None, ALU.logical_shift_left)
        V("tensor_tensor", BA, BA, SL, ALU.add)
        for s in range(g.NST):
            dst = idxfs[:, s * NTI:(s + 1) * NTI]
            V("tensor_scalar", M0, SL, s * g.STILE, None, ALU.is_ge)
            V("tensor_scalar", M1, SL, (s + 1) * g.STILE, None, ALU.is_lt)
            V("tensor_tensor", M0, M0, M1, ALU.mult)
            V("tensor_scalar", TQ, BA, s * g.STILE, None, ALU.subtract)
            V("tensor_tensor", TQ, TQ, M0, ALU.mult)
            V("tensor_scalar", M0, M0, -1000000, None, ALU.mult)
            V("tensor_scalar", M0, M0, 1000000, None, ALU.add)
            V("tensor_tensor", dst, TQ, M0, ALU.add)
    ve_fld = ctx.ve_inc()
    ctx.dbg_dump("idxs_d", idxf1s[:, 0:g.NST * (TL // 128)], [128, g.NST * (TL // 128)], I32)
    ctx.dbg_dump("idxs2_d", idxf2s[:, 0:g.NST * (TL // 128)], [128, g.NST * (TL // 128)], I32)
    ctx.dbg_dump("idxb_d", idxf1[:], [128, TL // 128], I32)
    ctx.phase_barrier()
    ctx.areset(am6)

    gbbc = ctx.aalloc(C)
    bank = 6
    ctx.bank_wait(bank)
    T_("transpose", ctx.ps[bank][0:8, 0:128], dct("gb"), ident[:])
    mm_gb = ctx.mm_inc()
    ctx.wait("v", "MM", mm_gb)
    V("tensor_copy", S1[0:8, 0:128], ctx.ps[bank][0:8, 0:128])
    ve_gb1 = ctx.ve_inc()
    ctx.bank_set(bank, "VE", ve_gb1)
    ctx.wait("g", "VE", ve_gb1)
    sp_gb = ctx.sp(gbrow_d.rearrange("o (j p) -> (o j) p", p=128), S1[0:8, 0:128])
    sq_gb = ctx.sq()
    ctx.wait("y", "SQ", sq_gb)
    gbrow = stats
    ld_gb = ctx.ld(gbrow[0:1, 0:C], gbrow_d)
    ld_gb = ctx.lq()
    for h in range(2 if C > 512 else 1):
        w_ = min(512, C)
        bank = 6 + h
        ctx.bank_wait(bank)
        ctx.wait("t", "LQ", ld_gb)
        T_("matmul", ctx.ps[bank][:, 0:w_], onesrow[0:1, :], gbrow[0:1, h * 512:h * 512 + w_],
           start=True, stop=True)
        mm_gb2 = ctx.mm_inc()
        ctx.wait("v", "MM", mm_gb2)
        V("tensor_copy", gbbc[:, h * 512:h * 512 + w_], ctx.ps[bank][:, 0:w_])
        ctx.bank_set(bank, "VE", ctx.ve_inc())
    ve_gbbc = ctx.VE.n


    am7 = ctx.amark()
    GRP = 4
    NGRPF = 2
    HROT = ctx.aalloc(GRP * NGRPF * 1024)
    W1B = ctx.aalloc(NJ * g.I)
    ld_w1 = 0
    for j in range(NJ):
        ld_w1 = ctx.ld(W1B[:, j * g.I:(j + 1) * g.I], ap["w1T"][j * 128:(j + 1) * 128, :])
    ld_w1 = ctx.lq()
    sp_z = 0
    for r in range(0, g.CAP, 128):
        sp_z = ctx.sp(xbuf[r:r + 128, :], zz[:])
    bc = nc.gpsimd.alloc_register("bcreg")
    nc.gpsimd.reg_mov(bc, g.CAP - 1)
    ctx.wait("g", "SP", sp_z)
    ctx.wait("g", "VE", ve_idx)
    ctx.wait("y", "CC", ctx.CC.n)
    sqg = {}
    lqg = {}
    NG = T // 128 // GRP
    for grp in range(NG):
        if grp >= NGRPF:
            ctx.wait("y", "SQ", sqg[grp - NGRPF])
        for t in range(GRP):
            i = grp * GRP + t
            slot = i % (GRP * NGRPF)
            ctx.ld(HROT[:, slot * 1024:(slot + 1) * 1024],
                   ag2_out[i * 128:(i + 1) * 128, :])
        lqg[grp] = ctx.lq()
        ctx.wait("g", "LQ", lqg[grp])
        for t in range(GRP):
            i = grp * GRP + t
            slot = i % (GRP * NGRPF)
            nc.gpsimd.indirect_dma_start(
                xbuf, IndirectOffsetOnAxis(idx32[:, i:i + 1], 0),
                HROT[:, slot * 1024:(slot + 1) * 1024], None,
                bounds_check=bc, oob_is_err=False,
            ).then_inc(ctx.SP.h, 16)
            ctx.SP.n += 16
        sqg[grp] = ctx.sq()
    sp_scat_all = ctx.SQ.n

    W2B = ctx.aalloc(2 * 1024)
    XTOK = ctx.aalloc(2 * 1024)
    XFM = ctx.aalloc(NJ * g.STILE)
    INTB = ctx.aalloc(g.NIO * g.STILE)
    sp_e = 0
    sq_e = 0
    for stile in range(g.NST):
        sbase = stile * g.STILE
        ctx.wait("y", "SQ", sp_scat_all)
        lq_xt = {}
        mm_xt = {}
        for ss in range(min(2, g.NSS)):
            ctx.ld(XTOK[:, ss * 1024:(ss + 1) * 1024],
                   xbuf[sbase + ss * 128:sbase + (ss + 1) * 128, :])
            lq_xt[ss] = ctx.lq()
        for ss in range(g.NSS):
            b0 = 4 + (ss % 2) * 2
            ctx.bank_wait(b0)
            ctx.bank_wait(b0 + 1)
            ctx.wait("t", "LQ", lq_xt[ss])
            if stile > 0 and ss == 0:
                ctx.wait("t", "MM", mm_w1_prev)
            for j in range(NJ):
                bank = b0 + j // 4
                T_("transpose", ctx.ps[bank][:, (j % 4) * 128:(j % 4 + 1) * 128],
                   XTOK[:, (ss % 2) * 1024 + j * 128:(ss % 2) * 1024 + (j + 1) * 128],
                   ident[:])
            mm_xt[ss] = ctx.mm_inc()
            if ss + 2 < g.NSS:
                ctx.wait("y", "MM", mm_xt[ss])
                ctx.ld(XTOK[:, (ss % 2) * 1024:(ss % 2) * 1024 + 1024],
                       xbuf[sbase + (ss + 2) * 128:sbase + (ss + 3) * 128, :])
                lq_xt[ss + 2] = ctx.lq()
            ctx.wait("v", "MM", mm_xt[ss])
            for j in range(NJ):
                bank = b0 + j // 4
                V("tensor_copy", XFM[:, j * g.STILE + ss * 128:j * g.STILE + (ss + 1) * 128],
                  ctx.ps[bank][:, (j % 4) * 128:(j % 4 + 1) * 128])
            ve_x = ctx.ve_inc()
            ctx.bank_set(b0, "VE", ve_x)
            ctx.bank_set(b0 + 1, "VE", ve_x)
        ve_xfm = ctx.VE.n
        if stile > 0:
            ctx.wait("s", "MM", mm_e_prev)
        for io in range(g.NIO):
            bank = io % 2
            ctx.bank_wait(bank)
            ctx.wait("t", "LQ", ld_w1)
            ctx.wait("t", "VE", ve_xfm)
            for cin in range(NJ):
                T_("matmul", ctx.ps[bank][:, 0:g.STILE],
                   W1B[:, cin * g.I + io * 128:cin * g.I + (io + 1) * 128],
                   XFM[:, cin * g.STILE:(cin + 1) * g.STILE],
                   start=(cin == 0), stop=(cin == NJ - 1))
            mm1 = ctx.mm_inc()
            ctx.wait("s", "MM", mm1)
            S("activation", INTB[:, io * g.STILE:(io + 1) * g.STILE],
              ctx.ps[bank][:, 0:g.STILE], AF.Relu)
            ctx.bank_set(bank, "SE", ctx.se_inc())
        se_int = ctx.SE.n
        mm_w1_prev = ctx.MM.n
        nh = 2 if C > 512 else 1
        ld_w2 = {}
        mm_e = {}
        for io in range(min(2, g.NIO)):
            ld_w2[io] = ctx.ld(W2B[:, (io % 2) * 1024:(io % 2) * 1024 + 1024],
                               ap["w2T"][io * 128:(io + 1) * 128, :])
            ld_w2[io] = ctx.lq()
        for io in range(g.NIO):
            ctx.wait("t", "LQ", ld_w2[io])
            ctx.wait("t", "SE", se_int)
            if io == 0:
                for b in range(g.NSS * nh):
                    ctx.bank_wait(b)
            for ss in range(g.NSS):
                for h in range(nh):
                    bank = ss * nh + h
                    T_("matmul", ctx.ps[bank][:, 0:512],
                       INTB[:, io * g.STILE + ss * 128:io * g.STILE + (ss + 1) * 128],
                       W2B[:, (io % 2) * 1024 + h * 512:(io % 2) * 1024 + (h + 1) * 512],
                       start=(io == 0), stop=(io == g.NIO - 1))
            mm_e[io] = ctx.mm_inc()
            if io + 2 < g.NIO:
                ctx.wait("y", "MM", mm_e[io])
                ld_w2[io + 2] = ctx.ld(W2B[:, (io % 2) * 1024:(io % 2) * 1024 + 1024],
                                       ap["w2T"][(io + 2) * 128:(io + 3) * 128, :])
                ld_w2[io + 2] = ctx.lq()
        mm_e_prev = mm_e[g.NIO - 1]
        ctx.wait("v", "MM", mm_e_prev)
        sq_ss = {}
        for ss in range(g.NSS):
            pair = (S1, S2) if ss % 2 == 0 else (S3, S4)
            if ss >= 2:
                ctx.wait("v", "SQ", sq_ss[ss - 2])
            elif stile > 0:
                ctx.wait("v", "SQ", sq_e)
            for h in range(nh):
                bank = ss * nh + h
                V("tensor_copy", pair[h][:, 0:512], ctx.ps[bank][:, 0:512])
                ctx.bank_set(bank, "VE", ctx.VE.n + 1)
            ve_ss = ctx.ve_inc()
            ctx.wait("g", "VE", ve_ss)
            for h in range(nh):
                sp_e = ctx.sp(ag3_in[sbase + ss * 128:sbase + (ss + 1) * 128,
                                     h * 512:h * 512 + 512], pair[h][:, 0:512])
                if "eout_dbg" in ctx.dbg_set:
                    sp_e = ctx.sp(eout_dbg[sbase + ss * 128:sbase + (ss + 1) * 128,
                                           h * 512:h * 512 + 512], pair[h][:, 0:512])
            sq_ss[ss] = ctx.sq()
        sq_e = ctx.SQ.n
        ctx.allgather(ag3_in[sbase:sbase + g.STILE, :], ag3c_out[stile])
    cc_ag3 = ctx.CC.n
    ctx.phase_barrier()
    ctx.areset(am7)

    am8 = ctx.amark()
    R1S = ctx.aalloc(2 * g.NST * 1024)
    R2S = ctx.aalloc(2 * g.NST * 1024)
    R1 = ctx.aalloc(2 * 1024)
    R2 = ctx.aalloc(2 * 1024)
    XMB = ctx.aalloc(2 * 1024)
    YB = ctx.aalloc(2 * 1024)
    sp_y = 0
    ve_y = {}
    sp_y_hist = {}
    bc3 = nc.gpsimd.alloc_register("bc3reg")
    nc.gpsimd.reg_mov(bc3, 8 * g.STILE - 1)
    for i in range(TL // 128):
        par = i % 2
        for s in range(g.NST):
            V("memset", R1S[:, (par * g.NST + s) * 1024:(par * g.NST + s + 1) * 1024], 0.0)
            V("memset", R2S[:, (par * g.NST + s) * 1024:(par * g.NST + s + 1) * 1024], 0.0)
        ve_z = ctx.ve_inc()
        ctx.wait("g", "CC", cc_ag3)
        ctx.wait("g", "VE", ve_fld)
        ctx.wait("g", "VE", ve_z)
        for s in range(g.NST):
            nc.gpsimd.indirect_dma_start(
                R1S[:, (par * g.NST + s) * 1024:(par * g.NST + s + 1) * 1024], None,
                ag3c_out[s], IndirectOffsetOnAxis(idxf1s[:, s * (TL // 128) + i:
                                                         s * (TL // 128) + i + 1], 0),
                bounds_check=bc3, oob_is_err=False,
            ).then_inc(ctx.SP.h, 16)
            ctx.SP.n += 16
            nc.gpsimd.indirect_dma_start(
                R2S[:, (par * g.NST + s) * 1024:(par * g.NST + s + 1) * 1024], None,
                ag3c_out[s], IndirectOffsetOnAxis(idxf2s[:, s * (TL // 128) + i:
                                                         s * (TL // 128) + i + 1], 0),
                bounds_check=bc3, oob_is_err=False,
            ).then_inc(ctx.SP.h, 16)
            ctx.SP.n += 16
        sq_g = ctx.sq()
        if i >= 2:
            ctx.wait("y", "VE", ve_y[i - 2])
        ld_xm = ctx.ld(XMB[:, par * 1024:(par + 1) * 1024],
                       xmid_sp[i * 128:(i + 1) * 128, :])
        ld_xm = ctx.lq()
        ctx.wait("v", "SQ", sq_g)
        ctx.wait("v", "LQ", ld_xm)
        if i >= 2:
            ctx.wait("v", "SQ", sp_y_hist[i - 2])
        for RS, RD_ in ((R1S, R1), (R2S, R2)):
            b0 = par * g.NST
            if g.NST == 1:
                V("tensor_copy", RD_[:, par * 1024:(par + 1) * 1024],
                  RS[:, b0 * 1024:(b0 + 1) * 1024])
            else:
                V("tensor_add", RD_[:, par * 1024:(par + 1) * 1024],
                  RS[:, b0 * 1024:(b0 + 1) * 1024],
                  RS[:, (b0 + 1) * 1024:(b0 + 2) * 1024])
                for s in range(2, g.NST):
                    V("tensor_add", RD_[:, par * 1024:(par + 1) * 1024],
                      RD_[:, par * 1024:(par + 1) * 1024],
                      RS[:, (b0 + s) * 1024:(b0 + s + 1) * 1024])
        rb = routebuf[:, i * 18:(i + 1) * 18]
        V("tensor_mul", SB[:, 0:1], rb[:, 16:17], fld32[:, i:i + 1])
        V("tensor_mul", SB[:, 1:2], rb[:, 17:18], fld32[:, 32 + i:33 + i])
        V("tensor_single_scalar", YB[:, par * 1024:(par + 1) * 1024],
          R1[:, par * 1024:(par + 1) * 1024], SB[:, 0:1], ALU.mult)
        V("scalar_tensor_tensor", YB[:, par * 1024:(par + 1) * 1024],
          R2[:, par * 1024:(par + 1) * 1024], SB[:, 1:2],
          YB[:, par * 1024:(par + 1) * 1024], ALU.mult, ALU.add)
        ctx.wait("v", "VE", ve_gbbc)
        V("tensor_mul", YB[:, par * 1024:(par + 1) * 1024],
          YB[:, par * 1024:(par + 1) * 1024], gbbc[:])
        V("tensor_add", YB[:, par * 1024:(par + 1) * 1024],
          YB[:, par * 1024:(par + 1) * 1024], XMB[:, par * 1024:(par + 1) * 1024])
        ve_y[i] = ctx.ve_inc()
        ctx.wait("g", "VE", ve_y[i])
        sp_y = ctx.sp(y_out[i * 128:(i + 1) * 128, :], YB[:, par * 1024:(par + 1) * 1024])
        sp_y_hist[i] = ctx.sq()
    ctx.wait("g", "SP", sp_y)
    ctx.phase_barrier()
    ctx.areset(am8)
    return nc, ctx



def _vec_tile(v):
    return np.ascontiguousarray(np.asarray(v, np.float32).reshape(8, 128).T)


def make_inmaps(g: Geo, inputs):
    x = np.asarray(inputs["L0_attn_x"], np.float32).reshape(g.T, g.C)
    vecs = np.stack([_vec_tile(inputs[k]) for k in (
        "L0_attn_ln1_w", "L0_attn_ln1_b", "L0_attn_ln2_w", "L0_attn_ln2_b",
        "L0_attn_time_decay", "L0_attn_time_first", "L0_attn_tmk", "L0_attn_tmv",
        "L0_attn_tmr", "L0_attn_f_tmk", "L0_attn_f_tmr")], 0)
    shared = {
        "kwT": np.ascontiguousarray(np.asarray(inputs["L0_attn_att_kw"], np.float32).T),
        "vwT": np.ascontiguousarray(np.asarray(inputs["L0_attn_att_vw"], np.float32).T),
        "rwT": np.ascontiguousarray(np.asarray(inputs["L0_attn_att_rw"], np.float32).T),
        "owT": np.ascontiguousarray(np.asarray(inputs["L0_attn_att_ow"], np.float32).T),
        "frwT": np.ascontiguousarray(np.asarray(inputs["L0_attn_f_rw"], np.float32).T),
        "fkwT": np.ascontiguousarray(np.asarray(inputs["L0_attn_f_kw"], np.float32).T),
        "fvwT": np.ascontiguousarray(np.asarray(inputs["L0_attn_f_vw"], np.float32).T),
        "embT": np.ascontiguousarray(np.asarray(inputs["L0_ffn_expert_embeddings"], np.float32).T),
        "wmod": np.ascontiguousarray(np.asarray(inputs["L0_wmod"], np.float32)),
        "cmod": _vec_tile(np.asarray(inputs["L0_c"], np.float32).reshape(g.C)),
        "vecs": vecs,
        "ident": np.eye(128, dtype=np.float32),
        "iotarev": np.ascontiguousarray(np.broadcast_to(
            np.arange(g.TC - 1, -1, -1, dtype=np.float32), (128, g.TC))),
        "oh8id": np.eye(8, dtype=np.float32),
        "ecap": (np.arange(8, dtype=np.float32) * g.CAP - 1.0).reshape(8, 1),
    }
    w1 = np.asarray(inputs["L0_ffn_first_linear"], np.float32)
    w2 = np.asarray(inputs["L0_ffn_second_linear"], np.float32)
    maps = []
    for i in range(g.NCORES):
        seg = x[i * g.TL:(i + 1) * g.TL]
        bnd = np.zeros((1, g.C), np.float32) if i == 0 else x[i * g.TL - 1:i * g.TL]
        xT = np.ascontiguousarray(np.concatenate([bnd, seg], 0).T)
        masklt = (np.arange(8) < i).astype(np.float32)
        m = dict(shared)
        m.update({
            "xT": xT,
            "w1T": np.ascontiguousarray(w1[i].T),
            "w2T": np.ascontiguousarray(w2[i].T),
            "ohmy": np.eye(8, dtype=np.float32)[:, i:i + 1].copy(),
            "ohs": (np.eye(8, dtype=np.float32)[:, i:i + 1] *
                    (-(i * g.CAP + g.BIG))).astype(np.float32),
            "masklt8": np.tile(masklt[None, :], (8, 1)).astype(np.float32),
            "mneg8": ((1.0 - np.tile(masklt[None, :], (8, 1))) * NEG).astype(np.float32),
            "tlm8": (np.tile(masklt[None, :], (8, 1)) * g.TL).astype(np.float32),
            "mprev8": np.tile((np.arange(8) == i - 1).astype(np.float32)[None, :], (8, 1)).copy(),
            "ohown": np.tile((np.arange(8) == i).astype(np.float32)[None, :], (8, 1)).copy(),
            "maskfirst": np.full((8, 1), 0.0 if i == 0 else 1.0, np.float32),
            "maskfirstP": np.full((128, 1), 0.0 if i == 0 else 1.0, np.float32),
            "xbnd_in": _vec_tile(bnd[0]),
        })
        maps.append(m)
    return maps


def kernel(**inputs):
    g = Geo(16384, 512)
    nc, ctx = build_program(g)
    maps = make_inmaps(g, inputs)
    res = run_bass_kernel_spmd(nc, maps, core_ids=list(range(g.NCORES)))
    y = np.concatenate([res.results[i]["y"] for i in range(g.NCORES)], 0)
    return y.reshape(1, g.T, g.C).astype(np.float32)


# version 09
# speedup vs baseline: 1.0569x; 1.0098x over previous
import numpy as np
import concourse.bass as bass
import concourse.mybir as mybir
from concourse.bass_core_rust import IndirectOffsetOnAxis
from concourse.bass_types import AP as BassAP
from concourse.bass_utils import run_bass_kernel_spmd


def mkap(apref, offset, dims):
    return BassAP(apref.tensor, offset, [list(d) for d in dims])

F32 = mybir.dt.float32
I16 = mybir.dt.int16
I32 = mybir.dt.int32
AF = mybir.ActivationFunctionType
ALU = mybir.AluOpType
AX = mybir.AxisListType.X

LN_EPS = 1e-5
NEG = -1e38


class Geo:
    def __init__(self, T=16384, TC=512):
        self.T = T
        self.NCORES = 8
        self.TL = T // 8
        assert TC >= 128 and TC % 128 == 0 and TC <= 512
        self.TC = TC
        self.NCHK = self.TL // TC
        self.C = 1024
        self.NJ = 8
        self.F = 4096
        self.FS = 512
        self.NFS = self.F // self.FS
        self.I = 2048
        self.NIO = self.I // 128
        self.E = 8
        self.CAP = T // 8
        self.BIG = 20000
        self.STILE = min(512, self.CAP)
        self.NST = self.CAP // self.STILE
        self.NSS = self.STILE // 128


class Sem:
    def __init__(self, nc, name):
        self.h = nc.semaphore(name).__enter__()
        self.n = 0


class Ctx:
    def __init__(self, nc, g):
        self.nc = nc
        self.g = g
        self.LD = Sem(nc, "sLD")
        self.SP = Sem(nc, "sSP")
        self.MM = Sem(nc, "sMM")
        self.VE = Sem(nc, "sVE")
        self.SE = Sem(nc, "sSE")
        self.CC = Sem(nc, "sCC")
        self.LQ = Sem(nc, "sLQ")
        self.SQ = Sem(nc, "sSQ")
        self.sem = {"LD": self.LD, "SP": self.SP, "MM": self.MM,
                    "VE": self.VE, "SE": self.SE, "CC": self.CC,
                    "LQ": self.LQ, "SQ": self.SQ}
        self.last = {}
        self.ACOLS = 43008
        self.arena = nc.alloc_sbuf_tensor("arena", [128, self.ACOLS], F32)
        self.arena16 = self.arena.bitcast(mybir.dt.bfloat16)
        self.aoff = 0
        self.amax = 0
        self.ps = [nc.alloc_psum_tensor(f"psb{i}", [128, 512], F32) for i in range(8)]
        self.bank_free = [("VE", 0) for _ in range(8)]
        self.dbg_set = set()

    def amark(self):
        return self.aoff

    def areset(self, m):
        self.aoff = m

    def aalloc(self, cols):
        assert self.aoff + cols <= self.ACOLS, f"arena overflow {self.aoff}+{cols}"
        t = self.arena[:, self.aoff:self.aoff + cols]
        self.aoff += cols
        self.amax = max(self.amax, self.aoff)
        return t

    def aalloc16(self, cols16):
        f32c = (cols16 + 1) // 2
        assert self.aoff + f32c <= self.ACOLS, f"arena overflow {self.aoff}+{f32c}"
        t = self.arena16[:, 2 * self.aoff:2 * self.aoff + cols16]
        self.aoff += f32c
        self.amax = max(self.amax, self.aoff)
        return t

    def dram(self, name, shape, dtype=F32):
        kind = "ExternalOutput" if name in self.dbg_set else None
        if kind:
            return self.nc.dram_tensor(name, shape, dtype, kind=kind)
        return self.nc.dram_tensor(name, shape, dtype)

    def _tr(self, eng, inst):
        self.last[eng] = inst
        return inst

    def t(self, op, *a, **k):
        return self._tr("t", getattr(self.nc.tensor, op)(*a, **k))

    DRAINS = True

    def v(self, op, *a, **k):
        inst = getattr(self.nc.vector, op)(*a, **k)
        if self.DRAINS:
            inst = self.nc.vector.drain()
        return self._tr("v", inst)

    def s(self, op, *a, **k):
        inst = getattr(self.nc.scalar, op)(*a, **k)
        if self.DRAINS:
            inst = self.nc.scalar.drain()
        return self._tr("s", inst)

    def mm_inc(self):
        self.last["t"].then_inc(self.MM.h, 1)
        self.MM.n += 1
        return self.MM.n

    def ve_inc(self):
        self.last["v"].then_inc(self.VE.h, 1)
        self.VE.n += 1
        return self.VE.n

    def se_inc(self):
        self.last["s"].then_inc(self.SE.h, 1)
        self.SE.n += 1
        return self.SE.n

    def ld(self, out, in_, transpose=False, slow=False):
        kw = {}
        if transpose:
            kw["transpose"] = True
        if slow:
            kw["allow_slow_non_contiguous"] = True
        self.nc.sync.dma_start(out=out, in_=in_, **kw).then_inc(self.LD.h, 16)
        self.LD.n += 16
        return self.LD.n

    def sp(self, out, in_):
        self.nc.gpsimd.dma_start(out=out, in_=in_).then_inc(self.SP.h, 16)
        self.SP.n += 16
        return self.SP.n

    def lq(self):
        self.nc.sync.wait_ge(self.LD.h, self.LD.n)
        self.nc.sync.sem_inc(self.LQ.h, 1)
        self.LQ.n += 1
        return self.LQ.n

    def sq(self):
        self.nc.gpsimd.wait_ge(self.SP.h, self.SP.n)
        self.nc.gpsimd.sem_inc(self.SQ.h, 1)
        self.SQ.n += 1
        return self.SQ.n

    def wait(self, eng, semname, val):
        if val <= 0:
            return
        e = {"t": self.nc.tensor, "v": self.nc.vector, "s": self.nc.scalar,
             "g": self.nc.gpsimd, "y": self.nc.sync}[eng]
        e.wait_ge(self.sem[semname].h, val)

    def bank_wait(self, i):
        s, v = self.bank_free[i]
        self.wait("t", s, v)

    def bank_set(self, i, semname, val):
        self.bank_free[i] = (semname, val)

    def phase_barrier(self):
        self.lq()
        self.sq()
        cur = {"LQ": self.LQ.n, "SQ": self.SQ.n, "MM": self.MM.n,
               "VE": self.VE.n, "SE": self.SE.n}
        for eng in ("y", "g", "v", "s", "t"):
            for k, v in cur.items():
                self.wait(eng, k, v)

    def dbg_dump(self, name, ap_, shape, dtype=F32):
        if name not in self.dbg_set:
            return
        t = self.nc.dram_tensor(name, shape, dtype, kind="ExternalOutput")
        self.wait("g", "VE", self.VE.n)
        self.wait("g", "SE", self.SE.n)
        self.sp(t.ap(), ap_)

    def allgather(self, bin_ap, bout_ap):
        self.nc.gpsimd.collective_compute(
            "AllGather", ALU.bypass,
            replica_groups=[list(range(self.g.NCORES))],
            ins=[bin_ap.opt()], outs=[bout_ap.opt()],
        ).then_inc(self.CC.h, 1)
        self.CC.n += 1
        return self.CC.n


def build_program(g: Geo, dbg=()):
    nc = bass.Bass("TRN2", target_bir_lowering=False, debug=False,
                   detect_race_conditions=False)
    ctx = Ctx(nc, g)
    ctx.dbg_set = set(dbg)
    C, NJ, TC, NCHK, TL, T = g.C, g.NJ, g.TC, g.NCHK, g.TL, g.T
    V = ctx.v
    S = ctx.s
    T_ = ctx.t

    ap = {}
    def inp(name, shape, dtype=F32):
        ap[name] = nc.dram_tensor(name, shape, dtype, kind="ExternalInput").ap()

    inp("xT", [C, TL + 1])
    for nm in ("kwT", "vwT", "rwT", "owT", "frwT"):
        inp(nm, [C, C])
    inp("fkwT", [C, g.F]); inp("fvwT", [g.F, C])
    inp("w1T", [C, g.I]); inp("w2T", [g.I, C])
    inp("embT", [C, g.E]); inp("wmod", [C, 6 * C])
    inp("cmod", [128, NJ]); inp("vecs", [11, 128, NJ])
    inp("ident", [128, 128]); inp("iotarev", [128, TC])
    inp("oh8id", [8, 8]); inp("ohmy", [8, 1]); inp("ohs", [8, 1]); inp("ecap", [8, 1])
    inp("masklt8", [8, 8]); inp("mneg8", [8, 8]); inp("tlm8", [8, 8])
    inp("mprev8", [8, 8]); inp("ohown", [8, 8])
    inp("maskfirst", [8, 1]); inp("maskfirstP", [128, 1])
    inp("xbnd_in", [128, NJ])

    y_out = nc.dram_tensor("y", [TL, C], F32, kind="ExternalOutput").ap()

    kvs_sp = ctx.dram("kvs_sp", [3 * C, TL]).ap()
    h0_sp = ctx.dram("h0_sp", [C, TL]).ap()
    a_sp = ctx.dram("a_sp", [C, TL]).ap()
    xk2_sp = ctx.dram("xk2_sp", [C, TL]).ap()
    xk2h_sp = ctx.dram("xk2h_sp", [C, TL], mybir.dt.bfloat16).ap()
    xk2l_sp = ctx.dram("xk2l_sp", [C, TL], mybir.dt.bfloat16).ap()
    sig_sp = ctx.dram("sig_sp", [C, TL]).ap()
    xmid_sp = ctx.dram("xmid_sp", [TL, C]).ap()
    svrow_d = ctx.dram("svrow_d", [1, T], I16).ap()
    fld_d = ctx.dram("fld_d", [4, 16 * 128], I16).ap()
    m12_d = ctx.dram("m12_d", [16, TL]).ap()
    gbrow_d = ctx.dram("gbrow_d", [1, C]).ap()
    xbuf = ctx.dram("xbuf", [g.CAP, C]).ap()
    if "h1_dbg" in ctx.dbg_set:
        h1_dbg = nc.dram_tensor("h1_dbg", [TL, C], F32, kind="ExternalOutput").ap()
    if "eout_dbg" in ctx.dbg_set:
        eout_dbg = nc.dram_tensor("eout_dbg", [g.CAP, C], F32, kind="ExternalOutput").ap()
    ag1_in = ctx.dram("ag1_in", [24, 128]).ap()
    ag1_out = ctx.dram("ag1_out", [192, 128]).ap()
    ag1b_in = ctx.dram("ag1b_in", [8, 128]).ap()
    ag1b_out = ctx.dram("ag1b_out", [64, 128]).ap()
    ag2_in = ctx.dram("ag2_in", [TL, C]).ap()
    ag2_out = ctx.dram("ag2_out", [T, C]).ap()
    ag2b_in = ctx.dram("ag2b_in", [8, TL]).ap()
    ag2b_out = ctx.dram("ag2b_out", [64, TL]).ap()
    ag3_in = ctx.dram("ag3_in", [g.CAP, C]).ap()
    ag3c_out = [ctx.dram(f"ag3c_out{s}", [8 * g.STILE, C]).ap() for s in range(g.NST)]

    A = nc.alloc_sbuf_tensor
    vconst = A("vconst", [128, 11 * NJ], F32)
    dconst = A("dconst", [128, 8 * NJ], F32)
    ident = A("identt", [128, 128], F32)
    embTs = A("embTs", [128, NJ * g.E], F32)
    oh8id = A("oh8idt", [8, 8], F32)
    ohmy = A("ohmyt", [8, 1], F32)
    ohs = A("ohst", [8, 1], F32)
    ecap = A("ecapt", [8, 1], F32)
    masklt8 = A("masklt8t", [8, 8], F32)
    mneg8 = A("mneg8t", [8, 8], F32)
    tlm8 = A("tlm8t", [8, 8], F32)
    mprev8 = A("mprev8t", [8, 8], F32)
    ohown = A("ohownt", [8, 8], F32)
    maskfirst = A("maskfirstt", [8, 1], F32)
    maskfirstP = A("maskfirstPt", [128, 1], F32)
    onescol = A("onescol", [128, 1], F32)
    onesrow = A("onesrow", [1, 128], F32)
    ones8 = A("ones8", [8, 1], F32)
    state0 = A("state0", [128, 3 * NJ], F32)
    runsum = A("runsum", [128, 3 * NJ], F32)
    csum = A("csum", [128, 3 * NJ], F32)
    u2bnd = A("u2bnd", [128, NJ], F32)
    ubnd = A("ubnd", [128, NJ], F32)
    usave = A("usave", [128, NJ], F32)
    u2save = A("u2save", [128, NJ], F32)
    trow = A("trow", [8, 3 * 128], F32)
    routebuf = A("routebuf", [128, (TL // 128) * 18], F32)
    svtok = A("svtok", [128, T // 128], I16)
    fldtok = A("fldtok", [128, 64], I16)
    fld32 = A("fld32", [128, 64], F32)
    idxf1 = A("idxf1", [128, TL // 128], I32)
    idxf2 = A("idxf2", [128, TL // 128], I32)
    idxf1s = A("idxf1s", [128, 4 * (TL // 128)], I32)
    idxf2s = A("idxf2s", [128, 4 * (TL // 128)], I32)
    idxt = A("idxt", [128, 6 * (TL // 128)], I32)
    idx32 = A("idx32", [128, T // 128], I32)
    stats = A("stats", [1, 2608], F32)
    R16 = A("R16", [1, 1024], I16)
    SW = max(TC, 512)
    S1 = A("S1", [128, SW], F32)
    S2 = A("S2", [128, SW], F32)
    S3 = A("S3", [128, SW], F32)
    S4 = A("S4", [128, SW], F32)
    SQ2 = A("SQ2", [128, SW], F32)
    SB = A("SBs", [128, 32], F32)
    zz = A("zzt", [128, 1024], F32)
    lnscr = A("lnscr", [128, 2 * TC], F32)
    xbnd = A("xbndt", [128, NJ], F32)

    co = {"w": 0, "sa": 1, "ca1": 2, "ga": 3, "sb": 4, "cb1": 5, "gb": 6}
    vi = {"ln1w": 0, "ln1b": 1, "ln2w": 2, "ln2b": 3, "tdec": 4, "tfir": 5,
          "tmk": 6, "tmv": 7, "tmr": 8, "ftmk": 9, "ftmr": 10}

    def vc(name, j):
        return vconst[:, vi[name] * NJ + j: vi[name] * NJ + j + 1]

    def vct(name):
        return vconst[:, vi[name] * NJ:(vi[name] + 1) * NJ]

    def dc(name, j):
        return dconst[:, co[name] * NJ + j: co[name] * NJ + j + 1]

    def dct(name):
        return dconst[:, co[name] * NJ:(co[name] + 1) * NJ]

    def a_(buf, j, w=TC):
        return buf[:, j * w:(j + 1) * w]

    def a1_(buf, j):
        return buf[:, j * (TC + 1):(j + 1) * (TC + 1)]

    ctx.ld(vconst[:], mkap(ap["vecs"], 0, [[8, 128], [1024, 11], [1, NJ]]))
    ctx.ld(ident[:], ap["ident"])
    ctx.ld(embTs[:], mkap(ap["embT"], 0, [[g.E, 128], [128 * g.E, NJ], [1, g.E]]))
    for nm, tt in (("oh8id", oh8id), ("ohmy", ohmy), ("ohs", ohs), ("ecap", ecap),
                   ("masklt8", masklt8), ("mneg8", mneg8), ("tlm8", tlm8),
                   ("mprev8", mprev8), ("ohown", ohown), ("maskfirst", maskfirst),
                   ("maskfirstP", maskfirstP)):
        ctx.ld(tt[:], ap[nm])
    ctx.ld(SB[:, 0:NJ], ap["cmod"])
    ld_xb = ctx.ld(xbnd[:], ap["xbnd_in"])
    ld_const = ctx.lq()

    ctx.wait("v", "LQ", ld_const)
    V("memset", onescol[:], 1.0)
    V("memset", onesrow[:], 1.0)
    V("memset", ones8[:], 1.0)
    V("memset", zz[:], 0.0)
    V("memset", dconst[:], 0.0)
    V("memset", runsum[:, 0:NJ], NEG)
    V("memset", runsum[:, NJ:3 * NJ], 0.0)
    ve0 = ctx.ve_inc()
    ctx.wait("s", "LQ", ld_const)
    ctx.wait("s", "VE", ve0)
    S("activation", dct("w"), vct("tdec"), AF.Exp)
    S("activation", SB[:, NJ:2 * NJ], SB[:, 0:NJ], AF.Silu)
    se_silu = ctx.se_inc()
    ctx.wait("v", "SE", se_silu)
    V("tensor_scalar_mul", dct("w"), dct("w"), -1.0)
    ve_w = ctx.ve_inc()

    am0 = ctx.amark()
    modsb = ctx.aalloc(6 * C)
    wmbuf = ctx.aalloc(2 * NJ * 512)
    mm_prev = {}
    for o in range(12):
        ld_wm = 0
        for j in range(NJ):
            if o >= 2:
                ctx.wait("y", "MM", mm_prev[o - 2])
            ld_wm = ctx.ld(wmbuf[:, (o % 2) * NJ * 512 + j * 512:(o % 2) * NJ * 512 + (j + 1) * 512],
                           ap["wmod"][j * 128:(j + 1) * 128, o * 512:(o + 1) * 512])
        ld_wm = ctx.lq()
        bank = o % 2
        ctx.bank_wait(bank)
        ctx.wait("t", "LQ", ld_wm)
        ctx.wait("t", "SE", se_silu)
        for j in range(NJ):
            T_("matmul", ctx.ps[bank][0:1, 0:512], SB[:, NJ + j:NJ + j + 1],
               wmbuf[:, (o % 2) * NJ * 512 + j * 512:(o % 2) * NJ * 512 + (j + 1) * 512],
               start=(j == 0), stop=(j == NJ - 1))
        mm_prev[o] = ctx.mm_inc()
        ctx.wait("v", "MM", mm_prev[o])
        V("tensor_copy", modsb[0:1, o * 512:(o + 1) * 512], ctx.ps[bank][0:1, 0:512])
        ctx.bank_set(bank, "VE", ctx.ve_inc())

    bank = 2
    ctx.bank_wait(bank)
    ctx.wait("t", "VE", ctx.VE.n)
    for v6 in range(6):
        for j in range(NJ):
            T_("matmul", ctx.ps[bank][:, v6 * NJ + j:v6 * NJ + j + 1],
               modsb[0:1, v6 * C + j * 128:v6 * C + (j + 1) * 128],
               onesrow[0:1, 0:1], start=True, stop=True)
    mm_modT = ctx.mm_inc()
    ctx.wait("v", "MM", mm_modT)
    V("tensor_copy", dconst[:, NJ:7 * NJ], ctx.ps[bank][:, 0:48])
    V("tensor_scalar_add", dct("ca1"), dct("ca1"), 1.0)
    V("tensor_scalar_add", dct("cb1"), dct("cb1"), 1.0)
    for nm in ("ga", "gb"):
        V("tensor_single_scalar", SB[:, 16:16 + NJ], dct(nm), 0.0, ALU.is_ge)
        V("tensor_scalar", SB[:, 16:16 + NJ], SB[:, 16:16 + NJ], 2.0, 1.0, ALU.mult, ALU.subtract)
        V("tensor_mul", SB[:, 24:24 + NJ], dct(nm), SB[:, 16:16 + NJ])
        V("tensor_scalar_max", SB[:, 24:24 + NJ], SB[:, 24:24 + NJ], 0.25)
        V("tensor_mul", dct(nm), SB[:, 16:16 + NJ], SB[:, 24:24 + NJ])
    ve_dconst = ctx.ve_inc()
    ctx.bank_set(bank, "VE", ve_dconst)

    ctx.dbg_dump("dconst_d", dconst[:], [128, 8 * NJ])
    ctx.dbg_dump("vconst_d", vconst[:], [128, 11 * NJ])
    bank = 3
    ctx.bank_wait(bank)
    ctx.wait("t", "VE", ve_dconst)
    T_("transpose", ctx.ps[bank][0:8, 0:128], dct("w"), ident[:])
    T_("transpose", ctx.ps[bank][0:8, 128:256], vct("ln2w"), ident[:])
    T_("transpose", ctx.ps[bank][0:8, 256:384], vct("ln2b"), ident[:])
    mm_trow = ctx.mm_inc()
    ctx.wait("v", "MM", mm_trow)
    V("tensor_copy", trow[:], ctx.ps[bank][0:8, 0:384])
    ctx.bank_set(bank, "VE", ctx.ve_inc())

    def mini_ln(src, dst):
        bank = 4
        ctx.bank_wait(bank)
        ctx.wait("t", "VE", ctx.VE.n)
        ctx.wait("t", "SE", ctx.SE.n)
        ctx.wait("t", "LQ", ctx.LQ.n)
        T_("matmul", ctx.ps[bank][0:1, 0:NJ], onescol[:, 0:1], src, start=True, stop=True)
        ctx.mm_inc()
        ctx.wait("s", "VE", ctx.VE.n)
        ctx.wait("s", "LQ", ctx.LQ.n)
        ctx.wait("s", "MM", ctx.MM.n)
        S("activation", SB[:, 24:24 + NJ], src, AF.Square)
        se1 = ctx.se_inc()
        ctx.wait("t", "SE", se1)
        T_("matmul", ctx.ps[bank][0:1, NJ:2 * NJ], onescol[:, 0:1], SB[:, 24:24 + NJ],
           start=True, stop=True)
        mm2 = ctx.mm_inc()
        ctx.wait("v", "MM", mm2)
        st = stats
        V("reduce_sum", st[0:1, 0:1], ctx.ps[bank][0:1, 0:NJ], axis=AX)
        V("reduce_sum", st[0:1, 1:2], ctx.ps[bank][0:1, NJ:2 * NJ], axis=AX)
        V("tensor_scalar_mul", st[0:1, 0:2], st[0:1, 0:2], 1.0 / C)
        V("tensor_mul", st[0:1, 2:3], st[0:1, 0:1], st[0:1, 0:1])
        V("tensor_sub", st[0:1, 1:2], st[0:1, 1:2], st[0:1, 2:3])
        V("tensor_scalar_add", st[0:1, 1:2], st[0:1, 1:2], LN_EPS)
        ve1 = ctx.ve_inc()
        ctx.wait("s", "VE", ve1)
        S("activation", st[0:1, 3:4], st[0:1, 1:2], AF.Ln)
        se2 = ctx.se_inc()
        ctx.wait("v", "SE", se2)
        V("tensor_scalar_mul", st[0:1, 3:4], st[0:1, 3:4], -0.5)
        ve1b = ctx.ve_inc()
        ctx.wait("s", "VE", ve1b)
        S("activation", st[0:1, 4:5], st[0:1, 3:4], AF.Exp)
        se2b = ctx.se_inc()
        ctx.wait("v", "SE", se2b)
        V("tensor_copy", st[0:1, 4:5], st[0:1, 4:5])
        ve2 = ctx.ve_inc()
        ctx.wait("t", "VE", ve2)
        T_("matmul", ctx.ps[bank][:, 16:17], onesrow[0:1, :], st[0:1, 4:5], start=True, stop=True)
        T_("matmul", ctx.ps[bank][:, 17:18], onesrow[0:1, :], st[0:1, 0:1], start=True, stop=True)
        mm3 = ctx.mm_inc()
        ctx.wait("v", "MM", mm3)
        V("tensor_single_scalar", dst, src, ctx.ps[bank][:, 17:18], ALU.subtract)
        V("tensor_single_scalar", dst, dst, ctx.ps[bank][:, 16:17], ALU.mult)
        ctx.bank_set(bank, "VE", ctx.ve_inc())

    ctx.dbg_dump("xbnd_d", xbnd[:], [128, NJ])
    mini_ln(xbnd[:], SB[:, 0:NJ])
    ctx.dbg_dump("lnb_d", SB[:, 0:NJ], [128, NJ])
    ctx.dbg_dump("stats1_d", stats[0:1, 0:8], [1, 8])
    V("tensor_mul", SB[:, 0:NJ], SB[:, 0:NJ], dct("ca1"))
    V("tensor_add", SB[:, 0:NJ], SB[:, 0:NJ], dct("sa"))
    ctx.ve_inc()
    ctx.dbg_dump("h0b_d", SB[:, 0:NJ], [128, NJ])
    mini_ln(SB[:, 0:NJ], ubnd[:])
    V("tensor_mul", ubnd[:], ubnd[:], vct("ln1w"))
    V("tensor_add", ubnd[:], ubnd[:], vct("ln1b"))
    V("tensor_single_scalar", ubnd[:], ubnd[:], maskfirstP[:, 0:1], ALU.mult)
    ve_ubnd = ctx.ve_inc()
    ctx.dbg_dump("ubnd_d", ubnd[:], [128, NJ])
    ctx.phase_barrier()
    ctx.areset(am0)

    def ln_stats(tile_of, width, waits=()):
        bs, bq = 4, 5
        ctx.bank_wait(bs)
        ctx.bank_wait(bq)
        vew = ctx.VE.n
        ctx.wait("t", "VE", vew)
        for eng_sem_val in waits:
            ctx.wait("t", eng_sem_val[0], eng_sem_val[1])
        mmq = {}
        for j in range(NJ):
            T_("matmul", ctx.ps[bs][0:1, 0:width], onescol[:, 0:1], tile_of(j),
               start=(j == 0), stop=(j == NJ - 1))
            ctx.mm_inc()
        ctx.wait("s", "VE", vew)
        for eng_sem_val in waits:
            ctx.wait("s", eng_sem_val[0], eng_sem_val[1])
        for j in range(NJ):
            buf = S4 if (j % 2 == 0) else SQ2
            if j >= 2:
                ctx.wait("s", "MM", mmq[j - 2])
            S("activation", buf[:, 0:width], tile_of(j), AF.Square)
            seq = ctx.se_inc()
            ctx.wait("t", "SE", seq)
            T_("matmul", ctx.ps[bq][0:1, 0:width], onescol[:, 0:1], buf[:, 0:width],
               start=(j == 0), stop=(j == NJ - 1))
            mmq[j] = ctx.mm_inc()
        ctx.wait("v", "MM", mmq[NJ - 1])
        st = stats
        V("tensor_scalar_mul", st[0:1, 0:width], ctx.ps[bs][0:1, 0:width], 1.0 / C)
        V("tensor_scalar_mul", st[0:1, 512:512 + width], ctx.ps[bq][0:1, 0:width], 1.0 / C)
        V("tensor_mul", st[0:1, 1024:1024 + width], st[0:1, 0:width], st[0:1, 0:width])
        V("tensor_sub", st[0:1, 512:512 + width], st[0:1, 512:512 + width],
          st[0:1, 1024:1024 + width])
        V("tensor_scalar_add", st[0:1, 512:512 + width], st[0:1, 512:512 + width], LN_EPS)
        ve1 = ctx.ve_inc()
        ctx.wait("s", "VE", ve1)
        S("activation", st[0:1, 1024:1024 + width], st[0:1, 512:512 + width], AF.Ln)
        se1 = ctx.se_inc()
        ctx.wait("v", "SE", se1)
        V("tensor_scalar_mul", st[0:1, 1024:1024 + width], st[0:1, 1024:1024 + width], -0.5)
        ve1b = ctx.ve_inc()
        ctx.wait("s", "VE", ve1b)
        S("activation", st[0:1, 1536:1536 + width], st[0:1, 1024:1024 + width], AF.Exp)
        se1b = ctx.se_inc()
        ctx.wait("v", "SE", se1b)
        V("tensor_mul", st[0:1, 0:width], st[0:1, 0:width], st[0:1, 1536:1536 + width])
        V("tensor_scalar_mul", st[0:1, 0:width], st[0:1, 0:width], -1.0)
        ve2 = ctx.ve_inc()
        ctx.wait("t", "VE", ve2)
        T_("matmul", ctx.ps[bs][:, 0:width], onesrow[0:1, :], st[0:1, 1536:1536 + width],
           start=True, stop=True)
        T_("matmul", ctx.ps[bq][:, 0:width], onesrow[0:1, :], st[0:1, 0:width],
           start=True, stop=True)
        mmb = ctx.mm_inc()
        ctx.wait("v", "MM", mmb)
        V("tensor_copy", lnscr[:, 0:width], ctx.ps[bs][:, 0:width])
        V("tensor_copy", lnscr[:, TC:TC + width], ctx.ps[bq][:, 0:width])
        vee = ctx.ve_inc()
        ctx.bank_set(bs, "VE", vee)
        ctx.bank_set(bq, "VE", vee)

    am1 = ctx.amark()
    iotarev = ctx.aalloc(TC)
    ctx.ld(iotarev, ap["iotarev"])
    A0 = ctx.aalloc(NJ * (TC + 1))
    A1 = ctx.aalloc(NJ * TC)
    A2 = ctx.aalloc(NJ * (TC + 1))
    A3 = ctx.aalloc(NJ * TC)
    KB = ctx.aalloc(NJ * TC)
    VB = ctx.aalloc(NJ * TC)
    RB = ctx.aalloc(NJ * TC)
    WBUF = ctx.aalloc(NJ * 1024)

    wname = {"k": "kwT", "v": "vwT", "r": "rwT"}
    spill_done = 0
    sq_chunk = 0
    mm_w_last = 0
    ve_mix_done = 0
    for c in range(NCHK):
        base = c * TC
        ld_x = 0
        for j in range(NJ):
            if c > 0:
                ctx.wait("y", "VE", ve_h0_prev)
            ld_x = ctx.ld(a1_(A0, j), ap["xT"][j * 128:(j + 1) * 128, base:base + TC + 1])
        ld_x = ctx.lq()
        ln_stats(lambda j: a1_(A0, j)[:, 1:TC + 1], TC, waits=(("LQ", ld_x),))
        ctx.wait("v", "LQ", ld_x)
        ctx.wait("v", "SQ", sq_chunk)
        for j in range(NJ):
            V("tensor_mul", a_(A1, j), a1_(A0, j)[:, 1:TC + 1], lnscr[:, 0:TC])
            V("tensor_add", a_(A1, j), a_(A1, j), lnscr[:, TC:2 * TC])
            V("tensor_scalar", a_(A1, j), a_(A1, j), dc("ca1", j), dc("sa", j),
              ALU.mult, ALU.add)
        ve_h0 = ctx.ve_inc()
        ve_h0_prev = ve_h0
        for j in range(NJ):
            ctx.wait("g", "VE", ve_h0)
            spill_done = ctx.sp(h0_sp[j * 128:(j + 1) * 128, base:base + TC], a_(A1, j))
        ln_stats(lambda j: a_(A1, j), TC)
        for j in range(NJ):
            V("tensor_mul", a1_(A2, j)[:, 1:TC + 1], a_(A1, j), lnscr[:, 0:TC])
            V("tensor_add", a1_(A2, j)[:, 1:TC + 1], a1_(A2, j)[:, 1:TC + 1],
              lnscr[:, TC:2 * TC])
            V("tensor_scalar", a1_(A2, j)[:, 1:TC + 1], a1_(A2, j)[:, 1:TC + 1],
              vc("ln1w", j), vc("ln1b", j), ALU.mult, ALU.add)
            V("tensor_copy", a1_(A2, j)[:, 0:1],
              ubnd[:, j:j + 1] if c == 0 else usave[:, j:j + 1])
        ctx.ve_inc()
        for j in range(NJ):
            V("tensor_copy", usave[:, j:j + 1], a1_(A2, j)[:, TC:TC + 1])
        ctx.ve_inc()

        for nm in ("k", "v", "r"):
            tm = {"k": "tmk", "v": "tmv", "r": "tmr"}[nm]
            ctx.wait("v", "MM", mm_w_last)
            for j in range(NJ):
                V("tensor_sub", a_(A3, j), a1_(A2, j)[:, 1:TC + 1], a1_(A2, j)[:, 0:TC])
                V("scalar_tensor_tensor", a_(A3, j), a_(A3, j), vc(tm, j),
                  a1_(A2, j)[:, 0:TC], ALU.mult, ALU.add)
            ve_mix = ctx.ve_inc()
            ld_w = 0
            for j in range(NJ):
                ctx.wait("y", "MM", mm_w_last)
                ld_w = ctx.ld(WBUF[:, j * 1024:(j + 1) * 1024],
                              ap[wname[nm]][j * 128:(j + 1) * 128, :])
            ld_w = ctx.lq()
            dstbuf = {"k": KB, "v": VB, "r": RB}[nm]
            for cout in range(NJ):
                bank = cout % 2
                ctx.bank_wait(bank)
                ctx.wait("t", "LQ", ld_w)
                ctx.wait("t", "VE", ve_mix)
                for cin in range(NJ):
                    T_("matmul", ctx.ps[bank][:, 0:TC],
                       WBUF[:, cin * 1024 + cout * 128:cin * 1024 + (cout + 1) * 128],
                       a_(A3, cin), start=(cin == 0), stop=(cin == NJ - 1))
                mmv = ctx.mm_inc()
                if nm == "r":
                    ctx.wait("s", "MM", mmv)
                    S("activation", a_(dstbuf, cout), ctx.ps[bank][:, 0:TC], AF.Sigmoid)
                    ctx.bank_set(bank, "SE", ctx.se_inc())
                else:
                    ctx.wait("v", "MM", mmv)
                    V("tensor_copy", a_(dstbuf, cout), ctx.ps[bank][:, 0:TC])
                    ctx.bank_set(bank, "VE", ctx.ve_inc())
            mm_w_last = ctx.MM.n

        ve_k = ctx.VE.n
        se_r = ctx.SE.n
        for vi2, buf in ((0, KB), (1, VB), (2, RB)):
            for j in range(NJ):
                ctx.wait("g", "VE", ve_k)
                ctx.wait("g", "SE", se_r)
                spill_done = ctx.sp(
                    kvs_sp[vi2 * C + j * 128:vi2 * C + (j + 1) * 128, base:base + TC],
                    a_(buf, j))
        sq_chunk = ctx.sq()

        for j in range(NJ):
            V("tensor_scalar_mul", S1[:, 0:TC], iotarev, dc("w", j))
            V("tensor_add", S1[:, 0:TC], S1[:, 0:TC], a_(KB, j))
            V("reduce_max", csum[:, j:j + 1], S1[:, 0:TC], axis=AX)
            V("tensor_single_scalar", S1[:, 0:TC], S1[:, 0:TC], csum[:, j:j + 1], ALU.subtract)
            ve_e = ctx.ve_inc()
            ctx.wait("s", "VE", ve_e)
            S("activation", S2[:, 0:TC], S1[:, 0:TC], AF.Exp)
            se_e = ctx.se_inc()
            ctx.wait("v", "SE", se_e)
            V("reduce_sum", csum[:, 2 * NJ + j:2 * NJ + j + 1], S2[:, 0:TC], axis=AX)
            V("tensor_mul", S2[:, 0:TC], S2[:, 0:TC], a_(VB, j))
            V("reduce_sum", csum[:, NJ + j:NJ + j + 1], S2[:, 0:TC], axis=AX)
            ctx.ve_inc()
        Mr, Ar, Br = runsum[:, 0:NJ], runsum[:, NJ:2 * NJ], runsum[:, 2 * NJ:3 * NJ]
        Mc_, Ac_, Bc_ = csum[:, 0:NJ], csum[:, NJ:2 * NJ], csum[:, 2 * NJ:3 * NJ]
        V("scalar_tensor_tensor", SB[:, 0:NJ], dct("w"), float(TC), Mr, ALU.mult, ALU.add)
        V("tensor_max", SB[:, 8:8 + NJ], SB[:, 0:NJ], Mc_)
        V("tensor_sub", SB[:, 0:NJ], SB[:, 0:NJ], SB[:, 8:8 + NJ])
        V("tensor_sub", SB[:, 16:16 + NJ], Mc_, SB[:, 8:8 + NJ])
        ve_c1 = ctx.ve_inc()
        ctx.wait("s", "VE", ve_c1)
        S("activation", SB[:, 0:NJ], SB[:, 0:NJ], AF.Exp)
        S("activation", SB[:, 16:16 + NJ], SB[:, 16:16 + NJ], AF.Exp)
        se_c = ctx.se_inc()
        ctx.wait("v", "SE", se_c)
        V("tensor_mul", Ar, Ar, SB[:, 0:NJ])
        V("tensor_mul", SB[:, 24:24 + NJ], Ac_, SB[:, 16:16 + NJ])
        V("tensor_add", Ar, Ar, SB[:, 24:24 + NJ])
        V("tensor_mul", Br, Br, SB[:, 0:NJ])
        V("tensor_mul", SB[:, 24:24 + NJ], Bc_, SB[:, 16:16 + NJ])
        V("tensor_add", Br, Br, SB[:, 24:24 + NJ])
        V("tensor_copy", Mr, SB[:, 8:8 + NJ])
        ve_runsum = ctx.ve_inc()

    bank = 6
    ctx.bank_wait(bank)
    ctx.wait("t", "VE", ve_runsum)
    T_("transpose", ctx.ps[bank][0:24, 0:128], runsum[:], ident[:])
    mm_ag1 = ctx.mm_inc()
    ctx.wait("v", "MM", mm_ag1)
    V("tensor_copy", S1[0:24, 0:128], ctx.ps[bank][0:24, 0:128])
    ve_ag1 = ctx.ve_inc()
    ctx.bank_set(bank, "VE", ve_ag1)
    ctx.wait("g", "VE", ve_ag1)
    sp_ag1 = ctx.sp(ag1_in, S1[0:24, 0:128])
    ctx.wait("g", "SP", sp_ag1)
    cc_ag1 = ctx.allgather(ag1_in, ag1_out)

    allrM = ctx.aalloc(8 * 128)
    allrA = ctx.aalloc(8 * 128)
    allrB = ctx.aalloc(8 * 128)
    stP = ctx.aalloc(128)
    stA = ctx.aalloc(128)
    stB = ctx.aalloc(128)
    q1 = ctx.aalloc(128)
    q2 = ctx.aalloc(128)
    q3 = ctx.aalloc(128)
    ctx.wait("y", "CC", cc_ag1)
    for vv, dst in ((0, allrM), (1, allrA), (2, allrB)):
        ld_allr = ctx.ld(dst[0:8, :],
                         mkap(ag1_out, vv * 8 * 128, [[128, 8], [24 * 128, 8], [1, 128]]))
    ld_allr = ctx.lq()
    ctx.wait("v", "LQ", ld_allr)
    V("memset", stP[0:8, :], NEG)
    V("memset", stA[0:8, :], 0.0)
    V("memset", stB[0:8, :], 0.0)
    ctx.ve_inc()
    wrow = trow[0:8, 0:128]
    for k in range(8):
        Msl = allrM[0:8, k * 128:(k + 1) * 128]
        Asl = allrA[0:8, k * 128:(k + 1) * 128]
        Bsl = allrB[0:8, k * 128:(k + 1) * 128]
        V("tensor_scalar", q1[0:8, :], Msl, masklt8[:, k:k + 1], mneg8[:, k:k + 1],
          ALU.mult, ALU.add)
        V("scalar_tensor_tensor", q2[0:8, :], wrow, tlm8[:, k:k + 1], stP[0:8, :],
          ALU.mult, ALU.add)
        V("tensor_max", q3[0:8, :], q2[0:8, :], q1[0:8, :])
        V("tensor_sub", q2[0:8, :], q2[0:8, :], q3[0:8, :])
        V("tensor_sub", q1[0:8, :], q1[0:8, :], q3[0:8, :])
        ve_cb = ctx.ve_inc()
        ctx.wait("s", "VE", ve_cb)
        S("activation", q2[0:8, :], q2[0:8, :], AF.Exp)
        S("activation", q1[0:8, :], q1[0:8, :], AF.Exp)
        se_cb = ctx.se_inc()
        ctx.wait("v", "SE", se_cb)
        V("tensor_copy", stP[0:8, :], q3[0:8, :])
        V("tensor_mul", stA[0:8, :], stA[0:8, :], q2[0:8, :])
        V("tensor_single_scalar", q3[0:8, :], Asl, masklt8[:, k:k + 1], ALU.mult)
        V("tensor_mul", q3[0:8, :], q3[0:8, :], q1[0:8, :])
        V("tensor_add", stA[0:8, :], stA[0:8, :], q3[0:8, :])
        V("tensor_mul", stB[0:8, :], stB[0:8, :], q2[0:8, :])
        V("tensor_single_scalar", q3[0:8, :], Bsl, masklt8[:, k:k + 1], ALU.mult)
        V("tensor_mul", q3[0:8, :], q3[0:8, :], q1[0:8, :])
        V("tensor_add", stB[0:8, :], stB[0:8, :], q3[0:8, :])
        ctx.ve_inc()
    bank = 7
    ctx.bank_wait(bank)
    ctx.wait("t", "VE", ctx.VE.n)
    for sidx, st_t in enumerate((stP, stA, stB)):
        for j in range(NJ):
            T_("matmul", ctx.ps[bank][:, sidx * NJ + j:sidx * NJ + j + 1], st_t[0:8, :],
               oh8id[:, j:j + 1], start=True, stop=True)
    mm_st0 = ctx.mm_inc()
    ctx.wait("v", "MM", mm_st0)
    V("tensor_copy", state0[:], ctx.ps[bank][:, 0:3 * NJ])
    ve_state0 = ctx.ve_inc()
    ctx.dbg_dump("state0_d", state0[:], [128, 3 * NJ])
    ctx.dbg_dump("runsum_d", runsum[:], [128, 3 * NJ])
    ctx.bank_set(bank, "VE", ve_state0)
    ctx.phase_barrier()
    ctx.areset(am1)

    am4 = ctx.amark()
    KB = ctx.aalloc(NJ * TC)
    VB = ctx.aalloc(NJ * TC)
    RB = ctx.aalloc(NJ * TC)
    BPP = ctx.aalloc(NJ * (TC + 1))
    BAA = ctx.aalloc(NJ * (TC + 1))
    BBB = ctx.aalloc(NJ * (TC + 1))
    RW = ctx.aalloc(NJ * TC)
    AB = ctx.aalloc(NJ * TC)
    H0R = ctx.aalloc(2 * TC)
    D4 = ctx.aalloc(4 * TC)
    U4 = ctx.aalloc(4 * TC)
    UV4 = ctx.aalloc(4 * TC)
    WOB = ctx.aalloc(2 * NJ * 128)

    def ld_w_cout(wap, cout, slot):
        t = 0
        for cin in range(NJ):
            t = ctx.ld(WOB[:, slot * 1024 + cin * 128:slot * 1024 + (cin + 1) * 128],
                       wap[cin * 128:(cin + 1) * 128, cout * 128:(cout + 1) * 128])
        return ctx.lq()

    sp_a = 0
    sq_a = 0
    ve_kdone = 0
    mm_op_last = 0
    ve_adone_prev = 0
    cc_ag1b = 0
    for c in range(NCHK):
        base = c * TC
        ld_kvs = 0
        ctx.wait("y", "SQ", sq_chunk)
        ctx.wait("y", "VE", ve_kdone)
        for vi2, buf in ((0, KB), (1, VB), (2, RB)):
            for j in range(NJ):
                ld_kvs = ctx.ld(a_(buf, j),
                                kvs_sp[vi2 * C + j * 128:vi2 * C + (j + 1) * 128,
                                       base:base + TC])
        ld_kvs = ctx.lq()
        ctx.wait("v", "LQ", ld_kvs)
        ctx.wait("v", "VE", ve_state0)
        ctx.wait("v", "MM", mm_op_last)
        ctx.wait("v", "SQ", sq_a)
        for half in range(2):
            js = list(range(half * 4, half * 4 + 4))
            for jj, j in enumerate(js):
                bpp, baa, bbb = a1_(BPP, j), a1_(BAA, j), a1_(BBB, j)
                if c == 0:
                    V("tensor_copy", bpp[:, 0:1], state0[:, j:j + 1])
                    V("tensor_copy", baa[:, 0:1], state0[:, NJ + j:NJ + j + 1])
                    V("tensor_copy", bbb[:, 0:1], state0[:, 2 * NJ + j:2 * NJ + j + 1])
                else:
                    V("tensor_copy", bpp[:, 0:1], bpp[:, TC:TC + 1])
                    V("tensor_copy", baa[:, 0:1], baa[:, TC:TC + 1])
                    V("tensor_copy", bbb[:, 0:1], bbb[:, TC:TC + 1])
                V("tensor_tensor_scan", bpp[:, 1:TC + 1],
                  dc("w", j).broadcast_to([128, TC]), a_(KB, j), bpp[:, 0:1],
                  ALU.add, ALU.max)
            for jj, j in enumerate(js):
                bpp = a1_(BPP, j)
                V("scalar_tensor_tensor", a_(D4, jj), bpp[:, 0:TC], dc("w", j),
                  bpp[:, 1:TC + 1], ALU.add, ALU.subtract)
                V("tensor_sub", a_(U4, jj), a_(KB, j), bpp[:, 1:TC + 1])
            ve_b1 = ctx.ve_inc()
            ctx.wait("s", "VE", ve_b1)
            for jj in range(4):
                S("activation", a_(D4, jj), a_(D4, jj), AF.Exp)
                S("activation", a_(U4, jj), a_(U4, jj), AF.Exp)
            se_b1 = ctx.se_inc()
            ctx.wait("v", "SE", se_b1)
            for jj, j in enumerate(js):
                bpp, baa, bbb = a1_(BPP, j), a1_(BAA, j), a1_(BBB, j)
                V("tensor_mul", a_(UV4, jj), a_(U4, jj), a_(VB, j))
                V("tensor_tensor_scan", bbb[:, 1:TC + 1], a_(D4, jj), a_(U4, jj),
                  bbb[:, 0:1], ALU.mult, ALU.add)
                V("tensor_tensor_scan", baa[:, 1:TC + 1], a_(D4, jj), a_(UV4, jj),
                  baa[:, 0:1], ALU.mult, ALU.add)
            for jj, j in enumerate(js):
                bpp = a1_(BPP, j)
                V("tensor_single_scalar", a_(KB, j), a_(KB, j), vc("tfir", j), ALU.add)
                V("tensor_max", a_(UV4, jj), bpp[:, 0:TC], a_(KB, j))
                V("tensor_sub", a_(D4, jj), bpp[:, 0:TC], a_(UV4, jj))
                V("tensor_sub", a_(KB, j), a_(KB, j), a_(UV4, jj))
            ve_b2 = ctx.ve_inc()
            ctx.wait("s", "VE", ve_b2)
            for jj, j in enumerate(js):
                S("activation", a_(D4, jj), a_(D4, jj), AF.Exp)
                S("activation", a_(KB, j), a_(KB, j), AF.Exp)
            se_b2 = ctx.se_inc()
            ctx.wait("v", "SE", se_b2)
            for jj, j in enumerate(js):
                bbb = a1_(BBB, j)
                V("tensor_mul", a_(UV4, jj), a_(D4, jj), bbb[:, 0:TC])
                V("tensor_add", a_(UV4, jj), a_(UV4, jj), a_(KB, j))
            ve_b3 = ctx.ve_inc()
            ctx.wait("s", "VE", ve_b3)
            for jj in range(4):
                S("activation", a_(UV4, jj), a_(UV4, jj), AF.Ln)
            se_b3 = ctx.se_inc()
            ctx.wait("v", "SE", se_b3)
            for jj in range(4):
                V("tensor_scalar_mul", a_(UV4, jj), a_(UV4, jj), -1.0)
            ve_b4 = ctx.ve_inc()
            ctx.wait("s", "VE", ve_b4)
            for jj in range(4):
                S("activation", a_(UV4, jj), a_(UV4, jj), AF.Exp)
            se_b4 = ctx.se_inc()
            ctx.wait("v", "SE", se_b4)
            for jj, j in enumerate(js):
                baa = a1_(BAA, j)
                V("tensor_mul", a_(D4, jj), a_(D4, jj), baa[:, 0:TC])
                V("tensor_mul", a_(KB, j), a_(KB, j), a_(VB, j))
                V("tensor_add", a_(D4, jj), a_(D4, jj), a_(KB, j))
                V("tensor_mul", a_(D4, jj), a_(D4, jj), a_(UV4, jj))
                V("tensor_mul", a_(RW, j), a_(D4, jj), a_(RB, j))
            ctx.ve_inc()
        ve_kdone = ctx.VE.n

        evac_a = {}
        mmo_hist = {}
        lqw = {}

        def issue_op_load(cout):
            if cout >= 2:
                ctx.wait("y", "MM", mmo_hist[cout - 2])
                ctx.wait("y", "VE", evac_a[cout - 2])
            else:
                ctx.wait("y", "MM", mm_op_last)
                ctx.wait("y", "VE", ve_adone_prev)
            ld_w_cout(ap["owT"], cout, cout % 2)
            ctx.ld(H0R[:, (cout % 2) * TC:(cout % 2) * TC + TC],
                   h0_sp[cout * 128:(cout + 1) * 128, base:base + TC])
            lqw[cout] = ctx.lq()

        issue_op_load(0)
        issue_op_load(1)
        for cout in range(NJ):
            bank = cout % 2
            ctx.bank_wait(bank)
            ctx.wait("t", "LQ", lqw[cout])
            ctx.wait("t", "VE", ve_kdone)
            for cin in range(NJ):
                T_("matmul", ctx.ps[bank][:, 0:TC],
                   WOB[:, (cout % 2) * 1024 + cin * 128:(cout % 2) * 1024 + (cin + 1) * 128],
                   a_(RW, cin), start=(cin == 0), stop=(cin == NJ - 1))
            mmo = ctx.mm_inc()
            mmo_hist[cout] = mmo
            ctx.wait("v", "MM", mmo)
            V("tensor_add", a_(AB, cout), ctx.ps[bank][:, 0:TC],
              H0R[:, (cout % 2) * TC:(cout % 2) * TC + TC])
            evac_a[cout] = ctx.ve_inc()
            ctx.bank_set(bank, "VE", evac_a[cout])
            if cout + 2 < NJ:
                issue_op_load(cout + 2)
        mm_op_last = ctx.MM.n
        ve_adone = ctx.VE.n
        ve_adone_prev = ve_adone
        for j in range(NJ):
            ctx.wait("g", "VE", ve_adone)
            sp_a = ctx.sp(a_sp[j * 128:(j + 1) * 128, base:base + TC], a_(AB, j))
        sq_a = ctx.sq()
        if c == NCHK - 1:
            for j in range(NJ):
                V("tensor_copy", SB[:, j:j + 1], a_(AB, j)[:, TC - 1:TC])
            ve_al = ctx.ve_inc()
            bank = 6
            ctx.bank_wait(bank)
            ctx.wait("t", "VE", ve_al)
            T_("transpose", ctx.ps[bank][0:8, 0:128], SB[:, 0:NJ], ident[:])
            mm_al = ctx.mm_inc()
            ctx.wait("v", "MM", mm_al)
            V("tensor_copy", S1[0:8, 0:128], ctx.ps[bank][0:8, 0:128])
            ve_al2 = ctx.ve_inc()
            ctx.bank_set(bank, "VE", ve_al2)
            ctx.wait("g", "VE", ve_al2)
            sp_ag1b = ctx.sp(ag1b_in, S1[0:8, 0:128])
            ctx.wait("g", "SP", sp_ag1b)
            cc_ag1b = ctx.allgather(ag1b_in, ag1b_out)

    ctx.phase_barrier()
    ctx.areset(am4)
    ab8 = ctx.aalloc(8 * 128)
    ctx.wait("y", "CC", cc_ag1b)
    ld_ab8 = ctx.ld(ab8[0:8, :],
                    mkap(ag1b_out, 0, [[128, 8], [8 * 128, 8], [1, 128]]))
    ld_ab8 = ctx.lq()
    ctx.wait("v", "LQ", ld_ab8)
    V("memset", S1[0:8, 0:128], 0.0)
    for k in range(8):
        V("scalar_tensor_tensor", S1[0:8, 0:128], ab8[0:8, k * 128:(k + 1) * 128],
          mprev8[:, k:k + 1], S1[0:8, 0:128], ALU.mult, ALU.add)
    ve_am = ctx.ve_inc()
    bank = 4
    ctx.bank_wait(bank)
    V("reduce_sum", S2[0:8, 0:1], S1[0:8, 0:128], axis=AX)
    ve_rs = ctx.ve_inc()
    ctx.wait("s", "VE", ve_rs)
    S("activation", S3[0:8, 0:128], S1[0:8, 0:128], AF.Square)
    se_sq = ctx.se_inc()
    ctx.wait("v", "SE", se_sq)
    V("reduce_sum", S2[0:8, 1:2], S3[0:8, 0:128], axis=AX)
    ve_rs2 = ctx.ve_inc()
    ctx.wait("t", "VE", ve_rs2)
    T_("matmul", ctx.ps[bank][0:1, 0:2], ones8[:, 0:1], S2[0:8, 0:2], start=True, stop=True)
    mm_bn = ctx.mm_inc()
    ctx.wait("v", "MM", mm_bn)
    st = stats
    V("tensor_scalar_mul", st[0:1, 0:2], ctx.ps[bank][0:1, 0:2], 1.0 / C)
    V("tensor_mul", st[0:1, 2:3], st[0:1, 0:1], st[0:1, 0:1])
    V("tensor_sub", st[0:1, 1:2], st[0:1, 1:2], st[0:1, 2:3])
    V("tensor_scalar_add", st[0:1, 1:2], st[0:1, 1:2], LN_EPS)
    ve_bv = ctx.ve_inc()
    ctx.wait("s", "VE", ve_bv)
    S("activation", st[0:1, 3:4], st[0:1, 1:2], AF.Ln)
    se_bs = ctx.se_inc()
    ctx.wait("v", "SE", se_bs)
    V("tensor_scalar_mul", st[0:1, 3:4], st[0:1, 3:4], -0.5)
    ve_bv2 = ctx.ve_inc()
    ctx.wait("s", "VE", ve_bv2)
    S("activation", st[0:1, 4:5], st[0:1, 3:4], AF.Exp)
    se_bs2 = ctx.se_inc()
    ctx.wait("v", "SE", se_bs2)
    V("tensor_copy", st[0:1, 4:5], st[0:1, 4:5])
    ve_bA = ctx.ve_inc()
    ctx.wait("t", "VE", ve_bA)
    T_("matmul", ctx.ps[bank][0:8, 4:5], onesrow[0:1, 0:8], st[0:1, 4:5], start=True, stop=True)
    T_("matmul", ctx.ps[bank][0:8, 5:6], onesrow[0:1, 0:8], st[0:1, 0:1], start=True, stop=True)
    mm_bb = ctx.mm_inc()
    ctx.wait("v", "MM", mm_bb)
    V("tensor_single_scalar", S1[0:8, 0:128], S1[0:8, 0:128], ctx.ps[bank][0:8, 5:6],
      ALU.subtract)
    V("tensor_single_scalar", S1[0:8, 0:128], S1[0:8, 0:128], ctx.ps[bank][0:8, 4:5],
      ALU.mult)
    V("tensor_mul", S1[0:8, 0:128], S1[0:8, 0:128], trow[0:8, 128:256])
    V("tensor_add", S1[0:8, 0:128], S1[0:8, 0:128], trow[0:8, 256:384])
    V("tensor_single_scalar", S1[0:8, 0:128], S1[0:8, 0:128], maskfirst[:, 0:1], ALU.mult)
    ve_u2b = ctx.ve_inc()
    ctx.bank_set(bank, "VE", ve_u2b)
    bank = 7
    ctx.bank_wait(bank)
    ctx.wait("t", "VE", ve_u2b)
    for j in range(NJ):
        T_("matmul", ctx.ps[bank][:, j:j + 1], S1[0:8, 0:128], oh8id[:, j:j + 1],
           start=True, stop=True)
    mm_u2b = ctx.mm_inc()
    ctx.wait("v", "MM", mm_u2b)
    V("tensor_copy", u2bnd[:], ctx.ps[bank][:, 0:NJ])
    ve_u2bnd = ctx.ve_inc()
    ctx.bank_set(bank, "VE", ve_u2bnd)
    ctx.phase_barrier()
    ctx.areset(am4)

    am4b = ctx.amark()
    AB = ctx.aalloc(NJ * TC)
    U2 = ctx.aalloc(NJ * (TC + 1))
    A3 = ctx.aalloc(NJ * TC)
    FRB = ctx.aalloc(NJ * TC)
    X16 = ctx.aalloc16(2 * NJ * TC)
    WFR = ctx.aalloc(NJ * 1024)
    ld_wfr = 0
    for j in range(NJ):
        ld_wfr = ctx.ld(WFR[:, j * 1024:(j + 1) * 1024],
                        ap["frwT"][j * 128:(j + 1) * 128, :])
    ld_wfr = ctx.lq()

    sp_4b = 0
    sq_4b = 0
    sq_xk2 = 0
    ve_4b = 0
    mm_fr_last = 0
    for c in range(NCHK):
        base = c * TC
        ld_a = 0
        ctx.wait("y", "SQ", sq_a)
        ctx.wait("y", "VE", ve_4b)
        for j in range(NJ):
            ld_a = ctx.ld(a_(AB, j), a_sp[j * 128:(j + 1) * 128, base:base + TC])
        ld_a = ctx.lq()
        ln_stats(lambda j: a_(AB, j), TC, waits=(("LQ", ld_a),))
        ctx.wait("v", "LQ", ld_a)
        for j in range(NJ):
            V("tensor_mul", a1_(U2, j)[:, 1:TC + 1], a_(AB, j), lnscr[:, 0:TC])
            V("tensor_add", a1_(U2, j)[:, 1:TC + 1], a1_(U2, j)[:, 1:TC + 1],
              lnscr[:, TC:2 * TC])
            V("tensor_scalar", a1_(U2, j)[:, 1:TC + 1], a1_(U2, j)[:, 1:TC + 1],
              vc("ln2w", j), vc("ln2b", j), ALU.mult, ALU.add)
            V("tensor_copy", a1_(U2, j)[:, 0:1],
              u2bnd[:, j:j + 1] if c == 0 else u2save[:, j:j + 1])
        ctx.ve_inc()
        for j in range(NJ):
            V("tensor_copy", u2save[:, j:j + 1], a1_(U2, j)[:, TC:TC + 1])
        ctx.ve_inc()
        ctx.wait("v", "SQ", sq_4b)
        for j in range(NJ):
            V("tensor_sub", a_(A3, j), a1_(U2, j)[:, 1:TC + 1], a1_(U2, j)[:, 0:TC])
            V("scalar_tensor_tensor", a_(A3, j), a_(A3, j), vc("ftmk", j),
              a1_(U2, j)[:, 0:TC], ALU.mult, ALU.add)
        for j in range(NJ):
            V("tensor_copy", X16[:, j * TC:(j + 1) * TC], a_(A3, j))
            V("tensor_sub", S1[:, 0:TC], a_(A3, j), X16[:, j * TC:(j + 1) * TC])
            V("tensor_copy", X16[:, (NJ + j) * TC:(NJ + j + 1) * TC], S1[:, 0:TC])
        ve_xk2 = ctx.ve_inc()
        sp_xk2 = 0
        for j in range(NJ):
            ctx.wait("g", "VE", ve_xk2)
            sp_xk2 = ctx.sp(xk2_sp[j * 128:(j + 1) * 128, base:base + TC], a_(A3, j))
            sp_xk2 = ctx.sp(xk2h_sp[j * 128:(j + 1) * 128, base:base + TC],
                            X16[:, j * TC:(j + 1) * TC])
            sp_xk2 = ctx.sp(xk2l_sp[j * 128:(j + 1) * 128, base:base + TC],
                            X16[:, (NJ + j) * TC:(NJ + j + 1) * TC])
        sq_xk2 = ctx.sq()
        ctx.wait("v", "MM", mm_fr_last)
        for j in range(NJ):
            V("tensor_sub", a_(FRB, j), a1_(U2, j)[:, 1:TC + 1], a1_(U2, j)[:, 0:TC])
            V("scalar_tensor_tensor", a_(FRB, j), a_(FRB, j), vc("ftmr", j),
              a1_(U2, j)[:, 0:TC], ALU.mult, ALU.add)
        ve_xr2 = ctx.ve_inc()
        for cout in range(NJ):
            bank = cout % 2
            ctx.bank_wait(bank)
            ctx.wait("t", "LQ", ld_wfr)
            ctx.wait("t", "VE", ve_xr2)
            for cin in range(NJ):
                T_("matmul", ctx.ps[bank][:, 0:TC],
                   WFR[:, cin * 1024 + cout * 128:cin * 1024 + (cout + 1) * 128],
                   a_(FRB, cin), start=(cin == 0), stop=(cin == NJ - 1))
            mmf = ctx.mm_inc()
            ctx.wait("s", "MM", mmf)
            ctx.wait("s", "SQ", sq_xk2)
            S("activation", a_(A3, cout), ctx.ps[bank][:, 0:TC], AF.Sigmoid)
            ctx.bank_set(bank, "SE", ctx.se_inc())
        mm_fr_last = ctx.MM.n
        se_fr = ctx.SE.n
        for j in range(NJ):
            ctx.wait("g", "SE", se_fr)
            sp_4b = ctx.sp(sig_sp[j * 128:(j + 1) * 128, base:base + TC], a_(A3, j))
        sq_4b = ctx.sq()
        ve_4b = ctx.VE.n
    ctx.phase_barrier()
    ctx.areset(am4b)

    am5 = ctx.amark()
    KVSB = ctx.aalloc(NJ * TL)
    am5b = ctx.amark()
    WFK = ctx.aalloc(NJ * g.FS)
    WFV = ctx.aalloc((g.FS // 128) * 1024)
    NFO = g.FS // 128
    W16K = ctx.aalloc16(2 * NJ * g.FS)
    W16V = ctx.aalloc16(2 * NFO * 1024)
    X16T = ctx.aalloc16(2 * NJ * TC)
    F16 = ctx.aalloc16(2 * NFO * TC)
    NWK = NJ * g.FS
    NWV = NFO * 1024
    NXT = NJ * TC
    NFK = NFO * TC

    ctx.wait("v", "SQ", sq_4b)
    V("memset", KVSB[:, :], 0.0)
    ve_kv0 = ctx.ve_inc()
    mm_cm_last = 0
    for fs in range(g.NFS):
        ld_wf = 0
        for j in range(NJ):
            ctx.wait("y", "MM", mm_cm_last)
            ld_wf = ctx.ld(WFK[:, j * g.FS:(j + 1) * g.FS],
                           ap["fkwT"][j * 128:(j + 1) * 128, fs * g.FS:(fs + 1) * g.FS])
        for fo in range(NFO):
            ld_wf = ctx.ld(WFV[:, fo * 1024:(fo + 1) * 1024],
                           ap["fvwT"][fs * g.FS + fo * 128:fs * g.FS + (fo + 1) * 128, :])
        ld_wf = ctx.lq()
        ctx.wait("v", "LQ", ld_wf)
        ctx.wait("v", "MM", mm_cm_last)
        ctx.wait("v", "SE", ctx.SE.n)
        for buf, b16, n in ((WFK, W16K, NWK), (WFV, W16V, NWV)):
            for c0 in range(0, n, 512):
                V("tensor_copy", b16[:, c0:c0 + 512], buf[:, c0:c0 + 512])
                V("tensor_sub", S4[:, 0:512], buf[:, c0:c0 + 512], b16[:, c0:c0 + 512])
                V("tensor_copy", b16[:, n + c0:n + c0 + 512], S4[:, 0:512])
        ve_wsp = ctx.ve_inc()
        ve_fk_last = ve_wsp
        for tc in range(NCHK):
            base = tc * TC
            ld_x2 = 0
            for j in range(NJ):
                ctx.wait("y", "MM", mm_cm_last)
                ld_x2 = ctx.ld(X16T[:, j * TC:(j + 1) * TC],
                               xk2h_sp[j * 128:(j + 1) * 128, base:base + TC])
                ld_x2 = ctx.ld(X16T[:, NXT + j * TC:NXT + (j + 1) * TC],
                               xk2l_sp[j * 128:(j + 1) * 128, base:base + TC])
            ld_wx = ctx.lq()
            ve_fk = {}
            ctx.wait("s", "MM", mm_cm_last)
            ctx.wait("v", "MM", mm_cm_last)
            for fo in range(NFO):
                bank = fo % 2
                ctx.bank_wait(bank)
                ctx.wait("t", "LQ", ld_wx)
                ctx.wait("t", "VE", ve_wsp)
                first = True
                for woff, xoff in ((0, 0), (0, NXT), (NWK, 0)):
                    for cin in range(NJ):
                        T_("matmul", ctx.ps[bank][:, 0:TC],
                           W16K[:, woff + cin * g.FS + fo * 128:
                                woff + cin * g.FS + (fo + 1) * 128],
                           X16T[:, xoff + cin * TC:xoff + (cin + 1) * TC],
                           start=first, stop=(woff == NWK and cin == NJ - 1))
                        first = False
                mmk = ctx.mm_inc()
                ctx.wait("s", "MM", mmk)
                ctx.wait("s", "VE", ve_fk_last)
                S("activation", S4[:, 0:TC], ctx.ps[bank][:, 0:TC], AF.Relu)
                S("activation", SQ2[:, 0:TC], S4[:, 0:TC], AF.Square)
                se_fk = ctx.se_inc()
                ctx.wait("v", "SE", se_fk)
                V("tensor_copy", F16[:, fo * TC:(fo + 1) * TC], SQ2[:, 0:TC])
                V("tensor_sub", S3[:, 0:TC], SQ2[:, 0:TC], F16[:, fo * TC:(fo + 1) * TC])
                V("tensor_copy", F16[:, NFK + fo * TC:NFK + (fo + 1) * TC], S3[:, 0:TC])
                ve_fk[fo] = ctx.ve_inc()
                ve_fk_last = ve_fk[fo]
                ctx.bank_set(bank, "VE", ve_fk[fo])
            for cout in range(NJ):
                bank = 2 + (cout % 2)
                ctx.bank_wait(bank)
                ctx.wait("t", "VE", ve_fk[NFO - 1])
                first = True
                for woff, foff in ((0, 0), (0, NFK), (NWV, 0)):
                    for fo in range(NFO):
                        T_("matmul", ctx.ps[bank][:, 0:TC],
                           W16V[:, woff + fo * 1024 + cout * 128:
                                woff + fo * 1024 + (cout + 1) * 128],
                           F16[:, foff + fo * TC:foff + (fo + 1) * TC],
                           start=first, stop=(woff == NWV and fo == NFO - 1))
                        first = False
                mmkv = ctx.mm_inc()
                ctx.wait("v", "MM", mmkv)
                ctx.wait("v", "VE", ve_kv0)
                V("tensor_add", KVSB[:, cout * TL + base:cout * TL + base + TC],
                  KVSB[:, cout * TL + base:cout * TL + base + TC], ctx.ps[bank][:, 0:TC])
                ctx.bank_set(bank, "VE", ctx.ve_inc())
            mm_cm_last = ctx.MM.n
    ctx.phase_barrier()
    ctx.areset(am5b)

    AB5 = ctx.aalloc(NJ * TC)
    SGB = ctx.aalloc(NJ * TC)
    XB5 = ctx.aalloc(NJ * TC)
    H1B = ctx.aalloc(NJ * TC)
    TOK = ctx.aalloc((TC // 128) * 1024)
    sp_ag2 = 0
    sq_ag2 = 0
    ve_rt = 0
    for tc in range(NCHK):
        base = tc * TC
        ld_f = 0
        ctx.wait("y", "VE", ve_rt)
        ctx.wait("y", "SQ", sq_ag2)
        for j in range(NJ):
            ld_f = ctx.ld(a_(AB5, j), a_sp[j * 128:(j + 1) * 128, base:base + TC])
            ld_f = ctx.ld(a_(SGB, j), sig_sp[j * 128:(j + 1) * 128, base:base + TC])
            ld_f = ctx.ld(a_(XB5, j), ap["xT"][j * 128:(j + 1) * 128, 1 + base:1 + base + TC])
        ld_f = ctx.lq()
        ctx.wait("v", "LQ", ld_f)
        for j in range(NJ):
            V("tensor_mul", S1[:, 0:TC], a_(SGB, j), KVSB[:, j * TL + base:j * TL + base + TC])
            V("tensor_add", S1[:, 0:TC], S1[:, 0:TC], a_(AB5, j))
            V("scalar_tensor_tensor", a_(AB5, j), S1[:, 0:TC], dc("ga", j), a_(XB5, j),
              ALU.mult, ALU.add)
        ve_xm = ctx.ve_inc()
        ln_stats(lambda j: a_(AB5, j), TC)
        for j in range(NJ):
            V("tensor_mul", a_(H1B, j), a_(AB5, j), lnscr[:, 0:TC])
            V("tensor_add", a_(H1B, j), a_(H1B, j), lnscr[:, TC:2 * TC])
            V("tensor_scalar", a_(H1B, j), a_(H1B, j), dc("cb1", j), dc("sb", j),
              ALU.mult, ALU.add)
        ve_h1 = ctx.ve_inc()
        for buf, dst in ((AB5, xmid_sp), (H1B, ag2_in)):
            tok_free = sq_ag2
            for tsub in range(TC // 128):
                for half in range(2):
                    bank = 4 + half
                    ctx.bank_wait(bank)
                    ctx.wait("t", "VE", ve_h1)
                    for jj in range(4):
                        j = half * 4 + jj
                        T_("transpose", ctx.ps[bank][:, jj * 128:(jj + 1) * 128],
                           a_(buf, j)[:, tsub * 128:(tsub + 1) * 128], ident[:])
                    mm_tp = ctx.mm_inc()
                    ctx.wait("v", "MM", mm_tp)
                    ctx.wait("v", "SQ", tok_free)
                    V("tensor_copy",
                      TOK[:, tsub * 1024 + half * 512:tsub * 1024 + (half + 1) * 512],
                      ctx.ps[bank][:, 0:512])
                    ctx.bank_set(bank, "VE", ctx.ve_inc())
            ve_tok = ctx.VE.n
            for tsub in range(TC // 128):
                ctx.wait("g", "VE", ve_tok)
                sp_ag2 = ctx.sp(dst[base + tsub * 128:base + (tsub + 1) * 128, :],
                                TOK[:, tsub * 1024:(tsub + 1) * 1024])
                if buf is H1B and "h1_dbg" in ctx.dbg_set:
                    sp_ag2 = ctx.sp(h1_dbg[base + tsub * 128:base + (tsub + 1) * 128, :],
                                    TOK[:, tsub * 1024:(tsub + 1) * 1024])
            sq_ag2 = ctx.sq()
        for tsub in range(TC // 128):
            bank = 6
            ctx.bank_wait(bank)
            ctx.wait("t", "VE", ve_h1)
            for j in range(NJ):
                T_("matmul", ctx.ps[bank][:, 0:g.E],
                   a_(H1B, j)[:, tsub * 128:(tsub + 1) * 128],
                   embTs[:, j * g.E:(j + 1) * g.E], start=(j == 0), stop=(j == NJ - 1))
            mm_lg = ctx.mm_inc()
            ti = (base // 128) + tsub
            rb = routebuf[:, ti * 18:(ti + 1) * 18]
            ctx.wait("v", "MM", mm_lg)
            V("tensor_copy", SB[:, 0:8], ctx.ps[bank][:, 0:g.E])
            ve_l = ctx.ve_inc()
            ctx.bank_set(bank, "VE", ve_l)
            V("max", out=SB[:, 8:16], in_=SB[:, 0:8])
            V("tensor_single_scalar", rb[:, 0:8], SB[:, 0:8], SB[:, 8:9], ALU.is_ge)
            V("tensor_single_scalar", SB[:, 16:24], SB[:, 0:8], SB[:, 9:10], ALU.is_ge)
            V("tensor_sub", rb[:, 8:16], SB[:, 16:24], rb[:, 0:8])
            V("tensor_sub", SB[:, 24:25], SB[:, 9:10], SB[:, 8:9])
            ve_r1 = ctx.ve_inc()
            ctx.wait("s", "VE", ve_r1)
            S("activation", SB[:, 24:25], SB[:, 24:25], AF.Exp)
            se_r1 = ctx.se_inc()
            ctx.wait("v", "SE", se_r1)
            V("tensor_scalar_add", SB[:, 25:26], SB[:, 24:25], 1.0)
            ve_r1b = ctx.ve_inc()
            ctx.wait("s", "VE", ve_r1b)
            S("activation", SB[:, 26:27], SB[:, 25:26], AF.Ln)
            se_r2 = ctx.se_inc()
            ctx.wait("v", "SE", se_r2)
            V("tensor_scalar_mul", SB[:, 26:27], SB[:, 26:27], -1.0)
            ve_r1c = ctx.ve_inc()
            ctx.wait("s", "VE", ve_r1c)
            S("activation", SB[:, 27:28], SB[:, 26:27], AF.Exp)
            se_r3 = ctx.se_inc()
            ctx.wait("v", "SE", se_r3)
            V("tensor_copy", rb[:, 16:17], SB[:, 27:28])
            V("tensor_mul", rb[:, 17:18], SB[:, 24:25], rb[:, 16:17])
            ve_r2 = ctx.ve_inc()
            bank = 7
            ctx.bank_wait(bank)
            ctx.wait("t", "VE", ve_r2)
            T_("transpose", ctx.ps[bank][0:8, 0:128], SB[:, 16:24], ident[:])
            T_("transpose", ctx.ps[bank][0:8, 128:256], rb[:, 0:8], ident[:])
            T_("transpose", ctx.ps[bank][0:8, 256:384], rb[:, 8:16], ident[:])
            mm_tr = ctx.mm_inc()
            ctx.wait("v", "MM", mm_tr)
            ctx.wait("v", "SQ", sq_ag2)
            V("tensor_copy", S1[0:8, 0:384], ctx.ps[bank][0:8, 0:384])
            ve_tr = ctx.ve_inc()
            ctx.bank_set(bank, "VE", ve_tr)
            ctx.wait("g", "VE", ve_tr)
            ctx.sp(ag2b_in[0:8, ti * 128:(ti + 1) * 128], S1[0:8, 0:128])
            ctx.sp(m12_d[0:8, ti * 128:(ti + 1) * 128], S1[0:8, 128:256])
            sp_ag2 = ctx.sp(m12_d[8:16, ti * 128:(ti + 1) * 128], S1[0:8, 256:384])
            sq_ag2 = ctx.sq()
        ve_rt = ctx.VE.n
    ctx.wait("g", "SP", sp_ag2)
    cc_ag2b = ctx.allgather(ag2b_in, ag2b_out)
    cc_ag2 = ctx.allgather(ag2_in, ag2_out)
    ctx.phase_barrier()
    ctx.areset(am5)

    am6 = ctx.amark()
    GA = ctx.aalloc(T)
    GB = ctx.aalloc(T)
    QT = ctx.aalloc(max(TL, 512))
    M1E = ctx.aalloc(TL)
    M2E = ctx.aalloc(TL)
    ctx.wait("y", "CC", cc_ag2b)
    ld_og = ctx.ld(GA[0:8, :], mkap(ag2b_out, 0, [[TL, 8], [8 * TL, 8], [1, TL]]))
    ld_m1 = ctx.ld(M1E[0:8, 0:TL], m12_d[0:8, :])
    ld_m2 = ctx.ld(M2E[0:8, 0:TL], m12_d[8:16, :])
    ld_og = ctx.lq()
    ctx.wait("v", "LQ", ld_og)
    V("tensor_tensor_scan", GB[0:8, :], GA[0:8, :],
      zz[0:8, 0:1].broadcast_to([8, T]), 0.0, ALU.add, ALU.add)
    V("scalar_tensor_tensor", GA[0:8, :], GB[0:8, :], float(g.CAP), GA[0:8, :],
      ALU.is_le, ALU.mult)
    V("scalar_tensor_tensor", GB[0:8, :], GB[0:8, :], ecap[:, 0:1], GA[0:8, :],
      ALU.add, ALU.mult)
    ve_g = ctx.ve_inc()
    sp_sv = 0
    sp_sv_hist = {}
    for o in range(T // 512):
        bank = o % 2
        ctx.bank_wait(bank)
        ctx.wait("t", "VE", ve_g)
        T_("matmul", ctx.ps[bank][0:1, 0:512], ohmy[:, 0:1], GB[0:8, o * 512:(o + 1) * 512],
           start=True, stop=False)
        T_("matmul", ctx.ps[bank][0:1, 0:512], ohs[:, 0:1], GA[0:8, o * 512:(o + 1) * 512],
           start=False, stop=True)
        mm_sv = ctx.mm_inc()
        ctx.wait("v", "MM", mm_sv)
        if o >= 2:
            ctx.wait("v", "SQ", sp_sv_hist[o - 2])
        V("tensor_scalar_add", R16[0:1, (o % 2) * 512:(o % 2) * 512 + 512],
          ctx.ps[bank][0:1, 0:512], float(g.BIG))
        ve_sv = ctx.ve_inc()
        ctx.bank_set(bank, "VE", ve_sv)
        ctx.wait("g", "VE", ve_sv)
        sp_sv = ctx.sp(svrow_d[0:1, o * 512:(o + 1) * 512],
                       R16[0:1, (o % 2) * 512:(o % 2) * 512 + 512])
        sp_sv_hist[o] = ctx.sq()
    ctx.wait("y", "SQ", ctx.SQ.n)
    ld_svt = ctx.ld(svtok[:], svrow_d.rearrange("o (a b) -> (o a) b", b=128),
                    transpose=True)
    ld_svt = ctx.lq()
    ctx.wait("v", "LQ", ld_svt)
    V("tensor_copy", idx32[:], svtok[:])
    ve_idx = ctx.ve_inc()
    sp_fld = 0
    sq_fld = 0
    for fi in range(4):
        src_em = M1E if fi < 2 else M2E
        G_t = GA if (fi % 2 == 0) else GB
        V("memset", QT[0:8, 0:TL], 0.0)
        ve_z = ctx.ve_inc()
        for k in range(8):
            V("scalar_tensor_tensor", QT[0:8, 0:TL], G_t[0:8, k * TL:(k + 1) * TL],
              ohown[:, k:k + 1], QT[0:8, 0:TL], ALU.mult, ALU.add)
        V("tensor_mul", QT[0:8, 0:TL], QT[0:8, 0:TL], src_em[0:8, 0:TL])
        ve_q = ctx.ve_inc()
        for o in range(max(1, TL // 512)):
            w_ = min(512, TL)
            bank = 2
            ctx.bank_wait(bank)
            ctx.wait("t", "VE", ve_q)
            T_("matmul", ctx.ps[bank][0:1, 0:w_], ones8[:, 0:1],
               QT[0:8, o * 512:o * 512 + w_], start=True, stop=True)
            mm_q = ctx.mm_inc()
            ctx.wait("v", "MM", mm_q)
            ctx.wait("v", "SQ", sq_fld)
            V("tensor_copy", R16[0:1, 0:w_], ctx.ps[bank][0:1, 0:w_])
            ve_fq = ctx.ve_inc()
            ctx.bank_set(bank, "VE", ve_fq)
            ctx.wait("g", "VE", ve_fq)
            sp_fld = ctx.sp(fld_d[fi:fi + 1, o * 512:o * 512 + w_], R16[0:1, 0:w_])
            sq_fld = ctx.sq()
    ctx.wait("y", "SQ", sq_fld)
    ld_fld = 0
    for fi in range(4):
        ld_fld = ctx.ld(fldtok[:, fi * 16:(fi + 1) * 16],
                        fld_d[fi:fi + 1, :].rearrange("o (a b) -> (o a) b", b=128),
                        transpose=True)
    ld_fld = ctx.lq()
    ctx.wait("v", "LQ", ld_fld)
    V("tensor_copy", fld32[:], fldtok[:])
    V("tensor_copy", idxf1[:], fldtok[:, 16:16 + TL // 128])
    V("tensor_copy", idxf2[:], fldtok[:, 48:48 + TL // 128])
    NTI = TL // 128
    capbits = g.CAP.bit_length() - 1
    stbits = g.STILE.bit_length() - 1
    for f, (idxf, idxfs) in enumerate(((idxf1, idxf1s), (idxf2, idxf2s))):
        E_, SL, BA = idxt[:, 0:NTI], idxt[:, NTI:2 * NTI], idxt[:, 2 * NTI:3 * NTI]
        M0, M1, TQ = (idxt[:, 3 * NTI:4 * NTI], idxt[:, 4 * NTI:5 * NTI],
                      idxt[:, 5 * NTI:6 * NTI])
        V("tensor_scalar", E_, idxf[:], capbits, None, ALU.logical_shift_right)
        V("tensor_scalar", SL, idxf[:], g.CAP - 1, None, ALU.bitwise_and)
        V("tensor_scalar", BA, E_, stbits, None, ALU.logical_shift_left)
        V("tensor_tensor", BA, BA, SL, ALU.add)
        for s in range(g.NST):
            dst = idxfs[:, s * NTI:(s + 1) * NTI]
            V("tensor_scalar", M0, SL, s * g.STILE, None, ALU.is_ge)
            V("tensor_scalar", M1, SL, (s + 1) * g.STILE, None, ALU.is_lt)
            V("tensor_tensor", M0, M0, M1, ALU.mult)
            V("tensor_scalar", TQ, BA, s * g.STILE, None, ALU.subtract)
            V("tensor_tensor", TQ, TQ, M0, ALU.mult)
            V("tensor_scalar", M0, M0, -1000000, None, ALU.mult)
            V("tensor_scalar", M0, M0, 1000000, None, ALU.add)
            V("tensor_tensor", dst, TQ, M0, ALU.add)
    ve_fld = ctx.ve_inc()
    ctx.dbg_dump("idxs_d", idxf1s[:, 0:g.NST * (TL // 128)], [128, g.NST * (TL // 128)], I32)
    ctx.dbg_dump("idxs2_d", idxf2s[:, 0:g.NST * (TL // 128)], [128, g.NST * (TL // 128)], I32)
    ctx.dbg_dump("idxb_d", idxf1[:], [128, TL // 128], I32)
    ctx.phase_barrier()
    ctx.areset(am6)

    gbbc = ctx.aalloc(C)
    bank = 6
    ctx.bank_wait(bank)
    T_("transpose", ctx.ps[bank][0:8, 0:128], dct("gb"), ident[:])
    mm_gb = ctx.mm_inc()
    ctx.wait("v", "MM", mm_gb)
    V("tensor_copy", S1[0:8, 0:128], ctx.ps[bank][0:8, 0:128])
    ve_gb1 = ctx.ve_inc()
    ctx.bank_set(bank, "VE", ve_gb1)
    ctx.wait("g", "VE", ve_gb1)
    sp_gb = ctx.sp(gbrow_d.rearrange("o (j p) -> (o j) p", p=128), S1[0:8, 0:128])
    sq_gb = ctx.sq()
    ctx.wait("y", "SQ", sq_gb)
    gbrow = stats
    ld_gb = ctx.ld(gbrow[0:1, 0:C], gbrow_d)
    ld_gb = ctx.lq()
    for h in range(2 if C > 512 else 1):
        w_ = min(512, C)
        bank = 6 + h
        ctx.bank_wait(bank)
        ctx.wait("t", "LQ", ld_gb)
        T_("matmul", ctx.ps[bank][:, 0:w_], onesrow[0:1, :], gbrow[0:1, h * 512:h * 512 + w_],
           start=True, stop=True)
        mm_gb2 = ctx.mm_inc()
        ctx.wait("v", "MM", mm_gb2)
        V("tensor_copy", gbbc[:, h * 512:h * 512 + w_], ctx.ps[bank][:, 0:w_])
        ctx.bank_set(bank, "VE", ctx.ve_inc())
    ve_gbbc = ctx.VE.n


    am7 = ctx.amark()
    GRP = 4
    NGRPF = 2
    HROT = ctx.aalloc(GRP * NGRPF * 1024)
    W1B = ctx.aalloc(NJ * g.I)
    ld_w1 = 0
    for j in range(NJ):
        ld_w1 = ctx.ld(W1B[:, j * g.I:(j + 1) * g.I], ap["w1T"][j * 128:(j + 1) * 128, :])
    ld_w1 = ctx.lq()
    sp_z = 0
    for r in range(0, g.CAP, 128):
        sp_z = ctx.sp(xbuf[r:r + 128, :], zz[:])
    bc = nc.gpsimd.alloc_register("bcreg")
    nc.gpsimd.reg_mov(bc, g.CAP - 1)
    ctx.wait("g", "SP", sp_z)
    ctx.wait("g", "VE", ve_idx)
    ctx.wait("y", "CC", ctx.CC.n)
    sqg = {}
    lqg = {}
    NG = T // 128 // GRP
    for grp in range(NG):
        if grp >= NGRPF:
            ctx.wait("y", "SQ", sqg[grp - NGRPF])
        for t in range(GRP):
            i = grp * GRP + t
            slot = i % (GRP * NGRPF)
            ctx.ld(HROT[:, slot * 1024:(slot + 1) * 1024],
                   ag2_out[i * 128:(i + 1) * 128, :])
        lqg[grp] = ctx.lq()
        ctx.wait("g", "LQ", lqg[grp])
        for t in range(GRP):
            i = grp * GRP + t
            slot = i % (GRP * NGRPF)
            nc.gpsimd.indirect_dma_start(
                xbuf, IndirectOffsetOnAxis(idx32[:, i:i + 1], 0),
                HROT[:, slot * 1024:(slot + 1) * 1024], None,
                bounds_check=bc, oob_is_err=False,
            ).then_inc(ctx.SP.h, 16)
            ctx.SP.n += 16
        sqg[grp] = ctx.sq()
    sp_scat_all = ctx.SQ.n

    W2B = ctx.aalloc(2 * 1024)
    XTOK = ctx.aalloc(2 * 1024)
    XFM = ctx.aalloc(NJ * g.STILE)
    INTB = ctx.aalloc(g.NIO * g.STILE)
    sp_e = 0
    sq_e = 0
    for stile in range(g.NST):
        sbase = stile * g.STILE
        ctx.wait("y", "SQ", sp_scat_all)
        lq_xt = {}
        mm_xt = {}
        for ss in range(min(2, g.NSS)):
            ctx.ld(XTOK[:, ss * 1024:(ss + 1) * 1024],
                   xbuf[sbase + ss * 128:sbase + (ss + 1) * 128, :])
            lq_xt[ss] = ctx.lq()
        for ss in range(g.NSS):
            b0 = 4 + (ss % 2) * 2
            ctx.bank_wait(b0)
            ctx.bank_wait(b0 + 1)
            ctx.wait("t", "LQ", lq_xt[ss])
            if stile > 0 and ss == 0:
                ctx.wait("t", "MM", mm_w1_prev)
            for j in range(NJ):
                bank = b0 + j // 4
                T_("transpose", ctx.ps[bank][:, (j % 4) * 128:(j % 4 + 1) * 128],
                   XTOK[:, (ss % 2) * 1024 + j * 128:(ss % 2) * 1024 + (j + 1) * 128],
                   ident[:])
            mm_xt[ss] = ctx.mm_inc()
            if ss + 2 < g.NSS:
                ctx.wait("y", "MM", mm_xt[ss])
                ctx.ld(XTOK[:, (ss % 2) * 1024:(ss % 2) * 1024 + 1024],
                       xbuf[sbase + (ss + 2) * 128:sbase + (ss + 3) * 128, :])
                lq_xt[ss + 2] = ctx.lq()
            ctx.wait("v", "MM", mm_xt[ss])
            for j in range(NJ):
                bank = b0 + j // 4
                V("tensor_copy", XFM[:, j * g.STILE + ss * 128:j * g.STILE + (ss + 1) * 128],
                  ctx.ps[bank][:, (j % 4) * 128:(j % 4 + 1) * 128])
            ve_x = ctx.ve_inc()
            ctx.bank_set(b0, "VE", ve_x)
            ctx.bank_set(b0 + 1, "VE", ve_x)
        ve_xfm = ctx.VE.n
        if stile > 0:
            ctx.wait("s", "MM", mm_e_prev)
        for io in range(g.NIO):
            bank = io % 2
            ctx.bank_wait(bank)
            ctx.wait("t", "LQ", ld_w1)
            ctx.wait("t", "VE", ve_xfm)
            for cin in range(NJ):
                T_("matmul", ctx.ps[bank][:, 0:g.STILE],
                   W1B[:, cin * g.I + io * 128:cin * g.I + (io + 1) * 128],
                   XFM[:, cin * g.STILE:(cin + 1) * g.STILE],
                   start=(cin == 0), stop=(cin == NJ - 1))
            mm1 = ctx.mm_inc()
            ctx.wait("s", "MM", mm1)
            S("activation", INTB[:, io * g.STILE:(io + 1) * g.STILE],
              ctx.ps[bank][:, 0:g.STILE], AF.Relu)
            ctx.bank_set(bank, "SE", ctx.se_inc())
        se_int = ctx.SE.n
        mm_w1_prev = ctx.MM.n
        nh = 2 if C > 512 else 1
        ld_w2 = {}
        mm_e = {}
        for io in range(min(2, g.NIO)):
            ld_w2[io] = ctx.ld(W2B[:, (io % 2) * 1024:(io % 2) * 1024 + 1024],
                               ap["w2T"][io * 128:(io + 1) * 128, :])
            ld_w2[io] = ctx.lq()
        for io in range(g.NIO):
            ctx.wait("t", "LQ", ld_w2[io])
            ctx.wait("t", "SE", se_int)
            if io == 0:
                for b in range(g.NSS * nh):
                    ctx.bank_wait(b)
            for ss in range(g.NSS):
                for h in range(nh):
                    bank = ss * nh + h
                    T_("matmul", ctx.ps[bank][:, 0:512],
                       INTB[:, io * g.STILE + ss * 128:io * g.STILE + (ss + 1) * 128],
                       W2B[:, (io % 2) * 1024 + h * 512:(io % 2) * 1024 + (h + 1) * 512],
                       start=(io == 0), stop=(io == g.NIO - 1))
            mm_e[io] = ctx.mm_inc()
            if io + 2 < g.NIO:
                ctx.wait("y", "MM", mm_e[io])
                ld_w2[io + 2] = ctx.ld(W2B[:, (io % 2) * 1024:(io % 2) * 1024 + 1024],
                                       ap["w2T"][(io + 2) * 128:(io + 3) * 128, :])
                ld_w2[io + 2] = ctx.lq()
        mm_e_prev = mm_e[g.NIO - 1]
        ctx.wait("v", "MM", mm_e_prev)
        sq_ss = {}
        for ss in range(g.NSS):
            pair = (S1, S2) if ss % 2 == 0 else (S3, S4)
            if ss >= 2:
                ctx.wait("v", "SQ", sq_ss[ss - 2])
            elif stile > 0:
                ctx.wait("v", "SQ", sq_e)
            for h in range(nh):
                bank = ss * nh + h
                V("tensor_copy", pair[h][:, 0:512], ctx.ps[bank][:, 0:512])
                ctx.bank_set(bank, "VE", ctx.VE.n + 1)
            ve_ss = ctx.ve_inc()
            ctx.wait("g", "VE", ve_ss)
            for h in range(nh):
                sp_e = ctx.sp(ag3_in[sbase + ss * 128:sbase + (ss + 1) * 128,
                                     h * 512:h * 512 + 512], pair[h][:, 0:512])
                if "eout_dbg" in ctx.dbg_set:
                    sp_e = ctx.sp(eout_dbg[sbase + ss * 128:sbase + (ss + 1) * 128,
                                           h * 512:h * 512 + 512], pair[h][:, 0:512])
            sq_ss[ss] = ctx.sq()
        sq_e = ctx.SQ.n
        ctx.allgather(ag3_in[sbase:sbase + g.STILE, :], ag3c_out[stile])
    cc_ag3 = ctx.CC.n
    ctx.phase_barrier()
    ctx.areset(am7)

    am8 = ctx.amark()
    R1S = ctx.aalloc(2 * g.NST * 1024)
    R2S = ctx.aalloc(2 * g.NST * 1024)
    R1 = ctx.aalloc(2 * 1024)
    R2 = ctx.aalloc(2 * 1024)
    XMB = ctx.aalloc(2 * 1024)
    YB = ctx.aalloc(2 * 1024)
    sp_y = 0
    ve_y = {}
    sp_y_hist = {}
    bc3 = nc.gpsimd.alloc_register("bc3reg")
    nc.gpsimd.reg_mov(bc3, 8 * g.STILE - 1)
    for i in range(TL // 128):
        par = i % 2
        for s in range(g.NST):
            V("memset", R1S[:, (par * g.NST + s) * 1024:(par * g.NST + s + 1) * 1024], 0.0)
            V("memset", R2S[:, (par * g.NST + s) * 1024:(par * g.NST + s + 1) * 1024], 0.0)
        ve_z = ctx.ve_inc()
        ctx.wait("g", "CC", cc_ag3)
        ctx.wait("g", "VE", ve_fld)
        ctx.wait("g", "VE", ve_z)
        for s in range(g.NST):
            nc.gpsimd.indirect_dma_start(
                R1S[:, (par * g.NST + s) * 1024:(par * g.NST + s + 1) * 1024], None,
                ag3c_out[s], IndirectOffsetOnAxis(idxf1s[:, s * (TL // 128) + i:
                                                         s * (TL // 128) + i + 1], 0),
                bounds_check=bc3, oob_is_err=False,
            ).then_inc(ctx.SP.h, 16)
            ctx.SP.n += 16
            nc.gpsimd.indirect_dma_start(
                R2S[:, (par * g.NST + s) * 1024:(par * g.NST + s + 1) * 1024], None,
                ag3c_out[s], IndirectOffsetOnAxis(idxf2s[:, s * (TL // 128) + i:
                                                         s * (TL // 128) + i + 1], 0),
                bounds_check=bc3, oob_is_err=False,
            ).then_inc(ctx.SP.h, 16)
            ctx.SP.n += 16
        sq_g = ctx.sq()
        if i >= 2:
            ctx.wait("y", "VE", ve_y[i - 2])
        ld_xm = ctx.ld(XMB[:, par * 1024:(par + 1) * 1024],
                       xmid_sp[i * 128:(i + 1) * 128, :])
        ld_xm = ctx.lq()
        ctx.wait("v", "SQ", sq_g)
        ctx.wait("v", "LQ", ld_xm)
        if i >= 2:
            ctx.wait("v", "SQ", sp_y_hist[i - 2])
        for RS, RD_ in ((R1S, R1), (R2S, R2)):
            b0 = par * g.NST
            if g.NST == 1:
                V("tensor_copy", RD_[:, par * 1024:(par + 1) * 1024],
                  RS[:, b0 * 1024:(b0 + 1) * 1024])
            else:
                V("tensor_add", RD_[:, par * 1024:(par + 1) * 1024],
                  RS[:, b0 * 1024:(b0 + 1) * 1024],
                  RS[:, (b0 + 1) * 1024:(b0 + 2) * 1024])
                for s in range(2, g.NST):
                    V("tensor_add", RD_[:, par * 1024:(par + 1) * 1024],
                      RD_[:, par * 1024:(par + 1) * 1024],
                      RS[:, (b0 + s) * 1024:(b0 + s + 1) * 1024])
        rb = routebuf[:, i * 18:(i + 1) * 18]
        V("tensor_mul", SB[:, 0:1], rb[:, 16:17], fld32[:, i:i + 1])
        V("tensor_mul", SB[:, 1:2], rb[:, 17:18], fld32[:, 32 + i:33 + i])
        V("tensor_single_scalar", YB[:, par * 1024:(par + 1) * 1024],
          R1[:, par * 1024:(par + 1) * 1024], SB[:, 0:1], ALU.mult)
        V("scalar_tensor_tensor", YB[:, par * 1024:(par + 1) * 1024],
          R2[:, par * 1024:(par + 1) * 1024], SB[:, 1:2],
          YB[:, par * 1024:(par + 1) * 1024], ALU.mult, ALU.add)
        ctx.wait("v", "VE", ve_gbbc)
        V("tensor_mul", YB[:, par * 1024:(par + 1) * 1024],
          YB[:, par * 1024:(par + 1) * 1024], gbbc[:])
        V("tensor_add", YB[:, par * 1024:(par + 1) * 1024],
          YB[:, par * 1024:(par + 1) * 1024], XMB[:, par * 1024:(par + 1) * 1024])
        ve_y[i] = ctx.ve_inc()
        ctx.wait("g", "VE", ve_y[i])
        sp_y = ctx.sp(y_out[i * 128:(i + 1) * 128, :], YB[:, par * 1024:(par + 1) * 1024])
        sp_y_hist[i] = ctx.sq()
    ctx.wait("g", "SP", sp_y)
    ctx.phase_barrier()
    ctx.areset(am8)
    return nc, ctx



def _vec_tile(v):
    return np.ascontiguousarray(np.asarray(v, np.float32).reshape(8, 128).T)


def make_inmaps(g: Geo, inputs):
    x = np.asarray(inputs["L0_attn_x"], np.float32).reshape(g.T, g.C)
    vecs = np.stack([_vec_tile(inputs[k]) for k in (
        "L0_attn_ln1_w", "L0_attn_ln1_b", "L0_attn_ln2_w", "L0_attn_ln2_b",
        "L0_attn_time_decay", "L0_attn_time_first", "L0_attn_tmk", "L0_attn_tmv",
        "L0_attn_tmr", "L0_attn_f_tmk", "L0_attn_f_tmr")], 0)
    shared = {
        "kwT": np.ascontiguousarray(np.asarray(inputs["L0_attn_att_kw"], np.float32).T),
        "vwT": np.ascontiguousarray(np.asarray(inputs["L0_attn_att_vw"], np.float32).T),
        "rwT": np.ascontiguousarray(np.asarray(inputs["L0_attn_att_rw"], np.float32).T),
        "owT": np.ascontiguousarray(np.asarray(inputs["L0_attn_att_ow"], np.float32).T),
        "frwT": np.ascontiguousarray(np.asarray(inputs["L0_attn_f_rw"], np.float32).T),
        "fkwT": np.ascontiguousarray(np.asarray(inputs["L0_attn_f_kw"], np.float32).T),
        "fvwT": np.ascontiguousarray(np.asarray(inputs["L0_attn_f_vw"], np.float32).T),
        "embT": np.ascontiguousarray(np.asarray(inputs["L0_ffn_expert_embeddings"], np.float32).T),
        "wmod": np.ascontiguousarray(np.asarray(inputs["L0_wmod"], np.float32)),
        "cmod": _vec_tile(np.asarray(inputs["L0_c"], np.float32).reshape(g.C)),
        "vecs": vecs,
        "ident": np.eye(128, dtype=np.float32),
        "iotarev": np.ascontiguousarray(np.broadcast_to(
            np.arange(g.TC - 1, -1, -1, dtype=np.float32), (128, g.TC))),
        "oh8id": np.eye(8, dtype=np.float32),
        "ecap": (np.arange(8, dtype=np.float32) * g.CAP - 1.0).reshape(8, 1),
    }
    w1 = np.asarray(inputs["L0_ffn_first_linear"], np.float32)
    w2 = np.asarray(inputs["L0_ffn_second_linear"], np.float32)
    maps = []
    for i in range(g.NCORES):
        seg = x[i * g.TL:(i + 1) * g.TL]
        bnd = np.zeros((1, g.C), np.float32) if i == 0 else x[i * g.TL - 1:i * g.TL]
        xT = np.ascontiguousarray(np.concatenate([bnd, seg], 0).T)
        masklt = (np.arange(8) < i).astype(np.float32)
        m = dict(shared)
        m.update({
            "xT": xT,
            "w1T": np.ascontiguousarray(w1[i].T),
            "w2T": np.ascontiguousarray(w2[i].T),
            "ohmy": np.eye(8, dtype=np.float32)[:, i:i + 1].copy(),
            "ohs": (np.eye(8, dtype=np.float32)[:, i:i + 1] *
                    (-(i * g.CAP + g.BIG))).astype(np.float32),
            "masklt8": np.tile(masklt[None, :], (8, 1)).astype(np.float32),
            "mneg8": ((1.0 - np.tile(masklt[None, :], (8, 1))) * NEG).astype(np.float32),
            "tlm8": (np.tile(masklt[None, :], (8, 1)) * g.TL).astype(np.float32),
            "mprev8": np.tile((np.arange(8) == i - 1).astype(np.float32)[None, :], (8, 1)).copy(),
            "ohown": np.tile((np.arange(8) == i).astype(np.float32)[None, :], (8, 1)).copy(),
            "maskfirst": np.full((8, 1), 0.0 if i == 0 else 1.0, np.float32),
            "maskfirstP": np.full((128, 1), 0.0 if i == 0 else 1.0, np.float32),
            "xbnd_in": _vec_tile(bnd[0]),
        })
        maps.append(m)
    return maps


def kernel(**inputs):
    g = Geo(16384, 512)
    nc, ctx = build_program(g)
    maps = make_inmaps(g, inputs)
    res = run_bass_kernel_spmd(nc, maps, core_ids=list(range(g.NCORES)))
    y = np.concatenate([res.results[i]["y"] for i in range(g.NCORES)], 0)
    return y.reshape(1, g.T, g.C).astype(np.float32)


# version 10
# speedup vs baseline: 1.1079x; 1.0482x over previous
import numpy as np
import concourse.bass as bass
import concourse.mybir as mybir
from concourse.bass_core_rust import IndirectOffsetOnAxis
from concourse.bass_types import AP as BassAP
from concourse.bass_utils import run_bass_kernel_spmd


def mkap(apref, offset, dims):
    return BassAP(apref.tensor, offset, [list(d) for d in dims])

F32 = mybir.dt.float32
I16 = mybir.dt.int16
I32 = mybir.dt.int32
AF = mybir.ActivationFunctionType
ALU = mybir.AluOpType
AX = mybir.AxisListType.X

LN_EPS = 1e-5
NEG = -1e38


class Geo:
    def __init__(self, T=16384, TC=512):
        self.T = T
        self.NCORES = 8
        self.TL = T // 8
        assert TC >= 128 and TC % 128 == 0 and TC <= 512
        self.TC = TC
        self.NCHK = self.TL // TC
        self.C = 1024
        self.NJ = 8
        self.F = 4096
        self.FS = 512
        self.NFS = self.F // self.FS
        self.I = 2048
        self.NIO = self.I // 128
        self.E = 8
        self.CAP = T // 8
        self.BIG = 20000
        self.STILE = min(512, self.CAP)
        self.NST = self.CAP // self.STILE
        self.NSS = self.STILE // 128


class Sem:
    def __init__(self, nc, name):
        self.h = nc.semaphore(name).__enter__()
        self.n = 0


class Ctx:
    def __init__(self, nc, g):
        self.nc = nc
        self.g = g
        self.LD = Sem(nc, "sLD")
        self.SP = Sem(nc, "sSP")
        self.MM = Sem(nc, "sMM")
        self.VE = Sem(nc, "sVE")
        self.SE = Sem(nc, "sSE")
        self.CC = Sem(nc, "sCC")
        self.LQ = Sem(nc, "sLQ")
        self.SQ = Sem(nc, "sSQ")
        self.sem = {"LD": self.LD, "SP": self.SP, "MM": self.MM,
                    "VE": self.VE, "SE": self.SE, "CC": self.CC,
                    "LQ": self.LQ, "SQ": self.SQ}
        self.last = {}
        self.ACOLS = 43008
        self.arena = nc.alloc_sbuf_tensor("arena", [128, self.ACOLS], F32)
        self.arena16 = self.arena.bitcast(mybir.dt.bfloat16)
        self.aoff = 0
        self.amax = 0
        self.ps = [nc.alloc_psum_tensor(f"psb{i}", [128, 512], F32) for i in range(8)]
        self.bank_free = [("VE", 0) for _ in range(8)]
        self.dbg_set = set()

    def amark(self):
        return self.aoff

    def areset(self, m):
        self.aoff = m

    def aalloc(self, cols):
        assert self.aoff + cols <= self.ACOLS, f"arena overflow {self.aoff}+{cols}"
        t = self.arena[:, self.aoff:self.aoff + cols]
        self.aoff += cols
        self.amax = max(self.amax, self.aoff)
        return t

    def aalloc16(self, cols16):
        f32c = (cols16 + 1) // 2
        assert self.aoff + f32c <= self.ACOLS, f"arena overflow {self.aoff}+{f32c}"
        t = self.arena16[:, 2 * self.aoff:2 * self.aoff + cols16]
        self.aoff += f32c
        self.amax = max(self.amax, self.aoff)
        return t

    def dram(self, name, shape, dtype=F32):
        kind = "ExternalOutput" if name in self.dbg_set else None
        if kind:
            return self.nc.dram_tensor(name, shape, dtype, kind=kind)
        return self.nc.dram_tensor(name, shape, dtype)

    def _tr(self, eng, inst):
        self.last[eng] = inst
        return inst

    def t(self, op, *a, **k):
        return self._tr("t", getattr(self.nc.tensor, op)(*a, **k))

    DRAINS = True

    def v(self, op, *a, **k):
        inst = getattr(self.nc.vector, op)(*a, **k)
        if self.DRAINS:
            inst = self.nc.vector.drain()
        return self._tr("v", inst)

    def s(self, op, *a, **k):
        inst = getattr(self.nc.scalar, op)(*a, **k)
        if self.DRAINS:
            inst = self.nc.scalar.drain()
        return self._tr("s", inst)

    def mm_inc(self):
        self.last["t"].then_inc(self.MM.h, 1)
        self.MM.n += 1
        return self.MM.n

    def ve_inc(self):
        self.last["v"].then_inc(self.VE.h, 1)
        self.VE.n += 1
        return self.VE.n

    def se_inc(self):
        self.last["s"].then_inc(self.SE.h, 1)
        self.SE.n += 1
        return self.SE.n

    def ld(self, out, in_, transpose=False, slow=False):
        kw = {}
        if transpose:
            kw["transpose"] = True
        if slow:
            kw["allow_slow_non_contiguous"] = True
        self.nc.sync.dma_start(out=out, in_=in_, **kw).then_inc(self.LD.h, 16)
        self.LD.n += 16
        return self.LD.n

    def sp(self, out, in_):
        self.nc.gpsimd.dma_start(out=out, in_=in_).then_inc(self.SP.h, 16)
        self.SP.n += 16
        return self.SP.n

    def lq(self):
        self.nc.sync.wait_ge(self.LD.h, self.LD.n)
        self.nc.sync.sem_inc(self.LQ.h, 1)
        self.LQ.n += 1
        return self.LQ.n

    def sq(self):
        self.nc.gpsimd.wait_ge(self.SP.h, self.SP.n)
        self.nc.gpsimd.sem_inc(self.SQ.h, 1)
        self.SQ.n += 1
        return self.SQ.n

    def wait(self, eng, semname, val):
        if val <= 0:
            return
        e = {"t": self.nc.tensor, "v": self.nc.vector, "s": self.nc.scalar,
             "g": self.nc.gpsimd, "y": self.nc.sync}[eng]
        e.wait_ge(self.sem[semname].h, val)

    def bank_wait(self, i):
        s, v = self.bank_free[i]
        self.wait("t", s, v)

    def bank_set(self, i, semname, val):
        self.bank_free[i] = (semname, val)

    def phase_barrier(self):
        self.lq()
        self.sq()
        cur = {"LQ": self.LQ.n, "SQ": self.SQ.n, "MM": self.MM.n,
               "VE": self.VE.n, "SE": self.SE.n}
        for eng in ("y", "g", "v", "s", "t"):
            for k, v in cur.items():
                self.wait(eng, k, v)

    def dbg_dump(self, name, ap_, shape, dtype=F32):
        if name not in self.dbg_set:
            return
        t = self.nc.dram_tensor(name, shape, dtype, kind="ExternalOutput")
        self.wait("g", "VE", self.VE.n)
        self.wait("g", "SE", self.SE.n)
        self.sp(t.ap(), ap_)

    def allgather(self, bin_ap, bout_ap):
        self.nc.gpsimd.collective_compute(
            "AllGather", ALU.bypass,
            replica_groups=[list(range(self.g.NCORES))],
            ins=[bin_ap.opt()], outs=[bout_ap.opt()],
        ).then_inc(self.CC.h, 1)
        self.CC.n += 1
        return self.CC.n


def build_program(g: Geo, dbg=()):
    nc = bass.Bass("TRN2", target_bir_lowering=False, debug=False,
                   detect_race_conditions=False)
    ctx = Ctx(nc, g)
    ctx.dbg_set = set(dbg)
    C, NJ, TC, NCHK, TL, T = g.C, g.NJ, g.TC, g.NCHK, g.TL, g.T
    V = ctx.v
    S = ctx.s
    T_ = ctx.t

    ap = {}
    def inp(name, shape, dtype=F32):
        ap[name] = nc.dram_tensor(name, shape, dtype, kind="ExternalInput").ap()

    inp("xT", [C, TL + 1])
    for nm in ("kwT", "vwT", "rwT", "owT", "frwT"):
        inp(nm, [C, C])
    inp("fkwT", [C, g.F]); inp("fvwT", [g.F, C])
    inp("w1T", [C, g.I]); inp("w2T", [g.I, C])
    inp("embT", [C, g.E]); inp("wmod", [C, 6 * C])
    inp("cmod", [128, NJ]); inp("vecs", [11, 128, NJ])
    inp("ident", [128, 128]); inp("iotarev", [128, TC])
    inp("oh8id", [8, 8]); inp("ohmy", [8, 1]); inp("ohs", [8, 1]); inp("ecap", [8, 1])
    inp("masklt8", [8, 8]); inp("mneg8", [8, 8]); inp("tlm8", [8, 8])
    inp("mprev8", [8, 8]); inp("ohown", [8, 8])
    inp("maskfirst", [8, 1]); inp("maskfirstP", [128, 1])
    inp("xbnd_in", [128, NJ])

    y_out = nc.dram_tensor("y", [TL, C], F32, kind="ExternalOutput").ap()

    kvs_sp = ctx.dram("kvs_sp", [3 * C, TL]).ap()
    h0_sp = ctx.dram("h0_sp", [C, TL]).ap()
    a_sp = ctx.dram("a_sp", [C, TL]).ap()
    xk2_sp = ctx.dram("xk2_sp", [C, TL]).ap()
    xk2h_sp = ctx.dram("xk2h_sp", [C, TL], mybir.dt.bfloat16).ap()
    xk2l_sp = ctx.dram("xk2l_sp", [C, TL], mybir.dt.bfloat16).ap()
    sig_sp = ctx.dram("sig_sp", [C, TL]).ap()
    xmid_sp = ctx.dram("xmid_sp", [TL, C]).ap()
    svrow_d = ctx.dram("svrow_d", [1, T], I16).ap()
    fld_d = ctx.dram("fld_d", [4, 16 * 128], I16).ap()
    m12_d = ctx.dram("m12_d", [16, TL]).ap()
    gbrow_d = ctx.dram("gbrow_d", [1, C]).ap()
    xbuf = ctx.dram("xbuf", [g.CAP, C]).ap()
    if "h1_dbg" in ctx.dbg_set:
        h1_dbg = nc.dram_tensor("h1_dbg", [TL, C], F32, kind="ExternalOutput").ap()
    if "eout_dbg" in ctx.dbg_set:
        eout_dbg = nc.dram_tensor("eout_dbg", [g.CAP, C], F32, kind="ExternalOutput").ap()
    ag1_in = ctx.dram("ag1_in", [24, 128]).ap()
    ag1_out = ctx.dram("ag1_out", [192, 128]).ap()
    ag1b_in = ctx.dram("ag1b_in", [8, 128]).ap()
    ag1b_out = ctx.dram("ag1b_out", [64, 128]).ap()
    ag2_in = ctx.dram("ag2_in", [TL, C]).ap()
    ag2_out = ctx.dram("ag2_out", [T, C]).ap()
    ag2b_in = ctx.dram("ag2b_in", [8, TL]).ap()
    ag2b_out = ctx.dram("ag2b_out", [64, TL]).ap()
    ag3_in = ctx.dram("ag3_in", [g.CAP, C]).ap()
    ag3c_out = [ctx.dram(f"ag3c_out{s}", [8 * g.STILE, C]).ap() for s in range(g.NST)]

    A = nc.alloc_sbuf_tensor
    vconst = A("vconst", [128, 11 * NJ], F32)
    dconst = A("dconst", [128, 8 * NJ], F32)
    ident = A("identt", [128, 128], F32)
    embTs = A("embTs", [128, NJ * g.E], F32)
    oh8id = A("oh8idt", [8, 8], F32)
    ohmy = A("ohmyt", [8, 1], F32)
    ohs = A("ohst", [8, 1], F32)
    ecap = A("ecapt", [8, 1], F32)
    masklt8 = A("masklt8t", [8, 8], F32)
    mneg8 = A("mneg8t", [8, 8], F32)
    tlm8 = A("tlm8t", [8, 8], F32)
    mprev8 = A("mprev8t", [8, 8], F32)
    ohown = A("ohownt", [8, 8], F32)
    maskfirst = A("maskfirstt", [8, 1], F32)
    maskfirstP = A("maskfirstPt", [128, 1], F32)
    onescol = A("onescol", [128, 1], F32)
    onesrow = A("onesrow", [1, 128], F32)
    ones8 = A("ones8", [8, 1], F32)
    state0 = A("state0", [128, 3 * NJ], F32)
    runsum = A("runsum", [128, 3 * NJ], F32)
    csum = A("csum", [128, 3 * NJ], F32)
    u2bnd = A("u2bnd", [128, NJ], F32)
    ubnd = A("ubnd", [128, NJ], F32)
    usave = A("usave", [128, NJ], F32)
    u2save = A("u2save", [128, NJ], F32)
    trow = A("trow", [8, 3 * 128], F32)
    routebuf = A("routebuf", [128, (TL // 128) * 18], F32)
    svtok = A("svtok", [128, T // 128], I16)
    fldtok = A("fldtok", [128, 64], I16)
    fld32 = A("fld32", [128, 64], F32)
    idxf1 = A("idxf1", [128, TL // 128], I32)
    idxf2 = A("idxf2", [128, TL // 128], I32)
    idxf1s = A("idxf1s", [128, 4 * (TL // 128)], I32)
    idxf2s = A("idxf2s", [128, 4 * (TL // 128)], I32)
    idxt = A("idxt", [128, 6 * (TL // 128)], I32)
    idx32 = A("idx32", [128, T // 128], I32)
    stats = A("stats", [1, 2608], F32)
    R16 = A("R16", [1, 1024], I16)
    SW = max(TC, 512)
    S1 = A("S1", [128, SW], F32)
    S2 = A("S2", [128, SW], F32)
    S3 = A("S3", [128, SW], F32)
    S4 = A("S4", [128, SW], F32)
    SQ2 = A("SQ2", [128, SW], F32)
    SB = A("SBs", [128, 32], F32)
    zz = A("zzt", [128, 1024], F32)
    lnscr = A("lnscr", [128, 2 * TC], F32)
    xbnd = A("xbndt", [128, NJ], F32)

    co = {"w": 0, "sa": 1, "ca1": 2, "ga": 3, "sb": 4, "cb1": 5, "gb": 6}
    vi = {"ln1w": 0, "ln1b": 1, "ln2w": 2, "ln2b": 3, "tdec": 4, "tfir": 5,
          "tmk": 6, "tmv": 7, "tmr": 8, "ftmk": 9, "ftmr": 10}

    def vc(name, j):
        return vconst[:, vi[name] * NJ + j: vi[name] * NJ + j + 1]

    def vct(name):
        return vconst[:, vi[name] * NJ:(vi[name] + 1) * NJ]

    def dc(name, j):
        return dconst[:, co[name] * NJ + j: co[name] * NJ + j + 1]

    def dct(name):
        return dconst[:, co[name] * NJ:(co[name] + 1) * NJ]

    def a_(buf, j, w=TC):
        return buf[:, j * w:(j + 1) * w]

    def a1_(buf, j):
        return buf[:, j * (TC + 1):(j + 1) * (TC + 1)]

    ctx.ld(vconst[:], mkap(ap["vecs"], 0, [[8, 128], [1024, 11], [1, NJ]]))
    ctx.ld(ident[:], ap["ident"])
    ctx.ld(embTs[:], mkap(ap["embT"], 0, [[g.E, 128], [128 * g.E, NJ], [1, g.E]]))
    for nm, tt in (("oh8id", oh8id), ("ohmy", ohmy), ("ohs", ohs), ("ecap", ecap),
                   ("masklt8", masklt8), ("mneg8", mneg8), ("tlm8", tlm8),
                   ("mprev8", mprev8), ("ohown", ohown), ("maskfirst", maskfirst),
                   ("maskfirstP", maskfirstP)):
        ctx.ld(tt[:], ap[nm])
    ctx.ld(SB[:, 0:NJ], ap["cmod"])
    ld_xb = ctx.ld(xbnd[:], ap["xbnd_in"])
    ld_const = ctx.lq()

    ctx.wait("v", "LQ", ld_const)
    V("memset", onescol[:], 1.0)
    V("memset", onesrow[:], 1.0)
    V("memset", ones8[:], 1.0)
    V("memset", zz[:], 0.0)
    V("memset", dconst[:], 0.0)
    V("memset", runsum[:, 0:NJ], NEG)
    V("memset", runsum[:, NJ:3 * NJ], 0.0)
    ve0 = ctx.ve_inc()
    ctx.wait("s", "LQ", ld_const)
    ctx.wait("s", "VE", ve0)
    S("activation", dct("w"), vct("tdec"), AF.Exp)
    S("activation", SB[:, NJ:2 * NJ], SB[:, 0:NJ], AF.Silu)
    se_silu = ctx.se_inc()
    ctx.wait("v", "SE", se_silu)
    V("tensor_scalar_mul", dct("w"), dct("w"), -1.0)
    ve_w = ctx.ve_inc()

    am0 = ctx.amark()
    modsb = ctx.aalloc(6 * C)
    wmbuf = ctx.aalloc(2 * NJ * 512)
    mm_prev = {}
    for o in range(12):
        ld_wm = 0
        for j in range(NJ):
            if o >= 2:
                ctx.wait("y", "MM", mm_prev[o - 2])
            ld_wm = ctx.ld(wmbuf[:, (o % 2) * NJ * 512 + j * 512:(o % 2) * NJ * 512 + (j + 1) * 512],
                           ap["wmod"][j * 128:(j + 1) * 128, o * 512:(o + 1) * 512])
        ld_wm = ctx.lq()
        bank = o % 2
        ctx.bank_wait(bank)
        ctx.wait("t", "LQ", ld_wm)
        ctx.wait("t", "SE", se_silu)
        for j in range(NJ):
            T_("matmul", ctx.ps[bank][0:1, 0:512], SB[:, NJ + j:NJ + j + 1],
               wmbuf[:, (o % 2) * NJ * 512 + j * 512:(o % 2) * NJ * 512 + (j + 1) * 512],
               start=(j == 0), stop=(j == NJ - 1))
        mm_prev[o] = ctx.mm_inc()
        ctx.wait("v", "MM", mm_prev[o])
        V("tensor_copy", modsb[0:1, o * 512:(o + 1) * 512], ctx.ps[bank][0:1, 0:512])
        ctx.bank_set(bank, "VE", ctx.ve_inc())

    bank = 2
    ctx.bank_wait(bank)
    ctx.wait("t", "VE", ctx.VE.n)
    for v6 in range(6):
        for j in range(NJ):
            T_("matmul", ctx.ps[bank][:, v6 * NJ + j:v6 * NJ + j + 1],
               modsb[0:1, v6 * C + j * 128:v6 * C + (j + 1) * 128],
               onesrow[0:1, 0:1], start=True, stop=True)
    mm_modT = ctx.mm_inc()
    ctx.wait("v", "MM", mm_modT)
    V("tensor_copy", dconst[:, NJ:7 * NJ], ctx.ps[bank][:, 0:48])
    V("tensor_scalar_add", dct("ca1"), dct("ca1"), 1.0)
    V("tensor_scalar_add", dct("cb1"), dct("cb1"), 1.0)
    for nm in ("ga", "gb"):
        V("tensor_single_scalar", SB[:, 16:16 + NJ], dct(nm), 0.0, ALU.is_ge)
        V("tensor_scalar", SB[:, 16:16 + NJ], SB[:, 16:16 + NJ], 2.0, 1.0, ALU.mult, ALU.subtract)
        V("tensor_mul", SB[:, 24:24 + NJ], dct(nm), SB[:, 16:16 + NJ])
        V("tensor_scalar_max", SB[:, 24:24 + NJ], SB[:, 24:24 + NJ], 0.25)
        V("tensor_mul", dct(nm), SB[:, 16:16 + NJ], SB[:, 24:24 + NJ])
    ve_dconst = ctx.ve_inc()
    ctx.bank_set(bank, "VE", ve_dconst)

    ctx.dbg_dump("dconst_d", dconst[:], [128, 8 * NJ])
    ctx.dbg_dump("vconst_d", vconst[:], [128, 11 * NJ])
    bank = 3
    ctx.bank_wait(bank)
    ctx.wait("t", "VE", ve_dconst)
    T_("transpose", ctx.ps[bank][0:8, 0:128], dct("w"), ident[:])
    T_("transpose", ctx.ps[bank][0:8, 128:256], vct("ln2w"), ident[:])
    T_("transpose", ctx.ps[bank][0:8, 256:384], vct("ln2b"), ident[:])
    mm_trow = ctx.mm_inc()
    ctx.wait("v", "MM", mm_trow)
    V("tensor_copy", trow[:], ctx.ps[bank][0:8, 0:384])
    ctx.bank_set(bank, "VE", ctx.ve_inc())

    def mini_ln(src, dst):
        bank = 4
        ctx.bank_wait(bank)
        ctx.wait("t", "VE", ctx.VE.n)
        ctx.wait("t", "SE", ctx.SE.n)
        ctx.wait("t", "LQ", ctx.LQ.n)
        T_("matmul", ctx.ps[bank][0:1, 0:NJ], onescol[:, 0:1], src, start=True, stop=True)
        ctx.mm_inc()
        ctx.wait("s", "VE", ctx.VE.n)
        ctx.wait("s", "LQ", ctx.LQ.n)
        ctx.wait("s", "MM", ctx.MM.n)
        S("activation", SB[:, 24:24 + NJ], src, AF.Square)
        se1 = ctx.se_inc()
        ctx.wait("t", "SE", se1)
        T_("matmul", ctx.ps[bank][0:1, NJ:2 * NJ], onescol[:, 0:1], SB[:, 24:24 + NJ],
           start=True, stop=True)
        mm2 = ctx.mm_inc()
        ctx.wait("v", "MM", mm2)
        st = stats
        V("reduce_sum", st[0:1, 0:1], ctx.ps[bank][0:1, 0:NJ], axis=AX)
        V("reduce_sum", st[0:1, 1:2], ctx.ps[bank][0:1, NJ:2 * NJ], axis=AX)
        V("tensor_scalar_mul", st[0:1, 0:2], st[0:1, 0:2], 1.0 / C)
        V("tensor_mul", st[0:1, 2:3], st[0:1, 0:1], st[0:1, 0:1])
        V("tensor_sub", st[0:1, 1:2], st[0:1, 1:2], st[0:1, 2:3])
        V("tensor_scalar_add", st[0:1, 1:2], st[0:1, 1:2], LN_EPS)
        ve1 = ctx.ve_inc()
        ctx.wait("s", "VE", ve1)
        S("activation", st[0:1, 3:4], st[0:1, 1:2], AF.Ln)
        se2 = ctx.se_inc()
        ctx.wait("v", "SE", se2)
        V("tensor_scalar_mul", st[0:1, 3:4], st[0:1, 3:4], -0.5)
        ve1b = ctx.ve_inc()
        ctx.wait("s", "VE", ve1b)
        S("activation", st[0:1, 4:5], st[0:1, 3:4], AF.Exp)
        se2b = ctx.se_inc()
        ctx.wait("v", "SE", se2b)
        V("tensor_copy", st[0:1, 4:5], st[0:1, 4:5])
        ve2 = ctx.ve_inc()
        ctx.wait("t", "VE", ve2)
        T_("matmul", ctx.ps[bank][:, 16:17], onesrow[0:1, :], st[0:1, 4:5], start=True, stop=True)
        T_("matmul", ctx.ps[bank][:, 17:18], onesrow[0:1, :], st[0:1, 0:1], start=True, stop=True)
        mm3 = ctx.mm_inc()
        ctx.wait("v", "MM", mm3)
        V("tensor_single_scalar", dst, src, ctx.ps[bank][:, 17:18], ALU.subtract)
        V("tensor_single_scalar", dst, dst, ctx.ps[bank][:, 16:17], ALU.mult)
        ctx.bank_set(bank, "VE", ctx.ve_inc())

    ctx.dbg_dump("xbnd_d", xbnd[:], [128, NJ])
    mini_ln(xbnd[:], SB[:, 0:NJ])
    ctx.dbg_dump("lnb_d", SB[:, 0:NJ], [128, NJ])
    ctx.dbg_dump("stats1_d", stats[0:1, 0:8], [1, 8])
    V("tensor_mul", SB[:, 0:NJ], SB[:, 0:NJ], dct("ca1"))
    V("tensor_add", SB[:, 0:NJ], SB[:, 0:NJ], dct("sa"))
    ctx.ve_inc()
    ctx.dbg_dump("h0b_d", SB[:, 0:NJ], [128, NJ])
    mini_ln(SB[:, 0:NJ], ubnd[:])
    V("tensor_mul", ubnd[:], ubnd[:], vct("ln1w"))
    V("tensor_add", ubnd[:], ubnd[:], vct("ln1b"))
    V("tensor_single_scalar", ubnd[:], ubnd[:], maskfirstP[:, 0:1], ALU.mult)
    ve_ubnd = ctx.ve_inc()
    ctx.dbg_dump("ubnd_d", ubnd[:], [128, NJ])
    ctx.phase_barrier()
    ctx.areset(am0)

    def ln_stats(tile_of, width, waits=()):
        bs, bq = 4, 5
        ctx.bank_wait(bs)
        ctx.bank_wait(bq)
        vew = ctx.VE.n
        ctx.wait("t", "VE", vew)
        for eng_sem_val in waits:
            ctx.wait("t", eng_sem_val[0], eng_sem_val[1])
        mmq = {}
        for j in range(NJ):
            T_("matmul", ctx.ps[bs][0:1, 0:width], onescol[:, 0:1], tile_of(j),
               start=(j == 0), stop=(j == NJ - 1))
            ctx.mm_inc()
        ctx.wait("s", "VE", vew)
        for eng_sem_val in waits:
            ctx.wait("s", eng_sem_val[0], eng_sem_val[1])
        for j in range(NJ):
            buf = S4 if (j % 2 == 0) else SQ2
            if j >= 2:
                ctx.wait("s", "MM", mmq[j - 2])
            S("activation", buf[:, 0:width], tile_of(j), AF.Square)
            seq = ctx.se_inc()
            ctx.wait("t", "SE", seq)
            T_("matmul", ctx.ps[bq][0:1, 0:width], onescol[:, 0:1], buf[:, 0:width],
               start=(j == 0), stop=(j == NJ - 1))
            mmq[j] = ctx.mm_inc()
        ctx.wait("v", "MM", mmq[NJ - 1])
        st = stats
        V("tensor_scalar_mul", st[0:1, 0:width], ctx.ps[bs][0:1, 0:width], 1.0 / C)
        V("tensor_scalar_mul", st[0:1, 512:512 + width], ctx.ps[bq][0:1, 0:width], 1.0 / C)
        V("tensor_mul", st[0:1, 1024:1024 + width], st[0:1, 0:width], st[0:1, 0:width])
        V("tensor_sub", st[0:1, 512:512 + width], st[0:1, 512:512 + width],
          st[0:1, 1024:1024 + width])
        V("tensor_scalar_add", st[0:1, 512:512 + width], st[0:1, 512:512 + width], LN_EPS)
        ve1 = ctx.ve_inc()
        ctx.wait("s", "VE", ve1)
        S("activation", st[0:1, 1024:1024 + width], st[0:1, 512:512 + width], AF.Ln)
        se1 = ctx.se_inc()
        ctx.wait("v", "SE", se1)
        V("tensor_scalar_mul", st[0:1, 1024:1024 + width], st[0:1, 1024:1024 + width], -0.5)
        ve1b = ctx.ve_inc()
        ctx.wait("s", "VE", ve1b)
        S("activation", st[0:1, 1536:1536 + width], st[0:1, 1024:1024 + width], AF.Exp)
        se1b = ctx.se_inc()
        ctx.wait("v", "SE", se1b)
        V("tensor_mul", st[0:1, 0:width], st[0:1, 0:width], st[0:1, 1536:1536 + width])
        V("tensor_scalar_mul", st[0:1, 0:width], st[0:1, 0:width], -1.0)
        ve2 = ctx.ve_inc()
        ctx.wait("t", "VE", ve2)
        T_("matmul", ctx.ps[bs][:, 0:width], onesrow[0:1, :], st[0:1, 1536:1536 + width],
           start=True, stop=True)
        T_("matmul", ctx.ps[bq][:, 0:width], onesrow[0:1, :], st[0:1, 0:width],
           start=True, stop=True)
        mmb = ctx.mm_inc()
        ctx.wait("v", "MM", mmb)
        V("tensor_copy", lnscr[:, 0:width], ctx.ps[bs][:, 0:width])
        V("tensor_copy", lnscr[:, TC:TC + width], ctx.ps[bq][:, 0:width])
        vee = ctx.ve_inc()
        ctx.bank_set(bs, "VE", vee)
        ctx.bank_set(bq, "VE", vee)

    am1 = ctx.amark()
    iotarev = ctx.aalloc(TC)
    ctx.ld(iotarev, ap["iotarev"])
    A0 = ctx.aalloc(NJ * (TC + 1))
    A1 = ctx.aalloc(NJ * TC)
    A2 = ctx.aalloc(NJ * (TC + 1))
    A3 = ctx.aalloc(NJ * TC)
    KB = ctx.aalloc(NJ * TC)
    VB = ctx.aalloc(NJ * TC)
    RB = ctx.aalloc(NJ * TC)
    WBUF = ctx.aalloc(NJ * 1024)

    wname = {"k": "kwT", "v": "vwT", "r": "rwT"}
    spill_done = 0
    sq_chunk = 0
    mm_w_last = 0
    ve_mix_done = 0
    for c in range(NCHK):
        base = c * TC
        ld_x = 0
        for j in range(NJ):
            if c > 0:
                ctx.wait("y", "VE", ve_h0_prev)
            ld_x = ctx.ld(a1_(A0, j), ap["xT"][j * 128:(j + 1) * 128, base:base + TC + 1])
        ld_x = ctx.lq()
        ln_stats(lambda j: a1_(A0, j)[:, 1:TC + 1], TC, waits=(("LQ", ld_x),))
        ctx.wait("v", "LQ", ld_x)
        ctx.wait("v", "SQ", sq_chunk)
        for j in range(NJ):
            V("tensor_mul", a_(A1, j), a1_(A0, j)[:, 1:TC + 1], lnscr[:, 0:TC])
            V("tensor_add", a_(A1, j), a_(A1, j), lnscr[:, TC:2 * TC])
            V("tensor_scalar", a_(A1, j), a_(A1, j), dc("ca1", j), dc("sa", j),
              ALU.mult, ALU.add)
        ve_h0 = ctx.ve_inc()
        ve_h0_prev = ve_h0
        for j in range(NJ):
            ctx.wait("g", "VE", ve_h0)
            spill_done = ctx.sp(h0_sp[j * 128:(j + 1) * 128, base:base + TC], a_(A1, j))
        ln_stats(lambda j: a_(A1, j), TC)
        for j in range(NJ):
            V("tensor_mul", a1_(A2, j)[:, 1:TC + 1], a_(A1, j), lnscr[:, 0:TC])
            V("tensor_add", a1_(A2, j)[:, 1:TC + 1], a1_(A2, j)[:, 1:TC + 1],
              lnscr[:, TC:2 * TC])
            V("tensor_scalar", a1_(A2, j)[:, 1:TC + 1], a1_(A2, j)[:, 1:TC + 1],
              vc("ln1w", j), vc("ln1b", j), ALU.mult, ALU.add)
            V("tensor_copy", a1_(A2, j)[:, 0:1],
              ubnd[:, j:j + 1] if c == 0 else usave[:, j:j + 1])
        ctx.ve_inc()
        for j in range(NJ):
            V("tensor_copy", usave[:, j:j + 1], a1_(A2, j)[:, TC:TC + 1])
        ctx.ve_inc()

        for nm in ("k", "v", "r"):
            tm = {"k": "tmk", "v": "tmv", "r": "tmr"}[nm]
            ctx.wait("v", "MM", mm_w_last)
            for j in range(NJ):
                V("tensor_sub", a_(A3, j), a1_(A2, j)[:, 1:TC + 1], a1_(A2, j)[:, 0:TC])
                V("scalar_tensor_tensor", a_(A3, j), a_(A3, j), vc(tm, j),
                  a1_(A2, j)[:, 0:TC], ALU.mult, ALU.add)
            ve_mix = ctx.ve_inc()
            ld_w = 0
            for j in range(NJ):
                ctx.wait("y", "MM", mm_w_last)
                ld_w = ctx.ld(WBUF[:, j * 1024:(j + 1) * 1024],
                              ap[wname[nm]][j * 128:(j + 1) * 128, :])
            ld_w = ctx.lq()
            dstbuf = {"k": KB, "v": VB, "r": RB}[nm]
            for cout in range(NJ):
                bank = cout % 2
                ctx.bank_wait(bank)
                ctx.wait("t", "LQ", ld_w)
                ctx.wait("t", "VE", ve_mix)
                for cin in range(NJ):
                    T_("matmul", ctx.ps[bank][:, 0:TC],
                       WBUF[:, cin * 1024 + cout * 128:cin * 1024 + (cout + 1) * 128],
                       a_(A3, cin), start=(cin == 0), stop=(cin == NJ - 1))
                mmv = ctx.mm_inc()
                if nm == "r":
                    ctx.wait("s", "MM", mmv)
                    S("activation", a_(dstbuf, cout), ctx.ps[bank][:, 0:TC], AF.Sigmoid)
                    ctx.bank_set(bank, "SE", ctx.se_inc())
                else:
                    ctx.wait("v", "MM", mmv)
                    V("tensor_copy", a_(dstbuf, cout), ctx.ps[bank][:, 0:TC])
                    ctx.bank_set(bank, "VE", ctx.ve_inc())
            mm_w_last = ctx.MM.n

        ve_k = ctx.VE.n
        se_r = ctx.SE.n
        for vi2, buf in ((0, KB), (1, VB), (2, RB)):
            for j in range(NJ):
                ctx.wait("g", "VE", ve_k)
                ctx.wait("g", "SE", se_r)
                spill_done = ctx.sp(
                    kvs_sp[vi2 * C + j * 128:vi2 * C + (j + 1) * 128, base:base + TC],
                    a_(buf, j))
        sq_chunk = ctx.sq()

        for j in range(NJ):
            V("tensor_scalar_mul", S1[:, 0:TC], iotarev, dc("w", j))
            V("tensor_add", S1[:, 0:TC], S1[:, 0:TC], a_(KB, j))
            V("reduce_max", csum[:, j:j + 1], S1[:, 0:TC], axis=AX)
            V("tensor_single_scalar", S1[:, 0:TC], S1[:, 0:TC], csum[:, j:j + 1], ALU.subtract)
            ve_e = ctx.ve_inc()
            ctx.wait("s", "VE", ve_e)
            S("activation", S2[:, 0:TC], S1[:, 0:TC], AF.Exp)
            se_e = ctx.se_inc()
            ctx.wait("v", "SE", se_e)
            V("reduce_sum", csum[:, 2 * NJ + j:2 * NJ + j + 1], S2[:, 0:TC], axis=AX)
            V("tensor_mul", S2[:, 0:TC], S2[:, 0:TC], a_(VB, j))
            V("reduce_sum", csum[:, NJ + j:NJ + j + 1], S2[:, 0:TC], axis=AX)
            ctx.ve_inc()
        Mr, Ar, Br = runsum[:, 0:NJ], runsum[:, NJ:2 * NJ], runsum[:, 2 * NJ:3 * NJ]
        Mc_, Ac_, Bc_ = csum[:, 0:NJ], csum[:, NJ:2 * NJ], csum[:, 2 * NJ:3 * NJ]
        V("scalar_tensor_tensor", SB[:, 0:NJ], dct("w"), float(TC), Mr, ALU.mult, ALU.add)
        V("tensor_max", SB[:, 8:8 + NJ], SB[:, 0:NJ], Mc_)
        V("tensor_sub", SB[:, 0:NJ], SB[:, 0:NJ], SB[:, 8:8 + NJ])
        V("tensor_sub", SB[:, 16:16 + NJ], Mc_, SB[:, 8:8 + NJ])
        ve_c1 = ctx.ve_inc()
        ctx.wait("s", "VE", ve_c1)
        S("activation", SB[:, 0:NJ], SB[:, 0:NJ], AF.Exp)
        S("activation", SB[:, 16:16 + NJ], SB[:, 16:16 + NJ], AF.Exp)
        se_c = ctx.se_inc()
        ctx.wait("v", "SE", se_c)
        V("tensor_mul", Ar, Ar, SB[:, 0:NJ])
        V("tensor_mul", SB[:, 24:24 + NJ], Ac_, SB[:, 16:16 + NJ])
        V("tensor_add", Ar, Ar, SB[:, 24:24 + NJ])
        V("tensor_mul", Br, Br, SB[:, 0:NJ])
        V("tensor_mul", SB[:, 24:24 + NJ], Bc_, SB[:, 16:16 + NJ])
        V("tensor_add", Br, Br, SB[:, 24:24 + NJ])
        V("tensor_copy", Mr, SB[:, 8:8 + NJ])
        ve_runsum = ctx.ve_inc()

    bank = 6
    ctx.bank_wait(bank)
    ctx.wait("t", "VE", ve_runsum)
    T_("transpose", ctx.ps[bank][0:24, 0:128], runsum[:], ident[:])
    mm_ag1 = ctx.mm_inc()
    ctx.wait("v", "MM", mm_ag1)
    V("tensor_copy", S1[0:24, 0:128], ctx.ps[bank][0:24, 0:128])
    ve_ag1 = ctx.ve_inc()
    ctx.bank_set(bank, "VE", ve_ag1)
    ctx.wait("g", "VE", ve_ag1)
    sp_ag1 = ctx.sp(ag1_in, S1[0:24, 0:128])
    ctx.wait("g", "SP", sp_ag1)
    cc_ag1 = ctx.allgather(ag1_in, ag1_out)

    allrM = ctx.aalloc(8 * 128)
    allrA = ctx.aalloc(8 * 128)
    allrB = ctx.aalloc(8 * 128)
    stP = ctx.aalloc(128)
    stA = ctx.aalloc(128)
    stB = ctx.aalloc(128)
    q1 = ctx.aalloc(128)
    q2 = ctx.aalloc(128)
    q3 = ctx.aalloc(128)
    ctx.wait("y", "CC", cc_ag1)
    for vv, dst in ((0, allrM), (1, allrA), (2, allrB)):
        ld_allr = ctx.ld(dst[0:8, :],
                         mkap(ag1_out, vv * 8 * 128, [[128, 8], [24 * 128, 8], [1, 128]]))
    ld_allr = ctx.lq()
    ctx.wait("v", "LQ", ld_allr)
    V("memset", stP[0:8, :], NEG)
    V("memset", stA[0:8, :], 0.0)
    V("memset", stB[0:8, :], 0.0)
    ctx.ve_inc()
    wrow = trow[0:8, 0:128]
    for k in range(8):
        Msl = allrM[0:8, k * 128:(k + 1) * 128]
        Asl = allrA[0:8, k * 128:(k + 1) * 128]
        Bsl = allrB[0:8, k * 128:(k + 1) * 128]
        V("tensor_scalar", q1[0:8, :], Msl, masklt8[:, k:k + 1], mneg8[:, k:k + 1],
          ALU.mult, ALU.add)
        V("scalar_tensor_tensor", q2[0:8, :], wrow, tlm8[:, k:k + 1], stP[0:8, :],
          ALU.mult, ALU.add)
        V("tensor_max", q3[0:8, :], q2[0:8, :], q1[0:8, :])
        V("tensor_sub", q2[0:8, :], q2[0:8, :], q3[0:8, :])
        V("tensor_sub", q1[0:8, :], q1[0:8, :], q3[0:8, :])
        ve_cb = ctx.ve_inc()
        ctx.wait("s", "VE", ve_cb)
        S("activation", q2[0:8, :], q2[0:8, :], AF.Exp)
        S("activation", q1[0:8, :], q1[0:8, :], AF.Exp)
        se_cb = ctx.se_inc()
        ctx.wait("v", "SE", se_cb)
        V("tensor_copy", stP[0:8, :], q3[0:8, :])
        V("tensor_mul", stA[0:8, :], stA[0:8, :], q2[0:8, :])
        V("tensor_single_scalar", q3[0:8, :], Asl, masklt8[:, k:k + 1], ALU.mult)
        V("tensor_mul", q3[0:8, :], q3[0:8, :], q1[0:8, :])
        V("tensor_add", stA[0:8, :], stA[0:8, :], q3[0:8, :])
        V("tensor_mul", stB[0:8, :], stB[0:8, :], q2[0:8, :])
        V("tensor_single_scalar", q3[0:8, :], Bsl, masklt8[:, k:k + 1], ALU.mult)
        V("tensor_mul", q3[0:8, :], q3[0:8, :], q1[0:8, :])
        V("tensor_add", stB[0:8, :], stB[0:8, :], q3[0:8, :])
        ctx.ve_inc()
    bank = 7
    ctx.bank_wait(bank)
    ctx.wait("t", "VE", ctx.VE.n)
    for sidx, st_t in enumerate((stP, stA, stB)):
        for j in range(NJ):
            T_("matmul", ctx.ps[bank][:, sidx * NJ + j:sidx * NJ + j + 1], st_t[0:8, :],
               oh8id[:, j:j + 1], start=True, stop=True)
    mm_st0 = ctx.mm_inc()
    ctx.wait("v", "MM", mm_st0)
    V("tensor_copy", state0[:], ctx.ps[bank][:, 0:3 * NJ])
    ve_state0 = ctx.ve_inc()
    ctx.dbg_dump("state0_d", state0[:], [128, 3 * NJ])
    ctx.dbg_dump("runsum_d", runsum[:], [128, 3 * NJ])
    ctx.bank_set(bank, "VE", ve_state0)
    ctx.phase_barrier()
    ctx.areset(am1)

    am4 = ctx.amark()
    KB = ctx.aalloc(NJ * TC)
    VB = ctx.aalloc(NJ * TC)
    RB = ctx.aalloc(NJ * TC)
    BPP = ctx.aalloc(NJ * (TC + 1))
    BAA = ctx.aalloc(NJ * (TC + 1))
    BBB = ctx.aalloc(NJ * (TC + 1))
    RW = ctx.aalloc(NJ * TC)
    AB = ctx.aalloc(NJ * TC)
    H0R = ctx.aalloc(2 * TC)
    D4 = ctx.aalloc(4 * TC)
    U4 = ctx.aalloc(4 * TC)
    UV4 = ctx.aalloc(4 * TC)
    WOB = ctx.aalloc(2 * NJ * 128)

    def ld_w_cout(wap, cout, slot):
        t = 0
        for cin in range(NJ):
            t = ctx.ld(WOB[:, slot * 1024 + cin * 128:slot * 1024 + (cin + 1) * 128],
                       wap[cin * 128:(cin + 1) * 128, cout * 128:(cout + 1) * 128])
        return ctx.lq()

    sp_a = 0
    sq_a = 0
    ve_kdone = 0
    mm_op_last = 0
    ve_adone_prev = 0
    cc_ag1b = 0
    for c in range(NCHK):
        base = c * TC
        ld_kvs = 0
        ctx.wait("y", "SQ", sq_chunk)
        ctx.wait("y", "VE", ve_kdone)
        for vi2, buf in ((0, KB), (1, VB), (2, RB)):
            for j in range(NJ):
                ld_kvs = ctx.ld(a_(buf, j),
                                kvs_sp[vi2 * C + j * 128:vi2 * C + (j + 1) * 128,
                                       base:base + TC])
        ld_kvs = ctx.lq()
        ctx.wait("v", "LQ", ld_kvs)
        ctx.wait("v", "VE", ve_state0)
        ctx.wait("v", "MM", mm_op_last)
        ctx.wait("v", "SQ", sq_a)
        for half in range(2):
            js = list(range(half * 4, half * 4 + 4))
            for jj, j in enumerate(js):
                bpp, baa, bbb = a1_(BPP, j), a1_(BAA, j), a1_(BBB, j)
                if c == 0:
                    V("tensor_copy", bpp[:, 0:1], state0[:, j:j + 1])
                    V("tensor_copy", baa[:, 0:1], state0[:, NJ + j:NJ + j + 1])
                    V("tensor_copy", bbb[:, 0:1], state0[:, 2 * NJ + j:2 * NJ + j + 1])
                else:
                    V("tensor_copy", bpp[:, 0:1], bpp[:, TC:TC + 1])
                    V("tensor_copy", baa[:, 0:1], baa[:, TC:TC + 1])
                    V("tensor_copy", bbb[:, 0:1], bbb[:, TC:TC + 1])
                V("tensor_tensor_scan", bpp[:, 1:TC + 1],
                  dc("w", j).broadcast_to([128, TC]), a_(KB, j), bpp[:, 0:1],
                  ALU.add, ALU.max)
            for jj, j in enumerate(js):
                bpp = a1_(BPP, j)
                V("scalar_tensor_tensor", a_(D4, jj), bpp[:, 0:TC], dc("w", j),
                  bpp[:, 1:TC + 1], ALU.add, ALU.subtract)
                V("tensor_sub", a_(U4, jj), a_(KB, j), bpp[:, 1:TC + 1])
            ve_b1 = ctx.ve_inc()
            ctx.wait("s", "VE", ve_b1)
            for jj in range(4):
                S("activation", a_(D4, jj), a_(D4, jj), AF.Exp)
                S("activation", a_(U4, jj), a_(U4, jj), AF.Exp)
            se_b1 = ctx.se_inc()
            ctx.wait("v", "SE", se_b1)
            for jj, j in enumerate(js):
                bpp, baa, bbb = a1_(BPP, j), a1_(BAA, j), a1_(BBB, j)
                V("tensor_mul", a_(UV4, jj), a_(U4, jj), a_(VB, j))
                V("tensor_tensor_scan", bbb[:, 1:TC + 1], a_(D4, jj), a_(U4, jj),
                  bbb[:, 0:1], ALU.mult, ALU.add)
                V("tensor_tensor_scan", baa[:, 1:TC + 1], a_(D4, jj), a_(UV4, jj),
                  baa[:, 0:1], ALU.mult, ALU.add)
            for jj, j in enumerate(js):
                bpp = a1_(BPP, j)
                V("tensor_single_scalar", a_(KB, j), a_(KB, j), vc("tfir", j), ALU.add)
                V("tensor_max", a_(UV4, jj), bpp[:, 0:TC], a_(KB, j))
                V("tensor_sub", a_(D4, jj), bpp[:, 0:TC], a_(UV4, jj))
                V("tensor_sub", a_(KB, j), a_(KB, j), a_(UV4, jj))
            ve_b2 = ctx.ve_inc()
            ctx.wait("s", "VE", ve_b2)
            for jj, j in enumerate(js):
                S("activation", a_(D4, jj), a_(D4, jj), AF.Exp)
                S("activation", a_(KB, j), a_(KB, j), AF.Exp)
            se_b2 = ctx.se_inc()
            ctx.wait("v", "SE", se_b2)
            for jj, j in enumerate(js):
                bbb = a1_(BBB, j)
                V("tensor_mul", a_(UV4, jj), a_(D4, jj), bbb[:, 0:TC])
                V("tensor_add", a_(UV4, jj), a_(UV4, jj), a_(KB, j))
            ve_b3 = ctx.ve_inc()
            ctx.wait("s", "VE", ve_b3)
            for jj in range(4):
                S("activation", a_(UV4, jj), a_(UV4, jj), AF.Ln)
            se_b3 = ctx.se_inc()
            ctx.wait("v", "SE", se_b3)
            for jj in range(4):
                V("tensor_scalar_mul", a_(UV4, jj), a_(UV4, jj), -1.0)
            ve_b4 = ctx.ve_inc()
            ctx.wait("s", "VE", ve_b4)
            for jj in range(4):
                S("activation", a_(UV4, jj), a_(UV4, jj), AF.Exp)
            se_b4 = ctx.se_inc()
            ctx.wait("v", "SE", se_b4)
            for jj, j in enumerate(js):
                baa = a1_(BAA, j)
                V("tensor_mul", a_(D4, jj), a_(D4, jj), baa[:, 0:TC])
                V("tensor_mul", a_(KB, j), a_(KB, j), a_(VB, j))
                V("tensor_add", a_(D4, jj), a_(D4, jj), a_(KB, j))
                V("tensor_mul", a_(D4, jj), a_(D4, jj), a_(UV4, jj))
                V("tensor_mul", a_(RW, j), a_(D4, jj), a_(RB, j))
            ctx.ve_inc()
        ve_kdone = ctx.VE.n

        evac_a = {}
        mmo_hist = {}
        lqw = {}

        def issue_op_load(cout):
            if cout >= 2:
                ctx.wait("y", "MM", mmo_hist[cout - 2])
                ctx.wait("y", "VE", evac_a[cout - 2])
            else:
                ctx.wait("y", "MM", mm_op_last)
                ctx.wait("y", "VE", ve_adone_prev)
            ld_w_cout(ap["owT"], cout, cout % 2)
            ctx.ld(H0R[:, (cout % 2) * TC:(cout % 2) * TC + TC],
                   h0_sp[cout * 128:(cout + 1) * 128, base:base + TC])
            lqw[cout] = ctx.lq()

        issue_op_load(0)
        issue_op_load(1)
        for cout in range(NJ):
            bank = cout % 2
            ctx.bank_wait(bank)
            ctx.wait("t", "LQ", lqw[cout])
            ctx.wait("t", "VE", ve_kdone)
            for cin in range(NJ):
                T_("matmul", ctx.ps[bank][:, 0:TC],
                   WOB[:, (cout % 2) * 1024 + cin * 128:(cout % 2) * 1024 + (cin + 1) * 128],
                   a_(RW, cin), start=(cin == 0), stop=(cin == NJ - 1))
            mmo = ctx.mm_inc()
            mmo_hist[cout] = mmo
            ctx.wait("v", "MM", mmo)
            V("tensor_add", a_(AB, cout), ctx.ps[bank][:, 0:TC],
              H0R[:, (cout % 2) * TC:(cout % 2) * TC + TC])
            evac_a[cout] = ctx.ve_inc()
            ctx.bank_set(bank, "VE", evac_a[cout])
            if cout + 2 < NJ:
                issue_op_load(cout + 2)
        mm_op_last = ctx.MM.n
        ve_adone = ctx.VE.n
        ve_adone_prev = ve_adone
        for j in range(NJ):
            ctx.wait("g", "VE", ve_adone)
            sp_a = ctx.sp(a_sp[j * 128:(j + 1) * 128, base:base + TC], a_(AB, j))
        sq_a = ctx.sq()
        if c == NCHK - 1:
            for j in range(NJ):
                V("tensor_copy", SB[:, j:j + 1], a_(AB, j)[:, TC - 1:TC])
            ve_al = ctx.ve_inc()
            bank = 6
            ctx.bank_wait(bank)
            ctx.wait("t", "VE", ve_al)
            T_("transpose", ctx.ps[bank][0:8, 0:128], SB[:, 0:NJ], ident[:])
            mm_al = ctx.mm_inc()
            ctx.wait("v", "MM", mm_al)
            V("tensor_copy", S1[0:8, 0:128], ctx.ps[bank][0:8, 0:128])
            ve_al2 = ctx.ve_inc()
            ctx.bank_set(bank, "VE", ve_al2)
            ctx.wait("g", "VE", ve_al2)
            sp_ag1b = ctx.sp(ag1b_in, S1[0:8, 0:128])
            ctx.wait("g", "SP", sp_ag1b)
            cc_ag1b = ctx.allgather(ag1b_in, ag1b_out)

    ctx.phase_barrier()
    ctx.areset(am4)
    ab8 = ctx.aalloc(8 * 128)
    ctx.wait("y", "CC", cc_ag1b)
    ld_ab8 = ctx.ld(ab8[0:8, :],
                    mkap(ag1b_out, 0, [[128, 8], [8 * 128, 8], [1, 128]]))
    ld_ab8 = ctx.lq()
    ctx.wait("v", "LQ", ld_ab8)
    V("memset", S1[0:8, 0:128], 0.0)
    for k in range(8):
        V("scalar_tensor_tensor", S1[0:8, 0:128], ab8[0:8, k * 128:(k + 1) * 128],
          mprev8[:, k:k + 1], S1[0:8, 0:128], ALU.mult, ALU.add)
    ve_am = ctx.ve_inc()
    bank = 4
    ctx.bank_wait(bank)
    V("reduce_sum", S2[0:8, 0:1], S1[0:8, 0:128], axis=AX)
    ve_rs = ctx.ve_inc()
    ctx.wait("s", "VE", ve_rs)
    S("activation", S3[0:8, 0:128], S1[0:8, 0:128], AF.Square)
    se_sq = ctx.se_inc()
    ctx.wait("v", "SE", se_sq)
    V("reduce_sum", S2[0:8, 1:2], S3[0:8, 0:128], axis=AX)
    ve_rs2 = ctx.ve_inc()
    ctx.wait("t", "VE", ve_rs2)
    T_("matmul", ctx.ps[bank][0:1, 0:2], ones8[:, 0:1], S2[0:8, 0:2], start=True, stop=True)
    mm_bn = ctx.mm_inc()
    ctx.wait("v", "MM", mm_bn)
    st = stats
    V("tensor_scalar_mul", st[0:1, 0:2], ctx.ps[bank][0:1, 0:2], 1.0 / C)
    V("tensor_mul", st[0:1, 2:3], st[0:1, 0:1], st[0:1, 0:1])
    V("tensor_sub", st[0:1, 1:2], st[0:1, 1:2], st[0:1, 2:3])
    V("tensor_scalar_add", st[0:1, 1:2], st[0:1, 1:2], LN_EPS)
    ve_bv = ctx.ve_inc()
    ctx.wait("s", "VE", ve_bv)
    S("activation", st[0:1, 3:4], st[0:1, 1:2], AF.Ln)
    se_bs = ctx.se_inc()
    ctx.wait("v", "SE", se_bs)
    V("tensor_scalar_mul", st[0:1, 3:4], st[0:1, 3:4], -0.5)
    ve_bv2 = ctx.ve_inc()
    ctx.wait("s", "VE", ve_bv2)
    S("activation", st[0:1, 4:5], st[0:1, 3:4], AF.Exp)
    se_bs2 = ctx.se_inc()
    ctx.wait("v", "SE", se_bs2)
    V("tensor_copy", st[0:1, 4:5], st[0:1, 4:5])
    ve_bA = ctx.ve_inc()
    ctx.wait("t", "VE", ve_bA)
    T_("matmul", ctx.ps[bank][0:8, 4:5], onesrow[0:1, 0:8], st[0:1, 4:5], start=True, stop=True)
    T_("matmul", ctx.ps[bank][0:8, 5:6], onesrow[0:1, 0:8], st[0:1, 0:1], start=True, stop=True)
    mm_bb = ctx.mm_inc()
    ctx.wait("v", "MM", mm_bb)
    V("tensor_single_scalar", S1[0:8, 0:128], S1[0:8, 0:128], ctx.ps[bank][0:8, 5:6],
      ALU.subtract)
    V("tensor_single_scalar", S1[0:8, 0:128], S1[0:8, 0:128], ctx.ps[bank][0:8, 4:5],
      ALU.mult)
    V("tensor_mul", S1[0:8, 0:128], S1[0:8, 0:128], trow[0:8, 128:256])
    V("tensor_add", S1[0:8, 0:128], S1[0:8, 0:128], trow[0:8, 256:384])
    V("tensor_single_scalar", S1[0:8, 0:128], S1[0:8, 0:128], maskfirst[:, 0:1], ALU.mult)
    ve_u2b = ctx.ve_inc()
    ctx.bank_set(bank, "VE", ve_u2b)
    bank = 7
    ctx.bank_wait(bank)
    ctx.wait("t", "VE", ve_u2b)
    for j in range(NJ):
        T_("matmul", ctx.ps[bank][:, j:j + 1], S1[0:8, 0:128], oh8id[:, j:j + 1],
           start=True, stop=True)
    mm_u2b = ctx.mm_inc()
    ctx.wait("v", "MM", mm_u2b)
    V("tensor_copy", u2bnd[:], ctx.ps[bank][:, 0:NJ])
    ve_u2bnd = ctx.ve_inc()
    ctx.bank_set(bank, "VE", ve_u2bnd)
    ctx.phase_barrier()
    ctx.areset(am4)

    am4b = ctx.amark()
    AB = ctx.aalloc(NJ * TC)
    U2 = ctx.aalloc(NJ * (TC + 1))
    A3 = ctx.aalloc(NJ * TC)
    FRB = ctx.aalloc(NJ * TC)
    X16 = ctx.aalloc16(2 * NJ * TC)
    WFR = ctx.aalloc(NJ * 1024)
    ld_wfr = 0
    for j in range(NJ):
        ld_wfr = ctx.ld(WFR[:, j * 1024:(j + 1) * 1024],
                        ap["frwT"][j * 128:(j + 1) * 128, :])
    ld_wfr = ctx.lq()

    sp_4b = 0
    sq_4b = 0
    sq_xk2 = 0
    ve_4b = 0
    mm_fr_last = 0
    for c in range(NCHK):
        base = c * TC
        ld_a = 0
        ctx.wait("y", "SQ", sq_a)
        ctx.wait("y", "VE", ve_4b)
        for j in range(NJ):
            ld_a = ctx.ld(a_(AB, j), a_sp[j * 128:(j + 1) * 128, base:base + TC])
        ld_a = ctx.lq()
        ln_stats(lambda j: a_(AB, j), TC, waits=(("LQ", ld_a),))
        ctx.wait("v", "LQ", ld_a)
        for j in range(NJ):
            V("tensor_mul", a1_(U2, j)[:, 1:TC + 1], a_(AB, j), lnscr[:, 0:TC])
            V("tensor_add", a1_(U2, j)[:, 1:TC + 1], a1_(U2, j)[:, 1:TC + 1],
              lnscr[:, TC:2 * TC])
            V("tensor_scalar", a1_(U2, j)[:, 1:TC + 1], a1_(U2, j)[:, 1:TC + 1],
              vc("ln2w", j), vc("ln2b", j), ALU.mult, ALU.add)
            V("tensor_copy", a1_(U2, j)[:, 0:1],
              u2bnd[:, j:j + 1] if c == 0 else u2save[:, j:j + 1])
        ctx.ve_inc()
        for j in range(NJ):
            V("tensor_copy", u2save[:, j:j + 1], a1_(U2, j)[:, TC:TC + 1])
        ctx.ve_inc()
        ctx.wait("v", "SQ", sq_4b)
        for j in range(NJ):
            V("tensor_sub", a_(A3, j), a1_(U2, j)[:, 1:TC + 1], a1_(U2, j)[:, 0:TC])
            V("scalar_tensor_tensor", a_(A3, j), a_(A3, j), vc("ftmk", j),
              a1_(U2, j)[:, 0:TC], ALU.mult, ALU.add)
        for j in range(NJ):
            V("tensor_copy", X16[:, j * TC:(j + 1) * TC], a_(A3, j))
            V("tensor_sub", S1[:, 0:TC], a_(A3, j), X16[:, j * TC:(j + 1) * TC])
            V("tensor_copy", X16[:, (NJ + j) * TC:(NJ + j + 1) * TC], S1[:, 0:TC])
        ve_xk2 = ctx.ve_inc()
        sp_xk2 = 0
        for j in range(NJ):
            ctx.wait("g", "VE", ve_xk2)
            sp_xk2 = ctx.sp(xk2_sp[j * 128:(j + 1) * 128, base:base + TC], a_(A3, j))
            sp_xk2 = ctx.sp(xk2h_sp[j * 128:(j + 1) * 128, base:base + TC],
                            X16[:, j * TC:(j + 1) * TC])
            sp_xk2 = ctx.sp(xk2l_sp[j * 128:(j + 1) * 128, base:base + TC],
                            X16[:, (NJ + j) * TC:(NJ + j + 1) * TC])
        sq_xk2 = ctx.sq()
        ctx.wait("v", "MM", mm_fr_last)
        for j in range(NJ):
            V("tensor_sub", a_(FRB, j), a1_(U2, j)[:, 1:TC + 1], a1_(U2, j)[:, 0:TC])
            V("scalar_tensor_tensor", a_(FRB, j), a_(FRB, j), vc("ftmr", j),
              a1_(U2, j)[:, 0:TC], ALU.mult, ALU.add)
        ve_xr2 = ctx.ve_inc()
        for cout in range(NJ):
            bank = cout % 2
            ctx.bank_wait(bank)
            ctx.wait("t", "LQ", ld_wfr)
            ctx.wait("t", "VE", ve_xr2)
            for cin in range(NJ):
                T_("matmul", ctx.ps[bank][:, 0:TC],
                   WFR[:, cin * 1024 + cout * 128:cin * 1024 + (cout + 1) * 128],
                   a_(FRB, cin), start=(cin == 0), stop=(cin == NJ - 1))
            mmf = ctx.mm_inc()
            ctx.wait("s", "MM", mmf)
            ctx.wait("s", "SQ", sq_xk2)
            S("activation", a_(A3, cout), ctx.ps[bank][:, 0:TC], AF.Sigmoid)
            ctx.bank_set(bank, "SE", ctx.se_inc())
        mm_fr_last = ctx.MM.n
        se_fr = ctx.SE.n
        for j in range(NJ):
            ctx.wait("g", "SE", se_fr)
            sp_4b = ctx.sp(sig_sp[j * 128:(j + 1) * 128, base:base + TC], a_(A3, j))
        sq_4b = ctx.sq()
        ve_4b = ctx.VE.n
    ctx.phase_barrier()
    ctx.areset(am4b)

    am5 = ctx.amark()
    KVSB = ctx.aalloc(NJ * TL)
    am5b = ctx.amark()
    WFK = ctx.aalloc(NJ * g.FS)
    WFV = ctx.aalloc((g.FS // 128) * 1024)
    NFO = g.FS // 128
    W16K = ctx.aalloc16(2 * NJ * g.FS)
    W16V = ctx.aalloc16(2 * NFO * 1024)
    X16T = ctx.aalloc16(2 * 2 * NJ * TC)
    F16 = ctx.aalloc16(2 * NFO * TC)
    NWK = NJ * g.FS
    NWV = NFO * 1024
    NXT = NJ * TC
    NFK = NFO * TC

    ctx.wait("v", "SQ", sq_4b)
    V("memset", KVSB[:, :], 0.0)
    ve_kv0 = ctx.ve_inc()
    mm_cm_last = 0
    mm_fk_prev_fs = 0
    for fs in range(g.NFS):
        ld_wf = 0
        for j in range(NJ):
            ctx.wait("y", "MM", mm_cm_last)
            ld_wf = ctx.ld(WFK[:, j * g.FS:(j + 1) * g.FS],
                           ap["fkwT"][j * 128:(j + 1) * 128, fs * g.FS:(fs + 1) * g.FS])
        for fo in range(NFO):
            ld_wf = ctx.ld(WFV[:, fo * 1024:(fo + 1) * 1024],
                           ap["fvwT"][fs * g.FS + fo * 128:fs * g.FS + (fo + 1) * 128, :])
        ld_wf = ctx.lq()
        ctx.wait("v", "LQ", ld_wf)
        ctx.wait("v", "MM", mm_cm_last)
        ctx.wait("v", "SE", ctx.SE.n)
        for buf, b16, n in ((WFK, W16K, NWK), (WFV, W16V, NWV)):
            for c0 in range(0, n, 512):
                V("tensor_copy", b16[:, c0:c0 + 512], buf[:, c0:c0 + 512])
                V("tensor_sub", S4[:, 0:512], buf[:, c0:c0 + 512], b16[:, c0:c0 + 512])
                V("tensor_copy", b16[:, n + c0:n + c0 + 512], S4[:, 0:512])
        ve_wsp = ctx.ve_inc()
        ve_fk_last = ve_wsp
        lqx = {}
        mm_fk_done = {}

        def ld_x16(tc_):
            base_ = tc_ * TC
            s0 = (tc_ % 2) * 2 * NXT
            if tc_ >= 2:
                ctx.wait("y", "MM", mm_fk_done[tc_ - 2])
            else:
                ctx.wait("y", "MM", mm_fk_prev_fs)
            for j in range(NJ):
                ctx.ld(X16T[:, s0 + j * TC:s0 + (j + 1) * TC],
                       xk2h_sp[j * 128:(j + 1) * 128, base_:base_ + TC])
                ctx.ld(X16T[:, s0 + NXT + j * TC:s0 + NXT + (j + 1) * TC],
                       xk2l_sp[j * 128:(j + 1) * 128, base_:base_ + TC])
            lqx[tc_] = ctx.lq()

        ld_x16(0)
        if NCHK > 1:
            ld_x16(1)
        for tc in range(NCHK):
            base = tc * TC
            s0 = (tc % 2) * 2 * NXT
            ve_fk = {}
            ctx.wait("s", "MM", mm_cm_last)
            ctx.wait("v", "MM", mm_cm_last)
            for fo in range(NFO):
                bank = fo % 2
                ctx.bank_wait(bank)
                ctx.wait("t", "LQ", lqx[tc])
                ctx.wait("t", "VE", ve_wsp)
                first = True
                for woff, xoff in ((0, s0), (0, s0 + NXT), (NWK, s0)):
                    for cin in range(NJ):
                        T_("matmul", ctx.ps[bank][:, 0:TC],
                           W16K[:, woff + cin * g.FS + fo * 128:
                                woff + cin * g.FS + (fo + 1) * 128],
                           X16T[:, xoff + cin * TC:xoff + (cin + 1) * TC],
                           start=first, stop=(woff == NWK and cin == NJ - 1))
                        first = False
                mmk = ctx.mm_inc()
                if fo == NFO - 1:
                    mm_fk_done[tc] = mmk
                    if tc + 2 < NCHK:
                        ld_x16(tc + 2)
                ctx.wait("s", "MM", mmk)
                ctx.wait("s", "VE", ve_fk_last)
                S("activation", S4[:, 0:TC], ctx.ps[bank][:, 0:TC], AF.Relu)
                S("activation", SQ2[:, 0:TC], S4[:, 0:TC], AF.Square)
                se_fk = ctx.se_inc()
                ctx.wait("v", "SE", se_fk)
                V("tensor_copy", F16[:, fo * TC:(fo + 1) * TC], SQ2[:, 0:TC])
                V("tensor_sub", S3[:, 0:TC], SQ2[:, 0:TC], F16[:, fo * TC:(fo + 1) * TC])
                V("tensor_copy", F16[:, NFK + fo * TC:NFK + (fo + 1) * TC], S3[:, 0:TC])
                ve_fk[fo] = ctx.ve_inc()
                ve_fk_last = ve_fk[fo]
                ctx.bank_set(bank, "VE", ve_fk[fo])
            for cout in range(NJ):
                bank = 2 + (cout % 2)
                ctx.bank_wait(bank)
                ctx.wait("t", "VE", ve_fk[NFO - 1])
                first = True
                for woff, foff in ((0, 0), (0, NFK), (NWV, 0)):
                    for fo in range(NFO):
                        T_("matmul", ctx.ps[bank][:, 0:TC],
                           W16V[:, woff + fo * 1024 + cout * 128:
                                woff + fo * 1024 + (cout + 1) * 128],
                           F16[:, foff + fo * TC:foff + (fo + 1) * TC],
                           start=first, stop=(woff == NWV and fo == NFO - 1))
                        first = False
                mmkv = ctx.mm_inc()
                ctx.wait("v", "MM", mmkv)
                ctx.wait("v", "VE", ve_kv0)
                V("tensor_add", KVSB[:, cout * TL + base:cout * TL + base + TC],
                  KVSB[:, cout * TL + base:cout * TL + base + TC], ctx.ps[bank][:, 0:TC])
                ctx.bank_set(bank, "VE", ctx.ve_inc())
            mm_cm_last = ctx.MM.n
        mm_fk_prev_fs = mm_fk_done[NCHK - 1]
    ctx.phase_barrier()
    ctx.areset(am5b)

    AB5 = ctx.aalloc(NJ * TC)
    SGB = ctx.aalloc(NJ * TC)
    XB5 = ctx.aalloc(NJ * TC)
    H1B = ctx.aalloc(NJ * TC)
    TOK = ctx.aalloc((TC // 128) * 1024)
    sp_ag2 = 0
    sq_ag2 = 0
    ve_rt = 0
    for tc in range(NCHK):
        base = tc * TC
        ld_f = 0
        ctx.wait("y", "VE", ve_rt)
        ctx.wait("y", "SQ", sq_ag2)
        for j in range(NJ):
            ld_f = ctx.ld(a_(AB5, j), a_sp[j * 128:(j + 1) * 128, base:base + TC])
            ld_f = ctx.ld(a_(SGB, j), sig_sp[j * 128:(j + 1) * 128, base:base + TC])
            ld_f = ctx.ld(a_(XB5, j), ap["xT"][j * 128:(j + 1) * 128, 1 + base:1 + base + TC])
        ld_f = ctx.lq()
        ctx.wait("v", "LQ", ld_f)
        for j in range(NJ):
            V("tensor_mul", S1[:, 0:TC], a_(SGB, j), KVSB[:, j * TL + base:j * TL + base + TC])
            V("tensor_add", S1[:, 0:TC], S1[:, 0:TC], a_(AB5, j))
            V("scalar_tensor_tensor", a_(AB5, j), S1[:, 0:TC], dc("ga", j), a_(XB5, j),
              ALU.mult, ALU.add)
        ve_xm = ctx.ve_inc()
        ln_stats(lambda j: a_(AB5, j), TC)
        for j in range(NJ):
            V("tensor_mul", a_(H1B, j), a_(AB5, j), lnscr[:, 0:TC])
            V("tensor_add", a_(H1B, j), a_(H1B, j), lnscr[:, TC:2 * TC])
            V("tensor_scalar", a_(H1B, j), a_(H1B, j), dc("cb1", j), dc("sb", j),
              ALU.mult, ALU.add)
        ve_h1 = ctx.ve_inc()
        for buf, dst in ((AB5, xmid_sp), (H1B, ag2_in)):
            tok_free = sq_ag2
            for tsub in range(TC // 128):
                for half in range(2):
                    bank = 4 + half
                    ctx.bank_wait(bank)
                    ctx.wait("t", "VE", ve_h1)
                    for jj in range(4):
                        j = half * 4 + jj
                        T_("transpose", ctx.ps[bank][:, jj * 128:(jj + 1) * 128],
                           a_(buf, j)[:, tsub * 128:(tsub + 1) * 128], ident[:])
                    mm_tp = ctx.mm_inc()
                    ctx.wait("v", "MM", mm_tp)
                    ctx.wait("v", "SQ", tok_free)
                    V("tensor_copy",
                      TOK[:, tsub * 1024 + half * 512:tsub * 1024 + (half + 1) * 512],
                      ctx.ps[bank][:, 0:512])
                    ctx.bank_set(bank, "VE", ctx.ve_inc())
            ve_tok = ctx.VE.n
            for tsub in range(TC // 128):
                ctx.wait("g", "VE", ve_tok)
                sp_ag2 = ctx.sp(dst[base + tsub * 128:base + (tsub + 1) * 128, :],
                                TOK[:, tsub * 1024:(tsub + 1) * 1024])
                if buf is H1B and "h1_dbg" in ctx.dbg_set:
                    sp_ag2 = ctx.sp(h1_dbg[base + tsub * 128:base + (tsub + 1) * 128, :],
                                    TOK[:, tsub * 1024:(tsub + 1) * 1024])
            sq_ag2 = ctx.sq()
        for tsub in range(TC // 128):
            bank = 6
            ctx.bank_wait(bank)
            ctx.wait("t", "VE", ve_h1)
            for j in range(NJ):
                T_("matmul", ctx.ps[bank][:, 0:g.E],
                   a_(H1B, j)[:, tsub * 128:(tsub + 1) * 128],
                   embTs[:, j * g.E:(j + 1) * g.E], start=(j == 0), stop=(j == NJ - 1))
            mm_lg = ctx.mm_inc()
            ti = (base // 128) + tsub
            rb = routebuf[:, ti * 18:(ti + 1) * 18]
            ctx.wait("v", "MM", mm_lg)
            V("tensor_copy", SB[:, 0:8], ctx.ps[bank][:, 0:g.E])
            ve_l = ctx.ve_inc()
            ctx.bank_set(bank, "VE", ve_l)
            V("max", out=SB[:, 8:16], in_=SB[:, 0:8])
            V("tensor_single_scalar", rb[:, 0:8], SB[:, 0:8], SB[:, 8:9], ALU.is_ge)
            V("tensor_single_scalar", SB[:, 16:24], SB[:, 0:8], SB[:, 9:10], ALU.is_ge)
            V("tensor_sub", rb[:, 8:16], SB[:, 16:24], rb[:, 0:8])
            V("tensor_sub", SB[:, 24:25], SB[:, 9:10], SB[:, 8:9])
            ve_r1 = ctx.ve_inc()
            ctx.wait("s", "VE", ve_r1)
            S("activation", SB[:, 24:25], SB[:, 24:25], AF.Exp)
            se_r1 = ctx.se_inc()
            ctx.wait("v", "SE", se_r1)
            V("tensor_scalar_add", SB[:, 25:26], SB[:, 24:25], 1.0)
            ve_r1b = ctx.ve_inc()
            ctx.wait("s", "VE", ve_r1b)
            S("activation", SB[:, 26:27], SB[:, 25:26], AF.Ln)
            se_r2 = ctx.se_inc()
            ctx.wait("v", "SE", se_r2)
            V("tensor_scalar_mul", SB[:, 26:27], SB[:, 26:27], -1.0)
            ve_r1c = ctx.ve_inc()
            ctx.wait("s", "VE", ve_r1c)
            S("activation", SB[:, 27:28], SB[:, 26:27], AF.Exp)
            se_r3 = ctx.se_inc()
            ctx.wait("v", "SE", se_r3)
            V("tensor_copy", rb[:, 16:17], SB[:, 27:28])
            V("tensor_mul", rb[:, 17:18], SB[:, 24:25], rb[:, 16:17])
            ve_r2 = ctx.ve_inc()
            bank = 7
            ctx.bank_wait(bank)
            ctx.wait("t", "VE", ve_r2)
            T_("transpose", ctx.ps[bank][0:8, 0:128], SB[:, 16:24], ident[:])
            T_("transpose", ctx.ps[bank][0:8, 128:256], rb[:, 0:8], ident[:])
            T_("transpose", ctx.ps[bank][0:8, 256:384], rb[:, 8:16], ident[:])
            mm_tr = ctx.mm_inc()
            ctx.wait("v", "MM", mm_tr)
            ctx.wait("v", "SQ", sq_ag2)
            V("tensor_copy", S1[0:8, 0:384], ctx.ps[bank][0:8, 0:384])
            ve_tr = ctx.ve_inc()
            ctx.bank_set(bank, "VE", ve_tr)
            ctx.wait("g", "VE", ve_tr)
            ctx.sp(ag2b_in[0:8, ti * 128:(ti + 1) * 128], S1[0:8, 0:128])
            ctx.sp(m12_d[0:8, ti * 128:(ti + 1) * 128], S1[0:8, 128:256])
            sp_ag2 = ctx.sp(m12_d[8:16, ti * 128:(ti + 1) * 128], S1[0:8, 256:384])
            sq_ag2 = ctx.sq()
        ve_rt = ctx.VE.n
    ctx.wait("g", "SP", sp_ag2)
    cc_ag2b = ctx.allgather(ag2b_in, ag2b_out)
    cc_ag2 = ctx.allgather(ag2_in, ag2_out)
    ctx.phase_barrier()
    ctx.areset(am5)

    am6 = ctx.amark()
    GA = ctx.aalloc(T)
    GB = ctx.aalloc(T)
    QT = ctx.aalloc(max(TL, 512))
    M1E = ctx.aalloc(TL)
    M2E = ctx.aalloc(TL)
    ctx.wait("y", "CC", cc_ag2b)
    ld_og = ctx.ld(GA[0:8, :], mkap(ag2b_out, 0, [[TL, 8], [8 * TL, 8], [1, TL]]))
    ld_m1 = ctx.ld(M1E[0:8, 0:TL], m12_d[0:8, :])
    ld_m2 = ctx.ld(M2E[0:8, 0:TL], m12_d[8:16, :])
    ld_og = ctx.lq()
    ctx.wait("v", "LQ", ld_og)
    V("tensor_tensor_scan", GB[0:8, :], GA[0:8, :],
      zz[0:8, 0:1].broadcast_to([8, T]), 0.0, ALU.add, ALU.add)
    V("scalar_tensor_tensor", GA[0:8, :], GB[0:8, :], float(g.CAP), GA[0:8, :],
      ALU.is_le, ALU.mult)
    V("scalar_tensor_tensor", GB[0:8, :], GB[0:8, :], ecap[:, 0:1], GA[0:8, :],
      ALU.add, ALU.mult)
    ve_g = ctx.ve_inc()
    sp_sv = 0
    sp_sv_hist = {}
    for o in range(T // 512):
        bank = o % 2
        ctx.bank_wait(bank)
        ctx.wait("t", "VE", ve_g)
        T_("matmul", ctx.ps[bank][0:1, 0:512], ohmy[:, 0:1], GB[0:8, o * 512:(o + 1) * 512],
           start=True, stop=False)
        T_("matmul", ctx.ps[bank][0:1, 0:512], ohs[:, 0:1], GA[0:8, o * 512:(o + 1) * 512],
           start=False, stop=True)
        mm_sv = ctx.mm_inc()
        ctx.wait("v", "MM", mm_sv)
        if o >= 2:
            ctx.wait("v", "SQ", sp_sv_hist[o - 2])
        V("tensor_scalar_add", R16[0:1, (o % 2) * 512:(o % 2) * 512 + 512],
          ctx.ps[bank][0:1, 0:512], float(g.BIG))
        ve_sv = ctx.ve_inc()
        ctx.bank_set(bank, "VE", ve_sv)
        ctx.wait("g", "VE", ve_sv)
        sp_sv = ctx.sp(svrow_d[0:1, o * 512:(o + 1) * 512],
                       R16[0:1, (o % 2) * 512:(o % 2) * 512 + 512])
        sp_sv_hist[o] = ctx.sq()
    ctx.wait("y", "SQ", ctx.SQ.n)
    ld_svt = ctx.ld(svtok[:], svrow_d.rearrange("o (a b) -> (o a) b", b=128),
                    transpose=True)
    ld_svt = ctx.lq()
    ctx.wait("v", "LQ", ld_svt)
    V("tensor_copy", idx32[:], svtok[:])
    ve_idx = ctx.ve_inc()
    sp_fld = 0
    sq_fld = 0
    for fi in range(4):
        src_em = M1E if fi < 2 else M2E
        G_t = GA if (fi % 2 == 0) else GB
        V("memset", QT[0:8, 0:TL], 0.0)
        ve_z = ctx.ve_inc()
        for k in range(8):
            V("scalar_tensor_tensor", QT[0:8, 0:TL], G_t[0:8, k * TL:(k + 1) * TL],
              ohown[:, k:k + 1], QT[0:8, 0:TL], ALU.mult, ALU.add)
        V("tensor_mul", QT[0:8, 0:TL], QT[0:8, 0:TL], src_em[0:8, 0:TL])
        ve_q = ctx.ve_inc()
        for o in range(max(1, TL // 512)):
            w_ = min(512, TL)
            bank = 2
            ctx.bank_wait(bank)
            ctx.wait("t", "VE", ve_q)
            T_("matmul", ctx.ps[bank][0:1, 0:w_], ones8[:, 0:1],
               QT[0:8, o * 512:o * 512 + w_], start=True, stop=True)
            mm_q = ctx.mm_inc()
            ctx.wait("v", "MM", mm_q)
            ctx.wait("v", "SQ", sq_fld)
            V("tensor_copy", R16[0:1, 0:w_], ctx.ps[bank][0:1, 0:w_])
            ve_fq = ctx.ve_inc()
            ctx.bank_set(bank, "VE", ve_fq)
            ctx.wait("g", "VE", ve_fq)
            sp_fld = ctx.sp(fld_d[fi:fi + 1, o * 512:o * 512 + w_], R16[0:1, 0:w_])
            sq_fld = ctx.sq()
    ctx.wait("y", "SQ", sq_fld)
    ld_fld = 0
    for fi in range(4):
        ld_fld = ctx.ld(fldtok[:, fi * 16:(fi + 1) * 16],
                        fld_d[fi:fi + 1, :].rearrange("o (a b) -> (o a) b", b=128),
                        transpose=True)
    ld_fld = ctx.lq()
    ctx.wait("v", "LQ", ld_fld)
    V("tensor_copy", fld32[:], fldtok[:])
    V("tensor_copy", idxf1[:], fldtok[:, 16:16 + TL // 128])
    V("tensor_copy", idxf2[:], fldtok[:, 48:48 + TL // 128])
    NTI = TL // 128
    capbits = g.CAP.bit_length() - 1
    stbits = g.STILE.bit_length() - 1
    for f, (idxf, idxfs) in enumerate(((idxf1, idxf1s), (idxf2, idxf2s))):
        E_, SL, BA = idxt[:, 0:NTI], idxt[:, NTI:2 * NTI], idxt[:, 2 * NTI:3 * NTI]
        M0, M1, TQ = (idxt[:, 3 * NTI:4 * NTI], idxt[:, 4 * NTI:5 * NTI],
                      idxt[:, 5 * NTI:6 * NTI])
        V("tensor_scalar", E_, idxf[:], capbits, None, ALU.logical_shift_right)
        V("tensor_scalar", SL, idxf[:], g.CAP - 1, None, ALU.bitwise_and)
        V("tensor_scalar", BA, E_, stbits, None, ALU.logical_shift_left)
        V("tensor_tensor", BA, BA, SL, ALU.add)
        for s in range(g.NST):
            dst = idxfs[:, s * NTI:(s + 1) * NTI]
            V("tensor_scalar", M0, SL, s * g.STILE, None, ALU.is_ge)
            V("tensor_scalar", M1, SL, (s + 1) * g.STILE, None, ALU.is_lt)
            V("tensor_tensor", M0, M0, M1, ALU.mult)
            V("tensor_scalar", TQ, BA, s * g.STILE, None, ALU.subtract)
            V("tensor_tensor", TQ, TQ, M0, ALU.mult)
            V("tensor_scalar", M0, M0, -1000000, None, ALU.mult)
            V("tensor_scalar", M0, M0, 1000000, None, ALU.add)
            V("tensor_tensor", dst, TQ, M0, ALU.add)
    ve_fld = ctx.ve_inc()
    ctx.dbg_dump("idxs_d", idxf1s[:, 0:g.NST * (TL // 128)], [128, g.NST * (TL // 128)], I32)
    ctx.dbg_dump("idxs2_d", idxf2s[:, 0:g.NST * (TL // 128)], [128, g.NST * (TL // 128)], I32)
    ctx.dbg_dump("idxb_d", idxf1[:], [128, TL // 128], I32)
    ctx.phase_barrier()
    ctx.areset(am6)

    gbbc = ctx.aalloc(C)
    bank = 6
    ctx.bank_wait(bank)
    T_("transpose", ctx.ps[bank][0:8, 0:128], dct("gb"), ident[:])
    mm_gb = ctx.mm_inc()
    ctx.wait("v", "MM", mm_gb)
    V("tensor_copy", S1[0:8, 0:128], ctx.ps[bank][0:8, 0:128])
    ve_gb1 = ctx.ve_inc()
    ctx.bank_set(bank, "VE", ve_gb1)
    ctx.wait("g", "VE", ve_gb1)
    sp_gb = ctx.sp(gbrow_d.rearrange("o (j p) -> (o j) p", p=128), S1[0:8, 0:128])
    sq_gb = ctx.sq()
    ctx.wait("y", "SQ", sq_gb)
    gbrow = stats
    ld_gb = ctx.ld(gbrow[0:1, 0:C], gbrow_d)
    ld_gb = ctx.lq()
    for h in range(2 if C > 512 else 1):
        w_ = min(512, C)
        bank = 6 + h
        ctx.bank_wait(bank)
        ctx.wait("t", "LQ", ld_gb)
        T_("matmul", ctx.ps[bank][:, 0:w_], onesrow[0:1, :], gbrow[0:1, h * 512:h * 512 + w_],
           start=True, stop=True)
        mm_gb2 = ctx.mm_inc()
        ctx.wait("v", "MM", mm_gb2)
        V("tensor_copy", gbbc[:, h * 512:h * 512 + w_], ctx.ps[bank][:, 0:w_])
        ctx.bank_set(bank, "VE", ctx.ve_inc())
    ve_gbbc = ctx.VE.n


    am7 = ctx.amark()
    GRP = 4
    NGRPF = 2
    HROT = ctx.aalloc(GRP * NGRPF * 1024)
    W1B = ctx.aalloc(NJ * g.I)
    ld_w1 = 0
    for j in range(NJ):
        ld_w1 = ctx.ld(W1B[:, j * g.I:(j + 1) * g.I], ap["w1T"][j * 128:(j + 1) * 128, :])
    ld_w1 = ctx.lq()
    sp_z = 0
    for r in range(0, g.CAP, 128):
        sp_z = ctx.sp(xbuf[r:r + 128, :], zz[:])
    bc = nc.gpsimd.alloc_register("bcreg")
    nc.gpsimd.reg_mov(bc, g.CAP - 1)
    ctx.wait("g", "SP", sp_z)
    ctx.wait("g", "VE", ve_idx)
    ctx.wait("y", "CC", ctx.CC.n)
    sqg = {}
    lqg = {}
    NG = T // 128 // GRP
    for grp in range(NG):
        if grp >= NGRPF:
            ctx.wait("y", "SQ", sqg[grp - NGRPF])
        for t in range(GRP):
            i = grp * GRP + t
            slot = i % (GRP * NGRPF)
            ctx.ld(HROT[:, slot * 1024:(slot + 1) * 1024],
                   ag2_out[i * 128:(i + 1) * 128, :])
        lqg[grp] = ctx.lq()
        ctx.wait("g", "LQ", lqg[grp])
        for t in range(GRP):
            i = grp * GRP + t
            slot = i % (GRP * NGRPF)
            nc.gpsimd.indirect_dma_start(
                xbuf, IndirectOffsetOnAxis(idx32[:, i:i + 1], 0),
                HROT[:, slot * 1024:(slot + 1) * 1024], None,
                bounds_check=bc, oob_is_err=False,
            ).then_inc(ctx.SP.h, 16)
            ctx.SP.n += 16
        sqg[grp] = ctx.sq()
    sp_scat_all = ctx.SQ.n

    W2B = ctx.aalloc(2 * 1024)
    XTOK = ctx.aalloc(2 * 1024)
    XFM = ctx.aalloc(NJ * g.STILE)
    INTB = ctx.aalloc(g.NIO * g.STILE)
    sp_e = 0
    sq_e = 0
    for stile in range(g.NST):
        sbase = stile * g.STILE
        ctx.wait("y", "SQ", sp_scat_all)
        lq_xt = {}
        mm_xt = {}
        for ss in range(min(2, g.NSS)):
            ctx.ld(XTOK[:, ss * 1024:(ss + 1) * 1024],
                   xbuf[sbase + ss * 128:sbase + (ss + 1) * 128, :])
            lq_xt[ss] = ctx.lq()
        for ss in range(g.NSS):
            b0 = 4 + (ss % 2) * 2
            ctx.bank_wait(b0)
            ctx.bank_wait(b0 + 1)
            ctx.wait("t", "LQ", lq_xt[ss])
            if stile > 0 and ss == 0:
                ctx.wait("t", "MM", mm_w1_prev)
            for j in range(NJ):
                bank = b0 + j // 4
                T_("transpose", ctx.ps[bank][:, (j % 4) * 128:(j % 4 + 1) * 128],
                   XTOK[:, (ss % 2) * 1024 + j * 128:(ss % 2) * 1024 + (j + 1) * 128],
                   ident[:])
            mm_xt[ss] = ctx.mm_inc()
            if ss + 2 < g.NSS:
                ctx.wait("y", "MM", mm_xt[ss])
                ctx.ld(XTOK[:, (ss % 2) * 1024:(ss % 2) * 1024 + 1024],
                       xbuf[sbase + (ss + 2) * 128:sbase + (ss + 3) * 128, :])
                lq_xt[ss + 2] = ctx.lq()
            ctx.wait("v", "MM", mm_xt[ss])
            for j in range(NJ):
                bank = b0 + j // 4
                V("tensor_copy", XFM[:, j * g.STILE + ss * 128:j * g.STILE + (ss + 1) * 128],
                  ctx.ps[bank][:, (j % 4) * 128:(j % 4 + 1) * 128])
            ve_x = ctx.ve_inc()
            ctx.bank_set(b0, "VE", ve_x)
            ctx.bank_set(b0 + 1, "VE", ve_x)
        ve_xfm = ctx.VE.n
        if stile > 0:
            ctx.wait("s", "MM", mm_e_prev)
        for io in range(g.NIO):
            bank = io % 2
            ctx.bank_wait(bank)
            ctx.wait("t", "LQ", ld_w1)
            ctx.wait("t", "VE", ve_xfm)
            for cin in range(NJ):
                T_("matmul", ctx.ps[bank][:, 0:g.STILE],
                   W1B[:, cin * g.I + io * 128:cin * g.I + (io + 1) * 128],
                   XFM[:, cin * g.STILE:(cin + 1) * g.STILE],
                   start=(cin == 0), stop=(cin == NJ - 1))
            mm1 = ctx.mm_inc()
            ctx.wait("s", "MM", mm1)
            S("activation", INTB[:, io * g.STILE:(io + 1) * g.STILE],
              ctx.ps[bank][:, 0:g.STILE], AF.Relu)
            ctx.bank_set(bank, "SE", ctx.se_inc())
        se_int = ctx.SE.n
        mm_w1_prev = ctx.MM.n
        nh = 2 if C > 512 else 1
        ld_w2 = {}
        mm_e = {}
        for io in range(min(2, g.NIO)):
            ld_w2[io] = ctx.ld(W2B[:, (io % 2) * 1024:(io % 2) * 1024 + 1024],
                               ap["w2T"][io * 128:(io + 1) * 128, :])
            ld_w2[io] = ctx.lq()
        for io in range(g.NIO):
            ctx.wait("t", "LQ", ld_w2[io])
            ctx.wait("t", "SE", se_int)
            if io == 0:
                for b in range(g.NSS * nh):
                    ctx.bank_wait(b)
            for ss in range(g.NSS):
                for h in range(nh):
                    bank = ss * nh + h
                    T_("matmul", ctx.ps[bank][:, 0:512],
                       INTB[:, io * g.STILE + ss * 128:io * g.STILE + (ss + 1) * 128],
                       W2B[:, (io % 2) * 1024 + h * 512:(io % 2) * 1024 + (h + 1) * 512],
                       start=(io == 0), stop=(io == g.NIO - 1))
            mm_e[io] = ctx.mm_inc()
            if io + 2 < g.NIO:
                ctx.wait("y", "MM", mm_e[io])
                ld_w2[io + 2] = ctx.ld(W2B[:, (io % 2) * 1024:(io % 2) * 1024 + 1024],
                                       ap["w2T"][(io + 2) * 128:(io + 3) * 128, :])
                ld_w2[io + 2] = ctx.lq()
        mm_e_prev = mm_e[g.NIO - 1]
        ctx.wait("v", "MM", mm_e_prev)
        sq_ss = {}
        for ss in range(g.NSS):
            pair = (S1, S2) if ss % 2 == 0 else (S3, S4)
            if ss >= 2:
                ctx.wait("v", "SQ", sq_ss[ss - 2])
            elif stile > 0:
                ctx.wait("v", "SQ", sq_e)
            for h in range(nh):
                bank = ss * nh + h
                V("tensor_copy", pair[h][:, 0:512], ctx.ps[bank][:, 0:512])
                ctx.bank_set(bank, "VE", ctx.VE.n + 1)
            ve_ss = ctx.ve_inc()
            ctx.wait("g", "VE", ve_ss)
            for h in range(nh):
                sp_e = ctx.sp(ag3_in[sbase + ss * 128:sbase + (ss + 1) * 128,
                                     h * 512:h * 512 + 512], pair[h][:, 0:512])
                if "eout_dbg" in ctx.dbg_set:
                    sp_e = ctx.sp(eout_dbg[sbase + ss * 128:sbase + (ss + 1) * 128,
                                           h * 512:h * 512 + 512], pair[h][:, 0:512])
            sq_ss[ss] = ctx.sq()
        sq_e = ctx.SQ.n
        ctx.allgather(ag3_in[sbase:sbase + g.STILE, :], ag3c_out[stile])
    cc_ag3 = ctx.CC.n
    ctx.phase_barrier()
    ctx.areset(am7)

    am8 = ctx.amark()
    R1S = ctx.aalloc(2 * g.NST * 1024)
    R2S = ctx.aalloc(2 * g.NST * 1024)
    R1 = ctx.aalloc(2 * 1024)
    R2 = ctx.aalloc(2 * 1024)
    XMB = ctx.aalloc(2 * 1024)
    YB = ctx.aalloc(2 * 1024)
    sp_y = 0
    ve_y = {}
    sp_y_hist = {}
    bc3 = nc.gpsimd.alloc_register("bc3reg")
    nc.gpsimd.reg_mov(bc3, 8 * g.STILE - 1)
    for i in range(TL // 128):
        par = i % 2
        for s in range(g.NST):
            V("memset", R1S[:, (par * g.NST + s) * 1024:(par * g.NST + s + 1) * 1024], 0.0)
            V("memset", R2S[:, (par * g.NST + s) * 1024:(par * g.NST + s + 1) * 1024], 0.0)
        ve_z = ctx.ve_inc()
        ctx.wait("g", "CC", cc_ag3)
        ctx.wait("g", "VE", ve_fld)
        ctx.wait("g", "VE", ve_z)
        for s in range(g.NST):
            nc.gpsimd.indirect_dma_start(
                R1S[:, (par * g.NST + s) * 1024:(par * g.NST + s + 1) * 1024], None,
                ag3c_out[s], IndirectOffsetOnAxis(idxf1s[:, s * (TL // 128) + i:
                                                         s * (TL // 128) + i + 1], 0),
                bounds_check=bc3, oob_is_err=False,
            ).then_inc(ctx.SP.h, 16)
            ctx.SP.n += 16
            nc.gpsimd.indirect_dma_start(
                R2S[:, (par * g.NST + s) * 1024:(par * g.NST + s + 1) * 1024], None,
                ag3c_out[s], IndirectOffsetOnAxis(idxf2s[:, s * (TL // 128) + i:
                                                         s * (TL // 128) + i + 1], 0),
                bounds_check=bc3, oob_is_err=False,
            ).then_inc(ctx.SP.h, 16)
            ctx.SP.n += 16
        sq_g = ctx.sq()
        if i >= 2:
            ctx.wait("y", "VE", ve_y[i - 2])
        ld_xm = ctx.ld(XMB[:, par * 1024:(par + 1) * 1024],
                       xmid_sp[i * 128:(i + 1) * 128, :])
        ld_xm = ctx.lq()
        ctx.wait("v", "SQ", sq_g)
        ctx.wait("v", "LQ", ld_xm)
        if i >= 2:
            ctx.wait("v", "SQ", sp_y_hist[i - 2])
        for RS, RD_ in ((R1S, R1), (R2S, R2)):
            b0 = par * g.NST
            if g.NST == 1:
                V("tensor_copy", RD_[:, par * 1024:(par + 1) * 1024],
                  RS[:, b0 * 1024:(b0 + 1) * 1024])
            else:
                V("tensor_add", RD_[:, par * 1024:(par + 1) * 1024],
                  RS[:, b0 * 1024:(b0 + 1) * 1024],
                  RS[:, (b0 + 1) * 1024:(b0 + 2) * 1024])
                for s in range(2, g.NST):
                    V("tensor_add", RD_[:, par * 1024:(par + 1) * 1024],
                      RD_[:, par * 1024:(par + 1) * 1024],
                      RS[:, (b0 + s) * 1024:(b0 + s + 1) * 1024])
        rb = routebuf[:, i * 18:(i + 1) * 18]
        V("tensor_mul", SB[:, 0:1], rb[:, 16:17], fld32[:, i:i + 1])
        V("tensor_mul", SB[:, 1:2], rb[:, 17:18], fld32[:, 32 + i:33 + i])
        V("tensor_single_scalar", YB[:, par * 1024:(par + 1) * 1024],
          R1[:, par * 1024:(par + 1) * 1024], SB[:, 0:1], ALU.mult)
        V("scalar_tensor_tensor", YB[:, par * 1024:(par + 1) * 1024],
          R2[:, par * 1024:(par + 1) * 1024], SB[:, 1:2],
          YB[:, par * 1024:(par + 1) * 1024], ALU.mult, ALU.add)
        ctx.wait("v", "VE", ve_gbbc)
        V("tensor_mul", YB[:, par * 1024:(par + 1) * 1024],
          YB[:, par * 1024:(par + 1) * 1024], gbbc[:])
        V("tensor_add", YB[:, par * 1024:(par + 1) * 1024],
          YB[:, par * 1024:(par + 1) * 1024], XMB[:, par * 1024:(par + 1) * 1024])
        ve_y[i] = ctx.ve_inc()
        ctx.wait("g", "VE", ve_y[i])
        sp_y = ctx.sp(y_out[i * 128:(i + 1) * 128, :], YB[:, par * 1024:(par + 1) * 1024])
        sp_y_hist[i] = ctx.sq()
    ctx.wait("g", "SP", sp_y)
    ctx.phase_barrier()
    ctx.areset(am8)
    return nc, ctx



def _vec_tile(v):
    return np.ascontiguousarray(np.asarray(v, np.float32).reshape(8, 128).T)


def make_inmaps(g: Geo, inputs):
    x = np.asarray(inputs["L0_attn_x"], np.float32).reshape(g.T, g.C)
    vecs = np.stack([_vec_tile(inputs[k]) for k in (
        "L0_attn_ln1_w", "L0_attn_ln1_b", "L0_attn_ln2_w", "L0_attn_ln2_b",
        "L0_attn_time_decay", "L0_attn_time_first", "L0_attn_tmk", "L0_attn_tmv",
        "L0_attn_tmr", "L0_attn_f_tmk", "L0_attn_f_tmr")], 0)
    shared = {
        "kwT": np.ascontiguousarray(np.asarray(inputs["L0_attn_att_kw"], np.float32).T),
        "vwT": np.ascontiguousarray(np.asarray(inputs["L0_attn_att_vw"], np.float32).T),
        "rwT": np.ascontiguousarray(np.asarray(inputs["L0_attn_att_rw"], np.float32).T),
        "owT": np.ascontiguousarray(np.asarray(inputs["L0_attn_att_ow"], np.float32).T),
        "frwT": np.ascontiguousarray(np.asarray(inputs["L0_attn_f_rw"], np.float32).T),
        "fkwT": np.ascontiguousarray(np.asarray(inputs["L0_attn_f_kw"], np.float32).T),
        "fvwT": np.ascontiguousarray(np.asarray(inputs["L0_attn_f_vw"], np.float32).T),
        "embT": np.ascontiguousarray(np.asarray(inputs["L0_ffn_expert_embeddings"], np.float32).T),
        "wmod": np.ascontiguousarray(np.asarray(inputs["L0_wmod"], np.float32)),
        "cmod": _vec_tile(np.asarray(inputs["L0_c"], np.float32).reshape(g.C)),
        "vecs": vecs,
        "ident": np.eye(128, dtype=np.float32),
        "iotarev": np.ascontiguousarray(np.broadcast_to(
            np.arange(g.TC - 1, -1, -1, dtype=np.float32), (128, g.TC))),
        "oh8id": np.eye(8, dtype=np.float32),
        "ecap": (np.arange(8, dtype=np.float32) * g.CAP - 1.0).reshape(8, 1),
    }
    w1 = np.asarray(inputs["L0_ffn_first_linear"], np.float32)
    w2 = np.asarray(inputs["L0_ffn_second_linear"], np.float32)
    maps = []
    for i in range(g.NCORES):
        seg = x[i * g.TL:(i + 1) * g.TL]
        bnd = np.zeros((1, g.C), np.float32) if i == 0 else x[i * g.TL - 1:i * g.TL]
        xT = np.ascontiguousarray(np.concatenate([bnd, seg], 0).T)
        masklt = (np.arange(8) < i).astype(np.float32)
        m = dict(shared)
        m.update({
            "xT": xT,
            "w1T": np.ascontiguousarray(w1[i].T),
            "w2T": np.ascontiguousarray(w2[i].T),
            "ohmy": np.eye(8, dtype=np.float32)[:, i:i + 1].copy(),
            "ohs": (np.eye(8, dtype=np.float32)[:, i:i + 1] *
                    (-(i * g.CAP + g.BIG))).astype(np.float32),
            "masklt8": np.tile(masklt[None, :], (8, 1)).astype(np.float32),
            "mneg8": ((1.0 - np.tile(masklt[None, :], (8, 1))) * NEG).astype(np.float32),
            "tlm8": (np.tile(masklt[None, :], (8, 1)) * g.TL).astype(np.float32),
            "mprev8": np.tile((np.arange(8) == i - 1).astype(np.float32)[None, :], (8, 1)).copy(),
            "ohown": np.tile((np.arange(8) == i).astype(np.float32)[None, :], (8, 1)).copy(),
            "maskfirst": np.full((8, 1), 0.0 if i == 0 else 1.0, np.float32),
            "maskfirstP": np.full((128, 1), 0.0 if i == 0 else 1.0, np.float32),
            "xbnd_in": _vec_tile(bnd[0]),
        })
        maps.append(m)
    return maps


def kernel(**inputs):
    g = Geo(16384, 512)
    nc, ctx = build_program(g)
    maps = make_inmaps(g, inputs)
    res = run_bass_kernel_spmd(nc, maps, core_ids=list(range(g.NCORES)))
    y = np.concatenate([res.results[i]["y"] for i in range(g.NCORES)], 0)
    return y.reshape(1, g.T, g.C).astype(np.float32)
